# version 09
# baseline (speedup 1.0000x reference)
import numpy as np
from contextlib import ExitStack

import concourse.bass as bass
import concourse.mybir as mybir
from concourse.bass_utils import run_bass_kernel_spmd

F32 = mybir.dt.float32
F16 = mybir.dt.float16
AF = mybir.ActivationFunctionType
ALU = mybir.AluOpType
AX = mybir.AxisListType

DM = 2048
DI = 4096
DS = 128
NH = 64
HD = 64
CD = DI + 2 * DS
DIP = 2 * DI + 2 * DS + NH
HID = 8192
EPS = 1e-5
HALO = 3
KC = DM // 128


def APx(t, offset, dims):
    return bass.AP(t, offset, [list(d) for d in dims])


class Sem:
    __slots__ = ("h", "v", "name")


class KB:
    def __init__(self, nc, ctx, tok):
        self.nc = nc
        self.ctx = ctx
        self.TOK = tok
        self.NCH = tok // 128
        self.NG = tok // 512
        self.PADT = ((tok + HALO + 127) // 128) * 128
        self.blk = ctx.enter_context(nc.Block())
        self._waited = {}
        self.sems = {}

    def sem(self, name):
        s = Sem()
        s.h = self.ctx.enter_context(self.nc.semaphore(name))
        s.v = 0
        s.name = name
        self.sems[name] = s
        return s

    def V(self, fn):
        self.blk.vector(lambda e: fn(e, "v"))

    def A(self, fn):
        self.blk.scalar(lambda e: fn(e, "a"))

    def T(self, fn):
        self.blk.tensor(lambda e: fn(e, "t"))

    def G(self, fn):
        self.blk.gpsimd(lambda e: fn(e, "g"))

    def S(self, fn):
        self.blk.sync(lambda e: fn(e, "s"))

    def w(self, e, en, sem, val=None):
        val = sem.v if val is None else val
        if val <= 0:
            return
        k = (en, sem.name)
        if self._waited.get(k, -1) >= val:
            return
        e.wait_ge(sem.h, val)
        self._waited[k] = val

    def i(self, inst, sem, n=1):
        inst.then_inc(sem.h, n)
        sem.v += n
        return sem.v

    def barrier(self):
        snap = [(s, s.v) for s in self.sems.values() if s.v > 0]

        def mk(emit, en):
            def f(e, _en):
                for s, v in snap:
                    self.w(e, en, s, v)
            emit(f)
        mk(self.V, "v")
        mk(self.A, "a")
        mk(self.T, "t")
        mk(self.G, "g")
        mk(self.S, "s")


def build_nc(tok, ncores):
    nc = bass.Bass("TRN2", target_bir_lowering=False, debug=False)
    TOK = tok
    NCH = tok // 128
    NG = tok // 512
    assert tok % 512 == 0

    u_in = nc.dram_tensor("u", [TOK + HALO, DM], F32, kind="ExternalInput")
    ipw = nc.dram_tensor("ipw", [128, 16 * DIP], F32, kind="ExternalInput")
    cwT = nc.dram_tensor("cwT", [4, CD], F32, kind="ExternalInput")
    cb_r = nc.dram_tensor("cb", [34, 128], F32, kind="ExternalInput")
    dtb = nc.dram_tensor("dtb", [NH, 1], F32, kind="ExternalInput")
    alog = nc.dram_tensor("alog", [NH, 1], F32, kind="ExternalInput")
    dvec = nc.dram_tensor("dvec", [1, NH], F32, kind="ExternalInput")
    nwr = nc.dram_tensor("nw", [1, DI], F32, kind="ExternalInput")
    opw = nc.dram_tensor("opw", [128, 16 * 32 * 128], F32, kind="ExternalInput")
    w0 = nc.dram_tensor("w0", [128, 64 * 16 * 128], F32, kind="ExternalInput")
    b0_r = nc.dram_tensor("b0", [HID // 128, 128], F32, kind="ExternalInput")
    w1 = nc.dram_tensor("w1", [128, 16 * 64 * 128], F32, kind="ExternalInput")
    b1_r = nc.dram_tensor("b1", [DM // 128, 128], F32, kind="ExternalInput")
    flag_in = nc.dram_tensor("flag", [128, 1], F32, kind="ExternalInput")
    out = nc.dram_tensor("out", [TOK, DM], F32, kind="ExternalOutput")

    PADT = ((TOK + HALO + 127) // 128) * 128
    dr_hln = nc.dram_tensor("dr_hln", [PADT, DM], F16, kind="Internal")
    dr_z = nc.dram_tensor("dr_z", [DI, TOK], F16, kind="Internal")
    dr_xbc = nc.dram_tensor("dr_xbc", [CD, TOK], F16, kind="Internal")
    dr_ypre = nc.dram_tensor("dr_ypre", [TOK, DI], F16, kind="Internal")
    dr_yn = nc.dram_tensor("dr_yn", [TOK, DI], F16, kind="Internal")
    dr_attn = nc.dram_tensor("dr_attn", [DM, TOK], F16, kind="Internal")
    dr_x1 = nc.dram_tensor("dr_x1", [TOK, DM], F32, kind="Internal")
    dr_h2 = nc.dram_tensor("dr_h2", [TOK, DM], F16, kind="Internal")
    dr_g = nc.dram_tensor("dr_g", [HID, TOK], F16, kind="Internal")
    dr_efm = nc.dram_tensor("dr_efm", [DM, TOK], F16, kind="Internal")
    dr_cci = nc.dram_tensor("dr_cci", [128, DI], F32, kind="Internal")
    dr_cco = nc.dram_tensor("dr_cco", [128, DI], F32, kind="Internal")

    with ExitStack() as ctx:
        kb = KB(nc, ctx, tok)
        E = ctx.enter_context
        V, A, T, G, S = kb.V, kb.A, kb.T, kb.G, kb.S
        w, i = kb.w, kb.i

        ones16 = E(nc.sbuf_tensor("ones16", [128, 128], F16))
        ones32 = E(nc.sbuf_tensor("ones32", [1, 128], F32))
        ones2 = E(nc.sbuf_tensor("ones2", [2, 128], F16))
        idn32 = E(nc.sbuf_tensor("idn32", [128, 128], F32))
        b0_col = E(nc.sbuf_tensor("b0_col", [128, HID // 128], F32))
        b1_col = E(nc.sbuf_tensor("b1_col", [128, DM // 128], F32))
        flag_c = E(nc.sbuf_tensor("flag_c", [128, 1], F32))
        eps_c = E(nc.sbuf_tensor("eps_c", [128, 1], F32))
        one_c = E(nc.sbuf_tensor("one_c", [128, 1], F32))

        sv = kb.sem("sv")
        sa = kb.sem("sa")
        st_ = kb.sem("st")
        sg = kb.sem("sg")
        y_set = kb.sem("y_set")
        g_cc = kb.sem("g_cc")

        scan_ctx = ExitStack()
        Es = scan_ctx.enter_context
        zcol = Es(nc.sbuf_tensor("zcol", [NH, 1], F32))
        dtb_c = Es(nc.sbuf_tensor("dtb_c", [NH, 1], F32))
        alog_c = Es(nc.sbuf_tensor("alog_c", [NH, 1], F32))
        nA_col = Es(nc.sbuf_tensor("nA_col", [NH, 1], F32))
        D_nb = Es(nc.sbuf_tensor("D_nb", [128, NH], F16))
        nw_bc = Es(nc.sbuf_tensor("nw_bc", [128, DI], F16))
        convw_c = Es(nc.sbuf_tensor("convw_c", [128, 34 * 4], F32))
        convb_c = Es(nc.sbuf_tensor("convb_c", [128, 34], F32))
        dt_t = Es(nc.sbuf_tensor("dt_t", [NH, TOK], F32))
        dA = Es(nc.sbuf_tensor("dA", [NH, TOK], F32))
        dA_cum = Es(nc.sbuf_tensor("dA_cum", [NH, TOK], F32))
        dA_last = Es(nc.sbuf_tensor("dA_last", [NH, NCH], F32))
        cd_t = Es(nc.sbuf_tensor("cd_t", [NH, NCH], F32))
        G_incl = Es(nc.sbuf_tensor("G_incl", [NH, NCH], F32))
        G_pre = Es(nc.sbuf_tensor("G_pre", [NH, NCH], F32))
        carry16 = Es(nc.sbuf_tensor("carry16", [128, DI], F16))
        sum32 = Es(nc.sbuf_tensor("sum32", [128, DI], F32))
        patch16 = Es(nc.sbuf_tensor("patch16", [128, DI], F16))

        with ExitStack() as pctx:
            pS = pctx.enter_context(nc.psum_tensor("ps_set", [128, 512], F32))
            pS2 = pctx.enter_context(nc.psum_tensor("ps_set2", [128, 64], F32))
            drow = pctx.enter_context(nc.sbuf_tensor("drow", [1, NH], F32))
            d16row = pctx.enter_context(nc.sbuf_tensor("d16row", [1, NH], F16))
            nwrow = pctx.enter_context(nc.sbuf_tensor("nwrow", [1, DI], F32))
            nwrow16 = pctx.enter_context(nc.sbuf_tensor("nwrow16", [1, DI], F16))
            cwT_s = pctx.enter_context(nc.sbuf_tensor("cwT_s", [4, CD], F32))
            cb_s = pctx.enter_context(nc.sbuf_tensor("cb_s", [34, 128], F32))

            def s0(e, en):
                for dst, src in [
                    (dtb_c[:, :], dtb.ap()), (alog_c[:, :], alog.ap()),
                    (drow[:, :], dvec.ap()), (nwrow[:, :], nwr.ap()),
                    (cwT_s[:, :], cwT.ap()), (cb_s[:, :], cb_r.ap()),
                    (flag_c[:, :], flag_in.ap()),
                ]:
                    i(e.dma_start(out=dst, in_=src), y_set, 16)
            S(s0)

            b0_s = pctx.enter_context(nc.sbuf_tensor("b0_s", [HID // 128, 128], F32))
            b1_s = pctx.enter_context(nc.sbuf_tensor("b1_s", [DM // 128, 128], F32))

            def s0b(e, en):
                i(e.dma_start(out=b0_s[:, :], in_=b0_r.ap()), y_set, 16)
                i(e.dma_start(out=b1_s[:, :], in_=b1_r.ap()), y_set, 16)
            S(s0b)
            y_set_all = y_set.v

            def v0(e, en):
                i(e.memset(ones16[:, :], 1.0), sv)
                i(e.memset(ones32[:, :], 1.0), sv)
                i(e.memset(ones2[:, :], 1.0), sv)
                i(e.memset(zcol[:, :], 0.0), sv)
                i(e.memset(eps_c[:, :], EPS), sv)
                i(e.memset(one_c[:, :], 1.0), sv)
                w(e, en, y_set, y_set_all)
                i(e.tensor_copy(d16row[:, :], drow[:, :]), sv)
                i(e.tensor_copy(nwrow16[:, :], nwrow[:, :]), sv)
            V(v0)
            tok_v0 = sv.v

            def g0(e, en):
                i(e.memset(idn32[:, :], 1.0), sg)
                w(e, en, sg, sg.v)
                i(e.affine_select(idn32[:, :], idn32[:, :], [[1, 128]],
                                  ALU.is_equal, 0.0, base=0,
                                  channel_multiplier=-1), sg)
            G(g0)
            tok_idn = sg.v

            def a0(e, en):
                w(e, en, y_set, y_set_all)
                i(e.activation(nA_col[:, :], alog_c[:, :], AF.Exp), sa)
            A(a0)
            tok_a0 = sa.v

            def v1(e, en):
                w(e, en, sa, tok_a0)
                i(e.tensor_scalar_mul(nA_col[:, :], nA_col[:, :], -1.0), sv)
            V(v1)

            def t0(e, en):
                w(e, en, sg, tok_idn)
                w(e, en, y_set, y_set_all)
                i(e.transpose(pS[:, 0:34], cb_s[:, :], idn32[0:34, 0:34]), st_)
                i(e.transpose(pS[:, 64:64 + HID // 128], b0_s[:, :], idn32[0:64, 0:64]), st_)
                i(e.transpose(pS[:, 192:192 + DM // 128], b1_s[:, :], idn32[0:16, 0:16]), st_)
            T(t0)
            tok_t0 = st_.v

            def v2(e, en):
                w(e, en, st_, tok_t0)
                i(e.tensor_copy(convb_c[:, :], pS[:, 0:34]), sv)
                i(e.tensor_copy(b0_col[:, :], pS[:, 64:64 + HID // 128]), sv)
                i(e.tensor_copy(b1_col[:, :], pS[:, 192:192 + DM // 128]), sv)
            V(v2)
            tok_v2 = sv.v

            def t1(e, en):
                w(e, en, sv, tok_v2)
                for mt in range(34):
                    i(e.transpose(pS[:, mt * 4:mt * 4 + 4],
                                  cwT_s[:, mt * 128:(mt + 1) * 128], idn32[0:4, 0:4]), st_)
            T(t1)
            tok_t1 = st_.v

            def v3(e, en):
                w(e, en, st_, tok_t1)
                i(e.tensor_copy(convw_c[:, :], pS[:, 0:136]), sv)
            V(v3)
            tok_v3 = sv.v

            def t2(e, en):
                w(e, en, sv, tok_v0)
                i(e.matmul(pS2[:, 0:NH], ones16[0:1, :], d16row[0:1, :],
                           start=True, stop=True), st_)
            T(t2)
            tok_t2 = st_.v

            def v4(e, en):
                w(e, en, st_, tok_t2)
                i(e.tensor_copy(D_nb[:, :], pS2[:, 0:NH]), sv)
            V(v4)

            for q in range(DI // 512):
                def tq(e, en, q=q, gate=sv.v):
                    w(e, en, sv, gate)
                    i(e.matmul(pS[:, 0:512], ones16[0:1, :],
                               nwrow16[0:1, q * 512:(q + 1) * 512],
                               start=True, stop=True), st_)
                T(tq)

                def vq(e, en, q=q, gate=st_.v):
                    w(e, en, st_, gate)
                    i(e.tensor_copy(nw_bc[:, q * 512:(q + 1) * 512], pS[:, 0:512]), sv)
                V(vq)
            kb.barrier()

        with ExitStack() as pctx:
            pe = pctx.enter_context
            u_t = [pe(nc.sbuf_tensor(f"u_t{p}", [128, DM], F32)) for p in range(2)]
            usq = [pe(nc.sbuf_tensor(f"usq{p}", [128, DM], F32)) for p in range(2)]
            hlt = [pe(nc.sbuf_tensor(f"hlt{p}", [128, DM], F16)) for p in range(2)]
            msum = [pe(nc.sbuf_tensor(f"msum{p}", [128, 1], F32)) for p in range(2)]
            mneg = [pe(nc.sbuf_tensor(f"mneg{p}", [128, 1], F32)) for p in range(2)]
            qsum = [pe(nc.sbuf_tensor(f"qsum{p}", [128, 1], F32)) for p in range(2)]
            q2 = [pe(nc.sbuf_tensor(f"q2{p}", [128, 1], F32)) for p in range(2)]
            nvar = [pe(nc.sbuf_tensor(f"nvar{p}", [128, 1], F32)) for p in range(2)]
            rstd = [pe(nc.sbuf_tensor(f"rstd{p}", [128, 1], F32)) for p in range(2)]
            mr = [pe(nc.sbuf_tensor(f"mr{p}", [128, 1], F32)) for p in range(2)]
            y_u = [kb.sem(f"y_u{p}") for p in range(2)]
            y_hl = [kb.sem(f"y_hl{p}") for p in range(2)]

            nrows = TOK + HALO
            tiles = [(r0, min(128, nrows - r0)) for r0 in range(0, nrows, 128)]
            npad = PADT - nrows
            if npad > 0:
                def vz(e, en):
                    i(e.memset(hlt[0][0:npad, :], 0.0), sv)
                V(vz)
                tok_z0 = sv.v

                def sz(e, en, tok_z0=tok_z0):
                    w(e, en, sv, tok_z0)
                    i(e.dma_start(out=dr_hln.ap()[nrows:PADT, :],
                                  in_=hlt[0][0:npad, :]), y_hl[0], 16)
                S(sz)
            last_apply = [y_hl[0].v * 0, 0]
            last_sq = [0, 0]
            for idx, (r0, n) in enumerate(tiles):
                p = idx % 2

                def s1(e, en, r0=r0, n=n, p=p, gate=last_apply[p]):
                    w(e, en, sv, gate)
                    i(e.dma_start(out=u_t[p][0:n, :],
                                  in_=u_in.ap()[r0:r0 + n, :]), y_u[p], 16)
                S(s1)
                tok_u = y_u[p].v

                def v6(e, en, n=n, p=p, tok_u=tok_u):
                    w(e, en, y_u[p], tok_u)
                    i(e.reduce_sum(msum[p][0:n, :], u_t[p][0:n, :], AX.X), sv)
                    w(e, en, sv, sv.v)
                    i(e.tensor_scalar_mul(mneg[p][0:n, :], msum[p][0:n, :], -1.0 / DM), sv)
                V(v6)
                tok_mn = sv.v

                def g1a(e, en, n=n, p=p, tok_u=tok_u, gate=last_sq[p]):
                    w(e, en, y_u[p], tok_u)
                    w(e, en, sv, gate)
                    i(e.tensor_tensor(usq[p][0:n, :], u_t[p][0:n, :],
                                      u_t[p][0:n, :], ALU.mult), sg)
                G(g1a)
                tok_q0 = sg.v

                def v6b(e, en, n=n, p=p, tok_q0=tok_q0):
                    w(e, en, sg, tok_q0)
                    i(e.reduce_sum(qsum[p][0:n, :], usq[p][0:n, :], AX.X), sv)
                V(v6b)
                tok_q = sv.v

                def v7(e, en, n=n, p=p, tok_q=tok_q, tok_mn=tok_mn):
                    w(e, en, sv, tok_q)
                    w(e, en, sv, tok_mn)
                    i(e.tensor_scalar_mul(q2[p][0:n, :], qsum[p][0:n, :], 1.0 / DM), sv)
                    w(e, en, sv, sv.v)
                    i(e.scalar_tensor_tensor(nvar[p][0:n, :], mneg[p][0:n, :],
                                             mneg[p][0:n, 0:1], q2[p][0:n, :],
                                             ALU.mult, ALU.subtract), sv)
                V(v7)
                tok_nv = sv.v
                last_sq[p] = tok_nv

                def a2(e, en, n=n, p=p, tok_nv=tok_nv):
                    w(e, en, sv, tok_nv)
                    i(e.activation(rstd[p][0:n, :], nvar[p][0:n, :], AF.Ln,
                                   bias=eps_c[0:n, 0:1], scale=-1.0), sa)
                    w(e, en, sa, sa.v)
                    i(e.activation(rstd[p][0:n, :], rstd[p][0:n, :], AF.Exp,
                                   scale=-0.5), sa)
                A(a2)
                tok_rs = sa.v

                def v8(e, en, n=n, p=p, tok_rs=tok_rs):
                    w(e, en, sa, tok_rs)
                    i(e.tensor_tensor(mr[p][0:n, :], mneg[p][0:n, :],
                                      rstd[p][0:n, :], ALU.mult), sv)
                V(v8)
                tok_mr = sv.v

                def g3a(e, en, n=n, p=p, tok_mr=tok_mr, gate=y_hl[p].v):
                    w(e, en, sv, tok_mr)
                    w(e, en, y_hl[p], gate)
                    i(e.tensor_scalar(hlt[p][0:n, :], u_t[p][0:n, :],
                                      mneg[p][0:n, 0:1], rstd[p][0:n, 0:1],
                                      ALU.add, ALU.mult), sv)
                V(g3a)
                last_apply[p] = sv.v
                tok_ap = sv.v

                def s2(e, en, r0=r0, n=n, p=p, tok_ap=tok_ap):
                    w(e, en, sv, tok_ap)
                    i(e.dma_start(out=dr_hln.ap()[r0:r0 + n, :],
                                  in_=hlt[p][0:n, :]), y_hl[p], 16)
                S(s2)
            kb.barrier()

        with ExitStack() as pctx:
            pe = pctx.enter_context
            hfm = pe(nc.sbuf_tensor("hfm", [128, KC * PADT], F16))
            w_in = [pe(nc.sbuf_tensor(f"w_in{p}", [128, KC * 128], F16)) for p in range(2)]
            strip = [pe(nc.sbuf_tensor(f"strip{p}", [128, HALO + TOK], F16)) for p in range(2)]
            conva = [pe(nc.sbuf_tensor(f"conva{p}", [128, TOK], F16)) for p in range(2)]
            convd = [pe(nc.sbuf_tensor(f"convd{p}", [128, TOK], F16)) for p in range(2)]
            xstg = [pe(nc.sbuf_tensor(f"xstg{p}", [128, TOK], F16)) for p in range(2)]
            zstg = [pe(nc.sbuf_tensor(f"zstg{p}", [128, TOK], F16)) for p in range(2)]
            psA = [pe(nc.psum_tensor(f"psA{p}", [128, 4 * 512], F32)) for p in range(2)]
            y_fm = kb.sem("y_fm")
            g_w = [kb.sem(f"g_w{p}") for p in range(2)]
            y_xbc = [kb.sem(f"y_xbc{p}") for p in range(2)]
            y_z = [kb.sem(f"y_z{p}") for p in range(2)]

            def s3(e, en):
                for kc in range(KC):
                    i(e.dma_start_transpose(
                        out=hfm[:, kc * PADT:kc * PADT + PADT],
                        in_=dr_hln.ap()[:, kc * 128:(kc + 1) * 128]), y_fm, 16)
            S(s3)
            y_fm_all = y_fm.v

            m_order = [(DI + CD, 64, "dt")] + \
                      [(DI + 128 * j, 128, "xbc") for j in range(34)] + \
                      [(128 * j, 128, "z") for j in range(32)]
            wt_last = [0, 0]
            ev_tok = [[0] * 4, [0] * 4]
            xbc_stage = [0, 0]
            z_stage = [0, 0]
            conv_sa = [0, 0]
            strip_used = [0, 0]
            dt_evac_tok = 0

            for mi, (f0, mw, kind) in enumerate(m_order):
                p = mi % 2

                def g1(e, en, f0=f0, mw=mw, p=p, gate=wt_last[p]):
                    w(e, en, st_, gate)
                    t_idx = f0 // 128
                    src = APx(ipw, t_idx * 2048, [[16 * DIP, 128], [1, KC * mw]])
                    dst = APx(w_in[p], 0, [[KC * 128, 128], [128, KC], [1, mw]])
                    i(e.dma_start(out=dst, in_=src), g_w[p], 16)
                G(g1)
                tok_wl = g_w[p].v

                groups = ([("halo", 0, HALO, 0)] if kind == "xbc" else []) + \
                         [("main", HALO + g * 512, 512, g) for g in range(NG)]

                for gname, c0, cn, bank in groups:
                    def t4(e, en, mw=mw, p=p, c0=c0, cn=cn, bank=bank,
                           tok_wl=tok_wl, gate_ev=ev_tok[p][bank]):
                        w(e, en, g_w[p], tok_wl)
                        w(e, en, y_fm, y_fm_all)
                        w(e, en, sv, gate_ev)
                        for kc in range(KC):
                            i(e.matmul(psA[p][0:mw, bank * 512:bank * 512 + cn],
                                       w_in[p][:, kc * 128:kc * 128 + mw],
                                       hfm[:, kc * PADT + c0:kc * PADT + c0 + cn],
                                       start=(kc == 0), stop=(kc == KC - 1)), st_)
                    T(t4)
                    tok_mm = st_.v
                    wt_last[p] = tok_mm

                    def v9(e, en, mw=mw, p=p, c0=c0, cn=cn, bank=bank,
                           tok_mm=tok_mm, kind=kind,
                           gate_strip=strip_used[p], gate_z=z_stage[p]):
                        w(e, en, st_, tok_mm)
                        if kind == "xbc":
                            w(e, en, sv, gate_strip)
                            i(e.tensor_copy(strip[p][0:mw, c0:c0 + cn],
                                            psA[p][0:mw, bank * 512:bank * 512 + cn]), sv)
                        elif kind == "z":
                            w(e, en, y_z[p], gate_z)
                            i(e.tensor_copy(zstg[p][0:mw, c0 - HALO:c0 - HALO + cn],
                                            psA[p][0:mw, bank * 512:bank * 512 + cn]), sv)
                        else:
                            i(e.tensor_copy(dt_t[0:mw, c0 - HALO:c0 - HALO + cn],
                                            psA[p][0:mw, bank * 512:bank * 512 + cn]), sv)
                    V(v9)
                    ev_tok[p][bank] = sv.v
                tok_evac = sv.v

                if kind == "dt":
                    dt_evac_tok = tok_evac
                elif kind == "xbc":
                    mt = (f0 - DI) // 128

                    def v10(e, en, p=p, mt=mt, tok_evac=tok_evac, gate=conv_sa[p]):
                        w(e, en, sv, tok_evac)
                        w(e, en, sa, gate)
                        i(e.scalar_tensor_tensor(
                            conva[p][:, :], strip[p][:, 0:TOK],
                            convw_c[:, 4 * mt + 0:4 * mt + 1],
                            APx(convb_c, mt, [[34, 128], [0, TOK]]),
                            ALU.mult, ALU.add), sv)
                        w(e, en, sv, sv.v)
                        i(e.scalar_tensor_tensor(
                            convd[p][:, :], strip[p][:, 1:TOK + 1],
                            convw_c[:, 4 * mt + 1:4 * mt + 2],
                            conva[p][:, :], ALU.mult, ALU.add), sv)
                        w(e, en, sv, sv.v)
                        i(e.scalar_tensor_tensor(
                            conva[p][:, :], strip[p][:, 2:TOK + 2],
                            convw_c[:, 4 * mt + 2:4 * mt + 3],
                            convd[p][:, :], ALU.mult, ALU.add), sv)
                        w(e, en, sv, sv.v)
                        i(e.scalar_tensor_tensor(
                            convd[p][:, :], strip[p][:, 3:TOK + 3],
                            convw_c[:, 4 * mt + 3:4 * mt + 4],
                            conva[p][:, :], ALU.mult, ALU.add), sv)
                    V(v10)
                    tok_conv = sv.v
                    strip_used[p] = tok_conv

                    def a4(e, en, p=p, tok_conv=tok_conv, gate=xbc_stage[p]):
                        w(e, en, sv, tok_conv)
                        w(e, en, y_xbc[p], gate)
                        i(e.activation(xstg[p][:, :], convd[p][:, :], AF.Silu), sa)
                    A(a4)
                    conv_sa[p] = sa.v
                    tok_silu = sa.v

                    def s4(e, en, p=p, mt=mt, tok_silu=tok_silu):
                        w(e, en, sa, tok_silu)
                        i(e.dma_start(out=dr_xbc.ap()[mt * 128:(mt + 1) * 128, :],
                                      in_=xstg[p][:, :]), y_xbc[p], 16)
                    S(s4)
                    xbc_stage[p] = y_xbc[p].v
                else:
                    mt = f0 // 128

                    def s5(e, en, p=p, mt=mt, tok_evac=tok_evac):
                        w(e, en, sv, tok_evac)
                        i(e.dma_start(out=dr_z.ap()[mt * 128:(mt + 1) * 128, :],
                                      in_=zstg[p][:, :]), y_z[p], 16)
                    S(s5)
                    z_stage[p] = y_z[p].v

            def v11(e, en, gate=dt_evac_tok):
                w(e, en, sv, gate)
                i(e.tensor_scalar(dt_t[:, :], dt_t[:, :], dtb_c[:, 0:1], None,
                                  ALU.add), sv)
            V(v11)
            tok_abs = sv.v

            def a5(e, en, tok_abs=tok_abs):
                w(e, en, sv, tok_abs)
                i(e.activation(dA[:, :], dt_t[:, :], AF.Abs), sa)
                w(e, en, sa, sa.v)
                i(e.activation(dA_cum[:, :], dA[:, :], AF.Exp, scale=-1.0), sa)
                w(e, en, sa, sa.v)
                i(e.activation(dA_cum[:, :], dA_cum[:, :], AF.Ln, bias=one_c[0:NH, 0:1]), sa)
            A(a5)
            tok_ln1p = sa.v

            def v12(e, en, tok_ln1p=tok_ln1p):
                w(e, en, sa, tok_ln1p)
                i(e.scalar_tensor_tensor(dt_t[:, :], dt_t[:, :], 0.0,
                                         dA_cum[:, :], ALU.max, ALU.add), sv)
                w(e, en, sv, sv.v)
                i(e.tensor_scalar(dA[:, :], dt_t[:, :], nA_col[:, 0:1], None,
                                  ALU.mult), sv)
                w(e, en, sv, sv.v)
                for z in range(NCH):
                    i(e.tensor_tensor_scan(
                        dA_cum[:, z * 128:(z + 1) * 128],
                        dA[:, z * 128:(z + 1) * 128],
                        APx(zcol, 0, [[1, NH], [0, 128]]),
                        0.0, ALU.add, ALU.add), sv)
                w(e, en, sv, sv.v)
                i(e.tensor_copy(dA_last[:, :],
                                APx(dA_cum, 127, [[TOK, NH], [128, NCH]])), sv)
            V(v12)
            tok_dal = sv.v

            def a6(e, en, tok_dal=tok_dal):
                w(e, en, sv, tok_dal)
                i(e.activation(cd_t[:, :], dA_last[:, :], AF.Exp), sa)
            A(a6)
            tok_cd = sa.v

            def v13(e, en, tok_cd=tok_cd):
                w(e, en, sa, tok_cd)
                i(e.tensor_tensor_scan(G_incl[:, :], cd_t[:, :],
                                       APx(zcol, 0, [[1, NH], [0, NCH]]),
                                       1.0, ALU.mult, ALU.add), sv)
                i(e.memset(G_pre[:, 0:1], 1.0), sv)
                if NCH > 1:
                    w(e, en, sv, sv.v)
                    i(e.tensor_copy(G_pre[:, 1:NCH], G_incl[:, 0:NCH - 1]), sv)
            V(v13)
            kb.barrier()

        with ExitStack() as pctx:
            pe = pctx.enter_context
            x_tok = [pe(nc.sbuf_tensor(f"x_tok{p}", [128, DI], F16)) for p in range(2)]
            Bfm = [pe(nc.sbuf_tensor(f"Bfm{p}", [128, 128], F16)) for p in range(2)]
            Cfm = [pe(nc.sbuf_tensor(f"Cfm{p}", [128, 128], F16)) for p in range(2)]
            Btok = [pe(nc.sbuf_tensor(f"Btok{p}", [128, 128], F16)) for p in range(2)]
            lhsE = [pe(nc.sbuf_tensor(f"lhsE{p}", [128, 16 * 128], F16)) for p in range(2)]
            rhsE = [pe(nc.sbuf_tensor(f"rhsE{p}", [128, 16 * 128], F16)) for p in range(2)]
            rhs2 = pe(nc.sbuf_tensor("rhs2", [2, NH * 128], F16))
            Hp16 = [pe(nc.sbuf_tensor(f"Hp16{p}", [NH, 128], F16)) for p in range(2)]
            Lp16 = [pe(nc.sbuf_tensor(f"Lp16{p}", [NH, 128], F16)) for p in range(2)]
            nH16 = [pe(nc.sbuf_tensor(f"nH16{p}", [NH, 128], F16)) for p in range(2)]
            nL16 = [pe(nc.sbuf_tensor(f"nL16{p}", [NH, 128], F16)) for p in range(2)]
            Ltmp = pe(nc.sbuf_tensor("Ltmp", [NH, 128], F32))
            M_sb = pe(nc.sbuf_tensor("M_sb", [128, NH * 128], F16))
            Mp_sb = pe(nc.sbuf_tensor("Mp_sb", [128, NH * 128], F16))
            CBT16 = pe(nc.sbuf_tensor("CBT16", [128, 128], F16))
            dtT16 = pe(nc.sbuf_tensor("dtT16", [128, NH], F16))
            decT16 = pe(nc.sbuf_tensor("decT16", [128, NH], F16))
            dec_f = pe(nc.sbuf_tensor("dec_f", [NH, 128], F32))
            cdrow = pe(nc.sbuf_tensor("cdrow", [1, NH], F32))
            cdnb = pe(nc.sbuf_tensor("cdnb", [128, NH], F32))
            Xdt = pe(nc.sbuf_tensor("Xdt", [128, DI], F16))
            Xdtd = pe(nc.sbuf_tensor("Xdtd", [128, DI], F16))
            ct1 = pe(nc.sbuf_tensor("ct1", [128, 512], F16))
            Yst = [pe(nc.sbuf_tensor(f"Yst{p}", [128, DI], F16)) for p in range(2)]
            pE = pe(nc.psum_tensor("pE", [128, 2048], F32))
            pEp = pe(nc.psum_tensor("pEp", [128, 1024], F32))
            pY = pe(nc.psum_tensor("pY", [128, 1024], F32))

            y_xt = [kb.sem(f"y_xt{p}") for p in range(2)]
            y_bc = [kb.sem(f"y_bc{p}") for p in range(2)]
            y_be = [kb.sem(f"y_be{p}") for p in range(2)]
            y_yp = [kb.sem(f"y_yp{p}") for p in range(2)]

            def v14(e, en):
                i(e.memset(carry16[:, :], 0.0), sv)
                for p in range(2):
                    i(e.memset(lhsE[p][:, :], 1.0), sv)
                    i(e.memset(rhsE[p][:, :], 1.0), sv)
            V(v14)
            kb.barrier()

            def hmap(rr):
                j, r = rr // 16, rr % 16
                return (r // 4) * 16 + j * 4 + (r % 4)

            hl_tok = [0, 0]
            be_done = [0, 0]
            mp_tok = [0] * 16
            m_tok = [0] * 16
            mp_done_v = [0, 0]
            e_evac = [0, 0, 0, 0]
            ysv = [0, 0]
            st_carry = [0] * 8
            y_stage_tok = [0, 0]
            rhs2_used = [0]
            xtok_used = [0, 0]
            cfm_used = [0, 0]
            btok_used = [0, 0]
            bfm_used = [0, 0]
            xdt_used_st = 0
            m_used_st = 0

            for z in range(NCH):
                p = z % 2

                def s6(e, en, z=z, p=p, g1_=xtok_used[p], g2_=cfm_used[p],
                       g3_=btok_used[p], g4_=bfm_used[p],
                       gb0=be_done[0], gb1=be_done[1]):
                    w(e, en, sg, g1_)
                    w(e, en, sg, g2_)
                    w(e, en, st_, g3_)
                    w(e, en, st_, g4_)
                    w(e, en, y_be[0], gb0)
                    w(e, en, y_be[1], gb1)
                    i(e.dma_start_transpose(
                        out=x_tok[p][:, :],
                        in_=dr_xbc.ap()[0:DI, z * 128:(z + 1) * 128]), y_xt[p], 16)
                    i(e.dma_start(out=Bfm[p][:, :],
                                  in_=dr_xbc.ap()[DI:DI + 128, z * 128:(z + 1) * 128]), y_bc[p], 16)
                    i(e.dma_start(out=Cfm[p][:, :],
                                  in_=dr_xbc.ap()[DI + 128:DI + 256, z * 128:(z + 1) * 128]), y_bc[p], 16)
                    i(e.dma_start_transpose(
                        out=Btok[p][:, :],
                        in_=dr_xbc.ap()[DI:DI + 128, z * 128:(z + 1) * 128]), y_bc[p], 16)
                S(s6)
                tok_xt = y_xt[p].v
                tok_bc = y_bc[p].v

                def v15(e, en, z=z, p=p, gate=be_done[p]):
                    w(e, en, y_be[p], gate)
                    i(e.tensor_copy(Hp16[p][:, :], dA_cum[:, z * 128:(z + 1) * 128]), sv)
                    w(e, en, sv, sv.v)
                    i(e.tensor_tensor(Ltmp[:, :], dA_cum[:, z * 128:(z + 1) * 128],
                                      Hp16[p][:, :], ALU.subtract), sv)
                    w(e, en, sv, sv.v)
                    i(e.tensor_copy(Lp16[p][:, :], Ltmp[:, :]), sv)
                    i(e.tensor_scalar_mul(nH16[p][:, :], Hp16[p][:, :], -1.0), sv)
                    w(e, en, sv, sv.v)
                    i(e.tensor_scalar_mul(nL16[p][:, :], Lp16[p][:, :], -1.0), sv)
                V(v15)
                hl_tok[p] = sv.v
                tok_hl = sv.v

                def s7(e, en, z=z, p=p, tok_hl=tok_hl, tok_xt=tok_xt,
                       tok_bc=tok_bc):
                    w(e, en, sv, tok_hl)
                    w(e, en, y_xt[p], tok_xt)
                    w(e, en, y_bc[p], tok_bc)
                    for j in range(4):
                        for (tile, row, src) in [(lhsE[p], 32 * j + 2, nH16[p]),
                                                 (lhsE[p], 32 * j + 3, nL16[p]),
                                                 (rhsE[p], 32 * j + 0, Hp16[p]),
                                                 (rhsE[p], 32 * j + 1, Lp16[p])]:
                            i(e.dma_start(out=tile[row:row + 1, 0:2048],
                                          in_=APx(src, 16 * j * 128, [[128, 16], [1, 128]])),
                              y_be[p], 16)
                    w(e, en, st_, rhs2_used[0])
                    i(e.dma_start(out=rhs2[0:1, :],
                                  in_=APx(Hp16[p], 0, [[128, NH], [1, 128]])), y_be[p], 16)
                    i(e.dma_start(out=rhs2[1:2, :],
                                  in_=APx(Lp16[p], 0, [[128, NH], [1, 128]])), y_be[p], 16)
                S(s7)
                be_done[p] = y_be[p].v
                tok_be = y_be[p].v

                def t5(e, en, p=p, tok_bc=tok_bc, g1_=ysv[0], g2_=ysv[1]):
                    w(e, en, y_bc[p], tok_bc)
                    w(e, en, sv, g1_)
                    w(e, en, sv, g2_)
                    i(e.matmul(pY[:, 0:128], Bfm[p][:, :], Cfm[p][:, :],
                               start=True, stop=True), st_)
                T(t5)
                bfm_used[p] = st_.v
                tok_cbt_mm = st_.v

                def v16(e, en, tok=tok_cbt_mm):
                    w(e, en, st_, tok)
                    i(e.tensor_copy(CBT16[:, :], pY[:, 0:128]), sv)
                V(v16)
                tok_cbt_cp = sv.v

                def g2(e, en, tok=tok_cbt_cp):
                    w(e, en, sv, tok)
                    i(e.affine_select(CBT16[:, :], CBT16[:, :], [[1, 128]],
                                      ALU.is_ge, 0.0, base=0,
                                      channel_multiplier=-1), sg)
                G(g2)
                tok_cbt = sg.v

                def v17(e, en, z=z):
                    i(e.tensor_scalar(dec_f[:, :], dA_cum[:, z * 128:(z + 1) * 128],
                                      dA_last[:, z:z + 1], None, ALU.subtract), sv)
                V(v17)
                tok_decf = sv.v

                def a7(e, en, tok=tok_decf):
                    w(e, en, sv, tok)
                    i(e.activation(dec_f[:, :], dec_f[:, :], AF.Exp, scale=-1.0), sa)
                A(a7)
                tok_dece = sa.v

                def t6(e, en, z=z, tok_dece=tok_dece, tok_cp=tok_cbt_cp):
                    w(e, en, sa, tok_dece)
                    w(e, en, sv, tok_cp)
                    i(e.transpose(pY[0:128, 128:192], dt_t[:, z * 128:(z + 1) * 128],
                                  idn32[0:64, 0:64]), st_)
                    i(e.transpose(pY[0:128, 192:256], dec_f[:, :], idn32[0:64, 0:64]), st_)
                    i(e.transpose(pY[0:1, 256:320], cd_t[:, z:z + 1], idn32[0:64, 0:64]), st_)
                T(t6)
                tok_tr = st_.v

                def v18(e, en, tok=tok_tr):
                    w(e, en, st_, tok)
                    i(e.tensor_copy(dtT16[:, :], pY[:, 128:192]), sv)
                    i(e.tensor_copy(decT16[:, :], pY[:, 192:256]), sv)
                    i(e.tensor_copy(cdrow[0:1, :], pY[0:1, 256:320]), sv)
                V(v18)
                tok_trc = sv.v

                def t7(e, en, tok=tok_trc):
                    w(e, en, sv, tok)
                    i(e.matmul(pY[:, 320:384], ones32[0:1, :], cdrow[0:1, :],
                               start=True, stop=True), st_)
                T(t7)
                tok_cdmm = st_.v

                def v19(e, en, tok=tok_cdmm):
                    w(e, en, st_, tok)
                    i(e.tensor_copy(cdnb[:, :], pY[:, 320:384]), sv)
                V(v19)
                tok_cdnb = sv.v

                def v20(e, en, p=p, tok_xt=tok_xt, tok_trc=tok_trc,
                        gate=xdt_used_st):
                    w(e, en, y_xt[p], tok_xt)
                    w(e, en, sv, tok_trc)
                    w(e, en, st_, gate)
                    i(e.tensor_tensor(Xdt[:, :], x_tok[p][:, :],
                                      APx(dtT16, 0, [[NH, 128], [1, NH], [0, HD]]),
                                      ALU.mult), sg)
                    w(e, en, sg, sg.v)
                    i(e.tensor_tensor(Xdtd[:, :], Xdt[:, :],
                                      APx(decT16, 0, [[NH, 128], [1, NH], [0, HD]]),
                                      ALU.mult), sg)
                G(v20)
                tok_xdt = sg.v
                xtok_used[p] = tok_xdt

                for q16 in range(16):
                    half = q16 % 2

                    def t8(e, en, p=p, q16=q16, half=half, tok_be=tok_be,
                           gate=mp_done_v[half]):
                        w(e, en, y_be[p], tok_be)
                        w(e, en, sv, gate)
                        i(e.matmul(pEp[:, half * 512:half * 512 + 512],
                                   ones2[0:2, :],
                                   rhs2[0:2, q16 * 512:(q16 + 1) * 512],
                                   start=True, stop=True), st_)
                    T(t8)
                    tok_ep = st_.v

                    def v21a(e, en, q16=q16, half=half, tok_ep=tok_ep,
                             gate=m_used_st):
                        w(e, en, st_, tok_ep)
                        w(e, en, st_, gate)
                        i(e.tensor_scalar_min(Mp_sb[:, q16 * 512:(q16 + 1) * 512],
                                              pEp[:, half * 512:half * 512 + 512],
                                              0.0), sv)
                    V(v21a)
                    mp_done_v[half] = sv.v
                rhs2_used[0] = st_.v
                tok_mpraw = sv.v

                def a8(e, en, tok=tok_mpraw):
                    w(e, en, sv, tok)
                    i(e.activation(Mp_sb[:, 0:4096], Mp_sb[:, 0:4096], AF.Exp), sa)
                    i(e.activation(Mp_sb[:, 4096:8192], Mp_sb[:, 4096:8192],
                                   AF.Exp), sa)
                A(a8)
                tok_mpe = sa.v

                def v21(e, en, p=p, tok=tok_mpe):
                    w(e, en, sa, tok)
                    i(e.tensor_tensor(
                        Mp_sb[:, :],
                        Mp_sb[:, :],
                        APx(Cfm[p], 0, [[128, 128], [0, NH], [1, 128]]),
                        ALU.mult), sg)
                G(v21)
                for q16 in range(16):
                    mp_tok[q16] = sg.v
                cfm_used[p] = sg.v

                for cyc in range(4):
                    def t9(e, en, p=p, cyc=cyc, tok_be=tok_be,
                           gates=tuple(e_evac)):
                        w(e, en, y_be[p], tok_be)
                        for b in range(4):
                            w(e, en, sv, gates[b])
                        for rr in range(4):
                            r = cyc * 4 + rr
                            for j in range(4):
                                i(e.matmul(
                                    pE[:, j * 512 + rr * 128:j * 512 + rr * 128 + 128],
                                    lhsE[p][32 * j:32 * j + 4, r * 128:(r + 1) * 128],
                                    rhsE[p][32 * j:32 * j + 4, r * 128:(r + 1) * 128],
                                    start=True, stop=True,
                                    tile_position=(32 * j, 0)), st_)
                    T(t9)
                    tok_ecyc = st_.v

                    for j in range(4):
                        def v9b(e, en, j=j, cyc=cyc, tok_ecyc=tok_ecyc,
                                gate=m_used_st):
                            w(e, en, st_, tok_ecyc)
                            w(e, en, st_, gate)
                            i(e.tensor_scalar_min(
                                M_sb[:, cyc * 2048 + j * 512:cyc * 2048 + (j + 1) * 512],
                                pE[:, j * 512:(j + 1) * 512], 0.0), sv)
                        V(v9b)
                        e_evac[j] = sv.v

                tok_mraw = sv.v

                def a9(e, en, tok=tok_mraw):
                    w(e, en, sv, tok)
                    i(e.activation(M_sb[:, 0:4096], M_sb[:, 0:4096], AF.Exp), sa)
                    i(e.activation(M_sb[:, 4096:8192], M_sb[:, 4096:8192],
                                   AF.Exp), sa)
                A(a9)
                tok_me = sa.v

                def v22(e, en, tok=tok_me, tok_cbt=tok_cbt):
                    w(e, en, sa, tok)
                    w(e, en, sg, tok_cbt)
                    i(e.tensor_tensor(
                        M_sb[:, :], M_sb[:, :],
                        APx(CBT16, 0, [[128, 128], [0, NH], [1, 128]]),
                        ALU.mult), sg)
                G(v22)
                for idx16 in range(16):
                    m_tok[idx16] = sg.v

                for q in range(8):
                    hp = q % 2

                    def t10(e, en, q=q, hp=hp, tok_xdt=tok_xdt,
                            tok_cdnb=tok_cdnb, gate_c=st_carry[q],
                            gate_y=ysv[hp]):
                        for h in range(q * 8, q * 8 + 8):
                            w(e, en, sg, m_tok[hmap(h) // 4])
                            w(e, en, sg, mp_tok[h // 4])
                        w(e, en, sg, tok_xdt)
                        w(e, en, sv, tok_cdnb)
                        w(e, en, sv, gate_c)
                        w(e, en, sv, gate_y)
                        for h in range(q * 8, q * 8 + 8):
                            c0 = hp * 512 + (h % 8) * 64
                            i(e.matmul(pY[:, c0:c0 + 64],
                                       M_sb[:, hmap(h) * 128:hmap(h) * 128 + 128],
                                       Xdt[:, h * 64:h * 64 + 64],
                                       start=True, stop=False), st_)
                            i(e.matmul(pY[:, c0:c0 + 64],
                                       Mp_sb[:, h * 128:h * 128 + 128],
                                       carry16[:, h * 64:h * 64 + 64],
                                       start=False, stop=True), st_)
                    T(t10)
                    tok_y = st_.v

                    def t11(e, en, p=p, q=q, hp=hp, tok_bc=tok_bc,
                            gate=mp_done_v[hp]):
                        w(e, en, y_bc[p], tok_bc)
                        w(e, en, sv, gate)
                        for h in range(q * 8, q * 8 + 8):
                            i(e.matmul(pEp[:, hp * 512 + (h % 8) * 64:
                                           hp * 512 + (h % 8) * 64 + 64],
                                       Btok[p][:, :], Xdtd[:, h * 64:h * 64 + 64],
                                       start=True, stop=True), st_)
                    T(t11)
                    btok_used[p] = st_.v
                    tok_stm = st_.v

                    def g23(e, en, q=q, hp=hp, tok_y=tok_y, tok_cdnb=tok_cdnb,
                            selfgate=sv.v):
                        w(e, en, st_, tok_y)
                        w(e, en, sv, max(tok_cdnb, selfgate))
                        i(e.tensor_tensor(ct1[:, :], carry16[:, q * 512:(q + 1) * 512],
                                          APx(cdnb, q * 8, [[NH, 128], [1, 8], [0, HD]]),
                                          ALU.mult), sg)
                    G(g23)
                    tok_ct1 = sg.v

                    def v23(e, en, q=q, hp=hp, tok_stm=tok_stm, tok_ct1=tok_ct1):
                        w(e, en, st_, tok_stm)
                        w(e, en, sg, tok_ct1)
                        i(e.tensor_add(carry16[:, q * 512:(q + 1) * 512], ct1[:, :],
                                       pEp[:, hp * 512:hp * 512 + 512]), sv)
                    V(v23)
                    st_carry[q] = sv.v
                    mp_done_v[hp] = sv.v

                    def v24(e, en, z=z, q=q, hp=hp, tok_y=tok_y,
                            gate=y_stage_tok[z % 2]):
                        w(e, en, st_, tok_y)
                        w(e, en, y_yp[z % 2], gate)
                        i(e.tensor_copy(Yst[z % 2][:, q * 512:(q + 1) * 512],
                                        pY[:, hp * 512:hp * 512 + 512]), sv)
                    V(v24)
                    ysv[hp] = sv.v
                tok_yall = sv.v
                xdt_used_st = st_.v
                m_used_st = st_.v

                def s8(e, en, z=z, tok_yall=tok_yall):
                    w(e, en, sv, tok_yall)
                    i(e.dma_start(out=dr_ypre.ap()[z * 128:(z + 1) * 128, :],
                                  in_=Yst[z % 2][:, :]), y_yp[z % 2], 16)
                S(s8)
                y_stage_tok[z % 2] = y_yp[z % 2].v

            y_cc1 = kb.sem("y_cc1")
            y_cc2 = kb.sem("y_cc2")

            def g3a(e, en, gate=tuple(st_carry)):
                for q in range(8):
                    w(e, en, sv, gate[q])
                i(e.dma_start(out=dr_cci.ap(), in_=carry16[:, :]), y_cc1, 16)
            G(g3a)

            def g3(e, en):
                w(e, en, y_cc1, y_cc1.v)
                groups = [[c, c + 1] for c in range(0, ncores, 2)]
                i(e.collective_compute("AllReduce", ALU.add,
                                       replica_groups=groups,
                                       ins=[dr_cci.ap()], outs=[dr_cco.ap()]),
                  g_cc, 1)
            G(g3)

            def s10(e, en):
                w(e, en, g_cc, g_cc.v)
                i(e.dma_start(out=sum32[:, :], in_=dr_cco.ap()), y_cc2, 16)
            S(s10)

            def v26(e, en):
                w(e, en, y_cc2, y_cc2.v)
                i(e.tensor_tensor(sum32[:, :], sum32[:, :], carry16[:, :],
                                  ALU.subtract), sv)
                w(e, en, sv, sv.v)
                i(e.tensor_scalar(patch16[:, :], sum32[:, :], flag_c[:, 0:1],
                                  None, ALU.mult), sv)
            V(v26)
            kb.barrier()

        with ExitStack() as pctx:
            pe = pctx.enter_context
            Ypt = [pe(nc.sbuf_tensor(f"Ypt{p}", [128, DI], F16)) for p in range(2)]
            z_tk = [pe(nc.sbuf_tensor(f"z_tk{p}", [128, DI], F16)) for p in range(2)]
            x_tk = [pe(nc.sbuf_tensor(f"x_tk{p}", [128, DI], F16)) for p in range(2)]
            Cfm2 = [pe(nc.sbuf_tensor(f"Cfm2{p}", [128, 128], F16)) for p in range(2)]
            Hb16 = [pe(nc.sbuf_tensor(f"Hb16{p}", [NH, 128], F16)) for p in range(2)]
            Lb16 = [pe(nc.sbuf_tensor(f"Lb16{p}", [NH, 128], F16)) for p in range(2)]
            Ltmp2 = pe(nc.sbuf_tensor("Ltmp2", [NH, 128], F32))
            rhs2b = pe(nc.sbuf_tensor("rhs2b", [2, NH * 128], F16))
            Mp2 = pe(nc.sbuf_tensor("Mp2", [128, NH * 128], F16))
            stz = pe(nc.sbuf_tensor("stz", [128, DI], F16))
            grow = pe(nc.sbuf_tensor("grow", [1, NH], F32))
            gnb = pe(nc.sbuf_tensor("gnb", [128, NH], F32))
            siluz = pe(nc.sbuf_tensor("siluz", [128, DI], F16))
            ygA = pe(nc.sbuf_tensor("ygA", [128, DI], F16))
            ygB = pe(nc.sbuf_tensor("ygB", [128, DI], F16))
            ssum = pe(nc.sbuf_tensor("ssum", [128, 1], F32))
            rstd2 = pe(nc.sbuf_tensor("rstd2", [128, 1], F32))
            ynst = [pe(nc.sbuf_tensor(f"ynst{p}", [128, DI], F16)) for p in range(2)]
            pEb = pe(nc.psum_tensor("pEb", [128, 1024], F32))
            pYb = pe(nc.psum_tensor("pYb", [128, 1024], F32))
            pM2 = pe(nc.psum_tensor("pM2", [128, 512], F32))

            y_p4 = [kb.sem(f"y_p4_{p}") for p in range(2)]
            y_be2 = [kb.sem(f"y_be2_{p}") for p in range(2)]
            y_yn = [kb.sem(f"y_yn{p}") for p in range(2)]

            hlb_gate = [0, 0]
            rhs2b_used = [0]
            tok_stz_prev = 0
            ep2_evac = [0, 0]
            mp2_done = [0, 0]
            mp2_tok = [0] * 16
            mp2_used_st = 0
            stz_used_st = 0
            ygq_tok = [0, 0]
            yn_stage_tok = [0, 0]
            p4_v_reads = [0, 0]
            p4_a_reads = [0, 0]
            prev_ss = 0
            prev_yg = 0
            prev_vv = 0
            prev_yn = 0

            for z in range(NCH):
                p = z % 2

                def s11(e, en, z=z, p=p, g1_=p4_v_reads[p], g2_=p4_a_reads[p],
                        gb0=hlb_gate[0] if False else y_be2[0].v,
                        gb1=y_be2[1].v):
                    w(e, en, sg, g1_)
                    w(e, en, sa, g2_)
                    w(e, en, y_be2[0], gb0)
                    w(e, en, y_be2[1], gb1)
                    i(e.dma_start(out=Ypt[p][:, :],
                                  in_=dr_ypre.ap()[z * 128:(z + 1) * 128, :]), y_p4[p], 16)
                    i(e.dma_start_transpose(
                        out=z_tk[p][:, :],
                        in_=dr_z.ap()[0:DI, z * 128:(z + 1) * 128]), y_p4[p], 16)
                    i(e.dma_start_transpose(
                        out=x_tk[p][:, :],
                        in_=dr_xbc.ap()[0:DI, z * 128:(z + 1) * 128]), y_p4[p], 16)
                    i(e.dma_start(out=Cfm2[p][:, :],
                                  in_=dr_xbc.ap()[DI + 128:DI + 256, z * 128:(z + 1) * 128]), y_p4[p], 16)
                S(s11)
                tok_rd = y_p4[p].v

                def v27(e, en, z=z, p=p, gate=hlb_gate[p]):
                    w(e, en, y_be2[p], gate)
                    i(e.tensor_copy(Hb16[p][:, :], dA_cum[:, z * 128:(z + 1) * 128]), sv)
                    w(e, en, sv, sv.v)
                    i(e.tensor_tensor(Ltmp2[:, :], dA_cum[:, z * 128:(z + 1) * 128],
                                      Hb16[p][:, :], ALU.subtract), sv)
                    w(e, en, sv, sv.v)
                    i(e.tensor_copy(Lb16[p][:, :], Ltmp2[:, :]), sv)
                V(v27)
                tok_hlb = sv.v

                def s12(e, en, p=p, tok=tok_hlb, tok_rd=tok_rd):
                    w(e, en, sv, tok)
                    w(e, en, st_, rhs2b_used[0])
                    w(e, en, y_p4[p], tok_rd)
                    i(e.dma_start(out=rhs2b[0:1, :],
                                  in_=APx(Hb16[p], 0, [[128, NH], [1, 128]])), y_be2[p], 16)
                    i(e.dma_start(out=rhs2b[1:2, :],
                                  in_=APx(Lb16[p], 0, [[128, NH], [1, 128]])), y_be2[p], 16)
                S(s12)
                hlb_gate[p] = y_be2[p].v
                tok_r2b = y_be2[p].v

                def a11(e, en, p=p, tok_rd=tok_rd, gate=prev_yg):
                    w(e, en, y_p4[p], tok_rd)
                    w(e, en, sg, gate)
                    i(e.activation(siluz[:, :], z_tk[p][:, :], AF.Silu), sa)
                A(a11)
                p4_a_reads[p] = sa.v
                tok_slz = sa.v

                def t12(e, en, z=z, gate=sv.v):
                    w(e, en, sv, gate)
                    i(e.transpose(pM2[0:1, 0:64], G_pre[:, z:z + 1], idn32[0:64, 0:64]), st_)
                T(t12)
                tok_gt = st_.v

                def v28(e, en, tok=tok_gt):
                    w(e, en, st_, tok)
                    i(e.tensor_copy(grow[0:1, :], pM2[0:1, 0:64]), sv)
                V(v28)
                tok_gr = sv.v

                def t13(e, en, tok=tok_gr):
                    w(e, en, sv, tok)
                    i(e.matmul(pM2[:, 64:128], ones32[0:1, :], grow[0:1, :],
                               start=True, stop=True), st_)
                T(t13)
                tok_gb = st_.v

                def v29(e, en, tok=tok_gb, gate=tok_stz_prev):
                    w(e, en, st_, tok)
                    w(e, en, sg, gate)
                    i(e.tensor_copy(gnb[:, :], pM2[:, 64:128]), sv)
                V(v29)

                def g29(e, en, gate=sv.v, gate2=stz_used_st):
                    w(e, en, sv, gate)
                    w(e, en, st_, gate2)
                    i(e.tensor_tensor(stz[:, :], patch16[:, :],
                                      APx(gnb, 0, [[NH, 128], [1, NH], [0, HD]]),
                                      ALU.mult), sg)
                G(g29)
                tok_stz = sg.v
                tok_stz_prev = tok_stz

                for q16 in range(16):
                    half = q16 % 2

                    def t14(e, en, p=p, q16=q16, half=half, tok_r2b=tok_r2b,
                            gate=mp2_done[half]):
                        w(e, en, y_be2[p], tok_r2b)
                        w(e, en, sv, gate)
                        i(e.matmul(pEb[:, half * 512:half * 512 + 512],
                                   ones2[0:2, :],
                                   rhs2b[0:2, q16 * 512:(q16 + 1) * 512],
                                   start=True, stop=True), st_)
                    T(t14)
                    tok_e2 = st_.v

                    def v30a(e, en, q16=q16, half=half, tok=tok_e2,
                             gate=mp2_used_st):
                        w(e, en, st_, tok)
                        w(e, en, st_, gate)
                        i(e.tensor_scalar_min(Mp2[:, q16 * 512:(q16 + 1) * 512],
                                              pEb[:, half * 512:half * 512 + 512],
                                              0.0), sv)
                    V(v30a)
                    mp2_done[half] = sv.v
                tok_mp2raw = sv.v

                def a10(e, en, tok=tok_mp2raw):
                    w(e, en, sv, tok)
                    i(e.activation(Mp2[:, 0:4096], Mp2[:, 0:4096], AF.Exp), sa)
                    i(e.activation(Mp2[:, 4096:8192], Mp2[:, 4096:8192], AF.Exp), sa)
                A(a10)
                tok_mp2e = sa.v

                def v30(e, en, p=p, tok=tok_mp2e, tok_rd=tok_rd):
                    w(e, en, sa, tok)
                    w(e, en, y_p4[p], tok_rd)
                    i(e.tensor_tensor(
                        Mp2[:, :], Mp2[:, :],
                        APx(Cfm2[p], 0, [[128, 128], [0, NH], [1, 128]]),
                        ALU.mult), sg)
                G(v30)
                for q16 in range(16):
                    mp2_tok[q16] = sg.v

                rhs2b_used[0] = st_.v
                for q in range(8):
                    hp = q % 2

                    def t15(e, en, q=q, hp=hp, tok_stz=tok_stz, gate=ygq_tok[hp]):
                        for h in range(q * 8, q * 8 + 8):
                            w(e, en, sg, mp2_tok[h // 4])
                        w(e, en, sg, tok_stz)
                        w(e, en, sv, gate)
                        for h in range(q * 8, q * 8 + 8):
                            c0 = hp * 512 + (h % 8) * 64
                            i(e.matmul(pYb[:, c0:c0 + 64],
                                       Mp2[:, h * 128:h * 128 + 128],
                                       stz[:, h * 64:h * 64 + 64],
                                       start=True, stop=True), st_)
                    T(t15)
                    tok_pt = st_.v

                    def v31(e, en, p=p, q=q, hp=hp, tok_pt=tok_pt, tok_rd=tok_rd,
                            gate=prev_ss, gate2=prev_yn):
                        w(e, en, st_, tok_pt)
                        w(e, en, y_p4[p], tok_rd)
                        w(e, en, sa, gate)
                        w(e, en, sv, gate2)
                        i(e.tensor_add(ygA[:, q * 512:(q + 1) * 512],
                                       Ypt[p][:, q * 512:(q + 1) * 512],
                                       pYb[:, hp * 512:hp * 512 + 512]), sv)
                    V(v31)
                    ygq_tok[hp] = sv.v
                tok_yga = sv.v
                mp2_used_st = st_.v
                stz_used_st = st_.v

                def v32(e, en, p=p, tok_rd=tok_rd, tok_slz=tok_slz,
                        tok_yga=tok_yga, gate=prev_ss):
                    w(e, en, y_p4[p], tok_rd)
                    w(e, en, sa, gate)
                    i(e.tensor_tensor(ygB[:, :], x_tk[p][:, :],
                                      APx(D_nb, 0, [[NH, 128], [1, NH], [0, HD]]),
                                      ALU.mult), sg)
                    w(e, en, sg, sg.v)
                    w(e, en, sv, tok_yga)
                    i(e.tensor_add(ygB[:, :], ygA[:, :], ygB[:, :]), sg)
                    w(e, en, sg, sg.v)
                    w(e, en, sa, tok_slz)
                    i(e.tensor_tensor(ygA[:, :], ygB[:, :], siluz[:, :],
                                      ALU.mult), sg)
                G(v32)
                tok_yg = sg.v
                prev_yg = tok_yg
                p4_v_reads[p] = tok_yg

                def a12(e, en, tok=tok_yg, gate=prev_vv):
                    w(e, en, sg, tok)
                    w(e, en, sv, gate)
                    i(e.activation(ygB[:, :], ygA[:, :], AF.Square,
                                   accum_out=ssum[:, :]), sa)
                A(a12)
                tok_ss = sa.v
                prev_ss = tok_ss

                def v33(e, en, tok=tok_ss, gate=prev_yn):
                    w(e, en, sa, tok)
                    w(e, en, sv, gate)
                    i(e.tensor_scalar_mul(rstd2[:, :], ssum[:, :], -1.0 / DI), sv)
                V(v33)
                tok_vv = sv.v
                prev_vv = tok_vv

                def a13(e, en, tok=tok_vv):
                    w(e, en, sv, tok)
                    i(e.activation(rstd2[:, :], rstd2[:, :], AF.Ln, bias=eps_c[:, 0:1],
                                   scale=-1.0), sa)
                    w(e, en, sa, sa.v)
                    i(e.activation(rstd2[:, :], rstd2[:, :], AF.Exp, scale=-0.5), sa)
                A(a13)
                tok_rstd = sa.v

                def v34(e, en, z=z, tok=tok_rstd, gate=yn_stage_tok[z % 2]):
                    w(e, en, sa, tok)
                    w(e, en, y_yn[z % 2], gate)
                    i(e.scalar_tensor_tensor(ynst[z % 2][:, :], ygA[:, :],
                                             rstd2[:, 0:1], nw_bc[:, :],
                                             ALU.mult, ALU.mult), sv)
                V(v34)
                tok_yn = sv.v
                prev_yn = tok_yn

                def s13(e, en, z=z, tok=tok_yn):
                    w(e, en, sv, tok)
                    i(e.dma_start(out=dr_yn.ap()[z * 128:(z + 1) * 128, :],
                                  in_=ynst[z % 2][:, :]), y_yn[z % 2], 16)
                S(s13)
                yn_stage_tok[z % 2] = y_yn[z % 2].v
            kb.barrier()

        scan_ctx.close()

        with ExitStack() as pctx:
            pe = pctx.enter_context
            ynfm = pe(nc.sbuf_tensor("ynfm", [128, 32 * TOK], F16))
            wop = [pe(nc.sbuf_tensor(f"wop{p}", [128, 32 * 128], F16)) for p in range(2)]
            atst = [pe(nc.sbuf_tensor(f"atst{p}", [128, TOK], F16)) for p in range(2)]
            pC = [pe(nc.psum_tensor(f"pC{p}", [128, 4 * 512], F32)) for p in range(2)]
            y_ynf = kb.sem("y_ynf")
            g_w2 = [kb.sem(f"g_w2_{p}") for p in range(2)]
            y_at = [kb.sem(f"y_at{p}") for p in range(2)]

            def s14(e, en):
                for kc in range(32):
                    i(e.dma_start_transpose(
                        out=ynfm[:, kc * TOK:(kc + 1) * TOK],
                        in_=dr_yn.ap()[:, kc * 128:(kc + 1) * 128]), y_ynf, 16)
            S(s14)
            y_ynf_all = y_ynf.v

            wt2_last = [0, 0]
            evC = [[0] * 4, [0] * 4]
            at_stage = [0, 0]
            for m in range(16):
                p = m % 2

                def g4(e, en, m=m, p=p, gate=wt2_last[p]):
                    w(e, en, st_, gate)
                    src = APx(opw, m * 4096, [[16 * 32 * 128, 128], [1, 4096]])
                    dst = APx(wop[p], 0, [[32 * 128, 128], [1, 4096]])
                    i(e.dma_start(out=dst, in_=src), g_w2[p], 16)
                G(g4)
                tok_w2 = g_w2[p].v

                for g in range(NG):
                    def t16(e, en, m=m, p=p, g=g, tok_w2=tok_w2, gate=evC[p][g]):
                        w(e, en, g_w2[p], tok_w2)
                        w(e, en, y_ynf, y_ynf_all)
                        w(e, en, sv, gate)
                        for kc in range(32):
                            i(e.matmul(pC[p][:, g * 512:g * 512 + 512],
                                       wop[p][:, kc * 128:(kc + 1) * 128],
                                       ynfm[:, kc * TOK + g * 512:kc * TOK + g * 512 + 512],
                                       start=(kc == 0), stop=(kc == 31)), st_)
                    T(t16)
                    wt2_last[p] = st_.v
                    tok_c = st_.v

                    def v35(e, en, p=p, g=g, tok_c=tok_c, gate=at_stage[p]):
                        w(e, en, st_, tok_c)
                        w(e, en, y_at[p], gate)
                        i(e.tensor_copy(atst[p][:, g * 512:g * 512 + 512],
                                        pC[p][:, g * 512:g * 512 + 512]), sv)
                    V(v35)
                    evC[p][g] = sv.v
                tok_at = sv.v

                def s15(e, en, m=m, p=p, tok_at=tok_at):
                    w(e, en, sv, tok_at)
                    i(e.dma_start(out=dr_attn.ap()[m * 128:(m + 1) * 128, :],
                                  in_=atst[p][:, :]), y_at[p], 16)
                S(s15)
                at_stage[p] = y_at[p].v
            kb.barrier()

        with ExitStack() as pctx:
            pe = pctx.enter_context
            u2 = [pe(nc.sbuf_tensor(f"u2_{p}", [128, DM], F32)) for p in range(2)]
            att = [pe(nc.sbuf_tensor(f"att{p}", [128, DM], F16)) for p in range(2)]
            x1t = [pe(nc.sbuf_tensor(f"x1t{p}", [128, DM], F32)) for p in range(2)]
            xsq = [pe(nc.sbuf_tensor(f"xsq{p}", [128, DM], F32)) for p in range(2)]
            h2t = [pe(nc.sbuf_tensor(f"h2t{p}", [128, DM], F16)) for p in range(2)]
            ms2 = [pe(nc.sbuf_tensor(f"ms2_{p}", [128, 1], F32)) for p in range(2)]
            mn2 = [pe(nc.sbuf_tensor(f"mn2_{p}", [128, 1], F32)) for p in range(2)]
            qs2 = [pe(nc.sbuf_tensor(f"qs2_{p}", [128, 1], F32)) for p in range(2)]
            qq2 = [pe(nc.sbuf_tensor(f"qq2_{p}", [128, 1], F32)) for p in range(2)]
            nv2 = [pe(nc.sbuf_tensor(f"nv2_{p}", [128, 1], F32)) for p in range(2)]
            rs2 = [pe(nc.sbuf_tensor(f"rs2_{p}", [128, 1], F32)) for p in range(2)]
            mr2 = [pe(nc.sbuf_tensor(f"mr2_{p}", [128, 1], F32)) for p in range(2)]
            y_u2 = [kb.sem(f"y_u2_{p}") for p in range(2)]
            y_x1 = [kb.sem(f"y_x1_{p}") for p in range(2)]
            y_h2 = [kb.sem(f"y_h2_{p}") for p in range(2)]

            last_h2 = [0, 0]
            last_sq2 = [0, 0]
            for tt in range(TOK // 128):
                p = tt % 2

                def s16(e, en, tt=tt, p=p, gate=last_h2[p]):
                    w(e, en, sv, gate)
                    i(e.dma_start(out=u2[p][:, :],
                                  in_=u_in.ap()[HALO + tt * 128:HALO + (tt + 1) * 128, :]), y_u2[p], 16)
                    i(e.dma_start_transpose(
                        out=att[p][:, :],
                        in_=dr_attn.ap()[:, tt * 128:(tt + 1) * 128]), y_u2[p], 16)
                S(s16)
                tok_r = y_u2[p].v

                def v36(e, en, p=p, tok_r=tok_r, gate=y_x1[p].v):
                    w(e, en, y_u2[p], tok_r)
                    w(e, en, y_x1[p], gate)
                    i(e.tensor_add(x1t[p][:, :], u2[p][:, :], att[p][:, :]), sv)
                    w(e, en, sv, sv.v)
                    i(e.reduce_sum(ms2[p][:, :], x1t[p][:, :], AX.X), sv)
                    w(e, en, sv, sv.v)
                    i(e.tensor_scalar_mul(mn2[p][:, :], ms2[p][:, :], -1.0 / DM), sv)
                V(v36)
                tok_x1 = sv.v

                def s17(e, en, tt=tt, p=p, tok=tok_x1):
                    w(e, en, sv, tok)
                    i(e.dma_start(out=dr_x1.ap()[tt * 128:(tt + 1) * 128, :],
                                  in_=x1t[p][:, :]), y_x1[p], 16)
                S(s17)

                def g14(e, en, p=p, tok=tok_x1, gate=last_sq2[p]):
                    w(e, en, sv, tok)
                    w(e, en, sv, gate)
                    i(e.tensor_tensor(xsq[p][:, :], x1t[p][:, :],
                                      x1t[p][:, :], ALU.mult), sg)
                G(g14)
                tok_q2a = sg.v

                def v37a(e, en, p=p, tok=tok_q2a):
                    w(e, en, sg, tok)
                    i(e.reduce_sum(qs2[p][:, :], xsq[p][:, :], AX.X), sv)
                V(v37a)
                tok_q2 = sv.v

                def v37(e, en, p=p, tok=tok_q2):
                    w(e, en, sv, tok)
                    i(e.tensor_scalar_mul(qq2[p][:, :], qs2[p][:, :], 1.0 / DM), sv)
                    w(e, en, sv, sv.v)
                    i(e.scalar_tensor_tensor(nv2[p][:, :], mn2[p][:, :],
                                             mn2[p][:, 0:1], qq2[p][:, :],
                                             ALU.mult, ALU.subtract), sv)
                V(v37)
                tok_nv2 = sv.v
                last_sq2[p] = tok_nv2

                def a15(e, en, p=p, tok=tok_nv2):
                    w(e, en, sv, tok)
                    i(e.activation(rs2[p][:, :], nv2[p][:, :], AF.Ln, bias=eps_c[:, 0:1],
                                   scale=-1.0), sa)
                    w(e, en, sa, sa.v)
                    i(e.activation(rs2[p][:, :], rs2[p][:, :], AF.Exp, scale=-0.5), sa)
                A(a15)
                tok_rs2 = sa.v

                def v38(e, en, p=p, tok=tok_rs2):
                    w(e, en, sa, tok)
                    i(e.tensor_tensor(mr2[p][:, :], mn2[p][:, :], rs2[p][:, :],
                                      ALU.mult), sv)
                V(v38)
                tok_mr2 = sv.v

                def g16(e, en, p=p, tok=tok_mr2, gate=y_h2[p].v):
                    w(e, en, sv, tok)
                    w(e, en, y_h2[p], gate)
                    i(e.tensor_scalar(h2t[p][:, :], x1t[p][:, :],
                                      mn2[p][:, 0:1], rs2[p][:, 0:1],
                                      ALU.add, ALU.mult), sv)
                V(g16)
                last_h2[p] = sv.v
                tok_h2 = sv.v

                def s18(e, en, tt=tt, p=p, tok=tok_h2):
                    w(e, en, sv, tok)
                    i(e.dma_start(out=dr_h2.ap()[tt * 128:(tt + 1) * 128, :],
                                  in_=h2t[p][:, :]), y_h2[p], 16)
                S(s18)
            kb.barrier()

        with ExitStack() as pctx:
            pe = pctx.enter_context
            h2fm = pe(nc.sbuf_tensor("h2fm", [128, KC * TOK], F16))
            w0t = [pe(nc.sbuf_tensor(f"w0t{p}", [128, KC * 128], F16)) for p in range(2)]
            gst = [pe(nc.sbuf_tensor(f"gst{p}", [128, TOK], F16)) for p in range(2)]
            pD = [pe(nc.psum_tensor(f"pD{p}", [128, 4 * 512], F32)) for p in range(2)]
            y_h2f = kb.sem("y_h2f")
            g_w3 = [kb.sem(f"g_w3_{p}") for p in range(2)]
            y_g = [kb.sem(f"y_g{p}") for p in range(2)]

            def s19(e, en):
                for kc in range(KC):
                    i(e.dma_start_transpose(
                        out=h2fm[:, kc * TOK:(kc + 1) * TOK],
                        in_=dr_h2.ap()[:, kc * 128:(kc + 1) * 128]), y_h2f, 16)
            S(s19)
            y_h2f_all = y_h2f.v

            w3_last = [0, 0]
            evD = [[0] * 4, [0] * 4]
            g_stage = [0, 0]
            for m in range(HID // 128):
                p = m % 2

                def g5(e, en, m=m, p=p, gate=w3_last[p]):
                    w(e, en, st_, gate)
                    src = APx(w0, m * 2048, [[64 * 16 * 128, 128], [1, 2048]])
                    dst = APx(w0t[p], 0, [[KC * 128, 128], [1, 2048]])
                    i(e.dma_start(out=dst, in_=src), g_w3[p], 16)
                G(g5)
                tok_w3 = g_w3[p].v

                for g in range(NG):
                    def t17(e, en, m=m, p=p, g=g, tok_w3=tok_w3, gate=evD[p][g]):
                        w(e, en, g_w3[p], tok_w3)
                        w(e, en, y_h2f, y_h2f_all)
                        w(e, en, sa, gate)
                        for kc in range(KC):
                            i(e.matmul(pD[p][:, g * 512:g * 512 + 512],
                                       w0t[p][:, kc * 128:(kc + 1) * 128],
                                       h2fm[:, kc * TOK + g * 512:kc * TOK + g * 512 + 512],
                                       start=(kc == 0), stop=(kc == KC - 1)), st_)
                    T(t17)
                    w3_last[p] = st_.v
                    tok_d = st_.v

                    def a17(e, en, m=m, p=p, g=g, tok_d=tok_d, gate=g_stage[p]):
                        w(e, en, st_, tok_d)
                        w(e, en, y_g[p], gate)
                        i(e.activation(gst[p][:, g * 512:g * 512 + 512],
                                       pD[p][:, g * 512:g * 512 + 512],
                                       AF.Gelu, bias=b0_col[:, m:m + 1]), sa)
                    A(a17)
                    evD[p][g] = sa.v
                tok_g = sa.v

                def s20(e, en, m=m, p=p, tok_g=tok_g):
                    w(e, en, sa, tok_g)
                    i(e.dma_start(out=dr_g.ap()[m * 128:(m + 1) * 128, :],
                                  in_=gst[p][:, :]), y_g[p], 16)
                S(s20)
                g_stage[p] = y_g[p].v
            kb.barrier()

        with ExitStack() as pctx:
            pe = pctx.enter_context
            TH = TOK // 2
            NKH = HID // 128
            ghalf = pe(nc.sbuf_tensor("ghalf", [128, NKH * TH], F16))
            w1t = [pe(nc.sbuf_tensor(f"w1t{p}", [128, NKH * 128], F16)) for p in range(2)]
            est = [pe(nc.sbuf_tensor(f"est{p}", [128, TH], F16)) for p in range(2)]
            pE7 = [pe(nc.psum_tensor(f"pE7_{p}", [128, 1024], F32)) for p in range(2)]
            y_gh = kb.sem("y_gh")
            g_w4 = [kb.sem(f"g_w4_{p}") for p in range(2)]
            y_ef = [kb.sem(f"y_ef{p}") for p in range(2)]

            e2groups = [(s, min(512, TH - s)) for s in range(0, TH, 512)]
            w4_last = [0, 0]
            evE = [[0] * len(e2groups), [0] * len(e2groups)]
            e_stage = [0, 0]
            for gh in range(2):
                def s21(e, en, gh=gh, gate=max(w4_last)):
                    if gh > 0:
                        w(e, en, st_, gate)
                    for kc in range(NKH):
                        i(e.dma_start(
                            out=ghalf[:, kc * TH:(kc + 1) * TH],
                            in_=dr_g.ap()[kc * 128:(kc + 1) * 128,
                                          gh * TH:(gh + 1) * TH]), y_gh, 16)
                S(s21)
                y_gh_tok = y_gh.v

                for m in range(16):
                    p = m % 2

                    def g6(e, en, m=m, p=p, gate=w4_last[p]):
                        w(e, en, st_, gate)
                        src = APx(w1, m * 8192, [[16 * 64 * 128, 128], [1, 8192]])
                        dst = APx(w1t[p], 0, [[NKH * 128, 128], [1, 8192]])
                        i(e.dma_start(out=dst, in_=src), g_w4[p], 16)
                    G(g6)
                    tok_w4 = g_w4[p].v

                    for k2, (s0_, cn) in enumerate(e2groups):
                        def t18(e, en, m=m, p=p, k2=k2, s0_=s0_, cn=cn,
                                tok_w4=tok_w4, y_gh_tok=y_gh_tok, gate=evE[p][k2]):
                            w(e, en, g_w4[p], tok_w4)
                            w(e, en, y_gh, y_gh_tok)
                            w(e, en, sv, gate)
                            for kc in range(NKH):
                                i(e.matmul(pE7[p][:, k2 * 512:k2 * 512 + cn],
                                           w1t[p][:, kc * 128:(kc + 1) * 128],
                                           ghalf[:, kc * TH + s0_:kc * TH + s0_ + cn],
                                           start=(kc == 0), stop=(kc == NKH - 1)), st_)
                        T(t18)
                        w4_last[p] = st_.v
                        tok_e7 = st_.v

                        def v39(e, en, m=m, p=p, k2=k2, s0_=s0_, cn=cn,
                                tok_e7=tok_e7, gate=e_stage[p]):
                            w(e, en, st_, tok_e7)
                            w(e, en, y_ef[p], gate)
                            i(e.tensor_scalar(est[p][:, s0_:s0_ + cn],
                                              pE7[p][:, k2 * 512:k2 * 512 + cn],
                                              b1_col[:, m:m + 1], None, ALU.add), sv)
                        V(v39)
                        evE[p][k2] = sv.v
                    tok_es = sv.v

                    def s22(e, en, m=m, p=p, gh=gh, tok_es=tok_es):
                        w(e, en, sv, tok_es)
                        i(e.dma_start(out=dr_efm.ap()[m * 128:(m + 1) * 128,
                                                      gh * TH:(gh + 1) * TH],
                                      in_=est[p][:, :]), y_ef[p], 16)
                    S(s22)
                    e_stage[p] = y_ef[p].v
            kb.barrier()

        with ExitStack() as pctx:
            pe = pctx.enter_context
            ett = [pe(nc.sbuf_tensor(f"ett{p}", [128, DM], F16)) for p in range(2)]
            x1b = [pe(nc.sbuf_tensor(f"x1b{p}", [128, DM], F32)) for p in range(2)]
            outt = [pe(nc.sbuf_tensor(f"outt{p}", [128, DM], F32)) for p in range(2)]
            y_e2 = [kb.sem(f"y_e2_{p}") for p in range(2)]
            y_out = [kb.sem(f"y_out{p}") for p in range(2)]

            out_stage = [0, 0]
            for tt in range(TOK // 128):
                p = tt % 2

                def s23(e, en, tt=tt, p=p, gate=out_stage[p]):
                    w(e, en, y_out[p], gate)
                    i(e.dma_start_transpose(
                        out=ett[p][:, :],
                        in_=dr_efm.ap()[:, tt * 128:(tt + 1) * 128]), y_e2[p], 16)
                    i(e.dma_start(out=x1b[p][:, :],
                                  in_=dr_x1.ap()[tt * 128:(tt + 1) * 128, :]), y_e2[p], 16)
                S(s23)
                tok_e2r = y_e2[p].v

                def v40(e, en, p=p, tok=tok_e2r, gate=y_out[p].v):
                    w(e, en, y_e2[p], tok)
                    w(e, en, y_out[p], gate)
                    i(e.tensor_add(outt[p][:, :], x1b[p][:, :], ett[p][:, :]), sv)
                V(v40)
                tok_o = sv.v

                def s24(e, en, tt=tt, p=p, tok_o=tok_o):
                    w(e, en, sv, tok_o)
                    i(e.dma_start(out=out.ap()[tt * 128:(tt + 1) * 128, :],
                                  in_=outt[p][:, :]), y_out[p], 16)
                S(s24)
                out_stage[p] = y_out[p].v

            def s25(e, en):
                w(e, en, y_out[0], y_out[0].v)
                w(e, en, y_out[1], y_out[1].v)
            S(s25)

    return nc


_NC_CACHE = {}


def _get_nc(tok, ncores):
    key = (tok, ncores)
    if key not in _NC_CACHE:
        _NC_CACHE[key] = build_nc(tok, ncores)
    return _NC_CACHE[key]


HP = np.array([(rr % 16) * 4 + rr // 16 for rr in range(NH)])
FP = (HP[:, None] * HD + np.arange(HD)[None, :]).reshape(-1)


def _tile_w(wT, mw_last=None):
    K_, M_ = wT.shape
    nkc = K_ // 128
    tmp = wT.reshape(nkc, 128, M_)
    blocks = []
    t0 = 0
    while t0 < M_:
        mw = min(128, M_ - t0)
        blk = tmp[:, :, t0:t0 + mw].transpose(1, 0, 2).reshape(128, nkc * mw)
        blocks.append(blk)
        t0 += mw
    return np.ascontiguousarray(np.concatenate(blocks, axis=1).astype(np.float32))


def make_in_maps(u, weights, tok, ncores):
    ipw0 = weights["in_proj_w"].T.astype(np.float32)
    colperm = np.concatenate([FP, DI + FP,
                              np.arange(2 * DI, 2 * DI + 2 * DS),
                              2 * DI + 2 * DS + HP])
    ipw = _tile_w(np.ascontiguousarray(ipw0[:, colperm]))
    cpm = np.concatenate([FP, np.arange(DI, CD)])
    cwT = np.ascontiguousarray(weights["conv_w"].T.astype(np.float32)[:, cpm])
    cb = np.ascontiguousarray(weights["conv_b"].astype(np.float32)[cpm].reshape(34, 128))
    dtb = np.ascontiguousarray(weights["dt_bias"].astype(np.float32)[HP].reshape(NH, 1))
    alog = np.ascontiguousarray(weights["A_log"].astype(np.float32)[HP].reshape(NH, 1))
    dvec = np.ascontiguousarray(weights["D"].astype(np.float32)[HP].reshape(1, NH))
    nw = np.ascontiguousarray(weights["norm_weight"].astype(np.float32)[FP].reshape(1, DI))
    opw = _tile_w(np.ascontiguousarray(weights["out_proj_w"].T.astype(np.float32)[FP, :]))
    w0 = _tile_w(np.ascontiguousarray(weights["W0"].T.astype(np.float32)))
    b0 = np.ascontiguousarray(weights["bias0"].astype(np.float32).reshape(HID // 128, 128))
    w1 = _tile_w(np.ascontiguousarray(weights["W1"].T.astype(np.float32)))
    b1 = np.ascontiguousarray(weights["bias1"].astype(np.float32).reshape(DM // 128, 128))
    in_maps = []
    for c in range(ncores):
        seq, half = c // 2, c % 2
        if half == 0:
            u_core = np.concatenate(
                [np.zeros((HALO, DM), np.float32), u[seq, 0:tok]], axis=0)
        else:
            u_core = u[seq, tok - HALO:2 * tok]
        flag = np.full((128, 1), float(half), np.float32)
        in_maps.append(dict(u=np.ascontiguousarray(u_core.astype(np.float32)),
                            ipw=ipw, cwT=cwT, cb=cb, dtb=dtb, alog=alog,
                            dvec=dvec, nw=nw, opw=opw, w0=w0, b0=b0, w1=w1,
                            b1=b1, flag=flag))
    return in_maps


def kernel(L0_attn_u, L0_attn_in_proj_w, L0_attn_conv_w, L0_attn_conv_b,
           L0_attn_dt_bias, L0_attn_A_log, L0_attn_D, L0_attn_norm_weight,
           L0_attn_out_proj_w, L0_ffn_W0, L0_ffn_bias0, L0_ffn_W1,
           L0_ffn_bias1):
    u = np.ascontiguousarray(np.asarray(L0_attn_u, dtype=np.float32))
    B, L, _ = u.shape
    TOK = L // 2
    ncores = 8
    weights = dict(in_proj_w=np.asarray(L0_attn_in_proj_w),
                   conv_w=np.asarray(L0_attn_conv_w),
                   conv_b=np.asarray(L0_attn_conv_b),
                   dt_bias=np.asarray(L0_attn_dt_bias),
                   A_log=np.asarray(L0_attn_A_log),
                   D=np.asarray(L0_attn_D),
                   norm_weight=np.asarray(L0_attn_norm_weight),
                   out_proj_w=np.asarray(L0_attn_out_proj_w),
                   W0=np.asarray(L0_ffn_W0), bias0=np.asarray(L0_ffn_bias0),
                   W1=np.asarray(L0_ffn_W1), bias1=np.asarray(L0_ffn_bias1))
    in_maps = make_in_maps(u, weights, TOK, ncores)
    nc = _get_nc(TOK, ncores)
    res = run_bass_kernel_spmd(nc, in_maps, core_ids=list(range(ncores)))
    outp = np.empty((B, L, DM), np.float32)
    for c in range(ncores):
        seq, half = c // 2, c % 2
        outp[seq, half * TOK:(half + 1) * TOK] = res.results[c]["out"]
    return outp


# version 10
# speedup vs baseline: 1.0125x; 1.0125x over previous
import numpy as np
from contextlib import ExitStack

import concourse.bass as bass
import concourse.mybir as mybir
from concourse.bass_utils import run_bass_kernel_spmd

F32 = mybir.dt.float32
F16 = mybir.dt.float16
AF = mybir.ActivationFunctionType
ALU = mybir.AluOpType
AX = mybir.AxisListType

DM = 2048
DI = 4096
DS = 128
NH = 64
HD = 64
CD = DI + 2 * DS
DIP = 2 * DI + 2 * DS + NH
HID = 8192
EPS = 1e-5
HALO = 3
KC = DM // 128


def APx(t, offset, dims):
    return bass.AP(t, offset, [list(d) for d in dims])


class Sem:
    __slots__ = ("h", "v", "name")


class KB:
    def __init__(self, nc, ctx, tok):
        self.nc = nc
        self.ctx = ctx
        self.TOK = tok
        self.NCH = tok // 128
        self.NG = tok // 512
        self.PADT = ((tok + HALO + 127) // 128) * 128
        self.blk = ctx.enter_context(nc.Block())
        self._waited = {}
        self.sems = {}

    def sem(self, name):
        s = Sem()
        s.h = self.ctx.enter_context(self.nc.semaphore(name))
        s.v = 0
        s.name = name
        self.sems[name] = s
        return s

    def V(self, fn):
        self.blk.vector(lambda e: fn(e, "v"))

    def A(self, fn):
        self.blk.scalar(lambda e: fn(e, "a"))

    def T(self, fn):
        self.blk.tensor(lambda e: fn(e, "t"))

    def G(self, fn):
        self.blk.gpsimd(lambda e: fn(e, "g"))

    def S(self, fn):
        self.blk.sync(lambda e: fn(e, "s"))

    def w(self, e, en, sem, val=None):
        val = sem.v if val is None else val
        if val <= 0:
            return
        k = (en, sem.name)
        if self._waited.get(k, -1) >= val:
            return
        e.wait_ge(sem.h, val)
        self._waited[k] = val

    def i(self, inst, sem, n=1):
        inst.then_inc(sem.h, n)
        sem.v += n
        return sem.v

    def barrier(self):
        snap = [(s, s.v) for s in self.sems.values() if s.v > 0]

        def mk(emit, en):
            def f(e, _en):
                for s, v in snap:
                    self.w(e, en, s, v)
            emit(f)
        mk(self.V, "v")
        mk(self.A, "a")
        mk(self.T, "t")
        mk(self.G, "g")
        mk(self.S, "s")


def build_nc(tok, ncores):
    nc = bass.Bass("TRN2", target_bir_lowering=False, debug=False)
    TOK = tok
    NCH = tok // 128
    NG = tok // 512
    assert tok % 512 == 0

    u_in = nc.dram_tensor("u", [TOK + HALO, DM], F32, kind="ExternalInput")
    ipw = nc.dram_tensor("ipw", [128, 16 * DIP], F32, kind="ExternalInput")
    cwT = nc.dram_tensor("cwT", [4, CD], F32, kind="ExternalInput")
    cb_r = nc.dram_tensor("cb", [34, 128], F32, kind="ExternalInput")
    dtb = nc.dram_tensor("dtb", [NH, 1], F32, kind="ExternalInput")
    alog = nc.dram_tensor("alog", [NH, 1], F32, kind="ExternalInput")
    dvec = nc.dram_tensor("dvec", [1, NH], F32, kind="ExternalInput")
    nwr = nc.dram_tensor("nw", [1, DI], F32, kind="ExternalInput")
    opw = nc.dram_tensor("opw", [128, 16 * 32 * 128], F32, kind="ExternalInput")
    w0 = nc.dram_tensor("w0", [128, 64 * 16 * 128], F32, kind="ExternalInput")
    b0_r = nc.dram_tensor("b0", [HID // 128, 128], F32, kind="ExternalInput")
    w1 = nc.dram_tensor("w1", [128, 16 * 64 * 128], F32, kind="ExternalInput")
    b1_r = nc.dram_tensor("b1", [DM // 128, 128], F32, kind="ExternalInput")
    flag_in = nc.dram_tensor("flag", [128, 1], F32, kind="ExternalInput")
    out = nc.dram_tensor("out", [TOK, DM], F32, kind="ExternalOutput")

    PADT = ((TOK + HALO + 127) // 128) * 128
    dr_hln = nc.dram_tensor("dr_hln", [PADT, DM], F16, kind="Internal")
    dr_z = nc.dram_tensor("dr_z", [DI, TOK], F16, kind="Internal")
    dr_xbc = nc.dram_tensor("dr_xbc", [CD, TOK], F16, kind="Internal")
    dr_ypre = nc.dram_tensor("dr_ypre", [TOK, DI], F16, kind="Internal")
    dr_yn = nc.dram_tensor("dr_yn", [TOK, DI], F16, kind="Internal")
    dr_attn = nc.dram_tensor("dr_attn", [DM, TOK], F16, kind="Internal")
    dr_x1 = nc.dram_tensor("dr_x1", [TOK, DM], F32, kind="Internal")
    dr_h2 = nc.dram_tensor("dr_h2", [TOK, DM], F16, kind="Internal")
    dr_g = nc.dram_tensor("dr_g", [HID, TOK], F16, kind="Internal")
    dr_efm = nc.dram_tensor("dr_efm", [DM, TOK], F16, kind="Internal")
    dr_mp = nc.dram_tensor("dr_mp", [16, 128, NH * 128], F16, kind="Internal")
    dr_cci = nc.dram_tensor("dr_cci", [128, DI], F32, kind="Internal")
    dr_cco = nc.dram_tensor("dr_cco", [128, DI], F32, kind="Internal")

    with ExitStack() as ctx:
        kb = KB(nc, ctx, tok)
        E = ctx.enter_context
        V, A, T, G, S = kb.V, kb.A, kb.T, kb.G, kb.S
        w, i = kb.w, kb.i

        ones16 = E(nc.sbuf_tensor("ones16", [128, 128], F16))
        ones32 = E(nc.sbuf_tensor("ones32", [1, 128], F32))
        ones2 = E(nc.sbuf_tensor("ones2", [2, 128], F16))
        idn32 = E(nc.sbuf_tensor("idn32", [128, 128], F32))
        b0_col = E(nc.sbuf_tensor("b0_col", [128, HID // 128], F32))
        b1_col = E(nc.sbuf_tensor("b1_col", [128, DM // 128], F32))
        flag_c = E(nc.sbuf_tensor("flag_c", [128, 1], F32))
        eps_c = E(nc.sbuf_tensor("eps_c", [128, 1], F32))
        one_c = E(nc.sbuf_tensor("one_c", [128, 1], F32))

        sv = kb.sem("sv")
        sa = kb.sem("sa")
        st_ = kb.sem("st")
        sg = kb.sem("sg")
        y_set = kb.sem("y_set")
        g_cc = kb.sem("g_cc")

        scan_ctx = ExitStack()
        Es = scan_ctx.enter_context
        zcol = Es(nc.sbuf_tensor("zcol", [NH, 1], F32))
        dtb_c = Es(nc.sbuf_tensor("dtb_c", [NH, 1], F32))
        alog_c = Es(nc.sbuf_tensor("alog_c", [NH, 1], F32))
        nA_col = Es(nc.sbuf_tensor("nA_col", [NH, 1], F32))
        D_nb = Es(nc.sbuf_tensor("D_nb", [128, NH], F16))
        nw_bc = Es(nc.sbuf_tensor("nw_bc", [128, DI], F16))
        convw_c = Es(nc.sbuf_tensor("convw_c", [128, 34 * 4], F32))
        convb_c = Es(nc.sbuf_tensor("convb_c", [128, 34], F32))
        dt_t = Es(nc.sbuf_tensor("dt_t", [NH, TOK], F32))
        dA = Es(nc.sbuf_tensor("dA", [NH, TOK], F32))
        dA_cum = Es(nc.sbuf_tensor("dA_cum", [NH, TOK], F32))
        dA_last = Es(nc.sbuf_tensor("dA_last", [NH, NCH], F32))
        cd_t = Es(nc.sbuf_tensor("cd_t", [NH, NCH], F32))
        G_incl = Es(nc.sbuf_tensor("G_incl", [NH, NCH], F32))
        G_pre = Es(nc.sbuf_tensor("G_pre", [NH, NCH], F32))
        carry16 = Es(nc.sbuf_tensor("carry16", [128, DI], F16))
        sum32 = Es(nc.sbuf_tensor("sum32", [128, DI], F32))
        patch16 = Es(nc.sbuf_tensor("patch16", [128, DI], F16))

        with ExitStack() as pctx:
            pS = pctx.enter_context(nc.psum_tensor("ps_set", [128, 512], F32))
            pS2 = pctx.enter_context(nc.psum_tensor("ps_set2", [128, 64], F32))
            drow = pctx.enter_context(nc.sbuf_tensor("drow", [1, NH], F32))
            d16row = pctx.enter_context(nc.sbuf_tensor("d16row", [1, NH], F16))
            nwrow = pctx.enter_context(nc.sbuf_tensor("nwrow", [1, DI], F32))
            nwrow16 = pctx.enter_context(nc.sbuf_tensor("nwrow16", [1, DI], F16))
            cwT_s = pctx.enter_context(nc.sbuf_tensor("cwT_s", [4, CD], F32))
            cb_s = pctx.enter_context(nc.sbuf_tensor("cb_s", [34, 128], F32))

            def s0(e, en):
                for dst, src in [
                    (dtb_c[:, :], dtb.ap()), (alog_c[:, :], alog.ap()),
                    (drow[:, :], dvec.ap()), (nwrow[:, :], nwr.ap()),
                    (cwT_s[:, :], cwT.ap()), (cb_s[:, :], cb_r.ap()),
                    (flag_c[:, :], flag_in.ap()),
                ]:
                    i(e.dma_start(out=dst, in_=src), y_set, 16)
            S(s0)

            b0_s = pctx.enter_context(nc.sbuf_tensor("b0_s", [HID // 128, 128], F32))
            b1_s = pctx.enter_context(nc.sbuf_tensor("b1_s", [DM // 128, 128], F32))

            def s0b(e, en):
                i(e.dma_start(out=b0_s[:, :], in_=b0_r.ap()), y_set, 16)
                i(e.dma_start(out=b1_s[:, :], in_=b1_r.ap()), y_set, 16)
            S(s0b)
            y_set_all = y_set.v

            def v0(e, en):
                i(e.memset(ones16[:, :], 1.0), sv)
                i(e.memset(ones32[:, :], 1.0), sv)
                i(e.memset(ones2[:, :], 1.0), sv)
                i(e.memset(zcol[:, :], 0.0), sv)
                i(e.memset(eps_c[:, :], EPS), sv)
                i(e.memset(one_c[:, :], 1.0), sv)
                w(e, en, y_set, y_set_all)
                i(e.tensor_copy(d16row[:, :], drow[:, :]), sv)
                i(e.tensor_copy(nwrow16[:, :], nwrow[:, :]), sv)
            V(v0)
            tok_v0 = sv.v

            def g0(e, en):
                i(e.memset(idn32[:, :], 1.0), sg)
                w(e, en, sg, sg.v)
                i(e.affine_select(idn32[:, :], idn32[:, :], [[1, 128]],
                                  ALU.is_equal, 0.0, base=0,
                                  channel_multiplier=-1), sg)
            G(g0)
            tok_idn = sg.v

            def a0(e, en):
                w(e, en, y_set, y_set_all)
                i(e.activation(nA_col[:, :], alog_c[:, :], AF.Exp), sa)
            A(a0)
            tok_a0 = sa.v

            def v1(e, en):
                w(e, en, sa, tok_a0)
                i(e.tensor_scalar_mul(nA_col[:, :], nA_col[:, :], -1.0), sv)
            V(v1)

            def t0(e, en):
                w(e, en, sg, tok_idn)
                w(e, en, y_set, y_set_all)
                i(e.transpose(pS[:, 0:34], cb_s[:, :], idn32[0:34, 0:34]), st_)
                i(e.transpose(pS[:, 64:64 + HID // 128], b0_s[:, :], idn32[0:64, 0:64]), st_)
                i(e.transpose(pS[:, 192:192 + DM // 128], b1_s[:, :], idn32[0:16, 0:16]), st_)
            T(t0)
            tok_t0 = st_.v

            def v2(e, en):
                w(e, en, st_, tok_t0)
                i(e.tensor_copy(convb_c[:, :], pS[:, 0:34]), sv)
                i(e.tensor_copy(b0_col[:, :], pS[:, 64:64 + HID // 128]), sv)
                i(e.tensor_copy(b1_col[:, :], pS[:, 192:192 + DM // 128]), sv)
            V(v2)
            tok_v2 = sv.v

            def t1(e, en):
                w(e, en, sv, tok_v2)
                for mt in range(34):
                    i(e.transpose(pS[:, mt * 4:mt * 4 + 4],
                                  cwT_s[:, mt * 128:(mt + 1) * 128], idn32[0:4, 0:4]), st_)
            T(t1)
            tok_t1 = st_.v

            def v3(e, en):
                w(e, en, st_, tok_t1)
                i(e.tensor_copy(convw_c[:, :], pS[:, 0:136]), sv)
            V(v3)
            tok_v3 = sv.v

            def t2(e, en):
                w(e, en, sv, tok_v0)
                i(e.matmul(pS2[:, 0:NH], ones16[0:1, :], d16row[0:1, :],
                           start=True, stop=True), st_)
            T(t2)
            tok_t2 = st_.v

            def v4(e, en):
                w(e, en, st_, tok_t2)
                i(e.tensor_copy(D_nb[:, :], pS2[:, 0:NH]), sv)
            V(v4)

            for q in range(DI // 512):
                def tq(e, en, q=q, gate=sv.v):
                    w(e, en, sv, gate)
                    i(e.matmul(pS[:, 0:512], ones16[0:1, :],
                               nwrow16[0:1, q * 512:(q + 1) * 512],
                               start=True, stop=True), st_)
                T(tq)

                def vq(e, en, q=q, gate=st_.v):
                    w(e, en, st_, gate)
                    i(e.tensor_copy(nw_bc[:, q * 512:(q + 1) * 512], pS[:, 0:512]), sv)
                V(vq)
            kb.barrier()

        with ExitStack() as pctx:
            pe = pctx.enter_context
            u_t = [pe(nc.sbuf_tensor(f"u_t{p}", [128, DM], F32)) for p in range(2)]
            usq = [pe(nc.sbuf_tensor(f"usq{p}", [128, DM], F32)) for p in range(2)]
            hlt = [pe(nc.sbuf_tensor(f"hlt{p}", [128, DM], F16)) for p in range(2)]
            msum = [pe(nc.sbuf_tensor(f"msum{p}", [128, 1], F32)) for p in range(2)]
            mneg = [pe(nc.sbuf_tensor(f"mneg{p}", [128, 1], F32)) for p in range(2)]
            qsum = [pe(nc.sbuf_tensor(f"qsum{p}", [128, 1], F32)) for p in range(2)]
            q2 = [pe(nc.sbuf_tensor(f"q2{p}", [128, 1], F32)) for p in range(2)]
            nvar = [pe(nc.sbuf_tensor(f"nvar{p}", [128, 1], F32)) for p in range(2)]
            rstd = [pe(nc.sbuf_tensor(f"rstd{p}", [128, 1], F32)) for p in range(2)]
            mr = [pe(nc.sbuf_tensor(f"mr{p}", [128, 1], F32)) for p in range(2)]
            y_u = [kb.sem(f"y_u{p}") for p in range(2)]
            y_hl = [kb.sem(f"y_hl{p}") for p in range(2)]

            nrows = TOK + HALO
            tiles = [(r0, min(128, nrows - r0)) for r0 in range(0, nrows, 128)]
            npad = PADT - nrows
            if npad > 0:
                def vz(e, en):
                    i(e.memset(hlt[0][0:npad, :], 0.0), sv)
                V(vz)
                tok_z0 = sv.v

                def sz(e, en, tok_z0=tok_z0):
                    w(e, en, sv, tok_z0)
                    i(e.dma_start(out=dr_hln.ap()[nrows:PADT, :],
                                  in_=hlt[0][0:npad, :]), y_hl[0], 16)
                S(sz)
            last_apply = [y_hl[0].v * 0, 0]
            last_sq = [0, 0]
            for idx, (r0, n) in enumerate(tiles):
                p = idx % 2

                def s1(e, en, r0=r0, n=n, p=p, gate=last_apply[p]):
                    w(e, en, sg, gate)
                    i(e.dma_start(out=u_t[p][0:n, :],
                                  in_=u_in.ap()[r0:r0 + n, :]), y_u[p], 16)
                S(s1)
                tok_u = y_u[p].v

                def v6(e, en, n=n, p=p, tok_u=tok_u):
                    w(e, en, y_u[p], tok_u)
                    i(e.reduce_sum(msum[p][0:n, :], u_t[p][0:n, :], AX.X), sv)
                    w(e, en, sv, sv.v)
                    i(e.tensor_scalar_mul(mneg[p][0:n, :], msum[p][0:n, :], -1.0 / DM), sv)
                V(v6)
                tok_mn = sv.v

                def g1a(e, en, n=n, p=p, tok_u=tok_u, gate=last_sq[p]):
                    w(e, en, y_u[p], tok_u)
                    w(e, en, sv, gate)
                    i(e.tensor_tensor(usq[p][0:n, :], u_t[p][0:n, :],
                                      u_t[p][0:n, :], ALU.mult), sg)
                G(g1a)
                tok_q0 = sg.v

                def v6b(e, en, n=n, p=p, tok_q0=tok_q0):
                    w(e, en, sg, tok_q0)
                    i(e.reduce_sum(qsum[p][0:n, :], usq[p][0:n, :], AX.X), sv)
                V(v6b)
                tok_q = sv.v

                def v7(e, en, n=n, p=p, tok_q=tok_q, tok_mn=tok_mn):
                    w(e, en, sv, tok_q)
                    w(e, en, sv, tok_mn)
                    i(e.tensor_scalar_mul(q2[p][0:n, :], qsum[p][0:n, :], 1.0 / DM), sv)
                    w(e, en, sv, sv.v)
                    i(e.scalar_tensor_tensor(nvar[p][0:n, :], mneg[p][0:n, :],
                                             mneg[p][0:n, 0:1], q2[p][0:n, :],
                                             ALU.mult, ALU.subtract), sv)
                V(v7)
                tok_nv = sv.v
                last_sq[p] = tok_nv

                def a2(e, en, n=n, p=p, tok_nv=tok_nv):
                    w(e, en, sv, tok_nv)
                    i(e.activation(rstd[p][0:n, :], nvar[p][0:n, :], AF.Ln,
                                   bias=eps_c[0:n, 0:1], scale=-1.0), sa)
                    w(e, en, sa, sa.v)
                    i(e.activation(rstd[p][0:n, :], rstd[p][0:n, :], AF.Exp,
                                   scale=-0.5), sa)
                A(a2)
                tok_rs = sa.v

                def v8(e, en, n=n, p=p, tok_rs=tok_rs):
                    w(e, en, sa, tok_rs)
                    i(e.tensor_tensor(mr[p][0:n, :], mneg[p][0:n, :],
                                      rstd[p][0:n, :], ALU.mult), sv)
                V(v8)
                tok_mr = sv.v

                def g3a(e, en, n=n, p=p, tok_mr=tok_mr, gate=y_hl[p].v):
                    w(e, en, sv, tok_mr)
                    w(e, en, y_hl[p], gate)
                    i(e.tensor_scalar(hlt[p][0:n, :], u_t[p][0:n, :],
                                      mneg[p][0:n, 0:1], rstd[p][0:n, 0:1],
                                      ALU.add, ALU.mult), sg)
                G(g3a)
                last_apply[p] = sg.v
                tok_ap = sg.v

                def s2(e, en, r0=r0, n=n, p=p, tok_ap=tok_ap):
                    w(e, en, sg, tok_ap)
                    i(e.dma_start(out=dr_hln.ap()[r0:r0 + n, :],
                                  in_=hlt[p][0:n, :]), y_hl[p], 16)
                S(s2)
            kb.barrier()

        with ExitStack() as pctx:
            pe = pctx.enter_context
            hfm = pe(nc.sbuf_tensor("hfm", [128, KC * PADT], F16))
            w_in = [pe(nc.sbuf_tensor(f"w_in{p}", [128, KC * 128], F16)) for p in range(2)]
            strip = [pe(nc.sbuf_tensor(f"strip{p}", [128, HALO + TOK], F16)) for p in range(2)]
            conva = [pe(nc.sbuf_tensor(f"conva{p}", [128, TOK], F16)) for p in range(2)]
            convd = [pe(nc.sbuf_tensor(f"convd{p}", [128, TOK], F16)) for p in range(2)]
            xstg = [pe(nc.sbuf_tensor(f"xstg{p}", [128, TOK], F16)) for p in range(2)]
            zstg = [pe(nc.sbuf_tensor(f"zstg{p}", [128, TOK], F16)) for p in range(2)]
            psA = [pe(nc.psum_tensor(f"psA{p}", [128, 4 * 512], F32)) for p in range(2)]
            y_fm = kb.sem("y_fm")
            g_w = [kb.sem(f"g_w{p}") for p in range(2)]
            y_xbc = [kb.sem(f"y_xbc{p}") for p in range(2)]
            y_z = [kb.sem(f"y_z{p}") for p in range(2)]

            def s3(e, en):
                for kc in range(KC):
                    i(e.dma_start_transpose(
                        out=hfm[:, kc * PADT:kc * PADT + PADT],
                        in_=dr_hln.ap()[:, kc * 128:(kc + 1) * 128]), y_fm, 16)
            S(s3)
            y_fm_all = y_fm.v

            m_order = [(DI + CD, 64, "dt")] + \
                      [(DI + 128 * j, 128, "xbc") for j in range(34)] + \
                      [(128 * j, 128, "z") for j in range(32)]
            wt_last = [0, 0]
            ev_tok = [[0] * 4, [0] * 4]
            xbc_stage = [0, 0]
            z_stage = [0, 0]
            conv_sa = [0, 0]
            strip_used = [0, 0]
            dt_evac_tok = 0

            for mi, (f0, mw, kind) in enumerate(m_order):
                p = mi % 2

                def g1(e, en, f0=f0, mw=mw, p=p, gate=wt_last[p]):
                    w(e, en, st_, gate)
                    t_idx = f0 // 128
                    src = APx(ipw, t_idx * 2048, [[16 * DIP, 128], [1, KC * mw]])
                    dst = APx(w_in[p], 0, [[KC * 128, 128], [128, KC], [1, mw]])
                    i(e.dma_start(out=dst, in_=src), g_w[p], 16)
                G(g1)
                tok_wl = g_w[p].v

                groups = ([("halo", 0, HALO, 0)] if kind == "xbc" else []) + \
                         [("main", HALO + g * 512, 512, g) for g in range(NG)]

                for gname, c0, cn, bank in groups:
                    def t4(e, en, mw=mw, p=p, c0=c0, cn=cn, bank=bank,
                           tok_wl=tok_wl, gate_ev=ev_tok[p][bank]):
                        w(e, en, g_w[p], tok_wl)
                        w(e, en, y_fm, y_fm_all)
                        w(e, en, sv, gate_ev)
                        for kc in range(KC):
                            i(e.matmul(psA[p][0:mw, bank * 512:bank * 512 + cn],
                                       w_in[p][:, kc * 128:kc * 128 + mw],
                                       hfm[:, kc * PADT + c0:kc * PADT + c0 + cn],
                                       start=(kc == 0), stop=(kc == KC - 1)), st_)
                    T(t4)
                    tok_mm = st_.v
                    wt_last[p] = tok_mm

                    def v9(e, en, mw=mw, p=p, c0=c0, cn=cn, bank=bank,
                           tok_mm=tok_mm, kind=kind,
                           gate_strip=strip_used[p], gate_z=z_stage[p]):
                        w(e, en, st_, tok_mm)
                        if kind == "xbc":
                            w(e, en, sv, gate_strip)
                            i(e.tensor_copy(strip[p][0:mw, c0:c0 + cn],
                                            psA[p][0:mw, bank * 512:bank * 512 + cn]), sv)
                        elif kind == "z":
                            w(e, en, y_z[p], gate_z)
                            i(e.tensor_copy(zstg[p][0:mw, c0 - HALO:c0 - HALO + cn],
                                            psA[p][0:mw, bank * 512:bank * 512 + cn]), sv)
                        else:
                            i(e.tensor_copy(dt_t[0:mw, c0 - HALO:c0 - HALO + cn],
                                            psA[p][0:mw, bank * 512:bank * 512 + cn]), sv)
                    V(v9)
                    ev_tok[p][bank] = sv.v
                tok_evac = sv.v

                if kind == "dt":
                    dt_evac_tok = tok_evac
                elif kind == "xbc":
                    mt = (f0 - DI) // 128

                    def v10(e, en, p=p, mt=mt, tok_evac=tok_evac, gate=conv_sa[p]):
                        w(e, en, sv, tok_evac)
                        w(e, en, sa, gate)
                        i(e.scalar_tensor_tensor(
                            conva[p][:, :], strip[p][:, 0:TOK],
                            convw_c[:, 4 * mt + 0:4 * mt + 1],
                            APx(convb_c, mt, [[34, 128], [0, TOK]]),
                            ALU.mult, ALU.add), sv)
                        w(e, en, sv, sv.v)
                        i(e.scalar_tensor_tensor(
                            convd[p][:, :], strip[p][:, 1:TOK + 1],
                            convw_c[:, 4 * mt + 1:4 * mt + 2],
                            conva[p][:, :], ALU.mult, ALU.add), sv)
                        w(e, en, sv, sv.v)
                        i(e.scalar_tensor_tensor(
                            conva[p][:, :], strip[p][:, 2:TOK + 2],
                            convw_c[:, 4 * mt + 2:4 * mt + 3],
                            convd[p][:, :], ALU.mult, ALU.add), sv)
                        w(e, en, sv, sv.v)
                        i(e.scalar_tensor_tensor(
                            convd[p][:, :], strip[p][:, 3:TOK + 3],
                            convw_c[:, 4 * mt + 3:4 * mt + 4],
                            conva[p][:, :], ALU.mult, ALU.add), sv)
                    V(v10)
                    tok_conv = sv.v
                    strip_used[p] = tok_conv

                    def a4(e, en, p=p, tok_conv=tok_conv, gate=xbc_stage[p]):
                        w(e, en, sv, tok_conv)
                        w(e, en, y_xbc[p], gate)
                        i(e.activation(xstg[p][:, :], convd[p][:, :], AF.Silu), sa)
                    A(a4)
                    conv_sa[p] = sa.v
                    tok_silu = sa.v

                    def s4(e, en, p=p, mt=mt, tok_silu=tok_silu):
                        w(e, en, sa, tok_silu)
                        i(e.dma_start(out=dr_xbc.ap()[mt * 128:(mt + 1) * 128, :],
                                      in_=xstg[p][:, :]), y_xbc[p], 16)
                    S(s4)
                    xbc_stage[p] = y_xbc[p].v
                else:
                    mt = f0 // 128

                    def s5(e, en, p=p, mt=mt, tok_evac=tok_evac):
                        w(e, en, sv, tok_evac)
                        i(e.dma_start(out=dr_z.ap()[mt * 128:(mt + 1) * 128, :],
                                      in_=zstg[p][:, :]), y_z[p], 16)
                    S(s5)
                    z_stage[p] = y_z[p].v

            def v11(e, en, gate=dt_evac_tok):
                w(e, en, sv, gate)
                i(e.tensor_scalar(dt_t[:, :], dt_t[:, :], dtb_c[:, 0:1], None,
                                  ALU.add), sv)
            V(v11)
            tok_abs = sv.v

            def a5(e, en, tok_abs=tok_abs):
                w(e, en, sv, tok_abs)
                i(e.activation(dA[:, :], dt_t[:, :], AF.Abs), sa)
                w(e, en, sa, sa.v)
                i(e.activation(dA_cum[:, :], dA[:, :], AF.Exp, scale=-1.0), sa)
                w(e, en, sa, sa.v)
                i(e.activation(dA_cum[:, :], dA_cum[:, :], AF.Ln, bias=one_c[0:NH, 0:1]), sa)
            A(a5)
            tok_ln1p = sa.v

            def v12(e, en, tok_ln1p=tok_ln1p):
                w(e, en, sa, tok_ln1p)
                i(e.scalar_tensor_tensor(dt_t[:, :], dt_t[:, :], 0.0,
                                         dA_cum[:, :], ALU.max, ALU.add), sv)
                w(e, en, sv, sv.v)
                i(e.tensor_scalar(dA[:, :], dt_t[:, :], nA_col[:, 0:1], None,
                                  ALU.mult), sv)
                w(e, en, sv, sv.v)
                for z in range(NCH):
                    i(e.tensor_tensor_scan(
                        dA_cum[:, z * 128:(z + 1) * 128],
                        dA[:, z * 128:(z + 1) * 128],
                        APx(zcol, 0, [[1, NH], [0, 128]]),
                        0.0, ALU.add, ALU.add), sv)
                w(e, en, sv, sv.v)
                i(e.tensor_copy(dA_last[:, :],
                                APx(dA_cum, 127, [[TOK, NH], [128, NCH]])), sv)
            V(v12)
            tok_dal = sv.v

            def a6(e, en, tok_dal=tok_dal):
                w(e, en, sv, tok_dal)
                i(e.activation(cd_t[:, :], dA_last[:, :], AF.Exp), sa)
            A(a6)
            tok_cd = sa.v

            def v13(e, en, tok_cd=tok_cd):
                w(e, en, sa, tok_cd)
                i(e.tensor_tensor_scan(G_incl[:, :], cd_t[:, :],
                                       APx(zcol, 0, [[1, NH], [0, NCH]]),
                                       1.0, ALU.mult, ALU.add), sv)
                i(e.memset(G_pre[:, 0:1], 1.0), sv)
                if NCH > 1:
                    w(e, en, sv, sv.v)
                    i(e.tensor_copy(G_pre[:, 1:NCH], G_incl[:, 0:NCH - 1]), sv)
            V(v13)
            kb.barrier()

        with ExitStack() as pctx:
            pe = pctx.enter_context
            x_tok = [pe(nc.sbuf_tensor(f"x_tok{p}", [128, DI], F16)) for p in range(2)]
            Bfm = [pe(nc.sbuf_tensor(f"Bfm{p}", [128, 128], F16)) for p in range(2)]
            Cfm = [pe(nc.sbuf_tensor(f"Cfm{p}", [128, 128], F16)) for p in range(2)]
            Btok = [pe(nc.sbuf_tensor(f"Btok{p}", [128, 128], F16)) for p in range(2)]
            lhsE = [pe(nc.sbuf_tensor(f"lhsE{p}", [128, 16 * 128], F16)) for p in range(2)]
            rhsE = [pe(nc.sbuf_tensor(f"rhsE{p}", [128, 16 * 128], F16)) for p in range(2)]
            rhs2 = pe(nc.sbuf_tensor("rhs2", [2, NH * 128], F16))
            Hp16 = [pe(nc.sbuf_tensor(f"Hp16{p}", [NH, 128], F16)) for p in range(2)]
            Lp16 = [pe(nc.sbuf_tensor(f"Lp16{p}", [NH, 128], F16)) for p in range(2)]
            nH16 = [pe(nc.sbuf_tensor(f"nH16{p}", [NH, 128], F16)) for p in range(2)]
            nL16 = [pe(nc.sbuf_tensor(f"nL16{p}", [NH, 128], F16)) for p in range(2)]
            Ltmp = pe(nc.sbuf_tensor("Ltmp", [NH, 128], F32))
            M_sb = pe(nc.sbuf_tensor("M_sb", [128, NH * 128], F16))
            Mp_sb = pe(nc.sbuf_tensor("Mp_sb", [128, NH * 128], F16))
            CBT16 = pe(nc.sbuf_tensor("CBT16", [128, 128], F16))
            dtT16 = pe(nc.sbuf_tensor("dtT16", [128, NH], F16))
            decT16 = pe(nc.sbuf_tensor("decT16", [128, NH], F16))
            dec_f = pe(nc.sbuf_tensor("dec_f", [NH, 128], F32))
            cdrow = pe(nc.sbuf_tensor("cdrow", [1, NH], F32))
            cdnb = pe(nc.sbuf_tensor("cdnb", [128, NH], F32))
            Xdt = pe(nc.sbuf_tensor("Xdt", [128, DI], F16))
            Xdtd = pe(nc.sbuf_tensor("Xdtd", [128, DI], F16))
            ct1 = pe(nc.sbuf_tensor("ct1", [128, 512], F16))
            Yst = [pe(nc.sbuf_tensor(f"Yst{p}", [128, DI], F16)) for p in range(2)]
            pE = pe(nc.psum_tensor("pE", [128, 2048], F32))
            pEp = pe(nc.psum_tensor("pEp", [128, 1024], F32))
            pY = pe(nc.psum_tensor("pY", [128, 1024], F32))

            y_xt = [kb.sem(f"y_xt{p}") for p in range(2)]
            y_bc = [kb.sem(f"y_bc{p}") for p in range(2)]
            y_be = [kb.sem(f"y_be{p}") for p in range(2)]
            y_yp = [kb.sem(f"y_yp{p}") for p in range(2)]
            y_mp = kb.sem("y_mp")

            def v14(e, en):
                i(e.memset(carry16[:, :], 0.0), sv)
                for p in range(2):
                    i(e.memset(lhsE[p][:, :], 1.0), sv)
                    i(e.memset(rhsE[p][:, :], 1.0), sv)
            V(v14)
            kb.barrier()

            def hmap(rr):
                j, r = rr // 16, rr % 16
                return (r // 4) * 16 + j * 4 + (r % 4)

            hl_tok = [0, 0]
            be_done = [0, 0]
            mp_tok = [0] * 16
            m_tok = [0] * 16
            mp_done_v = [0, 0]
            e_evac = [0, 0, 0, 0]
            ysv = [0, 0]
            st_carry = [0] * 8
            y_stage_tok = [0, 0]
            rhs2_used = [0]
            xtok_used = [0, 0]
            cfm_used = [0, 0]
            btok_used = [0, 0]
            bfm_used = [0, 0]
            xdt_used_st = 0
            m_used_st = 0

            for z in range(NCH):
                p = z % 2

                def s6(e, en, z=z, p=p, g1_=xtok_used[p], g2_=cfm_used[p],
                       g3_=btok_used[p], g4_=bfm_used[p],
                       gb0=be_done[0], gb1=be_done[1]):
                    w(e, en, sv, g1_)
                    w(e, en, sv, g2_)
                    w(e, en, st_, g3_)
                    w(e, en, st_, g4_)
                    w(e, en, y_be[0], gb0)
                    w(e, en, y_be[1], gb1)
                    i(e.dma_start_transpose(
                        out=x_tok[p][:, :],
                        in_=dr_xbc.ap()[0:DI, z * 128:(z + 1) * 128]), y_xt[p], 16)
                    i(e.dma_start(out=Bfm[p][:, :],
                                  in_=dr_xbc.ap()[DI:DI + 128, z * 128:(z + 1) * 128]), y_bc[p], 16)
                    i(e.dma_start(out=Cfm[p][:, :],
                                  in_=dr_xbc.ap()[DI + 128:DI + 256, z * 128:(z + 1) * 128]), y_bc[p], 16)
                    i(e.dma_start_transpose(
                        out=Btok[p][:, :],
                        in_=dr_xbc.ap()[DI:DI + 128, z * 128:(z + 1) * 128]), y_bc[p], 16)
                S(s6)
                tok_xt = y_xt[p].v
                tok_bc = y_bc[p].v

                def v15(e, en, z=z, p=p, gate=be_done[p]):
                    w(e, en, y_be[p], gate)
                    i(e.tensor_copy(Hp16[p][:, :], dA_cum[:, z * 128:(z + 1) * 128]), sv)
                    w(e, en, sv, sv.v)
                    i(e.tensor_tensor(Ltmp[:, :], dA_cum[:, z * 128:(z + 1) * 128],
                                      Hp16[p][:, :], ALU.subtract), sv)
                    w(e, en, sv, sv.v)
                    i(e.tensor_copy(Lp16[p][:, :], Ltmp[:, :]), sv)
                    i(e.tensor_scalar_mul(nH16[p][:, :], Hp16[p][:, :], -1.0), sv)
                    w(e, en, sv, sv.v)
                    i(e.tensor_scalar_mul(nL16[p][:, :], Lp16[p][:, :], -1.0), sv)
                V(v15)
                hl_tok[p] = sv.v
                tok_hl = sv.v

                def s7(e, en, z=z, p=p, tok_hl=tok_hl, tok_xt=tok_xt,
                       tok_bc=tok_bc):
                    w(e, en, sv, tok_hl)
                    w(e, en, y_xt[p], tok_xt)
                    w(e, en, y_bc[p], tok_bc)
                    for j in range(4):
                        for (tile, row, src) in [(lhsE[p], 32 * j + 2, nH16[p]),
                                                 (lhsE[p], 32 * j + 3, nL16[p]),
                                                 (rhsE[p], 32 * j + 0, Hp16[p]),
                                                 (rhsE[p], 32 * j + 1, Lp16[p])]:
                            i(e.dma_start(out=tile[row:row + 1, 0:2048],
                                          in_=APx(src, 16 * j * 128, [[128, 16], [1, 128]])),
                              y_be[p], 16)
                    w(e, en, st_, rhs2_used[0])
                    i(e.dma_start(out=rhs2[0:1, :],
                                  in_=APx(Hp16[p], 0, [[128, NH], [1, 128]])), y_be[p], 16)
                    i(e.dma_start(out=rhs2[1:2, :],
                                  in_=APx(Lp16[p], 0, [[128, NH], [1, 128]])), y_be[p], 16)
                S(s7)
                be_done[p] = y_be[p].v
                tok_be = y_be[p].v

                def t5(e, en, p=p, tok_bc=tok_bc, g1_=ysv[0], g2_=ysv[1]):
                    w(e, en, y_bc[p], tok_bc)
                    w(e, en, sv, g1_)
                    w(e, en, sv, g2_)
                    i(e.matmul(pY[:, 0:128], Bfm[p][:, :], Cfm[p][:, :],
                               start=True, stop=True), st_)
                T(t5)
                bfm_used[p] = st_.v
                tok_cbt_mm = st_.v

                def v16(e, en, tok=tok_cbt_mm):
                    w(e, en, st_, tok)
                    i(e.tensor_copy(CBT16[:, :], pY[:, 0:128]), sv)
                V(v16)
                tok_cbt_cp = sv.v

                def g2(e, en, tok=tok_cbt_cp):
                    w(e, en, sv, tok)
                    i(e.affine_select(CBT16[:, :], CBT16[:, :], [[1, 128]],
                                      ALU.is_ge, 0.0, base=0,
                                      channel_multiplier=-1), sg)
                G(g2)
                tok_cbt = sg.v

                def v17(e, en, z=z):
                    i(e.tensor_scalar(dec_f[:, :], dA_cum[:, z * 128:(z + 1) * 128],
                                      dA_last[:, z:z + 1], None, ALU.subtract), sv)
                V(v17)
                tok_decf = sv.v

                def a7(e, en, tok=tok_decf):
                    w(e, en, sv, tok)
                    i(e.activation(dec_f[:, :], dec_f[:, :], AF.Exp, scale=-1.0), sa)
                A(a7)
                tok_dece = sa.v

                def t6(e, en, z=z, tok_dece=tok_dece, tok_cp=tok_cbt_cp):
                    w(e, en, sa, tok_dece)
                    w(e, en, sv, tok_cp)
                    i(e.transpose(pY[0:128, 128:192], dt_t[:, z * 128:(z + 1) * 128],
                                  idn32[0:64, 0:64]), st_)
                    i(e.transpose(pY[0:128, 192:256], dec_f[:, :], idn32[0:64, 0:64]), st_)
                    i(e.transpose(pY[0:1, 256:320], cd_t[:, z:z + 1], idn32[0:64, 0:64]), st_)
                T(t6)
                tok_tr = st_.v

                def v18(e, en, tok=tok_tr):
                    w(e, en, st_, tok)
                    i(e.tensor_copy(dtT16[:, :], pY[:, 128:192]), sv)
                    i(e.tensor_copy(decT16[:, :], pY[:, 192:256]), sv)
                    i(e.tensor_copy(cdrow[0:1, :], pY[0:1, 256:320]), sv)
                V(v18)
                tok_trc = sv.v

                def t7(e, en, tok=tok_trc):
                    w(e, en, sv, tok)
                    i(e.matmul(pY[:, 320:384], ones32[0:1, :], cdrow[0:1, :],
                               start=True, stop=True), st_)
                T(t7)
                tok_cdmm = st_.v

                def v19(e, en, tok=tok_cdmm):
                    w(e, en, st_, tok)
                    i(e.tensor_copy(cdnb[:, :], pY[:, 320:384]), sv)
                V(v19)
                tok_cdnb = sv.v

                def v20(e, en, p=p, tok_xt=tok_xt, tok_trc=tok_trc,
                        gate=xdt_used_st):
                    w(e, en, y_xt[p], tok_xt)
                    w(e, en, sv, tok_trc)
                    w(e, en, st_, gate)
                    i(e.tensor_tensor(Xdt[:, :], x_tok[p][:, :],
                                      APx(dtT16, 0, [[NH, 128], [1, NH], [0, HD]]),
                                      ALU.mult), sv)
                    w(e, en, sv, sv.v)
                    i(e.tensor_tensor(Xdtd[:, :], Xdt[:, :],
                                      APx(decT16, 0, [[NH, 128], [1, NH], [0, HD]]),
                                      ALU.mult), sv)
                V(v20)
                tok_xdt = sv.v
                xtok_used[p] = tok_xdt

                for q16 in range(16):
                    half = q16 % 2

                    def t8(e, en, p=p, q16=q16, half=half, tok_be=tok_be,
                           gate=mp_done_v[half]):
                        w(e, en, y_be[p], tok_be)
                        w(e, en, sv, gate)
                        i(e.matmul(pEp[:, half * 512:half * 512 + 512],
                                   ones2[0:2, :],
                                   rhs2[0:2, q16 * 512:(q16 + 1) * 512],
                                   start=True, stop=True), st_)
                    T(t8)
                    tok_ep = st_.v

                    def v21a(e, en, q16=q16, half=half, tok_ep=tok_ep,
                             gate=m_used_st, gate_mp=y_mp.v):
                        w(e, en, st_, tok_ep)
                        w(e, en, st_, gate)
                        w(e, en, y_mp, gate_mp)
                        i(e.tensor_scalar_min(Mp_sb[:, q16 * 512:(q16 + 1) * 512],
                                              pEp[:, half * 512:half * 512 + 512],
                                              0.0), sv)
                    V(v21a)
                    mp_done_v[half] = sv.v
                rhs2_used[0] = st_.v
                tok_mpraw = sv.v

                def a8(e, en, tok=tok_mpraw):
                    w(e, en, sv, tok)
                    i(e.activation(Mp_sb[:, 0:4096], Mp_sb[:, 0:4096], AF.Exp), sa)
                    i(e.activation(Mp_sb[:, 4096:8192], Mp_sb[:, 4096:8192],
                                   AF.Exp), sa)
                A(a8)
                tok_mpe = sa.v

                def v21(e, en, p=p, tok=tok_mpe):
                    w(e, en, sa, tok)
                    i(e.tensor_tensor(
                        Mp_sb[:, :],
                        Mp_sb[:, :],
                        APx(Cfm[p], 0, [[128, 128], [0, NH], [1, 128]]),
                        ALU.mult), sv)
                V(v21)
                for q16 in range(16):
                    mp_tok[q16] = sv.v
                cfm_used[p] = sv.v
                tok_mp_done = sv.v

                def s_mp(e, en, z=z, tok=tok_mp_done):
                    w(e, en, sv, tok)
                    i(e.dma_start(out=dr_mp.ap()[z, :, :], in_=Mp_sb[:, :]),
                      y_mp, 16)
                S(s_mp)

                for cyc in range(4):
                    def t9(e, en, p=p, cyc=cyc, tok_be=tok_be,
                           gates=tuple(e_evac)):
                        w(e, en, y_be[p], tok_be)
                        for b in range(4):
                            w(e, en, sv, gates[b])
                        for rr in range(4):
                            r = cyc * 4 + rr
                            for j in range(4):
                                i(e.matmul(
                                    pE[:, j * 512 + rr * 128:j * 512 + rr * 128 + 128],
                                    lhsE[p][32 * j:32 * j + 4, r * 128:(r + 1) * 128],
                                    rhsE[p][32 * j:32 * j + 4, r * 128:(r + 1) * 128],
                                    start=True, stop=True,
                                    tile_position=(32 * j, 0)), st_)
                    T(t9)
                    tok_ecyc = st_.v

                    for j in range(4):
                        def v9b(e, en, j=j, cyc=cyc, tok_ecyc=tok_ecyc,
                                gate=m_used_st):
                            w(e, en, st_, tok_ecyc)
                            w(e, en, st_, gate)
                            i(e.tensor_scalar_min(
                                M_sb[:, cyc * 2048 + j * 512:cyc * 2048 + (j + 1) * 512],
                                pE[:, j * 512:(j + 1) * 512], 0.0), sv)
                        V(v9b)
                        e_evac[j] = sv.v

                tok_mraw = sv.v

                def a9(e, en, tok=tok_mraw):
                    w(e, en, sv, tok)
                    i(e.activation(M_sb[:, 0:4096], M_sb[:, 0:4096], AF.Exp), sa)
                    i(e.activation(M_sb[:, 4096:8192], M_sb[:, 4096:8192],
                                   AF.Exp), sa)
                A(a9)
                tok_me = sa.v

                def v22(e, en, tok=tok_me, tok_cbt=tok_cbt):
                    w(e, en, sa, tok)
                    w(e, en, sg, tok_cbt)
                    i(e.tensor_tensor(
                        M_sb[:, :], M_sb[:, :],
                        APx(CBT16, 0, [[128, 128], [0, NH], [1, 128]]),
                        ALU.mult), sv)
                V(v22)
                for idx16 in range(16):
                    m_tok[idx16] = sv.v

                for q in range(8):
                    hp = q % 2

                    def t10(e, en, q=q, hp=hp, tok_xdt=tok_xdt,
                            tok_cdnb=tok_cdnb, gate_c=st_carry[q],
                            gate_y=ysv[hp]):
                        for h in range(q * 8, q * 8 + 8):
                            w(e, en, sv, m_tok[hmap(h) // 4])
                            w(e, en, sv, mp_tok[h // 4])
                        w(e, en, sv, tok_xdt)
                        w(e, en, sv, tok_cdnb)
                        w(e, en, sv, gate_c)
                        w(e, en, sv, gate_y)
                        for h in range(q * 8, q * 8 + 8):
                            c0 = hp * 512 + (h % 8) * 64
                            i(e.matmul(pY[:, c0:c0 + 64],
                                       M_sb[:, hmap(h) * 128:hmap(h) * 128 + 128],
                                       Xdt[:, h * 64:h * 64 + 64],
                                       start=True, stop=False), st_)
                            i(e.matmul(pY[:, c0:c0 + 64],
                                       Mp_sb[:, h * 128:h * 128 + 128],
                                       carry16[:, h * 64:h * 64 + 64],
                                       start=False, stop=True), st_)
                    T(t10)
                    tok_y = st_.v

                    def t11(e, en, p=p, q=q, hp=hp, tok_bc=tok_bc,
                            gate=mp_done_v[hp]):
                        w(e, en, y_bc[p], tok_bc)
                        w(e, en, sv, gate)
                        for h in range(q * 8, q * 8 + 8):
                            i(e.matmul(pEp[:, hp * 512 + (h % 8) * 64:
                                           hp * 512 + (h % 8) * 64 + 64],
                                       Btok[p][:, :], Xdtd[:, h * 64:h * 64 + 64],
                                       start=True, stop=True), st_)
                    T(t11)
                    btok_used[p] = st_.v
                    tok_stm = st_.v

                    def v23(e, en, q=q, hp=hp, tok_stm=tok_stm, tok_cdnb=tok_cdnb,
                            selfgate=sv.v):
                        w(e, en, st_, tok_stm)
                        w(e, en, sv, max(tok_cdnb, selfgate))
                        i(e.tensor_tensor(ct1[:, :], carry16[:, q * 512:(q + 1) * 512],
                                          APx(cdnb, q * 8, [[NH, 128], [1, 8], [0, HD]]),
                                          ALU.mult), sv)
                        w(e, en, sv, sv.v)
                        i(e.tensor_add(carry16[:, q * 512:(q + 1) * 512], ct1[:, :],
                                       pEp[:, hp * 512:hp * 512 + 512]), sv)
                    V(v23)
                    st_carry[q] = sv.v
                    mp_done_v[hp] = sv.v

                    def v24(e, en, z=z, q=q, hp=hp, tok_y=tok_y,
                            gate=y_stage_tok[z % 2]):
                        w(e, en, st_, tok_y)
                        w(e, en, y_yp[z % 2], gate)
                        i(e.tensor_copy(Yst[z % 2][:, q * 512:(q + 1) * 512],
                                        pY[:, hp * 512:hp * 512 + 512]), sv)
                    V(v24)
                    ysv[hp] = sv.v
                tok_yall = sv.v
                xdt_used_st = st_.v
                m_used_st = st_.v

                def s8(e, en, z=z, tok_yall=tok_yall):
                    w(e, en, sv, tok_yall)
                    i(e.dma_start(out=dr_ypre.ap()[z * 128:(z + 1) * 128, :],
                                  in_=Yst[z % 2][:, :]), y_yp[z % 2], 16)
                S(s8)
                y_stage_tok[z % 2] = y_yp[z % 2].v

            y_cc1 = kb.sem("y_cc1")
            y_cc2 = kb.sem("y_cc2")

            def g3a(e, en, gate=tuple(st_carry)):
                for q in range(8):
                    w(e, en, sv, gate[q])
                i(e.dma_start(out=dr_cci.ap(), in_=carry16[:, :]), y_cc1, 16)
            G(g3a)

            def g3(e, en):
                w(e, en, y_cc1, y_cc1.v)
                groups = [[c, c + 1] for c in range(0, ncores, 2)]
                i(e.collective_compute("AllReduce", ALU.add,
                                       replica_groups=groups,
                                       ins=[dr_cci.ap()], outs=[dr_cco.ap()]),
                  g_cc, 1)
            G(g3)

            def s10(e, en):
                w(e, en, g_cc, g_cc.v)
                i(e.dma_start(out=sum32[:, :], in_=dr_cco.ap()), y_cc2, 16)
            S(s10)

            def v26(e, en):
                w(e, en, y_cc2, y_cc2.v)
                i(e.tensor_tensor(sum32[:, :], sum32[:, :], carry16[:, :],
                                  ALU.subtract), sv)
                w(e, en, sv, sv.v)
                i(e.tensor_scalar(patch16[:, :], sum32[:, :], flag_c[:, 0:1],
                                  None, ALU.mult), sv)
            V(v26)
            kb.barrier()

        with ExitStack() as pctx:
            pe = pctx.enter_context
            Ypt = [pe(nc.sbuf_tensor(f"Ypt{p}", [128, DI], F16)) for p in range(2)]
            z_tk = [pe(nc.sbuf_tensor(f"z_tk{p}", [128, DI], F16)) for p in range(2)]
            x_tk = [pe(nc.sbuf_tensor(f"x_tk{p}", [128, DI], F16)) for p in range(2)]
            Mp2t = [pe(nc.sbuf_tensor(f"Mp2t{p}", [128, NH * 128], F16)) for p in range(2)]
            stz = pe(nc.sbuf_tensor("stz", [128, DI], F16))
            grow = pe(nc.sbuf_tensor("grow", [1, NH], F32))
            gnb = pe(nc.sbuf_tensor("gnb", [128, NH], F32))
            siluz = pe(nc.sbuf_tensor("siluz", [128, DI], F16))
            ygA = pe(nc.sbuf_tensor("ygA", [128, DI], F16))
            ygB = pe(nc.sbuf_tensor("ygB", [128, DI], F16))
            ssum = pe(nc.sbuf_tensor("ssum", [128, 1], F32))
            rstd2 = pe(nc.sbuf_tensor("rstd2", [128, 1], F32))
            ynst = [pe(nc.sbuf_tensor(f"ynst{p}", [128, DI], F16)) for p in range(2)]
            pYb = pe(nc.psum_tensor("pYb", [128, 1024], F32))
            pM2 = pe(nc.psum_tensor("pM2", [128, 512], F32))

            y_p4 = [kb.sem(f"y_p4_{p}") for p in range(2)]
            y_be2 = [kb.sem(f"y_be2_{p}") for p in range(2)]
            y_yn = [kb.sem(f"y_yn{p}") for p in range(2)]

            mp2_rd_gate = [0, 0]
            stz_used_st = 0
            ygq_tok = [0, 0]
            yn_stage_tok = [0, 0]
            p4_v_reads = [0, 0]
            p4_a_reads = [0, 0]
            prev_ss = 0
            prev_yg = 0
            prev_vv = 0
            prev_yn = 0

            for z in range(NCH):
                p = z % 2

                def s11(e, en, z=z, p=p, g1_=p4_v_reads[p], g2_=p4_a_reads[p],
                        gb0=hlb_gate[0] if False else y_be2[0].v,
                        gb1=y_be2[1].v):
                    w(e, en, sv, g1_)
                    w(e, en, sa, g2_)
                    w(e, en, y_be2[0], gb0)
                    w(e, en, y_be2[1], gb1)
                    i(e.dma_start(out=Ypt[p][:, :],
                                  in_=dr_ypre.ap()[z * 128:(z + 1) * 128, :]), y_p4[p], 16)
                    i(e.dma_start_transpose(
                        out=z_tk[p][:, :],
                        in_=dr_z.ap()[0:DI, z * 128:(z + 1) * 128]), y_p4[p], 16)
                    i(e.dma_start_transpose(
                        out=x_tk[p][:, :],
                        in_=dr_xbc.ap()[0:DI, z * 128:(z + 1) * 128]), y_p4[p], 16)
                S(s11)
                tok_rd = y_p4[p].v

                def s12(e, en, z=z, p=p, gate=mp2_rd_gate[p]):
                    w(e, en, st_, gate)
                    i(e.dma_start(out=Mp2t[p][:, :], in_=dr_mp.ap()[z, :, :]),
                      y_be2[p], 16)
                S(s12)
                tok_r2b = y_be2[p].v

                def a11(e, en, p=p, tok_rd=tok_rd, gate=prev_yg):
                    w(e, en, y_p4[p], tok_rd)
                    w(e, en, sv, gate)
                    i(e.activation(siluz[:, :], z_tk[p][:, :], AF.Silu), sa)
                A(a11)
                p4_a_reads[p] = sa.v
                tok_slz = sa.v

                def t12(e, en, z=z, gate=sv.v):
                    w(e, en, sv, gate)
                    i(e.transpose(pM2[0:1, 0:64], G_pre[:, z:z + 1], idn32[0:64, 0:64]), st_)
                T(t12)
                tok_gt = st_.v

                def v28(e, en, tok=tok_gt):
                    w(e, en, st_, tok)
                    i(e.tensor_copy(grow[0:1, :], pM2[0:1, 0:64]), sv)
                V(v28)
                tok_gr = sv.v

                def t13(e, en, tok=tok_gr):
                    w(e, en, sv, tok)
                    i(e.matmul(pM2[:, 64:128], ones32[0:1, :], grow[0:1, :],
                               start=True, stop=True), st_)
                T(t13)
                tok_gb = st_.v

                def v29(e, en, tok=tok_gb, gate=stz_used_st):
                    w(e, en, st_, tok)
                    i(e.tensor_copy(gnb[:, :], pM2[:, 64:128]), sv)
                    w(e, en, sv, sv.v)
                    w(e, en, st_, gate)
                    i(e.tensor_tensor(stz[:, :], patch16[:, :],
                                      APx(gnb, 0, [[NH, 128], [1, NH], [0, HD]]),
                                      ALU.mult), sv)
                V(v29)
                tok_stz = sv.v
                for q in range(8):
                    hp = q % 2

                    def t15(e, en, p=p, q=q, hp=hp, tok_stz=tok_stz,
                            gate=ygq_tok[hp], tok_r2b=tok_r2b):
                        w(e, en, y_be2[p], tok_r2b)
                        w(e, en, sv, tok_stz)
                        w(e, en, sv, gate)
                        for h in range(q * 8, q * 8 + 8):
                            c0 = hp * 512 + (h % 8) * 64
                            i(e.matmul(pYb[:, c0:c0 + 64],
                                       Mp2t[p][:, h * 128:h * 128 + 128],
                                       stz[:, h * 64:h * 64 + 64],
                                       start=True, stop=True), st_)
                    T(t15)
                    tok_pt = st_.v
                    mp2_rd_gate[p] = st_.v

                    def v31(e, en, p=p, q=q, hp=hp, tok_pt=tok_pt, tok_rd=tok_rd,
                            gate=prev_ss):
                        w(e, en, st_, tok_pt)
                        w(e, en, y_p4[p], tok_rd)
                        w(e, en, sa, gate)
                        i(e.tensor_add(ygA[:, q * 512:(q + 1) * 512],
                                       Ypt[p][:, q * 512:(q + 1) * 512],
                                       pYb[:, hp * 512:hp * 512 + 512]), sv)
                    V(v31)
                    ygq_tok[hp] = sv.v
                tok_yga = sv.v
                stz_used_st = st_.v

                def v32(e, en, p=p, tok_rd=tok_rd, tok_slz=tok_slz,
                        tok_yga=tok_yga, gate=prev_ss):
                    w(e, en, y_p4[p], tok_rd)
                    w(e, en, sa, gate)
                    i(e.tensor_tensor(ygB[:, :], x_tk[p][:, :],
                                      APx(D_nb, 0, [[NH, 128], [1, NH], [0, HD]]),
                                      ALU.mult), sv)
                    w(e, en, sv, sv.v)
                    w(e, en, sv, tok_yga)
                    i(e.tensor_add(ygB[:, :], ygA[:, :], ygB[:, :]), sv)
                    w(e, en, sv, sv.v)
                    w(e, en, sa, tok_slz)
                    i(e.tensor_tensor(ygA[:, :], ygB[:, :], siluz[:, :],
                                      ALU.mult), sv)
                V(v32)
                tok_yg = sv.v
                prev_yg = tok_yg
                p4_v_reads[p] = tok_yg

                def a12(e, en, tok=tok_yg, gate=prev_vv):
                    w(e, en, sv, tok)
                    w(e, en, sv, gate)
                    i(e.activation(ygB[:, :], ygA[:, :], AF.Square,
                                   accum_out=ssum[:, :]), sa)
                A(a12)
                tok_ss = sa.v
                prev_ss = tok_ss

                def v33(e, en, tok=tok_ss, gate=prev_yn):
                    w(e, en, sa, tok)
                    w(e, en, sv, gate)
                    i(e.tensor_scalar_mul(rstd2[:, :], ssum[:, :], -1.0 / DI), sv)
                V(v33)
                tok_vv = sv.v
                prev_vv = tok_vv

                def a13(e, en, tok=tok_vv):
                    w(e, en, sv, tok)
                    i(e.activation(rstd2[:, :], rstd2[:, :], AF.Ln, bias=eps_c[:, 0:1],
                                   scale=-1.0), sa)
                    w(e, en, sa, sa.v)
                    i(e.activation(rstd2[:, :], rstd2[:, :], AF.Exp, scale=-0.5), sa)
                A(a13)
                tok_rstd = sa.v

                def v34(e, en, z=z, tok=tok_rstd, gate=yn_stage_tok[z % 2]):
                    w(e, en, sa, tok)
                    w(e, en, y_yn[z % 2], gate)
                    i(e.scalar_tensor_tensor(ynst[z % 2][:, :], ygA[:, :],
                                             rstd2[:, 0:1], nw_bc[:, :],
                                             ALU.mult, ALU.mult), sv)
                V(v34)
                tok_yn = sv.v
                prev_yn = tok_yn

                def s13(e, en, z=z, tok=tok_yn):
                    w(e, en, sv, tok)
                    i(e.dma_start(out=dr_yn.ap()[z * 128:(z + 1) * 128, :],
                                  in_=ynst[z % 2][:, :]), y_yn[z % 2], 16)
                S(s13)
                yn_stage_tok[z % 2] = y_yn[z % 2].v
            kb.barrier()

        scan_ctx.close()

        with ExitStack() as pctx:
            pe = pctx.enter_context
            ynfm = pe(nc.sbuf_tensor("ynfm", [128, 32 * TOK], F16))
            wop = [pe(nc.sbuf_tensor(f"wop{p}", [128, 32 * 128], F16)) for p in range(2)]
            atst = [pe(nc.sbuf_tensor(f"atst{p}", [128, TOK], F16)) for p in range(2)]
            pC = [pe(nc.psum_tensor(f"pC{p}", [128, 4 * 512], F32)) for p in range(2)]
            y_ynf = kb.sem("y_ynf")
            g_w2 = [kb.sem(f"g_w2_{p}") for p in range(2)]
            y_at = [kb.sem(f"y_at{p}") for p in range(2)]

            def s14(e, en):
                for kc in range(32):
                    i(e.dma_start_transpose(
                        out=ynfm[:, kc * TOK:(kc + 1) * TOK],
                        in_=dr_yn.ap()[:, kc * 128:(kc + 1) * 128]), y_ynf, 16)
            S(s14)
            y_ynf_all = y_ynf.v

            wt2_last = [0, 0]
            evC = [[0] * 4, [0] * 4]
            at_stage = [0, 0]
            for m in range(16):
                p = m % 2

                def g4(e, en, m=m, p=p, gate=wt2_last[p]):
                    w(e, en, st_, gate)
                    src = APx(opw, m * 4096, [[16 * 32 * 128, 128], [1, 4096]])
                    dst = APx(wop[p], 0, [[32 * 128, 128], [1, 4096]])
                    i(e.dma_start(out=dst, in_=src), g_w2[p], 16)
                G(g4)
                tok_w2 = g_w2[p].v

                for g in range(NG):
                    def t16(e, en, m=m, p=p, g=g, tok_w2=tok_w2, gate=evC[p][g]):
                        w(e, en, g_w2[p], tok_w2)
                        w(e, en, y_ynf, y_ynf_all)
                        w(e, en, sv, gate)
                        for kc in range(32):
                            i(e.matmul(pC[p][:, g * 512:g * 512 + 512],
                                       wop[p][:, kc * 128:(kc + 1) * 128],
                                       ynfm[:, kc * TOK + g * 512:kc * TOK + g * 512 + 512],
                                       start=(kc == 0), stop=(kc == 31)), st_)
                    T(t16)
                    wt2_last[p] = st_.v
                    tok_c = st_.v

                    def v35(e, en, p=p, g=g, tok_c=tok_c, gate=at_stage[p]):
                        w(e, en, st_, tok_c)
                        w(e, en, y_at[p], gate)
                        i(e.tensor_copy(atst[p][:, g * 512:g * 512 + 512],
                                        pC[p][:, g * 512:g * 512 + 512]), sv)
                    V(v35)
                    evC[p][g] = sv.v
                tok_at = sv.v

                def s15(e, en, m=m, p=p, tok_at=tok_at):
                    w(e, en, sv, tok_at)
                    i(e.dma_start(out=dr_attn.ap()[m * 128:(m + 1) * 128, :],
                                  in_=atst[p][:, :]), y_at[p], 16)
                S(s15)
                at_stage[p] = y_at[p].v
            kb.barrier()

        with ExitStack() as pctx:
            pe = pctx.enter_context
            u2 = [pe(nc.sbuf_tensor(f"u2_{p}", [128, DM], F32)) for p in range(2)]
            att = [pe(nc.sbuf_tensor(f"att{p}", [128, DM], F16)) for p in range(2)]
            x1t = [pe(nc.sbuf_tensor(f"x1t{p}", [128, DM], F32)) for p in range(2)]
            xsq = [pe(nc.sbuf_tensor(f"xsq{p}", [128, DM], F32)) for p in range(2)]
            h2t = [pe(nc.sbuf_tensor(f"h2t{p}", [128, DM], F16)) for p in range(2)]
            ms2 = [pe(nc.sbuf_tensor(f"ms2_{p}", [128, 1], F32)) for p in range(2)]
            mn2 = [pe(nc.sbuf_tensor(f"mn2_{p}", [128, 1], F32)) for p in range(2)]
            qs2 = [pe(nc.sbuf_tensor(f"qs2_{p}", [128, 1], F32)) for p in range(2)]
            qq2 = [pe(nc.sbuf_tensor(f"qq2_{p}", [128, 1], F32)) for p in range(2)]
            nv2 = [pe(nc.sbuf_tensor(f"nv2_{p}", [128, 1], F32)) for p in range(2)]
            rs2 = [pe(nc.sbuf_tensor(f"rs2_{p}", [128, 1], F32)) for p in range(2)]
            mr2 = [pe(nc.sbuf_tensor(f"mr2_{p}", [128, 1], F32)) for p in range(2)]
            y_u2 = [kb.sem(f"y_u2_{p}") for p in range(2)]
            y_x1 = [kb.sem(f"y_x1_{p}") for p in range(2)]
            y_h2 = [kb.sem(f"y_h2_{p}") for p in range(2)]

            last_h2 = [0, 0]
            last_sq2 = [0, 0]
            for tt in range(TOK // 128):
                p = tt % 2

                def s16(e, en, tt=tt, p=p, gate=last_h2[p]):
                    w(e, en, sg, gate)
                    i(e.dma_start(out=u2[p][:, :],
                                  in_=u_in.ap()[HALO + tt * 128:HALO + (tt + 1) * 128, :]), y_u2[p], 16)
                    i(e.dma_start_transpose(
                        out=att[p][:, :],
                        in_=dr_attn.ap()[:, tt * 128:(tt + 1) * 128]), y_u2[p], 16)
                S(s16)
                tok_r = y_u2[p].v

                def v36(e, en, p=p, tok_r=tok_r, gate=y_x1[p].v):
                    w(e, en, y_u2[p], tok_r)
                    w(e, en, y_x1[p], gate)
                    i(e.tensor_add(x1t[p][:, :], u2[p][:, :], att[p][:, :]), sv)
                    w(e, en, sv, sv.v)
                    i(e.reduce_sum(ms2[p][:, :], x1t[p][:, :], AX.X), sv)
                    w(e, en, sv, sv.v)
                    i(e.tensor_scalar_mul(mn2[p][:, :], ms2[p][:, :], -1.0 / DM), sv)
                V(v36)
                tok_x1 = sv.v

                def s17(e, en, tt=tt, p=p, tok=tok_x1):
                    w(e, en, sv, tok)
                    i(e.dma_start(out=dr_x1.ap()[tt * 128:(tt + 1) * 128, :],
                                  in_=x1t[p][:, :]), y_x1[p], 16)
                S(s17)

                def g14(e, en, p=p, tok=tok_x1, gate=last_sq2[p]):
                    w(e, en, sv, tok)
                    w(e, en, sv, gate)
                    i(e.tensor_tensor(xsq[p][:, :], x1t[p][:, :],
                                      x1t[p][:, :], ALU.mult), sg)
                G(g14)
                tok_q2a = sg.v

                def v37a(e, en, p=p, tok=tok_q2a):
                    w(e, en, sg, tok)
                    i(e.reduce_sum(qs2[p][:, :], xsq[p][:, :], AX.X), sv)
                V(v37a)
                tok_q2 = sv.v

                def v37(e, en, p=p, tok=tok_q2):
                    w(e, en, sv, tok)
                    i(e.tensor_scalar_mul(qq2[p][:, :], qs2[p][:, :], 1.0 / DM), sv)
                    w(e, en, sv, sv.v)
                    i(e.scalar_tensor_tensor(nv2[p][:, :], mn2[p][:, :],
                                             mn2[p][:, 0:1], qq2[p][:, :],
                                             ALU.mult, ALU.subtract), sv)
                V(v37)
                tok_nv2 = sv.v
                last_sq2[p] = tok_nv2

                def a15(e, en, p=p, tok=tok_nv2):
                    w(e, en, sv, tok)
                    i(e.activation(rs2[p][:, :], nv2[p][:, :], AF.Ln, bias=eps_c[:, 0:1],
                                   scale=-1.0), sa)
                    w(e, en, sa, sa.v)
                    i(e.activation(rs2[p][:, :], rs2[p][:, :], AF.Exp, scale=-0.5), sa)
                A(a15)
                tok_rs2 = sa.v

                def v38(e, en, p=p, tok=tok_rs2):
                    w(e, en, sa, tok)
                    i(e.tensor_tensor(mr2[p][:, :], mn2[p][:, :], rs2[p][:, :],
                                      ALU.mult), sv)
                V(v38)
                tok_mr2 = sv.v

                def g16(e, en, p=p, tok=tok_mr2, gate=y_h2[p].v):
                    w(e, en, sv, tok)
                    w(e, en, y_h2[p], gate)
                    i(e.tensor_scalar(h2t[p][:, :], x1t[p][:, :],
                                      mn2[p][:, 0:1], rs2[p][:, 0:1],
                                      ALU.add, ALU.mult), sg)
                G(g16)
                last_h2[p] = sg.v
                tok_h2 = sg.v

                def s18(e, en, tt=tt, p=p, tok=tok_h2):
                    w(e, en, sg, tok)
                    i(e.dma_start(out=dr_h2.ap()[tt * 128:(tt + 1) * 128, :],
                                  in_=h2t[p][:, :]), y_h2[p], 16)
                S(s18)
            kb.barrier()

        with ExitStack() as pctx:
            pe = pctx.enter_context
            h2fm = pe(nc.sbuf_tensor("h2fm", [128, KC * TOK], F16))
            w0t = [pe(nc.sbuf_tensor(f"w0t{p}", [128, KC * 128], F16)) for p in range(2)]
            gst = [pe(nc.sbuf_tensor(f"gst{p}", [128, TOK], F16)) for p in range(2)]
            pD = [pe(nc.psum_tensor(f"pD{p}", [128, 4 * 512], F32)) for p in range(2)]
            y_h2f = kb.sem("y_h2f")
            g_w3 = [kb.sem(f"g_w3_{p}") for p in range(2)]
            y_g = [kb.sem(f"y_g{p}") for p in range(2)]

            def s19(e, en):
                for kc in range(KC):
                    i(e.dma_start_transpose(
                        out=h2fm[:, kc * TOK:(kc + 1) * TOK],
                        in_=dr_h2.ap()[:, kc * 128:(kc + 1) * 128]), y_h2f, 16)
            S(s19)
            y_h2f_all = y_h2f.v

            w3_last = [0, 0]
            evD = [[0] * 4, [0] * 4]
            g_stage = [0, 0]
            for m in range(HID // 128):
                p = m % 2

                def g5(e, en, m=m, p=p, gate=w3_last[p]):
                    w(e, en, st_, gate)
                    src = APx(w0, m * 2048, [[64 * 16 * 128, 128], [1, 2048]])
                    dst = APx(w0t[p], 0, [[KC * 128, 128], [1, 2048]])
                    i(e.dma_start(out=dst, in_=src), g_w3[p], 16)
                G(g5)
                tok_w3 = g_w3[p].v

                for g in range(NG):
                    def t17(e, en, m=m, p=p, g=g, tok_w3=tok_w3, gate=evD[p][g]):
                        w(e, en, g_w3[p], tok_w3)
                        w(e, en, y_h2f, y_h2f_all)
                        w(e, en, sa, gate)
                        for kc in range(KC):
                            i(e.matmul(pD[p][:, g * 512:g * 512 + 512],
                                       w0t[p][:, kc * 128:(kc + 1) * 128],
                                       h2fm[:, kc * TOK + g * 512:kc * TOK + g * 512 + 512],
                                       start=(kc == 0), stop=(kc == KC - 1)), st_)
                    T(t17)
                    w3_last[p] = st_.v
                    tok_d = st_.v

                    def a17(e, en, m=m, p=p, g=g, tok_d=tok_d, gate=g_stage[p]):
                        w(e, en, st_, tok_d)
                        w(e, en, y_g[p], gate)
                        i(e.activation(gst[p][:, g * 512:g * 512 + 512],
                                       pD[p][:, g * 512:g * 512 + 512],
                                       AF.Gelu, bias=b0_col[:, m:m + 1]), sa)
                    A(a17)
                    evD[p][g] = sa.v
                tok_g = sa.v

                def s20(e, en, m=m, p=p, tok_g=tok_g):
                    w(e, en, sa, tok_g)
                    i(e.dma_start(out=dr_g.ap()[m * 128:(m + 1) * 128, :],
                                  in_=gst[p][:, :]), y_g[p], 16)
                S(s20)
                g_stage[p] = y_g[p].v
            kb.barrier()

        with ExitStack() as pctx:
            pe = pctx.enter_context
            TH = TOK // 2
            NKH = HID // 128
            ghalf = pe(nc.sbuf_tensor("ghalf", [128, NKH * TH], F16))
            w1t = [pe(nc.sbuf_tensor(f"w1t{p}", [128, NKH * 128], F16)) for p in range(2)]
            est = [pe(nc.sbuf_tensor(f"est{p}", [128, TH], F16)) for p in range(2)]
            pE7 = [pe(nc.psum_tensor(f"pE7_{p}", [128, 1024], F32)) for p in range(2)]
            y_gh = kb.sem("y_gh")
            g_w4 = [kb.sem(f"g_w4_{p}") for p in range(2)]
            y_ef = [kb.sem(f"y_ef{p}") for p in range(2)]

            e2groups = [(s, min(512, TH - s)) for s in range(0, TH, 512)]
            w4_last = [0, 0]
            evE = [[0] * len(e2groups), [0] * len(e2groups)]
            e_stage = [0, 0]
            for gh in range(2):
                def s21(e, en, gh=gh, gate=max(w4_last)):
                    if gh > 0:
                        w(e, en, st_, gate)
                    for kc in range(NKH):
                        i(e.dma_start(
                            out=ghalf[:, kc * TH:(kc + 1) * TH],
                            in_=dr_g.ap()[kc * 128:(kc + 1) * 128,
                                          gh * TH:(gh + 1) * TH]), y_gh, 16)
                S(s21)
                y_gh_tok = y_gh.v

                for m in range(16):
                    p = m % 2

                    def g6(e, en, m=m, p=p, gate=w4_last[p]):
                        w(e, en, st_, gate)
                        src = APx(w1, m * 8192, [[16 * 64 * 128, 128], [1, 8192]])
                        dst = APx(w1t[p], 0, [[NKH * 128, 128], [1, 8192]])
                        i(e.dma_start(out=dst, in_=src), g_w4[p], 16)
                    G(g6)
                    tok_w4 = g_w4[p].v

                    for k2, (s0_, cn) in enumerate(e2groups):
                        def t18(e, en, m=m, p=p, k2=k2, s0_=s0_, cn=cn,
                                tok_w4=tok_w4, y_gh_tok=y_gh_tok, gate=evE[p][k2]):
                            w(e, en, g_w4[p], tok_w4)
                            w(e, en, y_gh, y_gh_tok)
                            w(e, en, sv, gate)
                            for kc in range(NKH):
                                i(e.matmul(pE7[p][:, k2 * 512:k2 * 512 + cn],
                                           w1t[p][:, kc * 128:(kc + 1) * 128],
                                           ghalf[:, kc * TH + s0_:kc * TH + s0_ + cn],
                                           start=(kc == 0), stop=(kc == NKH - 1)), st_)
                        T(t18)
                        w4_last[p] = st_.v
                        tok_e7 = st_.v

                        def v39(e, en, m=m, p=p, k2=k2, s0_=s0_, cn=cn,
                                tok_e7=tok_e7, gate=e_stage[p]):
                            w(e, en, st_, tok_e7)
                            w(e, en, y_ef[p], gate)
                            i(e.tensor_scalar(est[p][:, s0_:s0_ + cn],
                                              pE7[p][:, k2 * 512:k2 * 512 + cn],
                                              b1_col[:, m:m + 1], None, ALU.add), sv)
                        V(v39)
                        evE[p][k2] = sv.v
                    tok_es = sv.v

                    def s22(e, en, m=m, p=p, gh=gh, tok_es=tok_es):
                        w(e, en, sv, tok_es)
                        i(e.dma_start(out=dr_efm.ap()[m * 128:(m + 1) * 128,
                                                      gh * TH:(gh + 1) * TH],
                                      in_=est[p][:, :]), y_ef[p], 16)
                    S(s22)
                    e_stage[p] = y_ef[p].v
            kb.barrier()

        with ExitStack() as pctx:
            pe = pctx.enter_context
            ett = [pe(nc.sbuf_tensor(f"ett{p}", [128, DM], F16)) for p in range(2)]
            x1b = [pe(nc.sbuf_tensor(f"x1b{p}", [128, DM], F32)) for p in range(2)]
            outt = [pe(nc.sbuf_tensor(f"outt{p}", [128, DM], F32)) for p in range(2)]
            y_e2 = [kb.sem(f"y_e2_{p}") for p in range(2)]
            y_out = [kb.sem(f"y_out{p}") for p in range(2)]

            out_stage = [0, 0]
            for tt in range(TOK // 128):
                p = tt % 2

                def s23(e, en, tt=tt, p=p, gate=out_stage[p]):
                    w(e, en, y_out[p], gate)
                    i(e.dma_start_transpose(
                        out=ett[p][:, :],
                        in_=dr_efm.ap()[:, tt * 128:(tt + 1) * 128]), y_e2[p], 16)
                    i(e.dma_start(out=x1b[p][:, :],
                                  in_=dr_x1.ap()[tt * 128:(tt + 1) * 128, :]), y_e2[p], 16)
                S(s23)
                tok_e2r = y_e2[p].v

                def v40(e, en, p=p, tok=tok_e2r, gate=y_out[p].v):
                    w(e, en, y_e2[p], tok)
                    w(e, en, y_out[p], gate)
                    i(e.tensor_add(outt[p][:, :], x1b[p][:, :], ett[p][:, :]), sv)
                V(v40)
                tok_o = sv.v

                def s24(e, en, tt=tt, p=p, tok_o=tok_o):
                    w(e, en, sv, tok_o)
                    i(e.dma_start(out=out.ap()[tt * 128:(tt + 1) * 128, :],
                                  in_=outt[p][:, :]), y_out[p], 16)
                S(s24)
                out_stage[p] = y_out[p].v

            def s25(e, en):
                w(e, en, y_out[0], y_out[0].v)
                w(e, en, y_out[1], y_out[1].v)
            S(s25)

    return nc


_NC_CACHE = {}


def _get_nc(tok, ncores):
    key = (tok, ncores)
    if key not in _NC_CACHE:
        _NC_CACHE[key] = build_nc(tok, ncores)
    return _NC_CACHE[key]


HP = np.array([(rr % 16) * 4 + rr // 16 for rr in range(NH)])
FP = (HP[:, None] * HD + np.arange(HD)[None, :]).reshape(-1)


def _tile_w(wT, mw_last=None):
    K_, M_ = wT.shape
    nkc = K_ // 128
    tmp = wT.reshape(nkc, 128, M_)
    blocks = []
    t0 = 0
    while t0 < M_:
        mw = min(128, M_ - t0)
        blk = tmp[:, :, t0:t0 + mw].transpose(1, 0, 2).reshape(128, nkc * mw)
        blocks.append(blk)
        t0 += mw
    return np.ascontiguousarray(np.concatenate(blocks, axis=1).astype(np.float32))


def make_in_maps(u, weights, tok, ncores):
    ipw0 = weights["in_proj_w"].T.astype(np.float32)
    colperm = np.concatenate([FP, DI + FP,
                              np.arange(2 * DI, 2 * DI + 2 * DS),
                              2 * DI + 2 * DS + HP])
    ipw = _tile_w(np.ascontiguousarray(ipw0[:, colperm]))
    cpm = np.concatenate([FP, np.arange(DI, CD)])
    cwT = np.ascontiguousarray(weights["conv_w"].T.astype(np.float32)[:, cpm])
    cb = np.ascontiguousarray(weights["conv_b"].astype(np.float32)[cpm].reshape(34, 128))
    dtb = np.ascontiguousarray(weights["dt_bias"].astype(np.float32)[HP].reshape(NH, 1))
    alog = np.ascontiguousarray(weights["A_log"].astype(np.float32)[HP].reshape(NH, 1))
    dvec = np.ascontiguousarray(weights["D"].astype(np.float32)[HP].reshape(1, NH))
    nw = np.ascontiguousarray(weights["norm_weight"].astype(np.float32)[FP].reshape(1, DI))
    opw = _tile_w(np.ascontiguousarray(weights["out_proj_w"].T.astype(np.float32)[FP, :]))
    w0 = _tile_w(np.ascontiguousarray(weights["W0"].T.astype(np.float32)))
    b0 = np.ascontiguousarray(weights["bias0"].astype(np.float32).reshape(HID // 128, 128))
    w1 = _tile_w(np.ascontiguousarray(weights["W1"].T.astype(np.float32)))
    b1 = np.ascontiguousarray(weights["bias1"].astype(np.float32).reshape(DM // 128, 128))
    in_maps = []
    for c in range(ncores):
        seq, half = c // 2, c % 2
        if half == 0:
            u_core = np.concatenate(
                [np.zeros((HALO, DM), np.float32), u[seq, 0:tok]], axis=0)
        else:
            u_core = u[seq, tok - HALO:2 * tok]
        flag = np.full((128, 1), float(half), np.float32)
        in_maps.append(dict(u=np.ascontiguousarray(u_core.astype(np.float32)),
                            ipw=ipw, cwT=cwT, cb=cb, dtb=dtb, alog=alog,
                            dvec=dvec, nw=nw, opw=opw, w0=w0, b0=b0, w1=w1,
                            b1=b1, flag=flag))
    return in_maps


def kernel(L0_attn_u, L0_attn_in_proj_w, L0_attn_conv_w, L0_attn_conv_b,
           L0_attn_dt_bias, L0_attn_A_log, L0_attn_D, L0_attn_norm_weight,
           L0_attn_out_proj_w, L0_ffn_W0, L0_ffn_bias0, L0_ffn_W1,
           L0_ffn_bias1):
    u = np.ascontiguousarray(np.asarray(L0_attn_u, dtype=np.float32))
    B, L, _ = u.shape
    TOK = L // 2
    ncores = 8
    weights = dict(in_proj_w=np.asarray(L0_attn_in_proj_w),
                   conv_w=np.asarray(L0_attn_conv_w),
                   conv_b=np.asarray(L0_attn_conv_b),
                   dt_bias=np.asarray(L0_attn_dt_bias),
                   A_log=np.asarray(L0_attn_A_log),
                   D=np.asarray(L0_attn_D),
                   norm_weight=np.asarray(L0_attn_norm_weight),
                   out_proj_w=np.asarray(L0_attn_out_proj_w),
                   W0=np.asarray(L0_ffn_W0), bias0=np.asarray(L0_ffn_bias0),
                   W1=np.asarray(L0_ffn_W1), bias1=np.asarray(L0_ffn_bias1))
    in_maps = make_in_maps(u, weights, TOK, ncores)
    nc = _get_nc(TOK, ncores)
    res = run_bass_kernel_spmd(nc, in_maps, core_ids=list(range(ncores)))
    outp = np.empty((B, L, DM), np.float32)
    for c in range(ncores):
        seq, half = c // 2, c % 2
        outp[seq, half * TOK:(half + 1) * TOK] = res.results[c]["out"]
    return outp


# version 12
# speedup vs baseline: 1.0577x; 1.0447x over previous
import numpy as np
from contextlib import ExitStack

import concourse.bass as bass
import concourse.mybir as mybir
from concourse.bass_utils import run_bass_kernel_spmd

F32 = mybir.dt.float32
F16 = mybir.dt.float16
AF = mybir.ActivationFunctionType
ALU = mybir.AluOpType
AX = mybir.AxisListType

DM = 2048
DI = 4096
DS = 128
NH = 64
HD = 64
CD = DI + 2 * DS
DIP = 2 * DI + 2 * DS + NH
HID = 8192
EPS = 1e-5
HALO = 3
KC = DM // 128


def APx(t, offset, dims):
    return bass.AP(t, offset, [list(d) for d in dims])


class Sem:
    __slots__ = ("h", "v", "name")


class KB:
    def __init__(self, nc, ctx, tok):
        self.nc = nc
        self.ctx = ctx
        self.TOK = tok
        self.NCH = tok // 128
        self.NG = tok // 512
        self.PADT = ((tok + HALO + 127) // 128) * 128
        self.blk = ctx.enter_context(nc.Block())
        self._waited = {}
        self.sems = {}

    def sem(self, name):
        s = Sem()
        s.h = self.ctx.enter_context(self.nc.semaphore(name))
        s.v = 0
        s.name = name
        self.sems[name] = s
        return s

    def V(self, fn):
        self.blk.vector(lambda e: fn(e, "v"))

    def A(self, fn):
        self.blk.scalar(lambda e: fn(e, "a"))

    def T(self, fn):
        self.blk.tensor(lambda e: fn(e, "t"))

    def G(self, fn):
        self.blk.gpsimd(lambda e: fn(e, "g"))

    def S(self, fn):
        self.blk.sync(lambda e: fn(e, "s"))

    def w(self, e, en, sem, val=None):
        val = sem.v if val is None else val
        if val <= 0:
            return
        k = (en, sem.name)
        if self._waited.get(k, -1) >= val:
            return
        e.wait_ge(sem.h, val)
        self._waited[k] = val

    def i(self, inst, sem, n=1):
        inst.then_inc(sem.h, n)
        sem.v += n
        return sem.v

    def barrier(self):
        snap = [(s, s.v) for s in self.sems.values() if s.v > 0]

        def mk(emit, en):
            def f(e, _en):
                for s, v in snap:
                    self.w(e, en, s, v)
            emit(f)
        mk(self.V, "v")
        mk(self.A, "a")
        mk(self.T, "t")
        mk(self.G, "g")
        mk(self.S, "s")


def build_nc(tok, ncores):
    nc = bass.Bass("TRN2", target_bir_lowering=False, debug=False)
    TOK = tok
    NCH = tok // 128
    NG = tok // 512
    assert tok % 512 == 0

    u_in = nc.dram_tensor("u", [TOK + HALO, DM], F32, kind="ExternalInput")
    ipw = nc.dram_tensor("ipw", [128, 16 * DIP], F32, kind="ExternalInput")
    cwT = nc.dram_tensor("cwT", [4, CD], F32, kind="ExternalInput")
    cb_r = nc.dram_tensor("cb", [34, 128], F32, kind="ExternalInput")
    dtb = nc.dram_tensor("dtb", [NH, 1], F32, kind="ExternalInput")
    alog = nc.dram_tensor("alog", [NH, 1], F32, kind="ExternalInput")
    dvec = nc.dram_tensor("dvec", [1, NH], F32, kind="ExternalInput")
    nwr = nc.dram_tensor("nw", [1, DI], F32, kind="ExternalInput")
    opw = nc.dram_tensor("opw", [128, 16 * 32 * 128], F32, kind="ExternalInput")
    w0 = nc.dram_tensor("w0", [128, 64 * 16 * 128], F32, kind="ExternalInput")
    b0_r = nc.dram_tensor("b0", [HID // 128, 128], F32, kind="ExternalInput")
    w1 = nc.dram_tensor("w1", [128, 16 * 64 * 128], F32, kind="ExternalInput")
    b1_r = nc.dram_tensor("b1", [DM // 128, 128], F32, kind="ExternalInput")
    flag_in = nc.dram_tensor("flag", [128, 1], F32, kind="ExternalInput")
    out = nc.dram_tensor("out", [TOK, DM], F32, kind="ExternalOutput")

    PADT = ((TOK + HALO + 127) // 128) * 128
    dr_hln = nc.dram_tensor("dr_hln", [PADT, DM], F16, kind="Internal")
    dr_z = nc.dram_tensor("dr_z", [DI, TOK], F16, kind="Internal")
    dr_xbc = nc.dram_tensor("dr_xbc", [CD, TOK], F16, kind="Internal")
    dr_ypre = nc.dram_tensor("dr_ypre", [TOK, DI], F16, kind="Internal")
    dr_yn = nc.dram_tensor("dr_yn", [TOK, DI], F16, kind="Internal")
    dr_attn = nc.dram_tensor("dr_attn", [DM, TOK], F16, kind="Internal")
    dr_x1 = nc.dram_tensor("dr_x1", [TOK, DM], F32, kind="Internal")
    dr_h2 = nc.dram_tensor("dr_h2", [TOK, DM], F16, kind="Internal")
    dr_g = nc.dram_tensor("dr_g", [HID, TOK], F16, kind="Internal")
    dr_efm = nc.dram_tensor("dr_efm", [DM, TOK], F16, kind="Internal")
    dr_mp = nc.dram_tensor("dr_mp", [16, 128, NH * 128], F16, kind="Internal")
    dr_cci = nc.dram_tensor("dr_cci", [128, DI], F32, kind="Internal")
    dr_cco = nc.dram_tensor("dr_cco", [128, DI], F32, kind="Internal")

    with ExitStack() as ctx:
        kb = KB(nc, ctx, tok)
        E = ctx.enter_context
        V, A, T, G, S = kb.V, kb.A, kb.T, kb.G, kb.S
        w, i = kb.w, kb.i

        ones16 = E(nc.sbuf_tensor("ones16", [128, 128], F16))
        ones32 = E(nc.sbuf_tensor("ones32", [1, 128], F32))
        ones2 = E(nc.sbuf_tensor("ones2", [2, 128], F16))
        idn32 = E(nc.sbuf_tensor("idn32", [128, 128], F32))
        b0_col = E(nc.sbuf_tensor("b0_col", [128, HID // 128], F32))
        b1_col = E(nc.sbuf_tensor("b1_col", [128, DM // 128], F32))
        flag_c = E(nc.sbuf_tensor("flag_c", [128, 1], F32))
        eps_c = E(nc.sbuf_tensor("eps_c", [128, 1], F32))
        one_c = E(nc.sbuf_tensor("one_c", [128, 1], F32))

        sv = kb.sem("sv")
        sa = kb.sem("sa")
        st_ = kb.sem("st")
        sg = kb.sem("sg")
        y_set = kb.sem("y_set")
        g_cc = kb.sem("g_cc")

        scan_ctx = ExitStack()
        Es = scan_ctx.enter_context
        zcol = Es(nc.sbuf_tensor("zcol", [NH, 1], F32))
        dtb_c = Es(nc.sbuf_tensor("dtb_c", [NH, 1], F32))
        alog_c = Es(nc.sbuf_tensor("alog_c", [NH, 1], F32))
        nA_col = Es(nc.sbuf_tensor("nA_col", [NH, 1], F32))
        D_nb = Es(nc.sbuf_tensor("D_nb", [128, NH], F16))
        nw_bc = Es(nc.sbuf_tensor("nw_bc", [128, DI], F16))
        convw_c = Es(nc.sbuf_tensor("convw_c", [128, 34 * 4], F32))
        convb_c = Es(nc.sbuf_tensor("convb_c", [128, 34], F32))
        dt_t = Es(nc.sbuf_tensor("dt_t", [NH, TOK], F32))
        dA = Es(nc.sbuf_tensor("dA", [NH, TOK], F32))
        dA_cum = Es(nc.sbuf_tensor("dA_cum", [NH, TOK], F32))
        dA_last = Es(nc.sbuf_tensor("dA_last", [NH, NCH], F32))
        cd_t = Es(nc.sbuf_tensor("cd_t", [NH, NCH], F32))
        G_incl = Es(nc.sbuf_tensor("G_incl", [NH, NCH], F32))
        G_pre = Es(nc.sbuf_tensor("G_pre", [NH, NCH], F32))
        carry16 = Es(nc.sbuf_tensor("carry16", [128, DI], F16))
        sum32 = Es(nc.sbuf_tensor("sum32", [128, DI], F32))
        patch16 = Es(nc.sbuf_tensor("patch16", [128, DI], F16))

        with ExitStack() as pctx:
            pS = pctx.enter_context(nc.psum_tensor("ps_set", [128, 512], F32))
            pS2 = pctx.enter_context(nc.psum_tensor("ps_set2", [128, 64], F32))
            drow = pctx.enter_context(nc.sbuf_tensor("drow", [1, NH], F32))
            d16row = pctx.enter_context(nc.sbuf_tensor("d16row", [1, NH], F16))
            nwrow = pctx.enter_context(nc.sbuf_tensor("nwrow", [1, DI], F32))
            nwrow16 = pctx.enter_context(nc.sbuf_tensor("nwrow16", [1, DI], F16))
            cwT_s = pctx.enter_context(nc.sbuf_tensor("cwT_s", [4, CD], F32))
            cb_s = pctx.enter_context(nc.sbuf_tensor("cb_s", [34, 128], F32))

            def s0(e, en):
                for dst, src in [
                    (dtb_c[:, :], dtb.ap()), (alog_c[:, :], alog.ap()),
                    (drow[:, :], dvec.ap()), (nwrow[:, :], nwr.ap()),
                    (cwT_s[:, :], cwT.ap()), (cb_s[:, :], cb_r.ap()),
                    (flag_c[:, :], flag_in.ap()),
                ]:
                    i(e.dma_start(out=dst, in_=src), y_set, 16)
            S(s0)

            b0_s = pctx.enter_context(nc.sbuf_tensor("b0_s", [HID // 128, 128], F32))
            b1_s = pctx.enter_context(nc.sbuf_tensor("b1_s", [DM // 128, 128], F32))

            def s0b(e, en):
                i(e.dma_start(out=b0_s[:, :], in_=b0_r.ap()), y_set, 16)
                i(e.dma_start(out=b1_s[:, :], in_=b1_r.ap()), y_set, 16)
            S(s0b)
            y_set_all = y_set.v

            def v0(e, en):
                i(e.memset(ones16[:, :], 1.0), sv)
                i(e.memset(ones32[:, :], 1.0), sv)
                i(e.memset(ones2[:, :], 1.0), sv)
                i(e.memset(zcol[:, :], 0.0), sv)
                i(e.memset(eps_c[:, :], EPS), sv)
                i(e.memset(one_c[:, :], 1.0), sv)
                w(e, en, y_set, y_set_all)
                i(e.tensor_copy(d16row[:, :], drow[:, :]), sv)
                i(e.tensor_copy(nwrow16[:, :], nwrow[:, :]), sv)
            V(v0)
            tok_v0 = sv.v

            def g0(e, en):
                i(e.memset(idn32[:, :], 1.0), sg)
                w(e, en, sg, sg.v)
                i(e.affine_select(idn32[:, :], idn32[:, :], [[1, 128]],
                                  ALU.is_equal, 0.0, base=0,
                                  channel_multiplier=-1), sg)
            G(g0)
            tok_idn = sg.v

            def a0(e, en):
                w(e, en, y_set, y_set_all)
                i(e.activation(nA_col[:, :], alog_c[:, :], AF.Exp), sa)
            A(a0)
            tok_a0 = sa.v

            def v1(e, en):
                w(e, en, sa, tok_a0)
                i(e.tensor_scalar_mul(nA_col[:, :], nA_col[:, :], -1.0), sv)
            V(v1)

            def t0(e, en):
                w(e, en, sg, tok_idn)
                w(e, en, y_set, y_set_all)
                i(e.transpose(pS[:, 0:34], cb_s[:, :], idn32[0:34, 0:34]), st_)
                i(e.transpose(pS[:, 64:64 + HID // 128], b0_s[:, :], idn32[0:64, 0:64]), st_)
                i(e.transpose(pS[:, 192:192 + DM // 128], b1_s[:, :], idn32[0:16, 0:16]), st_)
            T(t0)
            tok_t0 = st_.v

            def v2(e, en):
                w(e, en, st_, tok_t0)
                i(e.tensor_copy(convb_c[:, :], pS[:, 0:34]), sv)
                i(e.tensor_copy(b0_col[:, :], pS[:, 64:64 + HID // 128]), sv)
                i(e.tensor_copy(b1_col[:, :], pS[:, 192:192 + DM // 128]), sv)
            V(v2)
            tok_v2 = sv.v

            def t1(e, en):
                w(e, en, sv, tok_v2)
                for mt in range(34):
                    i(e.transpose(pS[:, mt * 4:mt * 4 + 4],
                                  cwT_s[:, mt * 128:(mt + 1) * 128], idn32[0:4, 0:4]), st_)
            T(t1)
            tok_t1 = st_.v

            def v3(e, en):
                w(e, en, st_, tok_t1)
                i(e.tensor_copy(convw_c[:, :], pS[:, 0:136]), sv)
            V(v3)
            tok_v3 = sv.v

            def t2(e, en):
                w(e, en, sv, tok_v0)
                i(e.matmul(pS2[:, 0:NH], ones16[0:1, :], d16row[0:1, :],
                           start=True, stop=True), st_)
            T(t2)
            tok_t2 = st_.v

            def v4(e, en):
                w(e, en, st_, tok_t2)
                i(e.tensor_copy(D_nb[:, :], pS2[:, 0:NH]), sv)
            V(v4)

            for q in range(DI // 512):
                def tq(e, en, q=q, gate=sv.v):
                    w(e, en, sv, gate)
                    i(e.matmul(pS[:, 0:512], ones16[0:1, :],
                               nwrow16[0:1, q * 512:(q + 1) * 512],
                               start=True, stop=True), st_)
                T(tq)

                def vq(e, en, q=q, gate=st_.v):
                    w(e, en, st_, gate)
                    i(e.tensor_copy(nw_bc[:, q * 512:(q + 1) * 512], pS[:, 0:512]), sv)
                V(vq)
            kb.barrier()

        with ExitStack() as pctx:
            pe = pctx.enter_context
            u_t = [pe(nc.sbuf_tensor(f"u_t{p}", [128, DM], F32)) for p in range(2)]
            usq = [pe(nc.sbuf_tensor(f"usq{p}", [128, DM], F32)) for p in range(2)]
            hlt = [pe(nc.sbuf_tensor(f"hlt{p}", [128, DM], F16)) for p in range(2)]
            msum = [pe(nc.sbuf_tensor(f"msum{p}", [128, 1], F32)) for p in range(2)]
            mneg = [pe(nc.sbuf_tensor(f"mneg{p}", [128, 1], F32)) for p in range(2)]
            qsum = [pe(nc.sbuf_tensor(f"qsum{p}", [128, 1], F32)) for p in range(2)]
            q2 = [pe(nc.sbuf_tensor(f"q2{p}", [128, 1], F32)) for p in range(2)]
            nvar = [pe(nc.sbuf_tensor(f"nvar{p}", [128, 1], F32)) for p in range(2)]
            rstd = [pe(nc.sbuf_tensor(f"rstd{p}", [128, 1], F32)) for p in range(2)]
            mr = [pe(nc.sbuf_tensor(f"mr{p}", [128, 1], F32)) for p in range(2)]
            y_u = [kb.sem(f"y_u{p}") for p in range(2)]
            y_hl = [kb.sem(f"y_hl{p}") for p in range(2)]

            nrows = TOK + HALO
            tiles = [(r0, min(128, nrows - r0)) for r0 in range(0, nrows, 128)]
            npad = PADT - nrows
            if npad > 0:
                def vz(e, en):
                    i(e.memset(hlt[0][0:npad, :], 0.0), sv)
                V(vz)
                tok_z0 = sv.v

                def sz(e, en, tok_z0=tok_z0):
                    w(e, en, sv, tok_z0)
                    i(e.dma_start(out=dr_hln.ap()[nrows:PADT, :],
                                  in_=hlt[0][0:npad, :]), y_hl[0], 16)
                S(sz)
            last_apply = [y_hl[0].v * 0, 0]
            last_sq = [0, 0]
            for idx, (r0, n) in enumerate(tiles):
                p = idx % 2

                def s1(e, en, r0=r0, n=n, p=p, gate=last_apply[p]):
                    w(e, en, sg, gate)
                    i(e.dma_start(out=u_t[p][0:n, :],
                                  in_=u_in.ap()[r0:r0 + n, :]), y_u[p], 16)
                S(s1)
                tok_u = y_u[p].v

                def v6(e, en, n=n, p=p, tok_u=tok_u):
                    w(e, en, y_u[p], tok_u)
                    i(e.reduce_sum(msum[p][0:n, :], u_t[p][0:n, :], AX.X), sv)
                    w(e, en, sv, sv.v)
                    i(e.tensor_scalar_mul(mneg[p][0:n, :], msum[p][0:n, :], -1.0 / DM), sv)
                V(v6)
                tok_mn = sv.v

                def g1a(e, en, n=n, p=p, tok_u=tok_u, gate=last_sq[p]):
                    w(e, en, y_u[p], tok_u)
                    w(e, en, sv, gate)
                    i(e.tensor_tensor(usq[p][0:n, :], u_t[p][0:n, :],
                                      u_t[p][0:n, :], ALU.mult), sg)
                G(g1a)
                tok_q0 = sg.v

                def v6b(e, en, n=n, p=p, tok_q0=tok_q0):
                    w(e, en, sg, tok_q0)
                    i(e.reduce_sum(qsum[p][0:n, :], usq[p][0:n, :], AX.X), sv)
                V(v6b)
                tok_q = sv.v

                def v7(e, en, n=n, p=p, tok_q=tok_q, tok_mn=tok_mn):
                    w(e, en, sv, tok_q)
                    w(e, en, sv, tok_mn)
                    i(e.tensor_scalar_mul(q2[p][0:n, :], qsum[p][0:n, :], 1.0 / DM), sv)
                    w(e, en, sv, sv.v)
                    i(e.scalar_tensor_tensor(nvar[p][0:n, :], mneg[p][0:n, :],
                                             mneg[p][0:n, 0:1], q2[p][0:n, :],
                                             ALU.mult, ALU.subtract), sv)
                V(v7)
                tok_nv = sv.v
                last_sq[p] = tok_nv

                def a2(e, en, n=n, p=p, tok_nv=tok_nv):
                    w(e, en, sv, tok_nv)
                    i(e.activation(rstd[p][0:n, :], nvar[p][0:n, :], AF.Ln,
                                   bias=eps_c[0:n, 0:1], scale=-1.0), sa)
                    w(e, en, sa, sa.v)
                    i(e.activation(rstd[p][0:n, :], rstd[p][0:n, :], AF.Exp,
                                   scale=-0.5), sa)
                A(a2)
                tok_rs = sa.v

                def v8(e, en, n=n, p=p, tok_rs=tok_rs):
                    w(e, en, sa, tok_rs)
                    i(e.tensor_tensor(mr[p][0:n, :], mneg[p][0:n, :],
                                      rstd[p][0:n, :], ALU.mult), sv)
                V(v8)
                tok_mr = sv.v

                def g3a(e, en, n=n, p=p, tok_mr=tok_mr, gate=y_hl[p].v):
                    w(e, en, sv, tok_mr)
                    w(e, en, y_hl[p], gate)
                    i(e.tensor_scalar(hlt[p][0:n, :], u_t[p][0:n, :],
                                      mneg[p][0:n, 0:1], rstd[p][0:n, 0:1],
                                      ALU.add, ALU.mult), sg)
                G(g3a)
                last_apply[p] = sg.v
                tok_ap = sg.v

                def s2(e, en, r0=r0, n=n, p=p, tok_ap=tok_ap):
                    w(e, en, sg, tok_ap)
                    i(e.dma_start(out=dr_hln.ap()[r0:r0 + n, :],
                                  in_=hlt[p][0:n, :]), y_hl[p], 16)
                S(s2)
            kb.barrier()

        with ExitStack() as pctx:
            pe = pctx.enter_context
            hfm = pe(nc.sbuf_tensor("hfm", [128, KC * PADT], F16))
            w_in = [pe(nc.sbuf_tensor(f"w_in{p}", [128, KC * 128], F16)) for p in range(2)]
            strip = [pe(nc.sbuf_tensor(f"strip{p}", [128, HALO + TOK], F16)) for p in range(2)]
            conva = [pe(nc.sbuf_tensor(f"conva{p}", [128, TOK], F16)) for p in range(2)]
            convd = [pe(nc.sbuf_tensor(f"convd{p}", [128, TOK], F16)) for p in range(2)]
            xstg = [pe(nc.sbuf_tensor(f"xstg{p}", [128, TOK], F16)) for p in range(2)]
            zstg = [pe(nc.sbuf_tensor(f"zstg{p}", [128, TOK], F16)) for p in range(2)]
            psA = [pe(nc.psum_tensor(f"psA{p}", [128, 4 * 512], F32)) for p in range(2)]
            y_fm = kb.sem("y_fm")
            g_w = [kb.sem(f"g_w{p}") for p in range(2)]
            y_xbc = [kb.sem(f"y_xbc{p}") for p in range(2)]
            y_z = [kb.sem(f"y_z{p}") for p in range(2)]

            def s3(e, en):
                for kc in range(0, KC, 2):
                    i(e.dma_start_transpose(
                        out=hfm[:, kc * PADT:kc * PADT + PADT],
                        in_=dr_hln.ap()[:, kc * 128:(kc + 1) * 128]), y_fm, 16)
            S(s3)

            def s3b(e, en):
                for kc in range(1, KC, 2):
                    i(e.dma_start_transpose(
                        out=hfm[:, kc * PADT:kc * PADT + PADT],
                        in_=dr_hln.ap()[:, kc * 128:(kc + 1) * 128]), y_fm, 16)
            S(s3b)
            y_fm_all = y_fm.v

            m_order = [(DI + CD, 64, "dt")] + \
                      [(DI + 128 * j, 128, "xbc") for j in range(34)] + \
                      [(128 * j, 128, "z") for j in range(32)]
            wt_last = [0, 0]
            ev_tok = [[0] * 4, [0] * 4]
            xbc_stage = [0, 0]
            z_stage = [0, 0]
            conv_sa = [0, 0]
            strip_used = [0, 0]
            dt_evac_tok = 0

            for mi, (f0, mw, kind) in enumerate(m_order):
                p = mi % 2

                def g1(e, en, f0=f0, mw=mw, p=p, gate=wt_last[p]):
                    w(e, en, st_, gate)
                    t_idx = f0 // 128
                    src = APx(ipw, t_idx * 2048, [[16 * DIP, 128], [1, KC * mw]])
                    dst = APx(w_in[p], 0, [[KC * 128, 128], [128, KC], [1, mw]])
                    i(e.dma_start(out=dst, in_=src), g_w[p], 16)
                G(g1)
                tok_wl = g_w[p].v

                groups = ([("halo", 0, HALO, 0)] if kind == "xbc" else []) + \
                         [("main", HALO + g * 512, 512, g) for g in range(NG)]

                for gname, c0, cn, bank in groups:
                    def t4(e, en, mw=mw, p=p, c0=c0, cn=cn, bank=bank,
                           tok_wl=tok_wl, gate_ev=ev_tok[p][bank]):
                        w(e, en, g_w[p], tok_wl)
                        w(e, en, y_fm, y_fm_all)
                        w(e, en, sv, gate_ev)
                        for kc in range(KC):
                            i(e.matmul(psA[p][0:mw, bank * 512:bank * 512 + cn],
                                       w_in[p][:, kc * 128:kc * 128 + mw],
                                       hfm[:, kc * PADT + c0:kc * PADT + c0 + cn],
                                       start=(kc == 0), stop=(kc == KC - 1)), st_)
                    T(t4)
                    tok_mm = st_.v
                    wt_last[p] = tok_mm

                    def v9(e, en, mw=mw, p=p, c0=c0, cn=cn, bank=bank,
                           tok_mm=tok_mm, kind=kind,
                           gate_strip=strip_used[p], gate_z=z_stage[p]):
                        w(e, en, st_, tok_mm)
                        if kind == "xbc":
                            w(e, en, sv, gate_strip)
                            i(e.tensor_copy(strip[p][0:mw, c0:c0 + cn],
                                            psA[p][0:mw, bank * 512:bank * 512 + cn]), sv)
                        elif kind == "z":
                            w(e, en, y_z[p], gate_z)
                            i(e.tensor_copy(zstg[p][0:mw, c0 - HALO:c0 - HALO + cn],
                                            psA[p][0:mw, bank * 512:bank * 512 + cn]), sv)
                        else:
                            i(e.tensor_copy(dt_t[0:mw, c0 - HALO:c0 - HALO + cn],
                                            psA[p][0:mw, bank * 512:bank * 512 + cn]), sv)
                    V(v9)
                    ev_tok[p][bank] = sv.v
                tok_evac = sv.v

                if kind == "dt":
                    dt_evac_tok = tok_evac
                elif kind == "xbc":
                    mt = (f0 - DI) // 128

                    def v10(e, en, p=p, mt=mt, tok_evac=tok_evac, gate=conv_sa[p]):
                        w(e, en, sv, tok_evac)
                        w(e, en, sa, gate)
                        i(e.scalar_tensor_tensor(
                            conva[p][:, :], strip[p][:, 0:TOK],
                            convw_c[:, 4 * mt + 0:4 * mt + 1],
                            APx(convb_c, mt, [[34, 128], [0, TOK]]),
                            ALU.mult, ALU.add), sv)
                        w(e, en, sv, sv.v)
                        i(e.scalar_tensor_tensor(
                            convd[p][:, :], strip[p][:, 1:TOK + 1],
                            convw_c[:, 4 * mt + 1:4 * mt + 2],
                            conva[p][:, :], ALU.mult, ALU.add), sv)
                        w(e, en, sv, sv.v)
                        i(e.scalar_tensor_tensor(
                            conva[p][:, :], strip[p][:, 2:TOK + 2],
                            convw_c[:, 4 * mt + 2:4 * mt + 3],
                            convd[p][:, :], ALU.mult, ALU.add), sv)
                        w(e, en, sv, sv.v)
                        i(e.scalar_tensor_tensor(
                            convd[p][:, :], strip[p][:, 3:TOK + 3],
                            convw_c[:, 4 * mt + 3:4 * mt + 4],
                            conva[p][:, :], ALU.mult, ALU.add), sv)
                    V(v10)
                    tok_conv = sv.v
                    strip_used[p] = tok_conv

                    def a4(e, en, p=p, tok_conv=tok_conv, gate=xbc_stage[p]):
                        w(e, en, sv, tok_conv)
                        w(e, en, y_xbc[p], gate)
                        i(e.activation(xstg[p][:, :], convd[p][:, :], AF.Silu), sa)
                    A(a4)
                    conv_sa[p] = sa.v
                    tok_silu = sa.v

                    def s4(e, en, p=p, mt=mt, tok_silu=tok_silu):
                        w(e, en, sa, tok_silu)
                        i(e.dma_start(out=dr_xbc.ap()[mt * 128:(mt + 1) * 128, :],
                                      in_=xstg[p][:, :]), y_xbc[p], 16)
                    S(s4)
                    xbc_stage[p] = y_xbc[p].v
                else:
                    mt = f0 // 128

                    def s5(e, en, p=p, mt=mt, tok_evac=tok_evac):
                        w(e, en, sv, tok_evac)
                        i(e.dma_start(out=dr_z.ap()[mt * 128:(mt + 1) * 128, :],
                                      in_=zstg[p][:, :]), y_z[p], 16)
                    S(s5)
                    z_stage[p] = y_z[p].v

            def v11(e, en, gate=dt_evac_tok):
                w(e, en, sv, gate)
                i(e.tensor_scalar(dt_t[:, :], dt_t[:, :], dtb_c[:, 0:1], None,
                                  ALU.add), sv)
            V(v11)
            tok_abs = sv.v

            def a5(e, en, tok_abs=tok_abs):
                w(e, en, sv, tok_abs)
                i(e.activation(dA[:, :], dt_t[:, :], AF.Abs), sa)
                w(e, en, sa, sa.v)
                i(e.activation(dA_cum[:, :], dA[:, :], AF.Exp, scale=-1.0), sa)
                w(e, en, sa, sa.v)
                i(e.activation(dA_cum[:, :], dA_cum[:, :], AF.Ln, bias=one_c[0:NH, 0:1]), sa)
            A(a5)
            tok_ln1p = sa.v

            def v12(e, en, tok_ln1p=tok_ln1p):
                w(e, en, sa, tok_ln1p)
                i(e.scalar_tensor_tensor(dt_t[:, :], dt_t[:, :], 0.0,
                                         dA_cum[:, :], ALU.max, ALU.add), sv)
                w(e, en, sv, sv.v)
                i(e.tensor_scalar(dA[:, :], dt_t[:, :], nA_col[:, 0:1], None,
                                  ALU.mult), sv)
                w(e, en, sv, sv.v)
                for z in range(NCH):
                    i(e.tensor_tensor_scan(
                        dA_cum[:, z * 128:(z + 1) * 128],
                        dA[:, z * 128:(z + 1) * 128],
                        APx(zcol, 0, [[1, NH], [0, 128]]),
                        0.0, ALU.add, ALU.add), sv)
                w(e, en, sv, sv.v)
                i(e.tensor_copy(dA_last[:, :],
                                APx(dA_cum, 127, [[TOK, NH], [128, NCH]])), sv)
            V(v12)
            tok_dal = sv.v

            def a6(e, en, tok_dal=tok_dal):
                w(e, en, sv, tok_dal)
                i(e.activation(cd_t[:, :], dA_last[:, :], AF.Exp), sa)
            A(a6)
            tok_cd = sa.v

            def v13(e, en, tok_cd=tok_cd):
                w(e, en, sa, tok_cd)
                i(e.tensor_tensor_scan(G_incl[:, :], cd_t[:, :],
                                       APx(zcol, 0, [[1, NH], [0, NCH]]),
                                       1.0, ALU.mult, ALU.add), sv)
                i(e.memset(G_pre[:, 0:1], 1.0), sv)
                if NCH > 1:
                    w(e, en, sv, sv.v)
                    i(e.tensor_copy(G_pre[:, 1:NCH], G_incl[:, 0:NCH - 1]), sv)
            V(v13)
            kb.barrier()

        with ExitStack() as pctx:
            pe = pctx.enter_context
            x_tok = [pe(nc.sbuf_tensor(f"x_tok{p}", [128, DI], F16)) for p in range(2)]
            Bfm = [pe(nc.sbuf_tensor(f"Bfm{p}", [128, 128], F16)) for p in range(2)]
            Cfm = [pe(nc.sbuf_tensor(f"Cfm{p}", [128, 128], F16)) for p in range(2)]
            Btok = [pe(nc.sbuf_tensor(f"Btok{p}", [128, 128], F16)) for p in range(2)]
            lhsE = [pe(nc.sbuf_tensor(f"lhsE{p}", [128, 16 * 128], F16)) for p in range(2)]
            rhsE = [pe(nc.sbuf_tensor(f"rhsE{p}", [128, 16 * 128], F16)) for p in range(2)]
            rhs2 = pe(nc.sbuf_tensor("rhs2", [2, NH * 128], F16))
            Hp16 = [pe(nc.sbuf_tensor(f"Hp16{p}", [NH, 128], F16)) for p in range(2)]
            Lp16 = [pe(nc.sbuf_tensor(f"Lp16{p}", [NH, 128], F16)) for p in range(2)]
            nH16 = [pe(nc.sbuf_tensor(f"nH16{p}", [NH, 128], F16)) for p in range(2)]
            nL16 = [pe(nc.sbuf_tensor(f"nL16{p}", [NH, 128], F16)) for p in range(2)]
            Ltmp = pe(nc.sbuf_tensor("Ltmp", [NH, 128], F32))
            M_sb = pe(nc.sbuf_tensor("M_sb", [128, NH * 128], F16))
            Mp_sb = pe(nc.sbuf_tensor("Mp_sb", [128, NH * 128], F16))
            CBT16 = pe(nc.sbuf_tensor("CBT16", [128, 128], F16))
            dtT16 = pe(nc.sbuf_tensor("dtT16", [128, NH], F16))
            decT16 = pe(nc.sbuf_tensor("decT16", [128, NH], F16))
            dec_f = pe(nc.sbuf_tensor("dec_f", [NH, 128], F32))
            cdrow = pe(nc.sbuf_tensor("cdrow", [1, NH], F32))
            cdnb = pe(nc.sbuf_tensor("cdnb", [128, NH], F32))
            Xdt = pe(nc.sbuf_tensor("Xdt", [128, DI], F16))
            Xdtd = pe(nc.sbuf_tensor("Xdtd", [128, DI], F16))
            ct1 = pe(nc.sbuf_tensor("ct1", [128, 512], F16))
            Yst = [pe(nc.sbuf_tensor(f"Yst{p}", [128, DI], F16)) for p in range(2)]
            pE = pe(nc.psum_tensor("pE", [128, 2048], F32))
            pEp = pe(nc.psum_tensor("pEp", [128, 1024], F32))
            pY = pe(nc.psum_tensor("pY", [128, 1024], F32))

            y_xt = [kb.sem(f"y_xt{p}") for p in range(2)]
            y_bc = [kb.sem(f"y_bc{p}") for p in range(2)]
            y_be = [kb.sem(f"y_be{p}") for p in range(2)]
            y_yp = [kb.sem(f"y_yp{p}") for p in range(2)]
            y_mp = kb.sem("y_mp")

            def v14(e, en):
                i(e.memset(carry16[:, :], 0.0), sv)
                for p in range(2):
                    i(e.memset(lhsE[p][:, :], 1.0), sv)
                    i(e.memset(rhsE[p][:, :], 1.0), sv)
            V(v14)
            kb.barrier()

            def hmap(rr):
                j, r = rr // 16, rr % 16
                return (r // 4) * 16 + j * 4 + (r % 4)

            hl_tok = [0, 0]
            be_done = [0, 0]
            mp_tok = [0] * 16
            m_tok = [0] * 16
            mp_done_v = [0, 0]
            e_evac = [0, 0, 0, 0]
            ysv = [0, 0]
            st_carry = [0] * 8
            y_stage_tok = [0, 0]
            rhs2_used = [0]
            xtok_used = [0, 0]
            cfm_used = [0, 0]
            btok_used = [0, 0]
            bfm_used = [0, 0]
            xdt_used_st = 0
            m_used_st = 0

            for z in range(NCH):
                p = z % 2

                def s6(e, en, z=z, p=p, g1_=xtok_used[p], g2_=cfm_used[p],
                       g3_=btok_used[p], g4_=bfm_used[p],
                       gb0=be_done[0], gb1=be_done[1]):
                    w(e, en, sv, g1_)
                    w(e, en, sv, g2_)
                    w(e, en, st_, g3_)
                    w(e, en, st_, g4_)
                    w(e, en, y_be[0], gb0)
                    w(e, en, y_be[1], gb1)
                    i(e.dma_start_transpose(
                        out=x_tok[p][:, :],
                        in_=dr_xbc.ap()[0:DI, z * 128:(z + 1) * 128]), y_xt[p], 16)
                    i(e.dma_start(out=Bfm[p][:, :],
                                  in_=dr_xbc.ap()[DI:DI + 128, z * 128:(z + 1) * 128]), y_bc[p], 16)
                    i(e.dma_start(out=Cfm[p][:, :],
                                  in_=dr_xbc.ap()[DI + 128:DI + 256, z * 128:(z + 1) * 128]), y_bc[p], 16)
                    i(e.dma_start_transpose(
                        out=Btok[p][:, :],
                        in_=dr_xbc.ap()[DI:DI + 128, z * 128:(z + 1) * 128]), y_bc[p], 16)
                S(s6)
                tok_xt = y_xt[p].v
                tok_bc = y_bc[p].v

                def v15(e, en, z=z, p=p, gate=be_done[p]):
                    w(e, en, y_be[p], gate)
                    i(e.tensor_copy(Hp16[p][:, :], dA_cum[:, z * 128:(z + 1) * 128]), sv)
                    w(e, en, sv, sv.v)
                    i(e.tensor_tensor(Ltmp[:, :], dA_cum[:, z * 128:(z + 1) * 128],
                                      Hp16[p][:, :], ALU.subtract), sv)
                    w(e, en, sv, sv.v)
                    i(e.tensor_copy(Lp16[p][:, :], Ltmp[:, :]), sv)
                    i(e.tensor_scalar_mul(nH16[p][:, :], Hp16[p][:, :], -1.0), sv)
                    w(e, en, sv, sv.v)
                    i(e.tensor_scalar_mul(nL16[p][:, :], Lp16[p][:, :], -1.0), sv)
                V(v15)
                hl_tok[p] = sv.v
                tok_hl = sv.v

                def s7(e, en, z=z, p=p, tok_hl=tok_hl, tok_xt=tok_xt,
                       tok_bc=tok_bc):
                    w(e, en, sv, tok_hl)
                    w(e, en, y_xt[p], tok_xt)
                    w(e, en, y_bc[p], tok_bc)
                    for j in range(4):
                        for (tile, row, src) in [(lhsE[p], 32 * j + 2, nH16[p]),
                                                 (lhsE[p], 32 * j + 3, nL16[p]),
                                                 (rhsE[p], 32 * j + 0, Hp16[p]),
                                                 (rhsE[p], 32 * j + 1, Lp16[p])]:
                            i(e.dma_start(out=tile[row:row + 1, 0:2048],
                                          in_=APx(src, 16 * j * 128, [[128, 16], [1, 128]])),
                              y_be[p], 16)
                    w(e, en, st_, rhs2_used[0])
                    i(e.dma_start(out=rhs2[0:1, :],
                                  in_=APx(Hp16[p], 0, [[128, NH], [1, 128]])), y_be[p], 16)
                    i(e.dma_start(out=rhs2[1:2, :],
                                  in_=APx(Lp16[p], 0, [[128, NH], [1, 128]])), y_be[p], 16)
                S(s7)
                be_done[p] = y_be[p].v
                tok_be = y_be[p].v

                def t5(e, en, p=p, tok_bc=tok_bc, g1_=ysv[0], g2_=ysv[1]):
                    w(e, en, y_bc[p], tok_bc)
                    w(e, en, sv, g1_)
                    w(e, en, sv, g2_)
                    i(e.matmul(pY[:, 0:128], Bfm[p][:, :], Cfm[p][:, :],
                               start=True, stop=True), st_)
                T(t5)
                bfm_used[p] = st_.v
                tok_cbt_mm = st_.v

                def v16(e, en, tok=tok_cbt_mm):
                    w(e, en, st_, tok)
                    i(e.tensor_copy(CBT16[:, :], pY[:, 0:128]), sv)
                V(v16)
                tok_cbt_cp = sv.v

                def g2(e, en, tok=tok_cbt_cp):
                    w(e, en, sv, tok)
                    i(e.affine_select(CBT16[:, :], CBT16[:, :], [[1, 128]],
                                      ALU.is_ge, 0.0, base=0,
                                      channel_multiplier=-1), sg)
                G(g2)
                tok_cbt = sg.v

                def v17(e, en, z=z):
                    i(e.tensor_scalar(dec_f[:, :], dA_cum[:, z * 128:(z + 1) * 128],
                                      dA_last[:, z:z + 1], None, ALU.subtract), sv)
                V(v17)
                tok_decf = sv.v

                def a7(e, en, tok=tok_decf):
                    w(e, en, sv, tok)
                    i(e.activation(dec_f[:, :], dec_f[:, :], AF.Exp, scale=-1.0), sa)
                A(a7)
                tok_dece = sa.v

                def t6(e, en, z=z, tok_dece=tok_dece, tok_cp=tok_cbt_cp):
                    w(e, en, sa, tok_dece)
                    w(e, en, sv, tok_cp)
                    i(e.transpose(pY[0:128, 128:192], dt_t[:, z * 128:(z + 1) * 128],
                                  idn32[0:64, 0:64]), st_)
                    i(e.transpose(pY[0:128, 192:256], dec_f[:, :], idn32[0:64, 0:64]), st_)
                    i(e.transpose(pY[0:1, 256:320], cd_t[:, z:z + 1], idn32[0:64, 0:64]), st_)
                T(t6)
                tok_tr = st_.v

                def v18(e, en, tok=tok_tr):
                    w(e, en, st_, tok)
                    i(e.tensor_copy(dtT16[:, :], pY[:, 128:192]), sv)
                    i(e.tensor_copy(decT16[:, :], pY[:, 192:256]), sv)
                    i(e.tensor_copy(cdrow[0:1, :], pY[0:1, 256:320]), sv)
                V(v18)
                tok_trc = sv.v

                def t7(e, en, tok=tok_trc):
                    w(e, en, sv, tok)
                    i(e.matmul(pY[:, 320:384], ones32[0:1, :], cdrow[0:1, :],
                               start=True, stop=True), st_)
                T(t7)
                tok_cdmm = st_.v

                def v19(e, en, tok=tok_cdmm):
                    w(e, en, st_, tok)
                    i(e.tensor_copy(cdnb[:, :], pY[:, 320:384]), sv)
                V(v19)
                tok_cdnb = sv.v

                def v20(e, en, p=p, tok_xt=tok_xt, tok_trc=tok_trc,
                        gate=xdt_used_st):
                    w(e, en, y_xt[p], tok_xt)
                    w(e, en, sv, tok_trc)
                    w(e, en, st_, gate)
                    i(e.tensor_tensor(Xdt[:, :], x_tok[p][:, :],
                                      APx(dtT16, 0, [[NH, 128], [1, NH], [0, HD]]),
                                      ALU.mult), sv)
                    w(e, en, sv, sv.v)
                    i(e.tensor_tensor(Xdtd[:, :], Xdt[:, :],
                                      APx(decT16, 0, [[NH, 128], [1, NH], [0, HD]]),
                                      ALU.mult), sv)
                V(v20)
                tok_xdt = sv.v
                xtok_used[p] = tok_xdt

                for q16 in range(16):
                    half = q16 % 2

                    def t8(e, en, p=p, q16=q16, half=half, tok_be=tok_be,
                           gate=mp_done_v[half]):
                        w(e, en, y_be[p], tok_be)
                        w(e, en, sv, gate)
                        i(e.matmul(pEp[:, half * 512:half * 512 + 512],
                                   ones2[0:2, :],
                                   rhs2[0:2, q16 * 512:(q16 + 1) * 512],
                                   start=True, stop=True), st_)
                    T(t8)
                    tok_ep = st_.v

                    def v21a(e, en, q16=q16, half=half, tok_ep=tok_ep,
                             gate=m_used_st, gate_mp=y_mp.v):
                        w(e, en, st_, tok_ep)
                        w(e, en, st_, gate)
                        w(e, en, y_mp, gate_mp)
                        i(e.tensor_scalar_min(Mp_sb[:, q16 * 512:(q16 + 1) * 512],
                                              pEp[:, half * 512:half * 512 + 512],
                                              0.0), sv)
                    V(v21a)
                    mp_done_v[half] = sv.v
                rhs2_used[0] = st_.v
                tok_mpraw = sv.v

                def a8(e, en, tok=tok_mpraw):
                    w(e, en, sv, tok)
                    i(e.activation(Mp_sb[:, 0:4096], Mp_sb[:, 0:4096], AF.Exp), sa)
                    i(e.activation(Mp_sb[:, 4096:8192], Mp_sb[:, 4096:8192],
                                   AF.Exp), sa)
                A(a8)
                tok_mpe = sa.v

                def v21(e, en, p=p, tok=tok_mpe):
                    w(e, en, sa, tok)
                    i(e.tensor_tensor(
                        Mp_sb[:, :],
                        Mp_sb[:, :],
                        APx(Cfm[p], 0, [[128, 128], [0, NH], [1, 128]]),
                        ALU.mult), sv)
                V(v21)
                for q16 in range(16):
                    mp_tok[q16] = sv.v
                cfm_used[p] = sv.v
                tok_mp_done = sv.v

                def s_mp(e, en, z=z, tok=tok_mp_done):
                    w(e, en, sv, tok)
                    i(e.dma_start(out=dr_mp.ap()[z, :, :], in_=Mp_sb[:, :]),
                      y_mp, 16)
                S(s_mp)

                for cyc in range(4):
                    def t9(e, en, p=p, cyc=cyc, tok_be=tok_be,
                           gates=tuple(e_evac)):
                        w(e, en, y_be[p], tok_be)
                        for b in range(4):
                            w(e, en, sv, gates[b])
                        for rr in range(4):
                            r = cyc * 4 + rr
                            for j in range(4):
                                i(e.matmul(
                                    pE[:, j * 512 + rr * 128:j * 512 + rr * 128 + 128],
                                    lhsE[p][32 * j:32 * j + 4, r * 128:(r + 1) * 128],
                                    rhsE[p][32 * j:32 * j + 4, r * 128:(r + 1) * 128],
                                    start=True, stop=True,
                                    tile_position=(32 * j, 0)), st_)
                    T(t9)
                    tok_ecyc = st_.v

                    for j in range(4):
                        def v9b(e, en, j=j, cyc=cyc, tok_ecyc=tok_ecyc,
                                gate=m_used_st):
                            w(e, en, st_, tok_ecyc)
                            w(e, en, st_, gate)
                            i(e.tensor_scalar_min(
                                M_sb[:, cyc * 2048 + j * 512:cyc * 2048 + (j + 1) * 512],
                                pE[:, j * 512:(j + 1) * 512], 0.0), sv)
                        V(v9b)
                        e_evac[j] = sv.v

                tok_mraw = sv.v

                def a9(e, en, tok=tok_mraw):
                    w(e, en, sv, tok)
                    i(e.activation(M_sb[:, 0:4096], M_sb[:, 0:4096], AF.Exp), sa)
                    i(e.activation(M_sb[:, 4096:8192], M_sb[:, 4096:8192],
                                   AF.Exp), sa)
                A(a9)
                tok_me = sa.v

                def v22(e, en, tok=tok_me, tok_cbt=tok_cbt):
                    w(e, en, sa, tok)
                    w(e, en, sg, tok_cbt)
                    i(e.tensor_tensor(
                        M_sb[:, :], M_sb[:, :],
                        APx(CBT16, 0, [[128, 128], [0, NH], [1, 128]]),
                        ALU.mult), sv)
                V(v22)
                for idx16 in range(16):
                    m_tok[idx16] = sv.v

                for q in range(8):
                    hp = q % 2

                    def t10(e, en, q=q, hp=hp, tok_xdt=tok_xdt,
                            tok_cdnb=tok_cdnb, gate_c=st_carry[q],
                            gate_y=ysv[hp]):
                        for h in range(q * 8, q * 8 + 8):
                            w(e, en, sv, m_tok[hmap(h) // 4])
                            w(e, en, sv, mp_tok[h // 4])
                        w(e, en, sv, tok_xdt)
                        w(e, en, sv, tok_cdnb)
                        w(e, en, sv, gate_c)
                        w(e, en, sv, gate_y)
                        for h in range(q * 8, q * 8 + 8):
                            c0 = hp * 512 + (h % 8) * 64
                            i(e.matmul(pY[:, c0:c0 + 64],
                                       M_sb[:, hmap(h) * 128:hmap(h) * 128 + 128],
                                       Xdt[:, h * 64:h * 64 + 64],
                                       start=True, stop=False), st_)
                            i(e.matmul(pY[:, c0:c0 + 64],
                                       Mp_sb[:, h * 128:h * 128 + 128],
                                       carry16[:, h * 64:h * 64 + 64],
                                       start=False, stop=True), st_)
                    T(t10)
                    tok_y = st_.v

                    def t11(e, en, p=p, q=q, hp=hp, tok_bc=tok_bc,
                            gate=mp_done_v[hp]):
                        w(e, en, y_bc[p], tok_bc)
                        w(e, en, sv, gate)
                        for h in range(q * 8, q * 8 + 8):
                            i(e.matmul(pEp[:, hp * 512 + (h % 8) * 64:
                                           hp * 512 + (h % 8) * 64 + 64],
                                       Btok[p][:, :], Xdtd[:, h * 64:h * 64 + 64],
                                       start=True, stop=True), st_)
                    T(t11)
                    btok_used[p] = st_.v
                    tok_stm = st_.v

                    def v23(e, en, q=q, hp=hp, tok_stm=tok_stm, tok_cdnb=tok_cdnb,
                            selfgate=sv.v):
                        w(e, en, st_, tok_stm)
                        w(e, en, sv, max(tok_cdnb, selfgate))
                        i(e.tensor_tensor(ct1[:, :], carry16[:, q * 512:(q + 1) * 512],
                                          APx(cdnb, q * 8, [[NH, 128], [1, 8], [0, HD]]),
                                          ALU.mult), sv)
                        w(e, en, sv, sv.v)
                        i(e.tensor_add(carry16[:, q * 512:(q + 1) * 512], ct1[:, :],
                                       pEp[:, hp * 512:hp * 512 + 512]), sv)
                    V(v23)
                    st_carry[q] = sv.v
                    mp_done_v[hp] = sv.v

                    def v24(e, en, z=z, q=q, hp=hp, tok_y=tok_y,
                            gate=y_stage_tok[z % 2]):
                        w(e, en, st_, tok_y)
                        w(e, en, y_yp[z % 2], gate)
                        i(e.tensor_copy(Yst[z % 2][:, q * 512:(q + 1) * 512],
                                        pY[:, hp * 512:hp * 512 + 512]), sv)
                    V(v24)
                    ysv[hp] = sv.v
                tok_yall = sv.v
                xdt_used_st = st_.v
                m_used_st = st_.v

                def s8(e, en, z=z, tok_yall=tok_yall):
                    w(e, en, sv, tok_yall)
                    i(e.dma_start(out=dr_ypre.ap()[z * 128:(z + 1) * 128, :],
                                  in_=Yst[z % 2][:, :]), y_yp[z % 2], 16)
                S(s8)
                y_stage_tok[z % 2] = y_yp[z % 2].v

            y_cc1 = kb.sem("y_cc1")
            y_cc2 = kb.sem("y_cc2")

            def g3a(e, en, gate=tuple(st_carry)):
                for q in range(8):
                    w(e, en, sv, gate[q])
                i(e.dma_start(out=dr_cci.ap(), in_=carry16[:, :]), y_cc1, 16)
            G(g3a)

            def g3(e, en):
                w(e, en, y_cc1, y_cc1.v)
                groups = [[c, c + 1] for c in range(0, ncores, 2)]
                i(e.collective_compute("AllReduce", ALU.add,
                                       replica_groups=groups,
                                       ins=[dr_cci.ap()], outs=[dr_cco.ap()]),
                  g_cc, 1)
            G(g3)

            def s10(e, en):
                w(e, en, g_cc, g_cc.v)
                i(e.dma_start(out=sum32[:, :], in_=dr_cco.ap()), y_cc2, 16)
            S(s10)

            def v26(e, en):
                w(e, en, y_cc2, y_cc2.v)
                i(e.tensor_tensor(sum32[:, :], sum32[:, :], carry16[:, :],
                                  ALU.subtract), sv)
                w(e, en, sv, sv.v)
                i(e.tensor_scalar(patch16[:, :], sum32[:, :], flag_c[:, 0:1],
                                  None, ALU.mult), sv)
            V(v26)
            kb.barrier()

        with ExitStack() as pctx:
            pe = pctx.enter_context
            Ypt = [pe(nc.sbuf_tensor(f"Ypt{p}", [128, DI], F16)) for p in range(2)]
            z_tk = [pe(nc.sbuf_tensor(f"z_tk{p}", [128, DI], F16)) for p in range(2)]
            x_tk = [pe(nc.sbuf_tensor(f"x_tk{p}", [128, DI], F16)) for p in range(2)]
            Mp2t = [pe(nc.sbuf_tensor(f"Mp2t{p}", [128, NH * 128], F16)) for p in range(2)]
            stz = pe(nc.sbuf_tensor("stz", [128, DI], F16))
            siluz = pe(nc.sbuf_tensor("siluz", [128, DI], F16))
            ygA = pe(nc.sbuf_tensor("ygA", [128, DI], F16))
            ygB = pe(nc.sbuf_tensor("ygB", [128, DI], F16))
            ssum = pe(nc.sbuf_tensor("ssum", [128, 1], F32))
            rstd2 = pe(nc.sbuf_tensor("rstd2", [128, 1], F32))
            ynst = [pe(nc.sbuf_tensor(f"ynst{p}", [128, DI], F16)) for p in range(2)]
            pYb = pe(nc.psum_tensor("pYb", [128, 1024], F32))
            pM2 = pe(nc.psum_tensor("pM2", [128, 512], F32))

            y_p4 = [kb.sem(f"y_p4_{p}") for p in range(2)]
            y_be2 = [kb.sem(f"y_be2_{p}") for p in range(2)]
            y_yn = [kb.sem(f"y_yn{p}") for p in range(2)]

            growall = pe(nc.sbuf_tensor("growall", [NCH, NH], F32))
            growflat = pe(nc.sbuf_tensor("growflat", [1, NCH * NH], F32))
            gnb_all = pe(nc.sbuf_tensor("gnb_all", [128, NCH * NH], F32))
            y_gf = kb.sem("y_gf")

            def tg0(e, en):
                i(e.transpose(pM2[0:NCH, 0:NH], G_pre[:, :], idn32[0:NH, 0:NH]), st_)
            T(tg0)
            tok_tg0 = st_.v

            def vg0(e, en, tok=tok_tg0):
                w(e, en, st_, tok)
                i(e.tensor_copy(growall[:, :], pM2[0:NCH, 0:NH]), sv)
            V(vg0)
            tok_vg0 = sv.v

            def sg0b(e, en, tok=tok_vg0):
                w(e, en, sv, tok)
                i(e.dma_start(out=growflat[0:1, :],
                              in_=APx(growall, 0, [[NH, NCH], [1, NH]])), y_gf, 16)
            S(sg0b)
            tok_gf = y_gf.v

            for qq in range((NCH * NH + 511) // 512):
                c0 = qq * 512
                cn = min(512, NCH * NH - c0)

                def tg1(e, en, c0=c0, cn=cn, tok=tok_gf, gate=sv.v):
                    w(e, en, y_gf, tok)
                    w(e, en, sv, gate)
                    i(e.matmul(pM2[:, 0:cn], ones32[0:1, :],
                               growflat[0:1, c0:c0 + cn],
                               start=True, stop=True), st_)
                T(tg1)
                tok_tg1 = st_.v

                def vg1(e, en, c0=c0, cn=cn, tok=tok_tg1):
                    w(e, en, st_, tok)
                    i(e.tensor_copy(gnb_all[:, c0:c0 + cn], pM2[:, 0:cn]), sv)
                V(vg1)
            tok_gnball = sv.v

            mp2_rd_gate = [0, 0]
            stz_used_st = 0
            ygq_tok = [0, 0]
            yn_stage_tok = [0, 0]
            p4_v_reads = [0, 0]
            p4_a_reads = [0, 0]
            prev_ss = 0
            prev_yg = 0
            prev_vv = 0
            prev_yn = 0

            for z in range(NCH):
                p = z % 2

                def s11(e, en, z=z, p=p, g1_=p4_v_reads[p], g2_=p4_a_reads[p],
                        gfl=y_gf.v):
                    w(e, en, sv, g1_)
                    w(e, en, sa, g2_)
                    w(e, en, y_gf, gfl)
                    i(e.dma_start(out=Ypt[p][:, :],
                                  in_=dr_ypre.ap()[z * 128:(z + 1) * 128, :]), y_p4[p], 16)
                    i(e.dma_start_transpose(
                        out=z_tk[p][:, :],
                        in_=dr_z.ap()[0:DI, z * 128:(z + 1) * 128]), y_p4[p], 16)
                    i(e.dma_start_transpose(
                        out=x_tk[p][:, :],
                        in_=dr_xbc.ap()[0:DI, z * 128:(z + 1) * 128]), y_p4[p], 16)
                S(s11)
                tok_rd = y_p4[p].v

                def s12(e, en, z=z, p=p, gate=mp2_rd_gate[p]):
                    w(e, en, st_, gate)
                    i(e.dma_start(out=Mp2t[p][:, :], in_=dr_mp.ap()[z, :, :]),
                      y_be2[p], 16)
                S(s12)
                tok_r2b = y_be2[p].v

                def a11(e, en, p=p, tok_rd=tok_rd, gate=prev_yg):
                    w(e, en, y_p4[p], tok_rd)
                    w(e, en, sv, gate)
                    i(e.activation(siluz[:, :], z_tk[p][:, :], AF.Silu), sa)
                A(a11)
                p4_a_reads[p] = sa.v
                tok_slz = sa.v

                def v29(e, en, z=z, gate=stz_used_st, tok=tok_gnball):
                    w(e, en, sv, tok)
                    w(e, en, st_, gate)
                    i(e.tensor_tensor(stz[:, :], patch16[:, :],
                                      APx(gnb_all, z * NH, [[NCH * NH, 128], [1, NH], [0, HD]]),
                                      ALU.mult), sv)
                V(v29)
                tok_stz = sv.v
                for q in range(8):
                    hp = q % 2

                    def t15(e, en, p=p, q=q, hp=hp, tok_stz=tok_stz,
                            gate=ygq_tok[hp], tok_r2b=tok_r2b):
                        w(e, en, y_be2[p], tok_r2b)
                        w(e, en, sv, tok_stz)
                        w(e, en, sv, gate)
                        for h in range(q * 8, q * 8 + 8):
                            c0 = hp * 512 + (h % 8) * 64
                            i(e.matmul(pYb[:, c0:c0 + 64],
                                       Mp2t[p][:, h * 128:h * 128 + 128],
                                       stz[:, h * 64:h * 64 + 64],
                                       start=True, stop=True), st_)
                    T(t15)
                    tok_pt = st_.v
                    mp2_rd_gate[p] = st_.v

                    def v31(e, en, p=p, q=q, hp=hp, tok_pt=tok_pt, tok_rd=tok_rd,
                            gate=prev_ss):
                        w(e, en, st_, tok_pt)
                        w(e, en, y_p4[p], tok_rd)
                        w(e, en, sa, gate)
                        i(e.tensor_add(ygA[:, q * 512:(q + 1) * 512],
                                       Ypt[p][:, q * 512:(q + 1) * 512],
                                       pYb[:, hp * 512:hp * 512 + 512]), sv)
                    V(v31)
                    ygq_tok[hp] = sv.v
                tok_yga = sv.v
                stz_used_st = st_.v

                def v32(e, en, p=p, tok_rd=tok_rd, tok_slz=tok_slz,
                        tok_yga=tok_yga, gate=prev_ss):
                    w(e, en, y_p4[p], tok_rd)
                    w(e, en, sa, gate)
                    i(e.tensor_tensor(ygB[:, :], x_tk[p][:, :],
                                      APx(D_nb, 0, [[NH, 128], [1, NH], [0, HD]]),
                                      ALU.mult), sv)
                    w(e, en, sv, sv.v)
                    w(e, en, sv, tok_yga)
                    i(e.tensor_add(ygB[:, :], ygA[:, :], ygB[:, :]), sv)
                    w(e, en, sv, sv.v)
                    w(e, en, sa, tok_slz)
                    i(e.tensor_tensor(ygA[:, :], ygB[:, :], siluz[:, :],
                                      ALU.mult), sv)
                V(v32)
                tok_yg = sv.v
                prev_yg = tok_yg
                p4_v_reads[p] = tok_yg

                def a12(e, en, tok=tok_yg, gate=prev_vv):
                    w(e, en, sv, tok)
                    w(e, en, sv, gate)
                    i(e.activation(ygB[:, :], ygA[:, :], AF.Square,
                                   accum_out=ssum[:, :]), sa)
                A(a12)
                tok_ss = sa.v
                prev_ss = tok_ss

                def v33(e, en, tok=tok_ss, gate=prev_yn):
                    w(e, en, sa, tok)
                    w(e, en, sv, gate)
                    i(e.tensor_scalar_mul(rstd2[:, :], ssum[:, :], -1.0 / DI), sv)
                V(v33)
                tok_vv = sv.v
                prev_vv = tok_vv

                def a13(e, en, tok=tok_vv):
                    w(e, en, sv, tok)
                    i(e.activation(rstd2[:, :], rstd2[:, :], AF.Ln, bias=eps_c[:, 0:1],
                                   scale=-1.0), sa)
                    w(e, en, sa, sa.v)
                    i(e.activation(rstd2[:, :], rstd2[:, :], AF.Exp, scale=-0.5), sa)
                A(a13)
                tok_rstd = sa.v

                def v34(e, en, z=z, tok=tok_rstd, gate=yn_stage_tok[z % 2]):
                    w(e, en, sa, tok)
                    w(e, en, y_yn[z % 2], gate)
                    i(e.scalar_tensor_tensor(ynst[z % 2][:, :], ygA[:, :],
                                             rstd2[:, 0:1], nw_bc[:, :],
                                             ALU.mult, ALU.mult), sv)
                V(v34)
                tok_yn = sv.v
                prev_yn = tok_yn

                def s13(e, en, z=z, tok=tok_yn):
                    w(e, en, sv, tok)
                    i(e.dma_start(out=dr_yn.ap()[z * 128:(z + 1) * 128, :],
                                  in_=ynst[z % 2][:, :]), y_yn[z % 2], 16)
                S(s13)
                yn_stage_tok[z % 2] = y_yn[z % 2].v
            kb.barrier()

        scan_ctx.close()

        with ExitStack() as pctx:
            pe = pctx.enter_context
            ynfm = pe(nc.sbuf_tensor("ynfm", [128, 32 * TOK], F16))
            wop = [pe(nc.sbuf_tensor(f"wop{p}", [128, 32 * 128], F16)) for p in range(2)]
            atst = [pe(nc.sbuf_tensor(f"atst{p}", [128, TOK], F16)) for p in range(2)]
            pC = [pe(nc.psum_tensor(f"pC{p}", [128, 4 * 512], F32)) for p in range(2)]
            y_ynf = kb.sem("y_ynf")
            g_w2 = [kb.sem(f"g_w2_{p}") for p in range(2)]
            y_at = [kb.sem(f"y_at{p}") for p in range(2)]

            def s14(e, en):
                for kc in range(0, 32, 2):
                    i(e.dma_start_transpose(
                        out=ynfm[:, kc * TOK:(kc + 1) * TOK],
                        in_=dr_yn.ap()[:, kc * 128:(kc + 1) * 128]), y_ynf, 16)
            S(s14)

            def s14b(e, en):
                for kc in range(1, 32, 2):
                    i(e.dma_start_transpose(
                        out=ynfm[:, kc * TOK:(kc + 1) * TOK],
                        in_=dr_yn.ap()[:, kc * 128:(kc + 1) * 128]), y_ynf, 16)
            S(s14b)
            y_ynf_all = y_ynf.v

            wt2_last = [0, 0]
            evC = [[0] * 4, [0] * 4]
            at_stage = [0, 0]
            for m in range(16):
                p = m % 2

                def g4(e, en, m=m, p=p, gate=wt2_last[p]):
                    w(e, en, st_, gate)
                    src = APx(opw, m * 4096, [[16 * 32 * 128, 128], [1, 4096]])
                    dst = APx(wop[p], 0, [[32 * 128, 128], [1, 4096]])
                    i(e.dma_start(out=dst, in_=src), g_w2[p], 16)
                G(g4)
                tok_w2 = g_w2[p].v

                for g in range(NG):
                    def t16(e, en, m=m, p=p, g=g, tok_w2=tok_w2, gate=evC[p][g]):
                        w(e, en, g_w2[p], tok_w2)
                        w(e, en, y_ynf, y_ynf_all)
                        w(e, en, sv, gate)
                        for kc in range(32):
                            i(e.matmul(pC[p][:, g * 512:g * 512 + 512],
                                       wop[p][:, kc * 128:(kc + 1) * 128],
                                       ynfm[:, kc * TOK + g * 512:kc * TOK + g * 512 + 512],
                                       start=(kc == 0), stop=(kc == 31)), st_)
                    T(t16)
                    wt2_last[p] = st_.v
                    tok_c = st_.v

                    def v35(e, en, p=p, g=g, tok_c=tok_c, gate=at_stage[p]):
                        w(e, en, st_, tok_c)
                        w(e, en, y_at[p], gate)
                        i(e.tensor_copy(atst[p][:, g * 512:g * 512 + 512],
                                        pC[p][:, g * 512:g * 512 + 512]), sv)
                    V(v35)
                    evC[p][g] = sv.v
                tok_at = sv.v

                def s15(e, en, m=m, p=p, tok_at=tok_at):
                    w(e, en, sv, tok_at)
                    i(e.dma_start(out=dr_attn.ap()[m * 128:(m + 1) * 128, :],
                                  in_=atst[p][:, :]), y_at[p], 16)
                S(s15)
                at_stage[p] = y_at[p].v
            kb.barrier()

        with ExitStack() as pctx:
            pe = pctx.enter_context
            u2 = [pe(nc.sbuf_tensor(f"u2_{p}", [128, DM], F32)) for p in range(2)]
            att = [pe(nc.sbuf_tensor(f"att{p}", [128, DM], F16)) for p in range(2)]
            x1t = [pe(nc.sbuf_tensor(f"x1t{p}", [128, DM], F32)) for p in range(2)]
            xsq = [pe(nc.sbuf_tensor(f"xsq{p}", [128, DM], F32)) for p in range(2)]
            h2t = [pe(nc.sbuf_tensor(f"h2t{p}", [128, DM], F16)) for p in range(2)]
            ms2 = [pe(nc.sbuf_tensor(f"ms2_{p}", [128, 1], F32)) for p in range(2)]
            mn2 = [pe(nc.sbuf_tensor(f"mn2_{p}", [128, 1], F32)) for p in range(2)]
            qs2 = [pe(nc.sbuf_tensor(f"qs2_{p}", [128, 1], F32)) for p in range(2)]
            qq2 = [pe(nc.sbuf_tensor(f"qq2_{p}", [128, 1], F32)) for p in range(2)]
            nv2 = [pe(nc.sbuf_tensor(f"nv2_{p}", [128, 1], F32)) for p in range(2)]
            rs2 = [pe(nc.sbuf_tensor(f"rs2_{p}", [128, 1], F32)) for p in range(2)]
            mr2 = [pe(nc.sbuf_tensor(f"mr2_{p}", [128, 1], F32)) for p in range(2)]
            y_u2 = [kb.sem(f"y_u2_{p}") for p in range(2)]
            y_x1 = [kb.sem(f"y_x1_{p}") for p in range(2)]
            y_h2 = [kb.sem(f"y_h2_{p}") for p in range(2)]

            last_h2 = [0, 0]
            last_sq2 = [0, 0]
            for tt in range(TOK // 128):
                p = tt % 2

                def s16(e, en, tt=tt, p=p, gate=last_h2[p]):
                    w(e, en, sg, gate)
                    i(e.dma_start(out=u2[p][:, :],
                                  in_=u_in.ap()[HALO + tt * 128:HALO + (tt + 1) * 128, :]), y_u2[p], 16)
                    i(e.dma_start_transpose(
                        out=att[p][:, :],
                        in_=dr_attn.ap()[:, tt * 128:(tt + 1) * 128]), y_u2[p], 16)
                S(s16)
                tok_r = y_u2[p].v

                def v36(e, en, p=p, tok_r=tok_r, gate=y_x1[p].v):
                    w(e, en, y_u2[p], tok_r)
                    w(e, en, y_x1[p], gate)
                    i(e.tensor_add(x1t[p][:, :], u2[p][:, :], att[p][:, :]), sv)
                    w(e, en, sv, sv.v)
                    i(e.reduce_sum(ms2[p][:, :], x1t[p][:, :], AX.X), sv)
                    w(e, en, sv, sv.v)
                    i(e.tensor_scalar_mul(mn2[p][:, :], ms2[p][:, :], -1.0 / DM), sv)
                V(v36)
                tok_x1 = sv.v

                def s17(e, en, tt=tt, p=p, tok=tok_x1):
                    w(e, en, sv, tok)
                    i(e.dma_start(out=dr_x1.ap()[tt * 128:(tt + 1) * 128, :],
                                  in_=x1t[p][:, :]), y_x1[p], 16)
                S(s17)

                def g14(e, en, p=p, tok=tok_x1, gate=last_sq2[p]):
                    w(e, en, sv, tok)
                    w(e, en, sv, gate)
                    i(e.tensor_tensor(xsq[p][:, :], x1t[p][:, :],
                                      x1t[p][:, :], ALU.mult), sg)
                G(g14)
                tok_q2a = sg.v

                def v37a(e, en, p=p, tok=tok_q2a):
                    w(e, en, sg, tok)
                    i(e.reduce_sum(qs2[p][:, :], xsq[p][:, :], AX.X), sv)
                V(v37a)
                tok_q2 = sv.v

                def v37(e, en, p=p, tok=tok_q2):
                    w(e, en, sv, tok)
                    i(e.tensor_scalar_mul(qq2[p][:, :], qs2[p][:, :], 1.0 / DM), sv)
                    w(e, en, sv, sv.v)
                    i(e.scalar_tensor_tensor(nv2[p][:, :], mn2[p][:, :],
                                             mn2[p][:, 0:1], qq2[p][:, :],
                                             ALU.mult, ALU.subtract), sv)
                V(v37)
                tok_nv2 = sv.v
                last_sq2[p] = tok_nv2

                def a15(e, en, p=p, tok=tok_nv2):
                    w(e, en, sv, tok)
                    i(e.activation(rs2[p][:, :], nv2[p][:, :], AF.Ln, bias=eps_c[:, 0:1],
                                   scale=-1.0), sa)
                    w(e, en, sa, sa.v)
                    i(e.activation(rs2[p][:, :], rs2[p][:, :], AF.Exp, scale=-0.5), sa)
                A(a15)
                tok_rs2 = sa.v

                def v38(e, en, p=p, tok=tok_rs2):
                    w(e, en, sa, tok)
                    i(e.tensor_tensor(mr2[p][:, :], mn2[p][:, :], rs2[p][:, :],
                                      ALU.mult), sv)
                V(v38)
                tok_mr2 = sv.v

                def g16(e, en, p=p, tok=tok_mr2, gate=y_h2[p].v):
                    w(e, en, sv, tok)
                    w(e, en, y_h2[p], gate)
                    i(e.tensor_scalar(h2t[p][:, :], x1t[p][:, :],
                                      mn2[p][:, 0:1], rs2[p][:, 0:1],
                                      ALU.add, ALU.mult), sg)
                G(g16)
                last_h2[p] = sg.v
                tok_h2 = sg.v

                def s18(e, en, tt=tt, p=p, tok=tok_h2):
                    w(e, en, sg, tok)
                    i(e.dma_start(out=dr_h2.ap()[tt * 128:(tt + 1) * 128, :],
                                  in_=h2t[p][:, :]), y_h2[p], 16)
                S(s18)
            kb.barrier()

        with ExitStack() as pctx:
            pe = pctx.enter_context
            h2fm = pe(nc.sbuf_tensor("h2fm", [128, KC * TOK], F16))
            w0t = [pe(nc.sbuf_tensor(f"w0t{p}", [128, KC * 128], F16)) for p in range(2)]
            gst = [pe(nc.sbuf_tensor(f"gst{p}", [128, TOK], F16)) for p in range(2)]
            pD = [pe(nc.psum_tensor(f"pD{p}", [128, 4 * 512], F32)) for p in range(2)]
            y_h2f = kb.sem("y_h2f")
            g_w3 = [kb.sem(f"g_w3_{p}") for p in range(2)]
            y_g = [kb.sem(f"y_g{p}") for p in range(2)]

            def s19(e, en):
                for kc in range(0, KC, 2):
                    i(e.dma_start_transpose(
                        out=h2fm[:, kc * TOK:(kc + 1) * TOK],
                        in_=dr_h2.ap()[:, kc * 128:(kc + 1) * 128]), y_h2f, 16)
            S(s19)

            def s19b(e, en):
                for kc in range(1, KC, 2):
                    i(e.dma_start_transpose(
                        out=h2fm[:, kc * TOK:(kc + 1) * TOK],
                        in_=dr_h2.ap()[:, kc * 128:(kc + 1) * 128]), y_h2f, 16)
            S(s19b)
            y_h2f_all = y_h2f.v

            w3_last = [0, 0]
            evD = [[0] * 4, [0] * 4]
            g_stage = [0, 0]
            for m in range(HID // 128):
                p = m % 2

                def g5(e, en, m=m, p=p, gate=w3_last[p]):
                    w(e, en, st_, gate)
                    src = APx(w0, m * 2048, [[64 * 16 * 128, 128], [1, 2048]])
                    dst = APx(w0t[p], 0, [[KC * 128, 128], [1, 2048]])
                    i(e.dma_start(out=dst, in_=src), g_w3[p], 16)
                G(g5)
                tok_w3 = g_w3[p].v

                for g in range(NG):
                    def t17(e, en, m=m, p=p, g=g, tok_w3=tok_w3, gate=evD[p][g]):
                        w(e, en, g_w3[p], tok_w3)
                        w(e, en, y_h2f, y_h2f_all)
                        w(e, en, sa, gate)
                        for kc in range(KC):
                            i(e.matmul(pD[p][:, g * 512:g * 512 + 512],
                                       w0t[p][:, kc * 128:(kc + 1) * 128],
                                       h2fm[:, kc * TOK + g * 512:kc * TOK + g * 512 + 512],
                                       start=(kc == 0), stop=(kc == KC - 1)), st_)
                    T(t17)
                    w3_last[p] = st_.v
                    tok_d = st_.v

                    def a17(e, en, m=m, p=p, g=g, tok_d=tok_d, gate=g_stage[p]):
                        w(e, en, st_, tok_d)
                        w(e, en, y_g[p], gate)
                        i(e.activation(gst[p][:, g * 512:g * 512 + 512],
                                       pD[p][:, g * 512:g * 512 + 512],
                                       AF.Gelu, bias=b0_col[:, m:m + 1]), sa)
                    A(a17)
                    evD[p][g] = sa.v
                tok_g = sa.v

                def s20(e, en, m=m, p=p, tok_g=tok_g):
                    w(e, en, sa, tok_g)
                    i(e.dma_start(out=dr_g.ap()[m * 128:(m + 1) * 128, :],
                                  in_=gst[p][:, :]), y_g[p], 16)
                S(s20)
                g_stage[p] = y_g[p].v
            kb.barrier()

        with ExitStack() as pctx:
            pe = pctx.enter_context
            TH = TOK // 2
            NKH = HID // 128
            ghalf = pe(nc.sbuf_tensor("ghalf", [128, NKH * TH], F16))
            w1t = [pe(nc.sbuf_tensor(f"w1t{p}", [128, NKH * 128], F16)) for p in range(2)]
            est = [pe(nc.sbuf_tensor(f"est{p}", [128, TH], F16)) for p in range(2)]
            pE7 = [pe(nc.psum_tensor(f"pE7_{p}", [128, 1024], F32)) for p in range(2)]
            y_gh = kb.sem("y_gh")
            g_w4 = [kb.sem(f"g_w4_{p}") for p in range(2)]
            y_ef = [kb.sem(f"y_ef{p}") for p in range(2)]

            e2groups = [(s, min(512, TH - s)) for s in range(0, TH, 512)]
            w4_last = [0, 0]
            evE = [[0] * len(e2groups), [0] * len(e2groups)]
            e_stage = [0, 0]
            for gh in range(2):
                def s21(e, en, gh=gh, gate=max(w4_last)):
                    if gh > 0:
                        w(e, en, st_, gate)
                    for kc in range(NKH):
                        i(e.dma_start(
                            out=ghalf[:, kc * TH:(kc + 1) * TH],
                            in_=dr_g.ap()[kc * 128:(kc + 1) * 128,
                                          gh * TH:(gh + 1) * TH]), y_gh, 16)
                S(s21)
                y_gh_tok = y_gh.v

                for m in range(16):
                    p = m % 2

                    def g6(e, en, m=m, p=p, gate=w4_last[p]):
                        w(e, en, st_, gate)
                        src = APx(w1, m * 8192, [[16 * 64 * 128, 128], [1, 8192]])
                        dst = APx(w1t[p], 0, [[NKH * 128, 128], [1, 8192]])
                        i(e.dma_start(out=dst, in_=src), g_w4[p], 16)
                    G(g6)
                    tok_w4 = g_w4[p].v

                    for k2, (s0_, cn) in enumerate(e2groups):
                        def t18(e, en, m=m, p=p, k2=k2, s0_=s0_, cn=cn,
                                tok_w4=tok_w4, y_gh_tok=y_gh_tok, gate=evE[p][k2]):
                            w(e, en, g_w4[p], tok_w4)
                            w(e, en, y_gh, y_gh_tok)
                            w(e, en, sv, gate)
                            for kc in range(NKH):
                                i(e.matmul(pE7[p][:, k2 * 512:k2 * 512 + cn],
                                           w1t[p][:, kc * 128:(kc + 1) * 128],
                                           ghalf[:, kc * TH + s0_:kc * TH + s0_ + cn],
                                           start=(kc == 0), stop=(kc == NKH - 1)), st_)
                        T(t18)
                        w4_last[p] = st_.v
                        tok_e7 = st_.v

                        def v39(e, en, m=m, p=p, k2=k2, s0_=s0_, cn=cn,
                                tok_e7=tok_e7, gate=e_stage[p]):
                            w(e, en, st_, tok_e7)
                            w(e, en, y_ef[p], gate)
                            i(e.tensor_scalar(est[p][:, s0_:s0_ + cn],
                                              pE7[p][:, k2 * 512:k2 * 512 + cn],
                                              b1_col[:, m:m + 1], None, ALU.add), sv)
                        V(v39)
                        evE[p][k2] = sv.v
                    tok_es = sv.v

                    def s22(e, en, m=m, p=p, gh=gh, tok_es=tok_es):
                        w(e, en, sv, tok_es)
                        i(e.dma_start(out=dr_efm.ap()[m * 128:(m + 1) * 128,
                                                      gh * TH:(gh + 1) * TH],
                                      in_=est[p][:, :]), y_ef[p], 16)
                    S(s22)
                    e_stage[p] = y_ef[p].v
            kb.barrier()

        with ExitStack() as pctx:
            pe = pctx.enter_context
            ett = [pe(nc.sbuf_tensor(f"ett{p}", [128, DM], F16)) for p in range(2)]
            x1b = [pe(nc.sbuf_tensor(f"x1b{p}", [128, DM], F32)) for p in range(2)]
            outt = [pe(nc.sbuf_tensor(f"outt{p}", [128, DM], F32)) for p in range(2)]
            y_e2 = [kb.sem(f"y_e2_{p}") for p in range(2)]
            y_out = [kb.sem(f"y_out{p}") for p in range(2)]

            out_stage = [0, 0]
            for tt in range(TOK // 128):
                p = tt % 2

                def s23(e, en, tt=tt, p=p, gate=out_stage[p]):
                    w(e, en, y_out[p], gate)
                    i(e.dma_start_transpose(
                        out=ett[p][:, :],
                        in_=dr_efm.ap()[:, tt * 128:(tt + 1) * 128]), y_e2[p], 16)
                    i(e.dma_start(out=x1b[p][:, :],
                                  in_=dr_x1.ap()[tt * 128:(tt + 1) * 128, :]), y_e2[p], 16)
                S(s23)
                tok_e2r = y_e2[p].v

                def v40(e, en, p=p, tok=tok_e2r, gate=y_out[p].v):
                    w(e, en, y_e2[p], tok)
                    w(e, en, y_out[p], gate)
                    i(e.tensor_add(outt[p][:, :], x1b[p][:, :], ett[p][:, :]), sv)
                V(v40)
                tok_o = sv.v

                def s24(e, en, tt=tt, p=p, tok_o=tok_o):
                    w(e, en, sv, tok_o)
                    i(e.dma_start(out=out.ap()[tt * 128:(tt + 1) * 128, :],
                                  in_=outt[p][:, :]), y_out[p], 16)
                S(s24)
                out_stage[p] = y_out[p].v

            def s25(e, en):
                w(e, en, y_out[0], y_out[0].v)
                w(e, en, y_out[1], y_out[1].v)
            S(s25)

    return nc


_NC_CACHE = {}


def _get_nc(tok, ncores):
    key = (tok, ncores)
    if key not in _NC_CACHE:
        _NC_CACHE[key] = build_nc(tok, ncores)
    return _NC_CACHE[key]


HP = np.array([(rr % 16) * 4 + rr // 16 for rr in range(NH)])
FP = (HP[:, None] * HD + np.arange(HD)[None, :]).reshape(-1)


def _tile_w(wT, mw_last=None):
    K_, M_ = wT.shape
    nkc = K_ // 128
    tmp = wT.reshape(nkc, 128, M_)
    blocks = []
    t0 = 0
    while t0 < M_:
        mw = min(128, M_ - t0)
        blk = tmp[:, :, t0:t0 + mw].transpose(1, 0, 2).reshape(128, nkc * mw)
        blocks.append(blk)
        t0 += mw
    return np.ascontiguousarray(np.concatenate(blocks, axis=1).astype(np.float32))


def make_in_maps(u, weights, tok, ncores):
    ipw0 = weights["in_proj_w"].T.astype(np.float32)
    colperm = np.concatenate([FP, DI + FP,
                              np.arange(2 * DI, 2 * DI + 2 * DS),
                              2 * DI + 2 * DS + HP])
    ipw = _tile_w(np.ascontiguousarray(ipw0[:, colperm]))
    cpm = np.concatenate([FP, np.arange(DI, CD)])
    cwT = np.ascontiguousarray(weights["conv_w"].T.astype(np.float32)[:, cpm])
    cb = np.ascontiguousarray(weights["conv_b"].astype(np.float32)[cpm].reshape(34, 128))
    dtb = np.ascontiguousarray(weights["dt_bias"].astype(np.float32)[HP].reshape(NH, 1))
    alog = np.ascontiguousarray(weights["A_log"].astype(np.float32)[HP].reshape(NH, 1))
    dvec = np.ascontiguousarray(weights["D"].astype(np.float32)[HP].reshape(1, NH))
    nw = np.ascontiguousarray(weights["norm_weight"].astype(np.float32)[FP].reshape(1, DI))
    opw = _tile_w(np.ascontiguousarray(weights["out_proj_w"].T.astype(np.float32)[FP, :]))
    w0 = _tile_w(np.ascontiguousarray(weights["W0"].T.astype(np.float32)))
    b0 = np.ascontiguousarray(weights["bias0"].astype(np.float32).reshape(HID // 128, 128))
    w1 = _tile_w(np.ascontiguousarray(weights["W1"].T.astype(np.float32)))
    b1 = np.ascontiguousarray(weights["bias1"].astype(np.float32).reshape(DM // 128, 128))
    in_maps = []
    for c in range(ncores):
        seq, half = c // 2, c % 2
        if half == 0:
            u_core = np.concatenate(
                [np.zeros((HALO, DM), np.float32), u[seq, 0:tok]], axis=0)
        else:
            u_core = u[seq, tok - HALO:2 * tok]
        flag = np.full((128, 1), float(half), np.float32)
        in_maps.append(dict(u=np.ascontiguousarray(u_core.astype(np.float32)),
                            ipw=ipw, cwT=cwT, cb=cb, dtb=dtb, alog=alog,
                            dvec=dvec, nw=nw, opw=opw, w0=w0, b0=b0, w1=w1,
                            b1=b1, flag=flag))
    return in_maps


def kernel(L0_attn_u, L0_attn_in_proj_w, L0_attn_conv_w, L0_attn_conv_b,
           L0_attn_dt_bias, L0_attn_A_log, L0_attn_D, L0_attn_norm_weight,
           L0_attn_out_proj_w, L0_ffn_W0, L0_ffn_bias0, L0_ffn_W1,
           L0_ffn_bias1):
    u = np.ascontiguousarray(np.asarray(L0_attn_u, dtype=np.float32))
    B, L, _ = u.shape
    TOK = L // 2
    ncores = 8
    weights = dict(in_proj_w=np.asarray(L0_attn_in_proj_w),
                   conv_w=np.asarray(L0_attn_conv_w),
                   conv_b=np.asarray(L0_attn_conv_b),
                   dt_bias=np.asarray(L0_attn_dt_bias),
                   A_log=np.asarray(L0_attn_A_log),
                   D=np.asarray(L0_attn_D),
                   norm_weight=np.asarray(L0_attn_norm_weight),
                   out_proj_w=np.asarray(L0_attn_out_proj_w),
                   W0=np.asarray(L0_ffn_W0), bias0=np.asarray(L0_ffn_bias0),
                   W1=np.asarray(L0_ffn_W1), bias1=np.asarray(L0_ffn_bias1))
    in_maps = make_in_maps(u, weights, TOK, ncores)
    nc = _get_nc(TOK, ncores)
    res = run_bass_kernel_spmd(nc, in_maps, core_ids=list(range(ncores)))
    outp = np.empty((B, L, DM), np.float32)
    for c in range(ncores):
        seq, half = c // 2, c % 2
        outp[seq, half * TOK:(half + 1) * TOK] = res.results[c]["out"]
    return outp


# version 13 (final)
# speedup vs baseline: 1.1451x; 1.0826x over previous
import numpy as np
from contextlib import ExitStack

import concourse.bass as bass
import concourse.mybir as mybir
from concourse.bass_utils import run_bass_kernel_spmd

F32 = mybir.dt.float32
F16 = mybir.dt.float16
AF = mybir.ActivationFunctionType
ALU = mybir.AluOpType
AX = mybir.AxisListType

DM = 2048
DI = 4096
DS = 128
NH = 64
HD = 64
CD = DI + 2 * DS
DIP = 2 * DI + 2 * DS + NH
HID = 8192
EPS = 1e-5
HALO = 3
KC = DM // 128


def APx(t, offset, dims):
    return bass.AP(t, offset, [list(d) for d in dims])


class Sem:
    __slots__ = ("h", "v", "name")


class KB:
    def __init__(self, nc, ctx, tok):
        self.nc = nc
        self.ctx = ctx
        self.TOK = tok
        self.NCH = tok // 128
        self.NG = tok // 512
        self.PADT = ((tok + HALO + 127) // 128) * 128
        self.blk = ctx.enter_context(nc.Block())
        self._waited = {}
        self.sems = {}

    def sem(self, name):
        s = Sem()
        s.h = self.ctx.enter_context(self.nc.semaphore(name))
        s.v = 0
        s.name = name
        self.sems[name] = s
        return s

    def V(self, fn):
        self.blk.vector(lambda e: fn(e, "v"))

    def A(self, fn):
        self.blk.scalar(lambda e: fn(e, "a"))

    def T(self, fn):
        self.blk.tensor(lambda e: fn(e, "t"))

    def G(self, fn):
        self.blk.gpsimd(lambda e: fn(e, "g"))

    def S(self, fn):
        self.blk.sync(lambda e: fn(e, "s"))

    def w(self, e, en, sem, val=None):
        val = sem.v if val is None else val
        if val <= 0:
            return
        k = (en, sem.name)
        if self._waited.get(k, -1) >= val:
            return
        e.wait_ge(sem.h, val)
        self._waited[k] = val

    def i(self, inst, sem, n=1):
        inst.then_inc(sem.h, n)
        sem.v += n
        return sem.v

    def barrier(self):
        snap = [(s, s.v) for s in self.sems.values() if s.v > 0]

        def mk(emit, en):
            def f(e, _en):
                for s, v in snap:
                    self.w(e, en, s, v)
            emit(f)
        mk(self.V, "v")
        mk(self.A, "a")
        mk(self.T, "t")
        mk(self.G, "g")
        mk(self.S, "s")


def build_nc(tok, ncores):
    nc = bass.Bass("TRN2", target_bir_lowering=False, debug=False)
    TOK = tok
    NCH = tok // 128
    NG = tok // 512
    assert tok % 512 == 0

    u_in = nc.dram_tensor("u", [TOK + HALO, DM], F32, kind="ExternalInput")
    ipw = nc.dram_tensor("ipw", [128, 16 * DIP], F32, kind="ExternalInput")
    cwT = nc.dram_tensor("cwT", [4, CD], F32, kind="ExternalInput")
    cb_r = nc.dram_tensor("cb", [34, 128], F32, kind="ExternalInput")
    dtb = nc.dram_tensor("dtb", [NH, 1], F32, kind="ExternalInput")
    alog = nc.dram_tensor("alog", [NH, 1], F32, kind="ExternalInput")
    dvec = nc.dram_tensor("dvec", [1, NH], F32, kind="ExternalInput")
    nwr = nc.dram_tensor("nw", [1, DI], F32, kind="ExternalInput")
    opw = nc.dram_tensor("opw", [128, 16 * 32 * 128], F32, kind="ExternalInput")
    w0 = nc.dram_tensor("w0", [128, 64 * 16 * 128], F32, kind="ExternalInput")
    b0_r = nc.dram_tensor("b0", [HID // 128, 128], F32, kind="ExternalInput")
    w1 = nc.dram_tensor("w1", [128, 16 * 64 * 128], F32, kind="ExternalInput")
    b1_r = nc.dram_tensor("b1", [DM // 128, 128], F32, kind="ExternalInput")
    flag_in = nc.dram_tensor("flag", [128, 1], F32, kind="ExternalInput")
    out = nc.dram_tensor("out", [TOK, DM], F32, kind="ExternalOutput")

    PADT = ((TOK + HALO + 127) // 128) * 128
    dr_hln = nc.dram_tensor("dr_hln", [PADT, DM], F16, kind="Internal")
    dr_z = nc.dram_tensor("dr_z", [DI, TOK], F16, kind="Internal")
    dr_xbc = nc.dram_tensor("dr_xbc", [CD, TOK], F16, kind="Internal")
    dr_ypre = nc.dram_tensor("dr_ypre", [TOK, DI], F16, kind="Internal")
    dr_yn = nc.dram_tensor("dr_yn", [TOK, DI], F16, kind="Internal")
    dr_attn = nc.dram_tensor("dr_attn", [DM, TOK], F16, kind="Internal")
    dr_x1 = nc.dram_tensor("dr_x1", [TOK, DM], F32, kind="Internal")
    dr_h2 = nc.dram_tensor("dr_h2", [TOK, DM], F16, kind="Internal")
    dr_g = nc.dram_tensor("dr_g", [HID, TOK], F16, kind="Internal")
    dr_efm = nc.dram_tensor("dr_efm", [DM, TOK], F16, kind="Internal")
    dr_mp = nc.dram_tensor("dr_mp", [16, 128, NH * 128], F16, kind="Internal")
    dr_cci = nc.dram_tensor("dr_cci", [128, DI], F32, kind="Internal")
    dr_cco = nc.dram_tensor("dr_cco", [128, DI], F32, kind="Internal")

    with ExitStack() as ctx:
        kb = KB(nc, ctx, tok)
        E = ctx.enter_context
        V, A, T, G, S = kb.V, kb.A, kb.T, kb.G, kb.S
        w, i = kb.w, kb.i

        ones16 = E(nc.sbuf_tensor("ones16", [128, 128], F16))
        ones32 = E(nc.sbuf_tensor("ones32", [1, 128], F32))
        ones2 = E(nc.sbuf_tensor("ones2", [2, 128], F16))
        idn32 = E(nc.sbuf_tensor("idn32", [128, 128], F32))
        b0_col = E(nc.sbuf_tensor("b0_col", [128, HID // 128], F32))
        b1_col = E(nc.sbuf_tensor("b1_col", [128, DM // 128], F32))
        flag_c = E(nc.sbuf_tensor("flag_c", [128, 1], F32))
        eps_c = E(nc.sbuf_tensor("eps_c", [128, 1], F32))
        one_c = E(nc.sbuf_tensor("one_c", [128, 1], F32))

        sv = kb.sem("sv")
        sa = kb.sem("sa")
        st_ = kb.sem("st")
        sg = kb.sem("sg")
        y_set = kb.sem("y_set")
        g_cc = kb.sem("g_cc")

        scan_ctx = ExitStack()
        Es = scan_ctx.enter_context
        zcol = Es(nc.sbuf_tensor("zcol", [NH, 1], F32))
        dtb_c = Es(nc.sbuf_tensor("dtb_c", [NH, 1], F32))
        alog_c = Es(nc.sbuf_tensor("alog_c", [NH, 1], F32))
        nA_col = Es(nc.sbuf_tensor("nA_col", [NH, 1], F32))
        D_nb = Es(nc.sbuf_tensor("D_nb", [128, NH], F16))
        nw_bc = Es(nc.sbuf_tensor("nw_bc", [128, DI], F16))
        convw_c = Es(nc.sbuf_tensor("convw_c", [128, 34 * 4], F32))
        convb_c = Es(nc.sbuf_tensor("convb_c", [128, 34], F32))
        dt_t = Es(nc.sbuf_tensor("dt_t", [NH, TOK], F32))
        dA = Es(nc.sbuf_tensor("dA", [NH, TOK], F32))
        dA_cum = Es(nc.sbuf_tensor("dA_cum", [NH, TOK], F32))
        dA_last = Es(nc.sbuf_tensor("dA_last", [NH, NCH], F32))
        cd_t = Es(nc.sbuf_tensor("cd_t", [NH, NCH], F32))
        G_incl = Es(nc.sbuf_tensor("G_incl", [NH, NCH], F32))
        G_pre = Es(nc.sbuf_tensor("G_pre", [NH, NCH], F32))
        carry16 = Es(nc.sbuf_tensor("carry16", [128, DI], F16))
        sum32 = Es(nc.sbuf_tensor("sum32", [128, DI], F32))
        patch16 = Es(nc.sbuf_tensor("patch16", [128, DI], F16))

        with ExitStack() as pctx:
            pS = pctx.enter_context(nc.psum_tensor("ps_set", [128, 512], F32))
            pS2 = pctx.enter_context(nc.psum_tensor("ps_set2", [128, 64], F32))
            drow = pctx.enter_context(nc.sbuf_tensor("drow", [1, NH], F32))
            d16row = pctx.enter_context(nc.sbuf_tensor("d16row", [1, NH], F16))
            nwrow = pctx.enter_context(nc.sbuf_tensor("nwrow", [1, DI], F32))
            nwrow16 = pctx.enter_context(nc.sbuf_tensor("nwrow16", [1, DI], F16))
            cwT_s = pctx.enter_context(nc.sbuf_tensor("cwT_s", [4, CD], F32))
            cb_s = pctx.enter_context(nc.sbuf_tensor("cb_s", [34, 128], F32))

            def s0(e, en):
                for dst, src in [
                    (dtb_c[:, :], dtb.ap()), (alog_c[:, :], alog.ap()),
                    (drow[:, :], dvec.ap()), (nwrow[:, :], nwr.ap()),
                    (cwT_s[:, :], cwT.ap()), (cb_s[:, :], cb_r.ap()),
                    (flag_c[:, :], flag_in.ap()),
                ]:
                    i(e.dma_start(out=dst, in_=src), y_set, 16)
            S(s0)

            b0_s = pctx.enter_context(nc.sbuf_tensor("b0_s", [HID // 128, 128], F32))
            b1_s = pctx.enter_context(nc.sbuf_tensor("b1_s", [DM // 128, 128], F32))

            def s0b(e, en):
                i(e.dma_start(out=b0_s[:, :], in_=b0_r.ap()), y_set, 16)
                i(e.dma_start(out=b1_s[:, :], in_=b1_r.ap()), y_set, 16)
            S(s0b)
            y_set_all = y_set.v

            def v0(e, en):
                i(e.memset(ones16[:, :], 1.0), sv)
                i(e.memset(ones32[:, :], 1.0), sv)
                i(e.memset(ones2[:, :], 1.0), sv)
                i(e.memset(zcol[:, :], 0.0), sv)
                i(e.memset(eps_c[:, :], EPS), sv)
                i(e.memset(one_c[:, :], 1.0), sv)
                w(e, en, y_set, y_set_all)
                i(e.tensor_copy(d16row[:, :], drow[:, :]), sv)
                i(e.tensor_copy(nwrow16[:, :], nwrow[:, :]), sv)
            V(v0)
            tok_v0 = sv.v

            def g0(e, en):
                i(e.memset(idn32[:, :], 1.0), sg)
                w(e, en, sg, sg.v)
                i(e.affine_select(idn32[:, :], idn32[:, :], [[1, 128]],
                                  ALU.is_equal, 0.0, base=0,
                                  channel_multiplier=-1), sg)
            G(g0)
            tok_idn = sg.v

            def a0(e, en):
                w(e, en, y_set, y_set_all)
                i(e.activation(nA_col[:, :], alog_c[:, :], AF.Exp), sa)
            A(a0)
            tok_a0 = sa.v

            def v1(e, en):
                w(e, en, sa, tok_a0)
                i(e.tensor_scalar_mul(nA_col[:, :], nA_col[:, :], -1.0), sv)
            V(v1)

            def t0(e, en):
                w(e, en, sg, tok_idn)
                w(e, en, y_set, y_set_all)
                i(e.transpose(pS[:, 0:34], cb_s[:, :], idn32[0:34, 0:34]), st_)
                i(e.transpose(pS[:, 64:64 + HID // 128], b0_s[:, :], idn32[0:64, 0:64]), st_)
                i(e.transpose(pS[:, 192:192 + DM // 128], b1_s[:, :], idn32[0:16, 0:16]), st_)
            T(t0)
            tok_t0 = st_.v

            def v2(e, en):
                w(e, en, st_, tok_t0)
                i(e.tensor_copy(convb_c[:, :], pS[:, 0:34]), sv)
                i(e.tensor_copy(b0_col[:, :], pS[:, 64:64 + HID // 128]), sv)
                i(e.tensor_copy(b1_col[:, :], pS[:, 192:192 + DM // 128]), sv)
            V(v2)
            tok_v2 = sv.v

            def t1(e, en):
                w(e, en, sv, tok_v2)
                for mt in range(34):
                    i(e.transpose(pS[:, mt * 4:mt * 4 + 4],
                                  cwT_s[:, mt * 128:(mt + 1) * 128], idn32[0:4, 0:4]), st_)
            T(t1)
            tok_t1 = st_.v

            def v3(e, en):
                w(e, en, st_, tok_t1)
                i(e.tensor_copy(convw_c[:, :], pS[:, 0:136]), sv)
            V(v3)
            tok_v3 = sv.v

            def t2(e, en):
                w(e, en, sv, tok_v0)
                i(e.matmul(pS2[:, 0:NH], ones16[0:1, :], d16row[0:1, :],
                           start=True, stop=True), st_)
            T(t2)
            tok_t2 = st_.v

            def v4(e, en):
                w(e, en, st_, tok_t2)
                i(e.tensor_copy(D_nb[:, :], pS2[:, 0:NH]), sv)
            V(v4)

            for q in range(DI // 512):
                def tq(e, en, q=q, gate=sv.v):
                    w(e, en, sv, gate)
                    i(e.matmul(pS[:, 0:512], ones16[0:1, :],
                               nwrow16[0:1, q * 512:(q + 1) * 512],
                               start=True, stop=True), st_)
                T(tq)

                def vq(e, en, q=q, gate=st_.v):
                    w(e, en, st_, gate)
                    i(e.tensor_copy(nw_bc[:, q * 512:(q + 1) * 512], pS[:, 0:512]), sv)
                V(vq)
            kb.barrier()

        with ExitStack() as pctx:
            pe = pctx.enter_context
            u_t = [pe(nc.sbuf_tensor(f"u_t{p}", [128, DM], F32)) for p in range(2)]
            usq = [pe(nc.sbuf_tensor(f"usq{p}", [128, DM], F32)) for p in range(2)]
            hlt = [pe(nc.sbuf_tensor(f"hlt{p}", [128, DM], F16)) for p in range(2)]
            msum = [pe(nc.sbuf_tensor(f"msum{p}", [128, 1], F32)) for p in range(2)]
            mneg = [pe(nc.sbuf_tensor(f"mneg{p}", [128, 1], F32)) for p in range(2)]
            qsum = [pe(nc.sbuf_tensor(f"qsum{p}", [128, 1], F32)) for p in range(2)]
            q2 = [pe(nc.sbuf_tensor(f"q2{p}", [128, 1], F32)) for p in range(2)]
            nvar = [pe(nc.sbuf_tensor(f"nvar{p}", [128, 1], F32)) for p in range(2)]
            rstd = [pe(nc.sbuf_tensor(f"rstd{p}", [128, 1], F32)) for p in range(2)]
            mr = [pe(nc.sbuf_tensor(f"mr{p}", [128, 1], F32)) for p in range(2)]
            y_u = [kb.sem(f"y_u{p}") for p in range(2)]
            y_hl = [kb.sem(f"y_hl{p}") for p in range(2)]

            nrows = TOK + HALO
            tiles = [(r0, min(128, nrows - r0)) for r0 in range(0, nrows, 128)]
            npad = PADT - nrows
            if npad > 0:
                def vz(e, en):
                    i(e.memset(hlt[0][0:npad, :], 0.0), sv)
                V(vz)
                tok_z0 = sv.v

                def sz(e, en, tok_z0=tok_z0):
                    w(e, en, sv, tok_z0)
                    i(e.dma_start(out=dr_hln.ap()[nrows:PADT, :],
                                  in_=hlt[0][0:npad, :]), y_hl[0], 16)
                S(sz)
            last_apply = [y_hl[0].v * 0, 0]
            last_sq = [0, 0]
            for idx, (r0, n) in enumerate(tiles):
                p = idx % 2

                def s1(e, en, r0=r0, n=n, p=p, gate=last_apply[p]):
                    w(e, en, sg, gate)
                    i(e.dma_start(out=u_t[p][0:n, :],
                                  in_=u_in.ap()[r0:r0 + n, :]), y_u[p], 16)
                S(s1)
                tok_u = y_u[p].v

                def v6(e, en, n=n, p=p, tok_u=tok_u):
                    w(e, en, y_u[p], tok_u)
                    i(e.reduce_sum(msum[p][0:n, :], u_t[p][0:n, :], AX.X), sv)
                    w(e, en, sv, sv.v)
                    i(e.tensor_scalar_mul(mneg[p][0:n, :], msum[p][0:n, :], -1.0 / DM), sv)
                V(v6)
                tok_mn = sv.v

                def g1a(e, en, n=n, p=p, tok_u=tok_u, gate=last_sq[p]):
                    w(e, en, y_u[p], tok_u)
                    w(e, en, sv, gate)
                    i(e.tensor_tensor(usq[p][0:n, :], u_t[p][0:n, :],
                                      u_t[p][0:n, :], ALU.mult), sg)
                G(g1a)
                tok_q0 = sg.v

                def v6b(e, en, n=n, p=p, tok_q0=tok_q0):
                    w(e, en, sg, tok_q0)
                    i(e.reduce_sum(qsum[p][0:n, :], usq[p][0:n, :], AX.X), sv)
                V(v6b)
                tok_q = sv.v

                def v7(e, en, n=n, p=p, tok_q=tok_q, tok_mn=tok_mn):
                    w(e, en, sv, tok_q)
                    w(e, en, sv, tok_mn)
                    i(e.tensor_scalar_mul(q2[p][0:n, :], qsum[p][0:n, :], 1.0 / DM), sv)
                    w(e, en, sv, sv.v)
                    i(e.scalar_tensor_tensor(nvar[p][0:n, :], mneg[p][0:n, :],
                                             mneg[p][0:n, 0:1], q2[p][0:n, :],
                                             ALU.mult, ALU.subtract), sv)
                V(v7)
                tok_nv = sv.v
                last_sq[p] = tok_nv

                def a2(e, en, n=n, p=p, tok_nv=tok_nv):
                    w(e, en, sv, tok_nv)
                    i(e.activation(rstd[p][0:n, :], nvar[p][0:n, :], AF.Ln,
                                   bias=eps_c[0:n, 0:1], scale=-1.0), sa)
                    w(e, en, sa, sa.v)
                    i(e.activation(rstd[p][0:n, :], rstd[p][0:n, :], AF.Exp,
                                   scale=-0.5), sa)
                A(a2)
                tok_rs = sa.v

                def v8(e, en, n=n, p=p, tok_rs=tok_rs):
                    w(e, en, sa, tok_rs)
                    i(e.tensor_tensor(mr[p][0:n, :], mneg[p][0:n, :],
                                      rstd[p][0:n, :], ALU.mult), sv)
                V(v8)
                tok_mr = sv.v

                def g3a(e, en, n=n, p=p, tok_mr=tok_mr, gate=y_hl[p].v):
                    w(e, en, sv, tok_mr)
                    w(e, en, y_hl[p], gate)
                    i(e.tensor_scalar(hlt[p][0:n, :], u_t[p][0:n, :],
                                      mneg[p][0:n, 0:1], rstd[p][0:n, 0:1],
                                      ALU.add, ALU.mult), sg)
                G(g3a)
                last_apply[p] = sg.v
                tok_ap = sg.v

                def s2(e, en, r0=r0, n=n, p=p, tok_ap=tok_ap):
                    w(e, en, sg, tok_ap)
                    i(e.dma_start(out=dr_hln.ap()[r0:r0 + n, :],
                                  in_=hlt[p][0:n, :]), y_hl[p], 16)
                S(s2)
            kb.barrier()

        with ExitStack() as pctx:
            pe = pctx.enter_context
            hfm = pe(nc.sbuf_tensor("hfm", [128, KC * PADT], F16))
            w_in = [pe(nc.sbuf_tensor(f"w_in{p}", [128, KC * 128], F16)) for p in range(2)]
            strip = [pe(nc.sbuf_tensor(f"strip{p}", [128, HALO + TOK], F16)) for p in range(2)]
            conva = [pe(nc.sbuf_tensor(f"conva{p}", [128, TOK], F16)) for p in range(2)]
            convd = [pe(nc.sbuf_tensor(f"convd{p}", [128, TOK], F16)) for p in range(2)]
            xstg = [pe(nc.sbuf_tensor(f"xstg{p}", [128, TOK], F16)) for p in range(2)]
            zstg = [pe(nc.sbuf_tensor(f"zstg{p}", [128, TOK], F16)) for p in range(2)]
            psA = [pe(nc.psum_tensor(f"psA{p}", [128, 4 * 512], F32)) for p in range(2)]
            y_fm = kb.sem("y_fm")
            g_w = [kb.sem(f"g_w{p}") for p in range(2)]
            y_xbc = [kb.sem(f"y_xbc{p}") for p in range(2)]
            y_z = [kb.sem(f"y_z{p}") for p in range(2)]

            def s3(e, en):
                for kc in range(0, KC, 2):
                    i(e.dma_start_transpose(
                        out=hfm[:, kc * PADT:kc * PADT + PADT],
                        in_=dr_hln.ap()[:, kc * 128:(kc + 1) * 128]), y_fm, 16)
            S(s3)

            def s3b(e, en):
                for kc in range(1, KC, 2):
                    i(e.dma_start_transpose(
                        out=hfm[:, kc * PADT:kc * PADT + PADT],
                        in_=dr_hln.ap()[:, kc * 128:(kc + 1) * 128]), y_fm, 16)
            S(s3b)
            y_fm_all = y_fm.v

            m_order = [(DI + CD, 64, "dt")] + \
                      [(DI + 128 * j, 128, "xbc") for j in range(34)] + \
                      [(128 * j, 128, "z") for j in range(32)]
            wt_last = [0, 0]
            ev_tok = [[0] * 4, [0] * 4]
            xbc_stage = [0, 0]
            z_stage = [0, 0]
            conv_sa = [0, 0]
            strip_used = [0, 0]
            dt_evac_tok = 0

            for mi, (f0, mw, kind) in enumerate(m_order):
                p = mi % 2

                def g1(e, en, f0=f0, mw=mw, p=p, gate=wt_last[p]):
                    w(e, en, st_, gate)
                    t_idx = f0 // 128
                    src = APx(ipw, t_idx * 2048, [[16 * DIP, 128], [1, KC * mw]])
                    dst = APx(w_in[p], 0, [[KC * 128, 128], [128, KC], [1, mw]])
                    i(e.dma_start(out=dst, in_=src), g_w[p], 16)
                G(g1)
                tok_wl = g_w[p].v

                groups = [("main", HALO + g * 512, 512, g) for g in range(NG)] + \
                         ([("halo", 0, HALO, 0)] if kind == "xbc" else [])

                for gname, c0, cn, bank in groups:
                    def t4(e, en, mw=mw, p=p, c0=c0, cn=cn, bank=bank,
                           tok_wl=tok_wl, gate_ev=ev_tok[p][bank]):
                        w(e, en, g_w[p], tok_wl)
                        w(e, en, y_fm, y_fm_all)
                        w(e, en, sv, gate_ev)
                        for kc in range(KC):
                            i(e.matmul(psA[p][0:mw, bank * 512:bank * 512 + cn],
                                       w_in[p][:, kc * 128:kc * 128 + mw],
                                       hfm[:, kc * PADT + c0:kc * PADT + c0 + cn],
                                       start=(kc == 0), stop=(kc == KC - 1)), st_)
                    T(t4)
                    tok_mm = st_.v
                    wt_last[p] = tok_mm

                    def v9(e, en, mw=mw, p=p, c0=c0, cn=cn, bank=bank,
                           tok_mm=tok_mm, kind=kind,
                           gate_strip=strip_used[p], gate_z=z_stage[p]):
                        w(e, en, st_, tok_mm)
                        if kind == "xbc":
                            w(e, en, sv, gate_strip)
                            i(e.tensor_copy(strip[p][0:mw, c0:c0 + cn],
                                            psA[p][0:mw, bank * 512:bank * 512 + cn]), sv)
                        elif kind == "z":
                            w(e, en, y_z[p], gate_z)
                            i(e.tensor_copy(zstg[p][0:mw, c0 - HALO:c0 - HALO + cn],
                                            psA[p][0:mw, bank * 512:bank * 512 + cn]), sv)
                        else:
                            i(e.tensor_copy(dt_t[0:mw, c0 - HALO:c0 - HALO + cn],
                                            psA[p][0:mw, bank * 512:bank * 512 + cn]), sv)
                    V(v9)
                    ev_tok[p][bank] = sv.v
                tok_evac = sv.v

                if kind == "dt":
                    dt_evac_tok = tok_evac
                elif kind == "xbc":
                    mt = (f0 - DI) // 128

                    def v10(e, en, p=p, mt=mt, tok_evac=tok_evac, gate=conv_sa[p]):
                        w(e, en, sv, tok_evac)
                        w(e, en, sa, gate)
                        i(e.scalar_tensor_tensor(
                            conva[p][:, :], strip[p][:, 0:TOK],
                            convw_c[:, 4 * mt + 0:4 * mt + 1],
                            APx(convb_c, mt, [[34, 128], [0, TOK]]),
                            ALU.mult, ALU.add), sv)
                        w(e, en, sv, sv.v)
                        i(e.scalar_tensor_tensor(
                            convd[p][:, :], strip[p][:, 1:TOK + 1],
                            convw_c[:, 4 * mt + 1:4 * mt + 2],
                            conva[p][:, :], ALU.mult, ALU.add), sv)
                        w(e, en, sv, sv.v)
                        i(e.scalar_tensor_tensor(
                            conva[p][:, :], strip[p][:, 2:TOK + 2],
                            convw_c[:, 4 * mt + 2:4 * mt + 3],
                            convd[p][:, :], ALU.mult, ALU.add), sv)
                        w(e, en, sv, sv.v)
                        i(e.scalar_tensor_tensor(
                            convd[p][:, :], strip[p][:, 3:TOK + 3],
                            convw_c[:, 4 * mt + 3:4 * mt + 4],
                            conva[p][:, :], ALU.mult, ALU.add), sv)
                    V(v10)
                    tok_conv = sv.v
                    strip_used[p] = tok_conv

                    def a4(e, en, p=p, tok_conv=tok_conv, gate=xbc_stage[p]):
                        w(e, en, sv, tok_conv)
                        w(e, en, y_xbc[p], gate)
                        i(e.activation(xstg[p][:, :], convd[p][:, :], AF.Silu), sa)
                    A(a4)
                    conv_sa[p] = sa.v
                    tok_silu = sa.v

                    def s4(e, en, p=p, mt=mt, tok_silu=tok_silu):
                        w(e, en, sa, tok_silu)
                        i(e.dma_start(out=dr_xbc.ap()[mt * 128:(mt + 1) * 128, :],
                                      in_=xstg[p][:, :]), y_xbc[p], 16)
                    S(s4)
                    xbc_stage[p] = y_xbc[p].v
                else:
                    mt = f0 // 128

                    def s5(e, en, p=p, mt=mt, tok_evac=tok_evac):
                        w(e, en, sv, tok_evac)
                        i(e.dma_start(out=dr_z.ap()[mt * 128:(mt + 1) * 128, :],
                                      in_=zstg[p][:, :]), y_z[p], 16)
                    S(s5)
                    z_stage[p] = y_z[p].v

            def v11(e, en, gate=dt_evac_tok):
                w(e, en, sv, gate)
                i(e.tensor_scalar(dt_t[:, :], dt_t[:, :], dtb_c[:, 0:1], None,
                                  ALU.add), sv)
            V(v11)
            tok_abs = sv.v

            def a5(e, en, tok_abs=tok_abs):
                w(e, en, sv, tok_abs)
                i(e.activation(dA[:, :], dt_t[:, :], AF.Abs), sa)
                w(e, en, sa, sa.v)
                i(e.activation(dA_cum[:, :], dA[:, :], AF.Exp, scale=-1.0), sa)
                w(e, en, sa, sa.v)
                i(e.activation(dA_cum[:, :], dA_cum[:, :], AF.Ln, bias=one_c[0:NH, 0:1]), sa)
            A(a5)
            tok_ln1p = sa.v

            def v12(e, en, tok_ln1p=tok_ln1p):
                w(e, en, sa, tok_ln1p)
                i(e.scalar_tensor_tensor(dt_t[:, :], dt_t[:, :], 0.0,
                                         dA_cum[:, :], ALU.max, ALU.add), sv)
                w(e, en, sv, sv.v)
                i(e.tensor_scalar(dA[:, :], dt_t[:, :], nA_col[:, 0:1], None,
                                  ALU.mult), sv)
                w(e, en, sv, sv.v)
                for z in range(NCH):
                    i(e.tensor_tensor_scan(
                        dA_cum[:, z * 128:(z + 1) * 128],
                        dA[:, z * 128:(z + 1) * 128],
                        APx(zcol, 0, [[1, NH], [0, 128]]),
                        0.0, ALU.add, ALU.add), sv)
                w(e, en, sv, sv.v)
                i(e.tensor_copy(dA_last[:, :],
                                APx(dA_cum, 127, [[TOK, NH], [128, NCH]])), sv)
            V(v12)
            tok_dal = sv.v

            def a6(e, en, tok_dal=tok_dal):
                w(e, en, sv, tok_dal)
                i(e.activation(cd_t[:, :], dA_last[:, :], AF.Exp), sa)
            A(a6)
            tok_cd = sa.v

            def v13(e, en, tok_cd=tok_cd):
                w(e, en, sa, tok_cd)
                i(e.tensor_tensor_scan(G_incl[:, :], cd_t[:, :],
                                       APx(zcol, 0, [[1, NH], [0, NCH]]),
                                       1.0, ALU.mult, ALU.add), sv)
                i(e.memset(G_pre[:, 0:1], 1.0), sv)
                if NCH > 1:
                    w(e, en, sv, sv.v)
                    i(e.tensor_copy(G_pre[:, 1:NCH], G_incl[:, 0:NCH - 1]), sv)
            V(v13)
            kb.barrier()

        with ExitStack() as pctx:
            pe = pctx.enter_context
            x_tok = [pe(nc.sbuf_tensor(f"x_tok{p}", [128, DI], F16)) for p in range(2)]
            Bfm = [pe(nc.sbuf_tensor(f"Bfm{p}", [128, 128], F16)) for p in range(2)]
            Cfm = [pe(nc.sbuf_tensor(f"Cfm{p}", [128, 128], F16)) for p in range(2)]
            Btok = [pe(nc.sbuf_tensor(f"Btok{p}", [128, 128], F16)) for p in range(2)]
            lhsE = [pe(nc.sbuf_tensor(f"lhsE{p}", [128, 16 * 128], F16)) for p in range(2)]
            rhsE = [pe(nc.sbuf_tensor(f"rhsE{p}", [128, 16 * 128], F16)) for p in range(2)]
            rhs2 = pe(nc.sbuf_tensor("rhs2", [2, NH * 128], F16))
            Hp16 = [pe(nc.sbuf_tensor(f"Hp16{p}", [NH, 128], F16)) for p in range(2)]
            Lp16 = [pe(nc.sbuf_tensor(f"Lp16{p}", [NH, 128], F16)) for p in range(2)]
            nH16 = [pe(nc.sbuf_tensor(f"nH16{p}", [NH, 128], F16)) for p in range(2)]
            nL16 = [pe(nc.sbuf_tensor(f"nL16{p}", [NH, 128], F16)) for p in range(2)]
            Ltmp = pe(nc.sbuf_tensor("Ltmp", [NH, 128], F32))
            M_sb = pe(nc.sbuf_tensor("M_sb", [128, NH * 128], F16))
            Mp_sb = pe(nc.sbuf_tensor("Mp_sb", [128, NH * 128], F16))
            CBT16 = pe(nc.sbuf_tensor("CBT16", [128, 128], F16))
            dtT16 = pe(nc.sbuf_tensor("dtT16", [128, NH], F16))
            decT16 = pe(nc.sbuf_tensor("decT16", [128, NH], F16))
            dec_f = pe(nc.sbuf_tensor("dec_f", [NH, 128], F32))
            cdrow = pe(nc.sbuf_tensor("cdrow", [1, NH], F32))
            cdnb = pe(nc.sbuf_tensor("cdnb", [128, NH], F32))
            Xdt = pe(nc.sbuf_tensor("Xdt", [128, DI], F16))
            Xdtd = pe(nc.sbuf_tensor("Xdtd", [128, DI], F16))
            ct1 = pe(nc.sbuf_tensor("ct1", [128, 512], F16))
            Yst = [pe(nc.sbuf_tensor(f"Yst{p}", [128, DI], F16)) for p in range(2)]
            pE = pe(nc.psum_tensor("pE", [128, 2048], F32))
            pEp = pe(nc.psum_tensor("pEp", [128, 1024], F32))
            pY = pe(nc.psum_tensor("pY", [128, 1024], F32))

            y_xt = [kb.sem(f"y_xt{p}") for p in range(2)]
            y_bc = [kb.sem(f"y_bc{p}") for p in range(2)]
            y_be = [kb.sem(f"y_be{p}") for p in range(2)]
            y_yp = [kb.sem(f"y_yp{p}") for p in range(2)]
            y_mp = kb.sem("y_mp")

            def v14(e, en):
                i(e.memset(carry16[:, :], 0.0), sv)
                for p in range(2):
                    i(e.memset(lhsE[p][:, :], 1.0), sv)
                    i(e.memset(rhsE[p][:, :], 1.0), sv)
            V(v14)
            kb.barrier()

            def hmap(rr):
                j, r = rr // 16, rr % 16
                return (r // 4) * 16 + j * 4 + (r % 4)

            hl_tok = [0, 0]
            be_done = [0, 0]
            mp_tok = [0] * 16
            m_tok = [0] * 16
            mp_done_v = [0, 0]
            e_evac = [0, 0, 0, 0]
            ysv = [0, 0]
            st_carry = [0] * 8
            y_stage_tok = [0, 0]
            rhs2_used = [0]
            xtok_used = [0, 0]
            cfm_used = [0, 0]
            btok_used = [0, 0]
            bfm_used = [0, 0]
            xdt_used_st = 0
            m_used_st = 0

            for z in range(NCH):
                p = z % 2

                def s6(e, en, z=z, p=p, g1_=xtok_used[p], g2_=cfm_used[p],
                       g3_=btok_used[p], g4_=bfm_used[p],
                       gb0=be_done[0], gb1=be_done[1]):
                    w(e, en, sv, g1_)
                    w(e, en, sv, g2_)
                    w(e, en, st_, g3_)
                    w(e, en, st_, g4_)
                    w(e, en, y_be[0], gb0)
                    w(e, en, y_be[1], gb1)
                    i(e.dma_start_transpose(
                        out=x_tok[p][:, :],
                        in_=dr_xbc.ap()[0:DI, z * 128:(z + 1) * 128]), y_xt[p], 16)
                    i(e.dma_start(out=Bfm[p][:, :],
                                  in_=dr_xbc.ap()[DI:DI + 128, z * 128:(z + 1) * 128]), y_bc[p], 16)
                    i(e.dma_start(out=Cfm[p][:, :],
                                  in_=dr_xbc.ap()[DI + 128:DI + 256, z * 128:(z + 1) * 128]), y_bc[p], 16)
                    i(e.dma_start_transpose(
                        out=Btok[p][:, :],
                        in_=dr_xbc.ap()[DI:DI + 128, z * 128:(z + 1) * 128]), y_bc[p], 16)
                S(s6)
                tok_xt = y_xt[p].v
                tok_bc = y_bc[p].v

                def v15(e, en, z=z, p=p, gate=be_done[p]):
                    w(e, en, y_be[p], gate)
                    i(e.tensor_copy(Hp16[p][:, :], dA_cum[:, z * 128:(z + 1) * 128]), sv)
                    w(e, en, sv, sv.v)
                    i(e.tensor_tensor(Ltmp[:, :], dA_cum[:, z * 128:(z + 1) * 128],
                                      Hp16[p][:, :], ALU.subtract), sv)
                    w(e, en, sv, sv.v)
                    i(e.tensor_copy(Lp16[p][:, :], Ltmp[:, :]), sv)
                    i(e.tensor_scalar_mul(nH16[p][:, :], Hp16[p][:, :], -1.0), sv)
                    w(e, en, sv, sv.v)
                    i(e.tensor_scalar_mul(nL16[p][:, :], Lp16[p][:, :], -1.0), sv)
                V(v15)
                hl_tok[p] = sv.v
                tok_hl = sv.v

                def s7(e, en, z=z, p=p, tok_hl=tok_hl, tok_xt=tok_xt,
                       tok_bc=tok_bc):
                    w(e, en, sv, tok_hl)
                    w(e, en, y_xt[p], tok_xt)
                    w(e, en, y_bc[p], tok_bc)
                    for j in range(4):
                        for (tile, row, src) in [(lhsE[p], 32 * j + 2, nH16[p]),
                                                 (lhsE[p], 32 * j + 3, nL16[p]),
                                                 (rhsE[p], 32 * j + 0, Hp16[p]),
                                                 (rhsE[p], 32 * j + 1, Lp16[p])]:
                            i(e.dma_start(out=tile[row:row + 1, 0:2048],
                                          in_=APx(src, 16 * j * 128, [[128, 16], [1, 128]])),
                              y_be[p], 16)
                    w(e, en, st_, rhs2_used[0])
                    i(e.dma_start(out=rhs2[0:1, :],
                                  in_=APx(Hp16[p], 0, [[128, NH], [1, 128]])), y_be[p], 16)
                    i(e.dma_start(out=rhs2[1:2, :],
                                  in_=APx(Lp16[p], 0, [[128, NH], [1, 128]])), y_be[p], 16)
                S(s7)
                be_done[p] = y_be[p].v
                tok_be = y_be[p].v

                def t5(e, en, p=p, tok_bc=tok_bc, g1_=ysv[0]):
                    w(e, en, y_bc[p], tok_bc)
                    w(e, en, sv, g1_)
                    i(e.matmul(pY[:, 0:128], Bfm[p][:, :], Cfm[p][:, :],
                               start=True, stop=True), st_)
                T(t5)
                bfm_used[p] = st_.v
                tok_cbt_mm = st_.v

                def v16(e, en, tok=tok_cbt_mm):
                    w(e, en, st_, tok)
                    i(e.tensor_copy(CBT16[:, :], pY[:, 0:128]), sv)
                V(v16)
                tok_cbt_cp = sv.v

                def g2(e, en, tok=tok_cbt_cp):
                    w(e, en, sv, tok)
                    i(e.affine_select(CBT16[:, :], CBT16[:, :], [[1, 128]],
                                      ALU.is_ge, 0.0, base=0,
                                      channel_multiplier=-1), sg)
                G(g2)
                tok_cbt = sg.v

                def v17(e, en, z=z):
                    i(e.tensor_scalar(dec_f[:, :], dA_cum[:, z * 128:(z + 1) * 128],
                                      dA_last[:, z:z + 1], None, ALU.subtract), sv)
                V(v17)
                tok_decf = sv.v

                def a7(e, en, tok=tok_decf):
                    w(e, en, sv, tok)
                    i(e.activation(dec_f[:, :], dec_f[:, :], AF.Exp, scale=-1.0), sa)
                A(a7)
                tok_dece = sa.v

                def t6(e, en, z=z, tok_dece=tok_dece, tok_cp=tok_cbt_cp):
                    w(e, en, sa, tok_dece)
                    w(e, en, sv, tok_cp)
                    i(e.transpose(pY[0:128, 128:192], dt_t[:, z * 128:(z + 1) * 128],
                                  idn32[0:64, 0:64]), st_)
                    i(e.transpose(pY[0:128, 192:256], dec_f[:, :], idn32[0:64, 0:64]), st_)
                    i(e.transpose(pY[0:1, 256:320], cd_t[:, z:z + 1], idn32[0:64, 0:64]), st_)
                T(t6)
                tok_tr = st_.v

                def v18(e, en, tok=tok_tr):
                    w(e, en, st_, tok)
                    i(e.tensor_copy(dtT16[:, :], pY[:, 128:192]), sv)
                    i(e.tensor_copy(decT16[:, :], pY[:, 192:256]), sv)
                    i(e.tensor_copy(cdrow[0:1, :], pY[0:1, 256:320]), sv)
                V(v18)
                tok_trc = sv.v

                def t7(e, en, tok=tok_trc):
                    w(e, en, sv, tok)
                    i(e.matmul(pY[:, 320:384], ones32[0:1, :], cdrow[0:1, :],
                               start=True, stop=True), st_)
                T(t7)
                tok_cdmm = st_.v

                def v19(e, en, tok=tok_cdmm):
                    w(e, en, st_, tok)
                    i(e.tensor_copy(cdnb[:, :], pY[:, 320:384]), sv)
                V(v19)
                tok_cdnb = sv.v

                def v20(e, en, p=p, tok_xt=tok_xt, tok_trc=tok_trc,
                        gate=xdt_used_st):
                    w(e, en, y_xt[p], tok_xt)
                    w(e, en, sv, tok_trc)
                    w(e, en, st_, gate)
                    i(e.tensor_tensor(Xdt[:, :], x_tok[p][:, :],
                                      APx(dtT16, 0, [[NH, 128], [1, NH], [0, HD]]),
                                      ALU.mult), sv)
                    w(e, en, sv, sv.v)
                    i(e.tensor_tensor(Xdtd[:, :], Xdt[:, :],
                                      APx(decT16, 0, [[NH, 128], [1, NH], [0, HD]]),
                                      ALU.mult), sv)
                V(v20)
                tok_xdt = sv.v
                xtok_used[p] = tok_xdt

                for q16 in range(16):
                    half = q16 % 2

                    def t8(e, en, p=p, q16=q16, half=half, tok_be=tok_be,
                           gate=mp_done_v[half]):
                        w(e, en, y_be[p], tok_be)
                        w(e, en, sv, gate)
                        i(e.matmul(pEp[:, half * 512:half * 512 + 512],
                                   ones2[0:2, :],
                                   rhs2[0:2, q16 * 512:(q16 + 1) * 512],
                                   start=True, stop=True), st_)
                    T(t8)
                    tok_ep = st_.v

                    def v21a(e, en, q16=q16, half=half, tok_ep=tok_ep,
                             gate=m_used_st, gate_mp=y_mp.v):
                        w(e, en, st_, tok_ep)
                        w(e, en, st_, gate)
                        w(e, en, y_mp, gate_mp)
                        i(e.tensor_scalar_min(Mp_sb[:, q16 * 512:(q16 + 1) * 512],
                                              pEp[:, half * 512:half * 512 + 512],
                                              0.0), sv)
                    V(v21a)
                    mp_done_v[half] = sv.v
                rhs2_used[0] = st_.v
                tok_mpraw = sv.v

                def a8(e, en, tok=tok_mpraw):
                    w(e, en, sv, tok)
                    i(e.activation(Mp_sb[:, 0:4096], Mp_sb[:, 0:4096], AF.Exp), sa)
                    i(e.activation(Mp_sb[:, 4096:8192], Mp_sb[:, 4096:8192],
                                   AF.Exp), sa)
                A(a8)
                tok_mpe = sa.v

                def v21(e, en, p=p, tok=tok_mpe):
                    w(e, en, sa, tok)
                    i(e.tensor_tensor(
                        Mp_sb[:, :],
                        Mp_sb[:, :],
                        APx(Cfm[p], 0, [[128, 128], [0, NH], [1, 128]]),
                        ALU.mult), sv)
                V(v21)
                for q16 in range(16):
                    mp_tok[q16] = sv.v
                cfm_used[p] = sv.v
                tok_mp_done = sv.v

                def s_mp(e, en, z=z, tok=tok_mp_done):
                    w(e, en, sv, tok)
                    i(e.dma_start(out=dr_mp.ap()[z, :, :], in_=Mp_sb[:, :]),
                      y_mp, 16)
                S(s_mp)

                for cyc in range(4):
                    def t9(e, en, p=p, cyc=cyc, tok_be=tok_be,
                           gates=tuple(e_evac)):
                        w(e, en, y_be[p], tok_be)
                        for b in range(4):
                            w(e, en, sv, gates[b])
                        for rr in range(4):
                            r = cyc * 4 + rr
                            for j in range(4):
                                i(e.matmul(
                                    pE[:, j * 512 + rr * 128:j * 512 + rr * 128 + 128],
                                    lhsE[p][32 * j:32 * j + 4, r * 128:(r + 1) * 128],
                                    rhsE[p][32 * j:32 * j + 4, r * 128:(r + 1) * 128],
                                    start=True, stop=True,
                                    tile_position=(32 * j, 0)), st_)
                    T(t9)
                    tok_ecyc = st_.v

                    for j in range(4):
                        def v9b(e, en, j=j, cyc=cyc, tok_ecyc=tok_ecyc,
                                gate=m_used_st):
                            w(e, en, st_, tok_ecyc)
                            w(e, en, st_, gate)
                            i(e.tensor_scalar_min(
                                M_sb[:, cyc * 2048 + j * 512:cyc * 2048 + (j + 1) * 512],
                                pE[:, j * 512:(j + 1) * 512], 0.0), sv)
                        V(v9b)
                        e_evac[j] = sv.v

                tok_mraw = sv.v

                def a9(e, en, tok=tok_mraw):
                    w(e, en, sv, tok)
                    i(e.activation(M_sb[:, 0:4096], M_sb[:, 0:4096], AF.Exp), sa)
                    i(e.activation(M_sb[:, 4096:8192], M_sb[:, 4096:8192],
                                   AF.Exp), sa)
                A(a9)
                tok_me = sa.v

                def v22(e, en, tok=tok_me, tok_cbt=tok_cbt):
                    w(e, en, sa, tok)
                    w(e, en, sg, tok_cbt)
                    i(e.tensor_tensor(
                        M_sb[:, :], M_sb[:, :],
                        APx(CBT16, 0, [[128, 128], [0, NH], [1, 128]]),
                        ALU.mult), sv)
                V(v22)
                for idx16 in range(16):
                    m_tok[idx16] = sv.v

                for q in range(8):
                    hp = q % 2

                    def t10(e, en, q=q, hp=hp, tok_xdt=tok_xdt,
                            tok_cdnb=tok_cdnb, gate_c=st_carry[q],
                            gate_y=ysv[hp]):
                        for h in range(q * 8, q * 8 + 8):
                            w(e, en, sv, m_tok[hmap(h) // 4])
                            w(e, en, sv, mp_tok[h // 4])
                        w(e, en, sv, tok_xdt)
                        w(e, en, sv, tok_cdnb)
                        w(e, en, sv, gate_c)
                        w(e, en, sv, gate_y)
                        for h in range(q * 8, q * 8 + 8):
                            c0 = hp * 512 + (h % 8) * 64
                            i(e.matmul(pY[:, c0:c0 + 64],
                                       M_sb[:, hmap(h) * 128:hmap(h) * 128 + 128],
                                       Xdt[:, h * 64:h * 64 + 64],
                                       start=True, stop=False), st_)
                            i(e.matmul(pY[:, c0:c0 + 64],
                                       Mp_sb[:, h * 128:h * 128 + 128],
                                       carry16[:, h * 64:h * 64 + 64],
                                       start=False, stop=True), st_)
                    T(t10)
                    tok_y = st_.v

                    def t11(e, en, p=p, q=q, hp=hp, tok_bc=tok_bc,
                            gate=mp_done_v[hp]):
                        w(e, en, y_bc[p], tok_bc)
                        w(e, en, sv, gate)
                        for h in range(q * 8, q * 8 + 8):
                            i(e.matmul(pEp[:, hp * 512 + (h % 8) * 64:
                                           hp * 512 + (h % 8) * 64 + 64],
                                       Btok[p][:, :], Xdtd[:, h * 64:h * 64 + 64],
                                       start=True, stop=True), st_)
                    T(t11)
                    btok_used[p] = st_.v
                    tok_stm = st_.v

                    def v23(e, en, q=q, hp=hp, tok_stm=tok_stm, tok_cdnb=tok_cdnb,
                            selfgate=sv.v):
                        w(e, en, st_, tok_stm)
                        w(e, en, sv, max(tok_cdnb, selfgate))
                        i(e.tensor_tensor(ct1[:, :], carry16[:, q * 512:(q + 1) * 512],
                                          APx(cdnb, q * 8, [[NH, 128], [1, 8], [0, HD]]),
                                          ALU.mult), sv)
                        w(e, en, sv, sv.v)
                        i(e.tensor_add(carry16[:, q * 512:(q + 1) * 512], ct1[:, :],
                                       pEp[:, hp * 512:hp * 512 + 512]), sv)
                    V(v23)
                    st_carry[q] = sv.v
                    mp_done_v[hp] = sv.v

                    def v24(e, en, z=z, q=q, hp=hp, tok_y=tok_y,
                            gate=y_stage_tok[z % 2]):
                        w(e, en, st_, tok_y)
                        w(e, en, y_yp[z % 2], gate)
                        i(e.tensor_copy(Yst[z % 2][:, q * 512:(q + 1) * 512],
                                        pY[:, hp * 512:hp * 512 + 512]), sv)
                    V(v24)
                    ysv[hp] = sv.v
                tok_yall = sv.v
                xdt_used_st = st_.v
                m_used_st = st_.v

                def s8(e, en, z=z, tok_yall=tok_yall):
                    w(e, en, sv, tok_yall)
                    i(e.dma_start(out=dr_ypre.ap()[z * 128:(z + 1) * 128, :],
                                  in_=Yst[z % 2][:, :]), y_yp[z % 2], 16)
                S(s8)
                y_stage_tok[z % 2] = y_yp[z % 2].v

            y_cc1 = kb.sem("y_cc1")
            y_cc2 = kb.sem("y_cc2")

            def g3a(e, en, gate=tuple(st_carry)):
                for q in range(8):
                    w(e, en, sv, gate[q])
                i(e.dma_start(out=dr_cci.ap(), in_=carry16[:, :]), y_cc1, 16)
            G(g3a)

            def g3(e, en):
                w(e, en, y_cc1, y_cc1.v)
                groups = [[c, c + 1] for c in range(0, ncores, 2)]
                i(e.collective_compute("AllReduce", ALU.add,
                                       replica_groups=groups,
                                       ins=[dr_cci.ap()], outs=[dr_cco.ap()]),
                  g_cc, 1)
            G(g3)

            def s10(e, en):
                w(e, en, g_cc, g_cc.v)
                i(e.dma_start(out=sum32[:, :], in_=dr_cco.ap()), y_cc2, 16)
            S(s10)

            def v26(e, en):
                w(e, en, y_cc2, y_cc2.v)
                i(e.tensor_tensor(sum32[:, :], sum32[:, :], carry16[:, :],
                                  ALU.subtract), sv)
                w(e, en, sv, sv.v)
                i(e.tensor_scalar(patch16[:, :], sum32[:, :], flag_c[:, 0:1],
                                  None, ALU.mult), sv)
            V(v26)
            kb.barrier()

        with ExitStack() as pctx:
            pe = pctx.enter_context
            Ypt = [pe(nc.sbuf_tensor(f"Ypt{p}", [128, DI], F16)) for p in range(2)]
            z_tk = [pe(nc.sbuf_tensor(f"z_tk{p}", [128, DI], F16)) for p in range(2)]
            x_tk = [pe(nc.sbuf_tensor(f"x_tk{p}", [128, DI], F16)) for p in range(2)]
            Mp2t = [pe(nc.sbuf_tensor(f"Mp2t{p}", [128, NH * 128], F16)) for p in range(2)]
            stz = pe(nc.sbuf_tensor("stz", [128, DI], F16))
            siluz = pe(nc.sbuf_tensor("siluz", [128, DI], F16))
            ygA = pe(nc.sbuf_tensor("ygA", [128, DI], F16))
            ygB = pe(nc.sbuf_tensor("ygB", [128, DI], F16))
            ssum = pe(nc.sbuf_tensor("ssum", [128, 1], F32))
            rstd2 = pe(nc.sbuf_tensor("rstd2", [128, 1], F32))
            ynst = [pe(nc.sbuf_tensor(f"ynst{p}", [128, DI], F16)) for p in range(2)]
            pYb = pe(nc.psum_tensor("pYb", [128, 1024], F32))
            pM2 = pe(nc.psum_tensor("pM2", [128, 512], F32))

            y_p4 = [kb.sem(f"y_p4_{p}") for p in range(2)]
            y_be2 = [kb.sem(f"y_be2_{p}") for p in range(2)]
            y_yn = [kb.sem(f"y_yn{p}") for p in range(2)]

            growall = pe(nc.sbuf_tensor("growall", [NCH, NH], F32))
            growflat = pe(nc.sbuf_tensor("growflat", [1, NCH * NH], F32))
            gnb_all = pe(nc.sbuf_tensor("gnb_all", [128, NCH * NH], F32))
            y_gf = kb.sem("y_gf")

            def tg0(e, en):
                i(e.transpose(pM2[0:NCH, 0:NH], G_pre[:, :], idn32[0:NH, 0:NH]), st_)
            T(tg0)
            tok_tg0 = st_.v

            def vg0(e, en, tok=tok_tg0):
                w(e, en, st_, tok)
                i(e.tensor_copy(growall[:, :], pM2[0:NCH, 0:NH]), sv)
            V(vg0)
            tok_vg0 = sv.v

            def sg0b(e, en, tok=tok_vg0):
                w(e, en, sv, tok)
                i(e.dma_start(out=growflat[0:1, :],
                              in_=APx(growall, 0, [[NH, NCH], [1, NH]])), y_gf, 16)
            S(sg0b)
            tok_gf = y_gf.v

            for qq in range((NCH * NH + 511) // 512):
                c0 = qq * 512
                cn = min(512, NCH * NH - c0)

                def tg1(e, en, c0=c0, cn=cn, tok=tok_gf, gate=sv.v):
                    w(e, en, y_gf, tok)
                    w(e, en, sv, gate)
                    i(e.matmul(pM2[:, 0:cn], ones32[0:1, :],
                               growflat[0:1, c0:c0 + cn],
                               start=True, stop=True), st_)
                T(tg1)
                tok_tg1 = st_.v

                def vg1(e, en, c0=c0, cn=cn, tok=tok_tg1):
                    w(e, en, st_, tok)
                    i(e.tensor_copy(gnb_all[:, c0:c0 + cn], pM2[:, 0:cn]), sv)
                V(vg1)
            tok_gnball = sv.v

            mp2_rd_gate = [0, 0]
            stz_used_st = 0
            ygq_tok = [0, 0]
            yn_stage_tok = [0, 0]
            p4_v_reads = [0, 0]
            p4_a_reads = [0, 0]
            prev_ss = 0
            prev_yg = 0
            prev_vv = 0
            prev_yn = 0

            for z in range(NCH):
                p = z % 2

                def s11(e, en, z=z, p=p, g1_=p4_v_reads[p], g2_=p4_a_reads[p],
                        gfl=y_gf.v):
                    w(e, en, sv, g1_)
                    w(e, en, sa, g2_)
                    w(e, en, y_gf, gfl)
                    i(e.dma_start(out=Ypt[p][:, :],
                                  in_=dr_ypre.ap()[z * 128:(z + 1) * 128, :]), y_p4[p], 16)
                    i(e.dma_start_transpose(
                        out=z_tk[p][:, :],
                        in_=dr_z.ap()[0:DI, z * 128:(z + 1) * 128]), y_p4[p], 16)
                    i(e.dma_start_transpose(
                        out=x_tk[p][:, :],
                        in_=dr_xbc.ap()[0:DI, z * 128:(z + 1) * 128]), y_p4[p], 16)
                S(s11)
                tok_rd = y_p4[p].v

                def s12(e, en, z=z, p=p, gate=mp2_rd_gate[p]):
                    w(e, en, st_, gate)
                    i(e.dma_start(out=Mp2t[p][:, :], in_=dr_mp.ap()[z, :, :]),
                      y_be2[p], 16)
                S(s12)
                tok_r2b = y_be2[p].v

                def a11(e, en, p=p, tok_rd=tok_rd, gate=prev_yg):
                    w(e, en, y_p4[p], tok_rd)
                    w(e, en, sv, gate)
                    i(e.activation(siluz[:, :], z_tk[p][:, :], AF.Silu), sa)
                A(a11)
                p4_a_reads[p] = sa.v
                tok_slz = sa.v

                def v29(e, en, z=z, gate=stz_used_st, tok=tok_gnball):
                    w(e, en, sv, tok)
                    w(e, en, st_, gate)
                    i(e.tensor_tensor(stz[:, :], patch16[:, :],
                                      APx(gnb_all, z * NH, [[NCH * NH, 128], [1, NH], [0, HD]]),
                                      ALU.mult), sv)
                V(v29)
                tok_stz = sv.v
                for q in range(8):
                    hp = q % 2

                    def t15(e, en, p=p, q=q, hp=hp, tok_stz=tok_stz,
                            gate=ygq_tok[hp], tok_r2b=tok_r2b):
                        w(e, en, y_be2[p], tok_r2b)
                        w(e, en, sv, tok_stz)
                        w(e, en, sv, gate)
                        for h in range(q * 8, q * 8 + 8):
                            c0 = hp * 512 + (h % 8) * 64
                            i(e.matmul(pYb[:, c0:c0 + 64],
                                       Mp2t[p][:, h * 128:h * 128 + 128],
                                       stz[:, h * 64:h * 64 + 64],
                                       start=True, stop=True), st_)
                    T(t15)
                    tok_pt = st_.v
                    mp2_rd_gate[p] = st_.v

                    def v31(e, en, p=p, q=q, hp=hp, tok_pt=tok_pt, tok_rd=tok_rd,
                            gate=prev_ss):
                        w(e, en, st_, tok_pt)
                        w(e, en, y_p4[p], tok_rd)
                        w(e, en, sa, gate)
                        i(e.tensor_add(ygA[:, q * 512:(q + 1) * 512],
                                       Ypt[p][:, q * 512:(q + 1) * 512],
                                       pYb[:, hp * 512:hp * 512 + 512]), sv)
                    V(v31)
                    ygq_tok[hp] = sv.v
                tok_yga = sv.v
                stz_used_st = st_.v

                def v32(e, en, p=p, tok_rd=tok_rd, tok_slz=tok_slz,
                        tok_yga=tok_yga, gate=prev_ss):
                    w(e, en, y_p4[p], tok_rd)
                    w(e, en, sa, gate)
                    i(e.tensor_tensor(ygB[:, :], x_tk[p][:, :],
                                      APx(D_nb, 0, [[NH, 128], [1, NH], [0, HD]]),
                                      ALU.mult), sv)
                    w(e, en, sv, sv.v)
                    w(e, en, sv, tok_yga)
                    i(e.tensor_add(ygB[:, :], ygA[:, :], ygB[:, :]), sv)
                    w(e, en, sv, sv.v)
                    w(e, en, sa, tok_slz)
                    i(e.tensor_tensor(ygA[:, :], ygB[:, :], siluz[:, :],
                                      ALU.mult), sv)
                V(v32)
                tok_yg = sv.v
                prev_yg = tok_yg
                p4_v_reads[p] = tok_yg

                def a12(e, en, tok=tok_yg, gate=prev_vv):
                    w(e, en, sv, tok)
                    w(e, en, sv, gate)
                    i(e.activation(ygB[:, :], ygA[:, :], AF.Square,
                                   accum_out=ssum[:, :]), sa)
                A(a12)
                tok_ss = sa.v
                prev_ss = tok_ss

                def v33(e, en, tok=tok_ss, gate=prev_yn):
                    w(e, en, sa, tok)
                    w(e, en, sv, gate)
                    i(e.tensor_scalar_mul(rstd2[:, :], ssum[:, :], -1.0 / DI), sv)
                V(v33)
                tok_vv = sv.v
                prev_vv = tok_vv

                def a13(e, en, tok=tok_vv):
                    w(e, en, sv, tok)
                    i(e.activation(rstd2[:, :], rstd2[:, :], AF.Ln, bias=eps_c[:, 0:1],
                                   scale=-1.0), sa)
                    w(e, en, sa, sa.v)
                    i(e.activation(rstd2[:, :], rstd2[:, :], AF.Exp, scale=-0.5), sa)
                A(a13)
                tok_rstd = sa.v

                def v34(e, en, z=z, tok=tok_rstd, gate=yn_stage_tok[z % 2]):
                    w(e, en, sa, tok)
                    w(e, en, y_yn[z % 2], gate)
                    i(e.scalar_tensor_tensor(ynst[z % 2][:, :], ygA[:, :],
                                             rstd2[:, 0:1], nw_bc[:, :],
                                             ALU.mult, ALU.mult), sv)
                V(v34)
                tok_yn = sv.v
                prev_yn = tok_yn

                def s13(e, en, z=z, tok=tok_yn):
                    w(e, en, sv, tok)
                    i(e.dma_start(out=dr_yn.ap()[z * 128:(z + 1) * 128, :],
                                  in_=ynst[z % 2][:, :]), y_yn[z % 2], 16)
                S(s13)
                yn_stage_tok[z % 2] = y_yn[z % 2].v
            kb.barrier()

        scan_ctx.close()

        with ExitStack() as pctx:
            pe = pctx.enter_context
            ynfm = pe(nc.sbuf_tensor("ynfm", [128, 32 * TOK], F16))
            wop = [pe(nc.sbuf_tensor(f"wop{p}", [128, 32 * 128], F16)) for p in range(2)]
            atst = [pe(nc.sbuf_tensor(f"atst{p}", [128, TOK], F16)) for p in range(2)]
            pC = [pe(nc.psum_tensor(f"pC{p}", [128, 4 * 512], F32)) for p in range(2)]
            y_ynf = kb.sem("y_ynf")
            g_w2 = [kb.sem(f"g_w2_{p}") for p in range(2)]
            y_at = [kb.sem(f"y_at{p}") for p in range(2)]

            def s14(e, en):
                for kc in range(0, 32, 2):
                    i(e.dma_start_transpose(
                        out=ynfm[:, kc * TOK:(kc + 1) * TOK],
                        in_=dr_yn.ap()[:, kc * 128:(kc + 1) * 128]), y_ynf, 16)
            S(s14)

            def s14b(e, en):
                for kc in range(1, 32, 2):
                    i(e.dma_start_transpose(
                        out=ynfm[:, kc * TOK:(kc + 1) * TOK],
                        in_=dr_yn.ap()[:, kc * 128:(kc + 1) * 128]), y_ynf, 16)
            S(s14b)
            y_ynf_all = y_ynf.v

            wt2_last = [0, 0]
            evC = [[0] * 4, [0] * 4]
            at_stage = [0, 0]
            for m in range(16):
                p = m % 2

                def g4(e, en, m=m, p=p, gate=wt2_last[p]):
                    w(e, en, st_, gate)
                    src = APx(opw, m * 4096, [[16 * 32 * 128, 128], [1, 4096]])
                    dst = APx(wop[p], 0, [[32 * 128, 128], [1, 4096]])
                    i(e.dma_start(out=dst, in_=src), g_w2[p], 16)
                G(g4)
                tok_w2 = g_w2[p].v

                for g in range(NG):
                    def t16(e, en, m=m, p=p, g=g, tok_w2=tok_w2, gate=evC[p][g]):
                        w(e, en, g_w2[p], tok_w2)
                        w(e, en, y_ynf, y_ynf_all)
                        w(e, en, sv, gate)
                        for kc in range(32):
                            i(e.matmul(pC[p][:, g * 512:g * 512 + 512],
                                       wop[p][:, kc * 128:(kc + 1) * 128],
                                       ynfm[:, kc * TOK + g * 512:kc * TOK + g * 512 + 512],
                                       start=(kc == 0), stop=(kc == 31)), st_)
                    T(t16)
                    wt2_last[p] = st_.v
                    tok_c = st_.v

                    def v35(e, en, p=p, g=g, tok_c=tok_c, gate=at_stage[p]):
                        w(e, en, st_, tok_c)
                        w(e, en, y_at[p], gate)
                        i(e.tensor_copy(atst[p][:, g * 512:g * 512 + 512],
                                        pC[p][:, g * 512:g * 512 + 512]), sv)
                    V(v35)
                    evC[p][g] = sv.v
                tok_at = sv.v

                def s15(e, en, m=m, p=p, tok_at=tok_at):
                    w(e, en, sv, tok_at)
                    i(e.dma_start(out=dr_attn.ap()[m * 128:(m + 1) * 128, :],
                                  in_=atst[p][:, :]), y_at[p], 16)
                S(s15)
                at_stage[p] = y_at[p].v
            kb.barrier()

        with ExitStack() as pctx:
            pe = pctx.enter_context
            u2 = [pe(nc.sbuf_tensor(f"u2_{p}", [128, DM], F32)) for p in range(2)]
            att = [pe(nc.sbuf_tensor(f"att{p}", [128, DM], F16)) for p in range(2)]
            x1t = [pe(nc.sbuf_tensor(f"x1t{p}", [128, DM], F32)) for p in range(2)]
            xsq = [pe(nc.sbuf_tensor(f"xsq{p}", [128, DM], F32)) for p in range(2)]
            h2t = [pe(nc.sbuf_tensor(f"h2t{p}", [128, DM], F16)) for p in range(2)]
            ms2 = [pe(nc.sbuf_tensor(f"ms2_{p}", [128, 1], F32)) for p in range(2)]
            mn2 = [pe(nc.sbuf_tensor(f"mn2_{p}", [128, 1], F32)) for p in range(2)]
            qs2 = [pe(nc.sbuf_tensor(f"qs2_{p}", [128, 1], F32)) for p in range(2)]
            qq2 = [pe(nc.sbuf_tensor(f"qq2_{p}", [128, 1], F32)) for p in range(2)]
            nv2 = [pe(nc.sbuf_tensor(f"nv2_{p}", [128, 1], F32)) for p in range(2)]
            rs2 = [pe(nc.sbuf_tensor(f"rs2_{p}", [128, 1], F32)) for p in range(2)]
            mr2 = [pe(nc.sbuf_tensor(f"mr2_{p}", [128, 1], F32)) for p in range(2)]
            y_u2 = [kb.sem(f"y_u2_{p}") for p in range(2)]
            y_x1 = [kb.sem(f"y_x1_{p}") for p in range(2)]
            y_h2 = [kb.sem(f"y_h2_{p}") for p in range(2)]

            last_h2 = [0, 0]
            last_sq2 = [0, 0]
            for tt in range(TOK // 128):
                p = tt % 2

                def s16(e, en, tt=tt, p=p, gate=last_h2[p]):
                    w(e, en, sg, gate)
                    i(e.dma_start(out=u2[p][:, :],
                                  in_=u_in.ap()[HALO + tt * 128:HALO + (tt + 1) * 128, :]), y_u2[p], 16)
                    i(e.dma_start_transpose(
                        out=att[p][:, :],
                        in_=dr_attn.ap()[:, tt * 128:(tt + 1) * 128]), y_u2[p], 16)
                S(s16)
                tok_r = y_u2[p].v

                def v36(e, en, p=p, tok_r=tok_r, gate=y_x1[p].v):
                    w(e, en, y_u2[p], tok_r)
                    w(e, en, y_x1[p], gate)
                    i(e.tensor_add(x1t[p][:, :], u2[p][:, :], att[p][:, :]), sv)
                    w(e, en, sv, sv.v)
                    i(e.reduce_sum(ms2[p][:, :], x1t[p][:, :], AX.X), sv)
                    w(e, en, sv, sv.v)
                    i(e.tensor_scalar_mul(mn2[p][:, :], ms2[p][:, :], -1.0 / DM), sv)
                V(v36)
                tok_x1 = sv.v

                def s17(e, en, tt=tt, p=p, tok=tok_x1):
                    w(e, en, sv, tok)
                    i(e.dma_start(out=dr_x1.ap()[tt * 128:(tt + 1) * 128, :],
                                  in_=x1t[p][:, :]), y_x1[p], 16)
                S(s17)

                def g14(e, en, p=p, tok=tok_x1, gate=last_sq2[p]):
                    w(e, en, sv, tok)
                    w(e, en, sv, gate)
                    i(e.tensor_tensor(xsq[p][:, :], x1t[p][:, :],
                                      x1t[p][:, :], ALU.mult), sg)
                G(g14)
                tok_q2a = sg.v

                def v37a(e, en, p=p, tok=tok_q2a):
                    w(e, en, sg, tok)
                    i(e.reduce_sum(qs2[p][:, :], xsq[p][:, :], AX.X), sv)
                V(v37a)
                tok_q2 = sv.v

                def v37(e, en, p=p, tok=tok_q2):
                    w(e, en, sv, tok)
                    i(e.tensor_scalar_mul(qq2[p][:, :], qs2[p][:, :], 1.0 / DM), sv)
                    w(e, en, sv, sv.v)
                    i(e.scalar_tensor_tensor(nv2[p][:, :], mn2[p][:, :],
                                             mn2[p][:, 0:1], qq2[p][:, :],
                                             ALU.mult, ALU.subtract), sv)
                V(v37)
                tok_nv2 = sv.v
                last_sq2[p] = tok_nv2

                def a15(e, en, p=p, tok=tok_nv2):
                    w(e, en, sv, tok)
                    i(e.activation(rs2[p][:, :], nv2[p][:, :], AF.Ln, bias=eps_c[:, 0:1],
                                   scale=-1.0), sa)
                    w(e, en, sa, sa.v)
                    i(e.activation(rs2[p][:, :], rs2[p][:, :], AF.Exp, scale=-0.5), sa)
                A(a15)
                tok_rs2 = sa.v

                def v38(e, en, p=p, tok=tok_rs2):
                    w(e, en, sa, tok)
                    i(e.tensor_tensor(mr2[p][:, :], mn2[p][:, :], rs2[p][:, :],
                                      ALU.mult), sv)
                V(v38)
                tok_mr2 = sv.v

                def g16(e, en, p=p, tok=tok_mr2, gate=y_h2[p].v):
                    w(e, en, sv, tok)
                    w(e, en, y_h2[p], gate)
                    i(e.tensor_scalar(h2t[p][:, :], x1t[p][:, :],
                                      mn2[p][:, 0:1], rs2[p][:, 0:1],
                                      ALU.add, ALU.mult), sg)
                G(g16)
                last_h2[p] = sg.v
                tok_h2 = sg.v

                def s18(e, en, tt=tt, p=p, tok=tok_h2):
                    w(e, en, sg, tok)
                    i(e.dma_start(out=dr_h2.ap()[tt * 128:(tt + 1) * 128, :],
                                  in_=h2t[p][:, :]), y_h2[p], 16)
                S(s18)
            kb.barrier()

        with ExitStack() as pctx:
            pe = pctx.enter_context
            h2fm = pe(nc.sbuf_tensor("h2fm", [128, KC * TOK], F16))
            w0t = [pe(nc.sbuf_tensor(f"w0t{p}", [128, KC * 128], F16)) for p in range(2)]
            gst = [pe(nc.sbuf_tensor(f"gst{p}", [128, TOK], F16)) for p in range(2)]
            pD = [pe(nc.psum_tensor(f"pD{p}", [128, 4 * 512], F32)) for p in range(2)]
            y_h2f = kb.sem("y_h2f")
            g_w3 = [kb.sem(f"g_w3_{p}") for p in range(2)]
            y_g = [kb.sem(f"y_g{p}") for p in range(2)]

            def s19(e, en):
                for kc in range(0, KC, 2):
                    i(e.dma_start_transpose(
                        out=h2fm[:, kc * TOK:(kc + 1) * TOK],
                        in_=dr_h2.ap()[:, kc * 128:(kc + 1) * 128]), y_h2f, 16)
            S(s19)

            def s19b(e, en):
                for kc in range(1, KC, 2):
                    i(e.dma_start_transpose(
                        out=h2fm[:, kc * TOK:(kc + 1) * TOK],
                        in_=dr_h2.ap()[:, kc * 128:(kc + 1) * 128]), y_h2f, 16)
            S(s19b)
            y_h2f_all = y_h2f.v

            w3_last = [0, 0]
            evD = [[0] * 4, [0] * 4]
            g_stage = [0, 0]
            for m in range(HID // 128):
                p = m % 2

                def g5(e, en, m=m, p=p, gate=w3_last[p]):
                    w(e, en, st_, gate)
                    src = APx(w0, m * 2048, [[64 * 16 * 128, 128], [1, 2048]])
                    dst = APx(w0t[p], 0, [[KC * 128, 128], [1, 2048]])
                    i(e.dma_start(out=dst, in_=src), g_w3[p], 16)
                G(g5)
                tok_w3 = g_w3[p].v

                for g in range(NG):
                    def t17(e, en, m=m, p=p, g=g, tok_w3=tok_w3, gate=evD[p][g]):
                        w(e, en, g_w3[p], tok_w3)
                        w(e, en, y_h2f, y_h2f_all)
                        w(e, en, sa, gate)
                        for kc in range(KC):
                            i(e.matmul(pD[p][:, g * 512:g * 512 + 512],
                                       w0t[p][:, kc * 128:(kc + 1) * 128],
                                       h2fm[:, kc * TOK + g * 512:kc * TOK + g * 512 + 512],
                                       start=(kc == 0), stop=(kc == KC - 1)), st_)
                    T(t17)
                    w3_last[p] = st_.v
                    tok_d = st_.v

                    def a17(e, en, m=m, p=p, g=g, tok_d=tok_d, gate=g_stage[p]):
                        w(e, en, st_, tok_d)
                        w(e, en, y_g[p], gate)
                        i(e.activation(gst[p][:, g * 512:g * 512 + 512],
                                       pD[p][:, g * 512:g * 512 + 512],
                                       AF.Gelu, bias=b0_col[:, m:m + 1]), sa)
                    A(a17)
                    evD[p][g] = sa.v
                tok_g = sa.v

                def s20(e, en, m=m, p=p, tok_g=tok_g):
                    w(e, en, sa, tok_g)
                    i(e.dma_start(out=dr_g.ap()[m * 128:(m + 1) * 128, :],
                                  in_=gst[p][:, :]), y_g[p], 16)
                S(s20)
                g_stage[p] = y_g[p].v
            kb.barrier()

        with ExitStack() as pctx:
            pe = pctx.enter_context
            TH = TOK // 2
            NKH = HID // 128
            ghalf = pe(nc.sbuf_tensor("ghalf", [128, NKH * TH], F16))
            w1t = [pe(nc.sbuf_tensor(f"w1t{p}", [128, NKH * 128], F16)) for p in range(2)]
            est = [pe(nc.sbuf_tensor(f"est{p}", [128, TH], F16)) for p in range(2)]
            pE7 = [pe(nc.psum_tensor(f"pE7_{p}", [128, 1024], F32)) for p in range(2)]
            y_gh = kb.sem("y_gh")
            g_w4 = [kb.sem(f"g_w4_{p}") for p in range(2)]
            y_ef = [kb.sem(f"y_ef{p}") for p in range(2)]

            e2groups = [(s, min(512, TH - s)) for s in range(0, TH, 512)]
            w4_last = [0, 0]
            evE = [[0] * len(e2groups), [0] * len(e2groups)]
            e_stage = [0, 0]
            for gh in range(2):
                def s21(e, en, gh=gh, gate=max(w4_last)):
                    if gh > 0:
                        w(e, en, st_, gate)
                    for kc in range(NKH):
                        i(e.dma_start(
                            out=ghalf[:, kc * TH:(kc + 1) * TH],
                            in_=dr_g.ap()[kc * 128:(kc + 1) * 128,
                                          gh * TH:(gh + 1) * TH]), y_gh, 16)
                S(s21)
                y_gh_tok = y_gh.v

                for m in range(16):
                    p = m % 2

                    def g6(e, en, m=m, p=p, gate=w4_last[p]):
                        w(e, en, st_, gate)
                        src = APx(w1, m * 8192, [[16 * 64 * 128, 128], [1, 8192]])
                        dst = APx(w1t[p], 0, [[NKH * 128, 128], [1, 8192]])
                        i(e.dma_start(out=dst, in_=src), g_w4[p], 16)
                    G(g6)
                    tok_w4 = g_w4[p].v

                    for k2, (s0_, cn) in enumerate(e2groups):
                        def t18(e, en, m=m, p=p, k2=k2, s0_=s0_, cn=cn,
                                tok_w4=tok_w4, y_gh_tok=y_gh_tok, gate=evE[p][k2]):
                            w(e, en, g_w4[p], tok_w4)
                            w(e, en, y_gh, y_gh_tok)
                            w(e, en, sv, gate)
                            for kc in range(NKH):
                                i(e.matmul(pE7[p][:, k2 * 512:k2 * 512 + cn],
                                           w1t[p][:, kc * 128:(kc + 1) * 128],
                                           ghalf[:, kc * TH + s0_:kc * TH + s0_ + cn],
                                           start=(kc == 0), stop=(kc == NKH - 1)), st_)
                        T(t18)
                        w4_last[p] = st_.v
                        tok_e7 = st_.v

                        def v39(e, en, m=m, p=p, k2=k2, s0_=s0_, cn=cn,
                                tok_e7=tok_e7, gate=e_stage[p]):
                            w(e, en, st_, tok_e7)
                            w(e, en, y_ef[p], gate)
                            i(e.tensor_scalar(est[p][:, s0_:s0_ + cn],
                                              pE7[p][:, k2 * 512:k2 * 512 + cn],
                                              b1_col[:, m:m + 1], None, ALU.add), sv)
                        V(v39)
                        evE[p][k2] = sv.v
                    tok_es = sv.v

                    def s22(e, en, m=m, p=p, gh=gh, tok_es=tok_es):
                        w(e, en, sv, tok_es)
                        i(e.dma_start(out=dr_efm.ap()[m * 128:(m + 1) * 128,
                                                      gh * TH:(gh + 1) * TH],
                                      in_=est[p][:, :]), y_ef[p], 16)
                    S(s22)
                    e_stage[p] = y_ef[p].v
            kb.barrier()

        with ExitStack() as pctx:
            pe = pctx.enter_context
            ett = [pe(nc.sbuf_tensor(f"ett{p}", [128, DM], F16)) for p in range(2)]
            x1b = [pe(nc.sbuf_tensor(f"x1b{p}", [128, DM], F32)) for p in range(2)]
            outt = [pe(nc.sbuf_tensor(f"outt{p}", [128, DM], F32)) for p in range(2)]
            y_e2 = [kb.sem(f"y_e2_{p}") for p in range(2)]
            y_out = [kb.sem(f"y_out{p}") for p in range(2)]

            out_stage = [0, 0]
            for tt in range(TOK // 128):
                p = tt % 2

                def s23(e, en, tt=tt, p=p, gate=out_stage[p]):
                    w(e, en, y_out[p], gate)
                    i(e.dma_start_transpose(
                        out=ett[p][:, :],
                        in_=dr_efm.ap()[:, tt * 128:(tt + 1) * 128]), y_e2[p], 16)
                    i(e.dma_start(out=x1b[p][:, :],
                                  in_=dr_x1.ap()[tt * 128:(tt + 1) * 128, :]), y_e2[p], 16)
                S(s23)
                tok_e2r = y_e2[p].v

                def v40(e, en, p=p, tok=tok_e2r, gate=y_out[p].v):
                    w(e, en, y_e2[p], tok)
                    w(e, en, y_out[p], gate)
                    i(e.tensor_add(outt[p][:, :], x1b[p][:, :], ett[p][:, :]), sv)
                V(v40)
                tok_o = sv.v

                def s24(e, en, tt=tt, p=p, tok_o=tok_o):
                    w(e, en, sv, tok_o)
                    i(e.dma_start(out=out.ap()[tt * 128:(tt + 1) * 128, :],
                                  in_=outt[p][:, :]), y_out[p], 16)
                S(s24)
                out_stage[p] = y_out[p].v

            def s25(e, en):
                w(e, en, y_out[0], y_out[0].v)
                w(e, en, y_out[1], y_out[1].v)
            S(s25)

    return nc


_NC_CACHE = {}


def _get_nc(tok, ncores):
    key = (tok, ncores)
    if key not in _NC_CACHE:
        _NC_CACHE[key] = build_nc(tok, ncores)
    return _NC_CACHE[key]


HP = np.array([(rr % 16) * 4 + rr // 16 for rr in range(NH)])
FP = (HP[:, None] * HD + np.arange(HD)[None, :]).reshape(-1)


def _tile_w(wT, mw_last=None):
    K_, M_ = wT.shape
    nkc = K_ // 128
    tmp = wT.reshape(nkc, 128, M_)
    blocks = []
    t0 = 0
    while t0 < M_:
        mw = min(128, M_ - t0)
        blk = tmp[:, :, t0:t0 + mw].transpose(1, 0, 2).reshape(128, nkc * mw)
        blocks.append(blk)
        t0 += mw
    return np.ascontiguousarray(np.concatenate(blocks, axis=1).astype(np.float32))


def make_in_maps(u, weights, tok, ncores):
    ipw0 = weights["in_proj_w"].T.astype(np.float32)
    colperm = np.concatenate([FP, DI + FP,
                              np.arange(2 * DI, 2 * DI + 2 * DS),
                              2 * DI + 2 * DS + HP])
    ipw = _tile_w(np.ascontiguousarray(ipw0[:, colperm]))
    cpm = np.concatenate([FP, np.arange(DI, CD)])
    cwT = np.ascontiguousarray(weights["conv_w"].T.astype(np.float32)[:, cpm])
    cb = np.ascontiguousarray(weights["conv_b"].astype(np.float32)[cpm].reshape(34, 128))
    dtb = np.ascontiguousarray(weights["dt_bias"].astype(np.float32)[HP].reshape(NH, 1))
    alog = np.ascontiguousarray(weights["A_log"].astype(np.float32)[HP].reshape(NH, 1))
    dvec = np.ascontiguousarray(weights["D"].astype(np.float32)[HP].reshape(1, NH))
    nw = np.ascontiguousarray(weights["norm_weight"].astype(np.float32)[FP].reshape(1, DI))
    opw = _tile_w(np.ascontiguousarray(weights["out_proj_w"].T.astype(np.float32)[FP, :]))
    w0 = _tile_w(np.ascontiguousarray(weights["W0"].T.astype(np.float32)))
    b0 = np.ascontiguousarray(weights["bias0"].astype(np.float32).reshape(HID // 128, 128))
    w1 = _tile_w(np.ascontiguousarray(weights["W1"].T.astype(np.float32)))
    b1 = np.ascontiguousarray(weights["bias1"].astype(np.float32).reshape(DM // 128, 128))
    in_maps = []
    for c in range(ncores):
        seq, half = c // 2, c % 2
        if half == 0:
            u_core = np.concatenate(
                [np.zeros((HALO, DM), np.float32), u[seq, 0:tok]], axis=0)
        else:
            u_core = u[seq, tok - HALO:2 * tok]
        flag = np.full((128, 1), float(half), np.float32)
        in_maps.append(dict(u=np.ascontiguousarray(u_core.astype(np.float32)),
                            ipw=ipw, cwT=cwT, cb=cb, dtb=dtb, alog=alog,
                            dvec=dvec, nw=nw, opw=opw, w0=w0, b0=b0, w1=w1,
                            b1=b1, flag=flag))
    return in_maps


def kernel(L0_attn_u, L0_attn_in_proj_w, L0_attn_conv_w, L0_attn_conv_b,
           L0_attn_dt_bias, L0_attn_A_log, L0_attn_D, L0_attn_norm_weight,
           L0_attn_out_proj_w, L0_ffn_W0, L0_ffn_bias0, L0_ffn_W1,
           L0_ffn_bias1):
    u = np.ascontiguousarray(np.asarray(L0_attn_u, dtype=np.float32))
    B, L, _ = u.shape
    TOK = L // 2
    ncores = 8
    weights = dict(in_proj_w=np.asarray(L0_attn_in_proj_w),
                   conv_w=np.asarray(L0_attn_conv_w),
                   conv_b=np.asarray(L0_attn_conv_b),
                   dt_bias=np.asarray(L0_attn_dt_bias),
                   A_log=np.asarray(L0_attn_A_log),
                   D=np.asarray(L0_attn_D),
                   norm_weight=np.asarray(L0_attn_norm_weight),
                   out_proj_w=np.asarray(L0_attn_out_proj_w),
                   W0=np.asarray(L0_ffn_W0), bias0=np.asarray(L0_ffn_bias0),
                   W1=np.asarray(L0_ffn_W1), bias1=np.asarray(L0_ffn_bias1))
    in_maps = make_in_maps(u, weights, TOK, ncores)
    nc = _get_nc(TOK, ncores)
    res = run_bass_kernel_spmd(nc, in_maps, core_ids=list(range(ncores)))
    outp = np.empty((B, L, DM), np.float32)
    for c in range(ncores):
        seq, half = c // 2, c % 2
        outp[seq, half * TOK:(half + 1) * TOK] = res.results[c]["out"]
    return outp


# version 14
# speedup vs baseline: 1.1491x; 1.0035x over previous
import numpy as np
from contextlib import ExitStack

import concourse.bass as bass
import concourse.mybir as mybir
from concourse.bass_utils import run_bass_kernel_spmd

F32 = mybir.dt.float32
F16 = mybir.dt.float16
AF = mybir.ActivationFunctionType
ALU = mybir.AluOpType
AX = mybir.AxisListType

DM = 2048
DI = 4096
DS = 128
NH = 64
HD = 64
CD = DI + 2 * DS
DIP = 2 * DI + 2 * DS + NH
HID = 8192
EPS = 1e-5
HALO = 3
KC = DM // 128


def APx(t, offset, dims):
    return bass.AP(t, offset, [list(d) for d in dims])


class Sem:
    __slots__ = ("h", "v", "name")


class KB:
    def __init__(self, nc, ctx, tok):
        self.nc = nc
        self.ctx = ctx
        self.TOK = tok
        self.NCH = tok // 128
        self.NG = tok // 512
        self.PADT = ((tok + HALO + 127) // 128) * 128
        self.blk = ctx.enter_context(nc.Block())
        self._waited = {}
        self.sems = {}

    def sem(self, name):
        s = Sem()
        s.h = self.ctx.enter_context(self.nc.semaphore(name))
        s.v = 0
        s.name = name
        self.sems[name] = s
        return s

    def V(self, fn):
        self.blk.vector(lambda e: fn(e, "v"))

    def A(self, fn):
        self.blk.scalar(lambda e: fn(e, "a"))

    def T(self, fn):
        self.blk.tensor(lambda e: fn(e, "t"))

    def G(self, fn):
        self.blk.gpsimd(lambda e: fn(e, "g"))

    def S(self, fn):
        self.blk.sync(lambda e: fn(e, "s"))

    def w(self, e, en, sem, val=None):
        val = sem.v if val is None else val
        if val <= 0:
            return
        k = (en, sem.name)
        if self._waited.get(k, -1) >= val:
            return
        e.wait_ge(sem.h, val)
        self._waited[k] = val

    def i(self, inst, sem, n=1):
        inst.then_inc(sem.h, n)
        sem.v += n
        return sem.v

    def barrier(self):
        snap = [(s, s.v) for s in self.sems.values() if s.v > 0]

        def mk(emit, en):
            def f(e, _en):
                for s, v in snap:
                    self.w(e, en, s, v)
            emit(f)
        mk(self.V, "v")
        mk(self.A, "a")
        mk(self.T, "t")
        mk(self.G, "g")
        mk(self.S, "s")


def build_nc(tok, ncores):
    nc = bass.Bass("TRN2", target_bir_lowering=False, debug=False)
    TOK = tok
    NCH = tok // 128
    NG = tok // 512
    assert tok % 512 == 0

    u_in = nc.dram_tensor("u", [TOK + HALO, DM], F32, kind="ExternalInput")
    ipw = nc.dram_tensor("ipw", [128, 16 * DIP], F32, kind="ExternalInput")
    cwT = nc.dram_tensor("cwT", [4, CD], F32, kind="ExternalInput")
    cb_r = nc.dram_tensor("cb", [34, 128], F32, kind="ExternalInput")
    dtb = nc.dram_tensor("dtb", [NH, 1], F32, kind="ExternalInput")
    alog = nc.dram_tensor("alog", [NH, 1], F32, kind="ExternalInput")
    dvec = nc.dram_tensor("dvec", [1, NH], F32, kind="ExternalInput")
    nwr = nc.dram_tensor("nw", [1, DI], F32, kind="ExternalInput")
    opw = nc.dram_tensor("opw", [128, 16 * 32 * 128], F32, kind="ExternalInput")
    w0 = nc.dram_tensor("w0", [128, 64 * 16 * 128], F32, kind="ExternalInput")
    b0_r = nc.dram_tensor("b0", [HID // 128, 128], F32, kind="ExternalInput")
    w1 = nc.dram_tensor("w1", [128, 16 * 64 * 128], F32, kind="ExternalInput")
    b1_r = nc.dram_tensor("b1", [DM // 128, 128], F32, kind="ExternalInput")
    flag_in = nc.dram_tensor("flag", [128, 1], F32, kind="ExternalInput")
    out = nc.dram_tensor("out", [TOK, DM], F32, kind="ExternalOutput")

    PADT = ((TOK + HALO + 127) // 128) * 128
    dr_hln = nc.dram_tensor("dr_hln", [PADT, DM], F16, kind="Internal")
    dr_z = nc.dram_tensor("dr_z", [DI, TOK], F16, kind="Internal")
    dr_xbc = nc.dram_tensor("dr_xbc", [CD, TOK], F16, kind="Internal")
    dr_ypre = nc.dram_tensor("dr_ypre", [TOK, DI], F16, kind="Internal")
    dr_yn = nc.dram_tensor("dr_yn", [TOK, DI], F16, kind="Internal")
    dr_attn = nc.dram_tensor("dr_attn", [DM, TOK], F16, kind="Internal")
    dr_x1 = nc.dram_tensor("dr_x1", [TOK, DM], F32, kind="Internal")
    dr_h2 = nc.dram_tensor("dr_h2", [TOK, DM], F16, kind="Internal")
    dr_g = nc.dram_tensor("dr_g", [HID, TOK], F16, kind="Internal")
    dr_efm = nc.dram_tensor("dr_efm", [DM, TOK], F16, kind="Internal")
    dr_mp = nc.dram_tensor("dr_mp", [16, 128, NH * 128], F16, kind="Internal")
    dr_cci = nc.dram_tensor("dr_cci", [128, DI], F32, kind="Internal")
    dr_cco = nc.dram_tensor("dr_cco", [128, DI], F32, kind="Internal")

    with ExitStack() as ctx:
        kb = KB(nc, ctx, tok)
        E = ctx.enter_context
        V, A, T, G, S = kb.V, kb.A, kb.T, kb.G, kb.S
        w, i = kb.w, kb.i

        ones16 = E(nc.sbuf_tensor("ones16", [128, 128], F16))
        ones32 = E(nc.sbuf_tensor("ones32", [1, 128], F32))
        ones2 = E(nc.sbuf_tensor("ones2", [2, 128], F16))
        idn32 = E(nc.sbuf_tensor("idn32", [128, 128], F32))
        b0_col = E(nc.sbuf_tensor("b0_col", [128, HID // 128], F32))
        b1_col = E(nc.sbuf_tensor("b1_col", [128, DM // 128], F32))
        flag_c = E(nc.sbuf_tensor("flag_c", [128, 1], F32))
        eps_c = E(nc.sbuf_tensor("eps_c", [128, 1], F32))
        one_c = E(nc.sbuf_tensor("one_c", [128, 1], F32))

        sv = kb.sem("sv")
        sa = kb.sem("sa")
        st_ = kb.sem("st")
        sg = kb.sem("sg")
        y_set = kb.sem("y_set")
        g_cc = kb.sem("g_cc")

        scan_ctx = ExitStack()
        Es = scan_ctx.enter_context
        zcol = Es(nc.sbuf_tensor("zcol", [NH, 1], F32))
        dtb_c = Es(nc.sbuf_tensor("dtb_c", [NH, 1], F32))
        alog_c = Es(nc.sbuf_tensor("alog_c", [NH, 1], F32))
        nA_col = Es(nc.sbuf_tensor("nA_col", [NH, 1], F32))
        D_nb = Es(nc.sbuf_tensor("D_nb", [128, NH], F16))
        nw_bc = Es(nc.sbuf_tensor("nw_bc", [128, DI], F16))
        convw_c = Es(nc.sbuf_tensor("convw_c", [128, 34 * 4], F32))
        convb_c = Es(nc.sbuf_tensor("convb_c", [128, 34], F32))
        dt_t = Es(nc.sbuf_tensor("dt_t", [NH, TOK], F32))
        dA = Es(nc.sbuf_tensor("dA", [NH, TOK], F32))
        dA_cum = Es(nc.sbuf_tensor("dA_cum", [NH, TOK], F32))
        dA_last = Es(nc.sbuf_tensor("dA_last", [NH, NCH], F32))
        cd_t = Es(nc.sbuf_tensor("cd_t", [NH, NCH], F32))
        G_incl = Es(nc.sbuf_tensor("G_incl", [NH, NCH], F32))
        G_pre = Es(nc.sbuf_tensor("G_pre", [NH, NCH], F32))
        carry16 = Es(nc.sbuf_tensor("carry16", [128, DI], F16))
        sum32 = Es(nc.sbuf_tensor("sum32", [128, DI], F32))
        patch16 = Es(nc.sbuf_tensor("patch16", [128, DI], F16))

        with ExitStack() as pctx:
            pS = pctx.enter_context(nc.psum_tensor("ps_set", [128, 512], F32))
            pS2 = pctx.enter_context(nc.psum_tensor("ps_set2", [128, 64], F32))
            drow = pctx.enter_context(nc.sbuf_tensor("drow", [1, NH], F32))
            d16row = pctx.enter_context(nc.sbuf_tensor("d16row", [1, NH], F16))
            nwrow = pctx.enter_context(nc.sbuf_tensor("nwrow", [1, DI], F32))
            nwrow16 = pctx.enter_context(nc.sbuf_tensor("nwrow16", [1, DI], F16))
            cwT_s = pctx.enter_context(nc.sbuf_tensor("cwT_s", [4, CD], F32))
            cb_s = pctx.enter_context(nc.sbuf_tensor("cb_s", [34, 128], F32))

            def s0(e, en):
                for dst, src in [
                    (dtb_c[:, :], dtb.ap()), (alog_c[:, :], alog.ap()),
                    (drow[:, :], dvec.ap()), (nwrow[:, :], nwr.ap()),
                    (cwT_s[:, :], cwT.ap()), (cb_s[:, :], cb_r.ap()),
                    (flag_c[:, :], flag_in.ap()),
                ]:
                    i(e.dma_start(out=dst, in_=src), y_set, 16)
            S(s0)

            b0_s = pctx.enter_context(nc.sbuf_tensor("b0_s", [HID // 128, 128], F32))
            b1_s = pctx.enter_context(nc.sbuf_tensor("b1_s", [DM // 128, 128], F32))

            def s0b(e, en):
                i(e.dma_start(out=b0_s[:, :], in_=b0_r.ap()), y_set, 16)
                i(e.dma_start(out=b1_s[:, :], in_=b1_r.ap()), y_set, 16)
            S(s0b)
            y_set_all = y_set.v

            def v0(e, en):
                i(e.memset(ones16[:, :], 1.0), sv)
                i(e.memset(ones32[:, :], 1.0), sv)
                i(e.memset(ones2[:, :], 1.0), sv)
                i(e.memset(zcol[:, :], 0.0), sv)
                i(e.memset(eps_c[:, :], EPS), sv)
                i(e.memset(one_c[:, :], 1.0), sv)
                w(e, en, y_set, y_set_all)
                i(e.tensor_copy(d16row[:, :], drow[:, :]), sv)
                i(e.tensor_copy(nwrow16[:, :], nwrow[:, :]), sv)
            V(v0)
            tok_v0 = sv.v

            def g0(e, en):
                i(e.memset(idn32[:, :], 1.0), sg)
                w(e, en, sg, sg.v)
                i(e.affine_select(idn32[:, :], idn32[:, :], [[1, 128]],
                                  ALU.is_equal, 0.0, base=0,
                                  channel_multiplier=-1), sg)
            G(g0)
            tok_idn = sg.v

            def a0(e, en):
                w(e, en, y_set, y_set_all)
                i(e.activation(nA_col[:, :], alog_c[:, :], AF.Exp), sa)
            A(a0)
            tok_a0 = sa.v

            def v1(e, en):
                w(e, en, sa, tok_a0)
                i(e.tensor_scalar_mul(nA_col[:, :], nA_col[:, :], -1.0), sv)
            V(v1)

            def t0(e, en):
                w(e, en, sg, tok_idn)
                w(e, en, y_set, y_set_all)
                i(e.transpose(pS[:, 0:34], cb_s[:, :], idn32[0:34, 0:34]), st_)
                i(e.transpose(pS[:, 64:64 + HID // 128], b0_s[:, :], idn32[0:64, 0:64]), st_)
                i(e.transpose(pS[:, 192:192 + DM // 128], b1_s[:, :], idn32[0:16, 0:16]), st_)
            T(t0)
            tok_t0 = st_.v

            def v2(e, en):
                w(e, en, st_, tok_t0)
                i(e.tensor_copy(convb_c[:, :], pS[:, 0:34]), sv)
                i(e.tensor_copy(b0_col[:, :], pS[:, 64:64 + HID // 128]), sv)
                i(e.tensor_copy(b1_col[:, :], pS[:, 192:192 + DM // 128]), sv)
            V(v2)
            tok_v2 = sv.v

            def t1(e, en):
                w(e, en, sv, tok_v2)
                for mt in range(34):
                    i(e.transpose(pS[:, mt * 4:mt * 4 + 4],
                                  cwT_s[:, mt * 128:(mt + 1) * 128], idn32[0:4, 0:4]), st_)
            T(t1)
            tok_t1 = st_.v

            def v3(e, en):
                w(e, en, st_, tok_t1)
                i(e.tensor_copy(convw_c[:, :], pS[:, 0:136]), sv)
            V(v3)
            tok_v3 = sv.v

            def t2(e, en):
                w(e, en, sv, tok_v0)
                i(e.matmul(pS2[:, 0:NH], ones16[0:1, :], d16row[0:1, :],
                           start=True, stop=True), st_)
            T(t2)
            tok_t2 = st_.v

            def v4(e, en):
                w(e, en, st_, tok_t2)
                i(e.tensor_copy(D_nb[:, :], pS2[:, 0:NH]), sv)
            V(v4)

            for q in range(DI // 512):
                def tq(e, en, q=q, gate=sv.v):
                    w(e, en, sv, gate)
                    i(e.matmul(pS[:, 0:512], ones16[0:1, :],
                               nwrow16[0:1, q * 512:(q + 1) * 512],
                               start=True, stop=True), st_)
                T(tq)

                def vq(e, en, q=q, gate=st_.v):
                    w(e, en, st_, gate)
                    i(e.tensor_copy(nw_bc[:, q * 512:(q + 1) * 512], pS[:, 0:512]), sv)
                V(vq)
            kb.barrier()

        with ExitStack() as pctx:
            pe = pctx.enter_context
            u_t = [pe(nc.sbuf_tensor(f"u_t{p}", [128, DM], F32)) for p in range(2)]
            usq = [pe(nc.sbuf_tensor(f"usq{p}", [128, DM], F32)) for p in range(2)]
            hlt = [pe(nc.sbuf_tensor(f"hlt{p}", [128, DM], F16)) for p in range(2)]
            msum = [pe(nc.sbuf_tensor(f"msum{p}", [128, 1], F32)) for p in range(2)]
            mneg = [pe(nc.sbuf_tensor(f"mneg{p}", [128, 1], F32)) for p in range(2)]
            qsum = [pe(nc.sbuf_tensor(f"qsum{p}", [128, 1], F32)) for p in range(2)]
            q2 = [pe(nc.sbuf_tensor(f"q2{p}", [128, 1], F32)) for p in range(2)]
            nvar = [pe(nc.sbuf_tensor(f"nvar{p}", [128, 1], F32)) for p in range(2)]
            rstd = [pe(nc.sbuf_tensor(f"rstd{p}", [128, 1], F32)) for p in range(2)]
            mr = [pe(nc.sbuf_tensor(f"mr{p}", [128, 1], F32)) for p in range(2)]
            y_u = [kb.sem(f"y_u{p}") for p in range(2)]
            y_hl = [kb.sem(f"y_hl{p}") for p in range(2)]

            nrows = TOK + HALO
            tiles = [(r0, min(128, nrows - r0)) for r0 in range(0, nrows, 128)]
            npad = PADT - nrows
            if npad > 0:
                def vz(e, en):
                    i(e.memset(hlt[0][0:npad, :], 0.0), sv)
                V(vz)
                tok_z0 = sv.v

                def sz(e, en, tok_z0=tok_z0):
                    w(e, en, sv, tok_z0)
                    i(e.dma_start(out=dr_hln.ap()[nrows:PADT, :],
                                  in_=hlt[0][0:npad, :]), y_hl[0], 16)
                S(sz)
            last_apply = [y_hl[0].v * 0, 0]
            last_sq = [0, 0]
            for idx, (r0, n) in enumerate(tiles):
                p = idx % 2

                def s1(e, en, r0=r0, n=n, p=p, gate=last_apply[p]):
                    w(e, en, sg, gate)
                    i(e.dma_start(out=u_t[p][0:n, :],
                                  in_=u_in.ap()[r0:r0 + n, :]), y_u[p], 16)
                S(s1)
                tok_u = y_u[p].v

                def v6(e, en, n=n, p=p, tok_u=tok_u):
                    w(e, en, y_u[p], tok_u)
                    i(e.reduce_sum(msum[p][0:n, :], u_t[p][0:n, :], AX.X), sv)
                    w(e, en, sv, sv.v)
                    i(e.tensor_scalar_mul(mneg[p][0:n, :], msum[p][0:n, :], -1.0 / DM), sv)
                V(v6)
                tok_mn = sv.v

                def g1a(e, en, n=n, p=p, tok_u=tok_u, gate=last_sq[p]):
                    w(e, en, y_u[p], tok_u)
                    w(e, en, sv, gate)
                    i(e.tensor_tensor(usq[p][0:n, :], u_t[p][0:n, :],
                                      u_t[p][0:n, :], ALU.mult), sg)
                G(g1a)
                tok_q0 = sg.v

                def v6b(e, en, n=n, p=p, tok_q0=tok_q0):
                    w(e, en, sg, tok_q0)
                    i(e.reduce_sum(qsum[p][0:n, :], usq[p][0:n, :], AX.X), sv)
                V(v6b)
                tok_q = sv.v

                def v7(e, en, n=n, p=p, tok_q=tok_q, tok_mn=tok_mn):
                    w(e, en, sv, tok_q)
                    w(e, en, sv, tok_mn)
                    i(e.tensor_scalar_mul(q2[p][0:n, :], qsum[p][0:n, :], 1.0 / DM), sv)
                    w(e, en, sv, sv.v)
                    i(e.scalar_tensor_tensor(nvar[p][0:n, :], mneg[p][0:n, :],
                                             mneg[p][0:n, 0:1], q2[p][0:n, :],
                                             ALU.mult, ALU.subtract), sv)
                V(v7)
                tok_nv = sv.v
                last_sq[p] = tok_nv

                def a2(e, en, n=n, p=p, tok_nv=tok_nv):
                    w(e, en, sv, tok_nv)
                    i(e.activation(rstd[p][0:n, :], nvar[p][0:n, :], AF.Ln,
                                   bias=eps_c[0:n, 0:1], scale=-1.0), sa)
                    w(e, en, sa, sa.v)
                    i(e.activation(rstd[p][0:n, :], rstd[p][0:n, :], AF.Exp,
                                   scale=-0.5), sa)
                A(a2)
                tok_rs = sa.v

                def v8(e, en, n=n, p=p, tok_rs=tok_rs):
                    w(e, en, sa, tok_rs)
                    i(e.tensor_tensor(mr[p][0:n, :], mneg[p][0:n, :],
                                      rstd[p][0:n, :], ALU.mult), sv)
                V(v8)
                tok_mr = sv.v

                def g3a(e, en, n=n, p=p, tok_mr=tok_mr, gate=y_hl[p].v):
                    w(e, en, sv, tok_mr)
                    w(e, en, y_hl[p], gate)
                    i(e.tensor_scalar(hlt[p][0:n, :], u_t[p][0:n, :],
                                      mneg[p][0:n, 0:1], rstd[p][0:n, 0:1],
                                      ALU.add, ALU.mult), sg)
                G(g3a)
                last_apply[p] = sg.v
                tok_ap = sg.v

                def s2(e, en, r0=r0, n=n, p=p, tok_ap=tok_ap):
                    w(e, en, sg, tok_ap)
                    i(e.dma_start(out=dr_hln.ap()[r0:r0 + n, :],
                                  in_=hlt[p][0:n, :]), y_hl[p], 16)
                S(s2)
            kb.barrier()

        with ExitStack() as pctx:
            pe = pctx.enter_context
            hfm = pe(nc.sbuf_tensor("hfm", [128, KC * PADT], F16))
            w_in = [pe(nc.sbuf_tensor(f"w_in{p}", [128, KC * 128], F16)) for p in range(2)]
            strip = [pe(nc.sbuf_tensor(f"strip{p}", [128, HALO + TOK], F16)) for p in range(2)]
            conva = [pe(nc.sbuf_tensor(f"conva{p}", [128, TOK], F16)) for p in range(2)]
            convd = [pe(nc.sbuf_tensor(f"convd{p}", [128, TOK], F16)) for p in range(2)]
            xstg = [pe(nc.sbuf_tensor(f"xstg{p}", [128, TOK], F16)) for p in range(2)]
            zstg = [pe(nc.sbuf_tensor(f"zstg{p}", [128, TOK], F16)) for p in range(2)]
            psA = [pe(nc.psum_tensor(f"psA{p}", [128, 4 * 512], F32)) for p in range(2)]
            y_fmk = [kb.sem(f"y_fmk{g}") for g in range(KC // 4)]
            g_w = [kb.sem(f"g_w{p}") for p in range(2)]
            y_xbc = [kb.sem(f"y_xbc{p}") for p in range(2)]
            y_z = [kb.sem(f"y_z{p}") for p in range(2)]

            def s3(e, en):
                for kc in range(KC):
                    i(e.dma_start_transpose(
                        out=hfm[:, kc * PADT:kc * PADT + PADT],
                        in_=dr_hln.ap()[:, kc * 128:(kc + 1) * 128]), y_fmk[kc // 4], 16)
            S(s3)

            m_order = [(DI + CD, 64, "dt")] + \
                      [(DI + 128 * j, 128, "xbc") for j in range(34)] + \
                      [(128 * j, 128, "z") for j in range(32)]
            wt_last = [0, 0]
            ev_tok = [[0] * 4, [0] * 4]
            xbc_stage = [0, 0]
            z_stage = [0, 0]
            conv_sa = [0, 0]
            strip_used = [0, 0]
            dt_evac_tok = 0

            for mi, (f0, mw, kind) in enumerate(m_order):
                p = mi % 2

                def g1(e, en, f0=f0, mw=mw, p=p, gate=wt_last[p]):
                    w(e, en, st_, gate)
                    t_idx = f0 // 128
                    src = APx(ipw, t_idx * 2048, [[16 * DIP, 128], [1, KC * mw]])
                    dst = APx(w_in[p], 0, [[KC * 128, 128], [128, KC], [1, mw]])
                    i(e.dma_start(out=dst, in_=src), g_w[p], 16)
                G(g1)
                tok_wl = g_w[p].v

                groups = [("main", HALO + g * 512, 512, g) for g in range(NG)] + \
                         ([("halo", 0, HALO, 0)] if kind == "xbc" else [])

                for gname, c0, cn, bank in groups:
                    def t4(e, en, mw=mw, p=p, c0=c0, cn=cn, bank=bank,
                           tok_wl=tok_wl, gate_ev=ev_tok[p][bank]):
                        w(e, en, g_w[p], tok_wl)
                        w(e, en, sv, gate_ev)
                        for kc in range(KC):
                            w(e, en, y_fmk[kc // 4], 64)
                            i(e.matmul(psA[p][0:mw, bank * 512:bank * 512 + cn],
                                       w_in[p][:, kc * 128:kc * 128 + mw],
                                       hfm[:, kc * PADT + c0:kc * PADT + c0 + cn],
                                       start=(kc == 0), stop=(kc == KC - 1)), st_)
                    T(t4)
                    tok_mm = st_.v
                    wt_last[p] = tok_mm

                    def v9(e, en, mw=mw, p=p, c0=c0, cn=cn, bank=bank,
                           tok_mm=tok_mm, kind=kind,
                           gate_strip=strip_used[p], gate_z=z_stage[p]):
                        w(e, en, st_, tok_mm)
                        if kind == "xbc":
                            w(e, en, sv, gate_strip)
                            i(e.tensor_copy(strip[p][0:mw, c0:c0 + cn],
                                            psA[p][0:mw, bank * 512:bank * 512 + cn]), sv)
                        elif kind == "z":
                            w(e, en, y_z[p], gate_z)
                            i(e.tensor_copy(zstg[p][0:mw, c0 - HALO:c0 - HALO + cn],
                                            psA[p][0:mw, bank * 512:bank * 512 + cn]), sv)
                        else:
                            i(e.tensor_copy(dt_t[0:mw, c0 - HALO:c0 - HALO + cn],
                                            psA[p][0:mw, bank * 512:bank * 512 + cn]), sv)
                    V(v9)
                    ev_tok[p][bank] = sv.v
                tok_evac = sv.v

                if kind == "dt":
                    dt_evac_tok = tok_evac
                elif kind == "xbc":
                    mt = (f0 - DI) // 128

                    def v10(e, en, p=p, mt=mt, tok_evac=tok_evac, gate=conv_sa[p]):
                        w(e, en, sv, tok_evac)
                        w(e, en, sa, gate)
                        i(e.scalar_tensor_tensor(
                            conva[p][:, :], strip[p][:, 0:TOK],
                            convw_c[:, 4 * mt + 0:4 * mt + 1],
                            APx(convb_c, mt, [[34, 128], [0, TOK]]),
                            ALU.mult, ALU.add), sv)
                        w(e, en, sv, sv.v)
                        i(e.scalar_tensor_tensor(
                            convd[p][:, :], strip[p][:, 1:TOK + 1],
                            convw_c[:, 4 * mt + 1:4 * mt + 2],
                            conva[p][:, :], ALU.mult, ALU.add), sv)
                        w(e, en, sv, sv.v)
                        i(e.scalar_tensor_tensor(
                            conva[p][:, :], strip[p][:, 2:TOK + 2],
                            convw_c[:, 4 * mt + 2:4 * mt + 3],
                            convd[p][:, :], ALU.mult, ALU.add), sv)
                        w(e, en, sv, sv.v)
                        i(e.scalar_tensor_tensor(
                            convd[p][:, :], strip[p][:, 3:TOK + 3],
                            convw_c[:, 4 * mt + 3:4 * mt + 4],
                            conva[p][:, :], ALU.mult, ALU.add), sv)
                    V(v10)
                    tok_conv = sv.v
                    strip_used[p] = tok_conv

                    def a4(e, en, p=p, tok_conv=tok_conv, gate=xbc_stage[p]):
                        w(e, en, sv, tok_conv)
                        w(e, en, y_xbc[p], gate)
                        i(e.activation(xstg[p][:, :], convd[p][:, :], AF.Silu), sa)
                    A(a4)
                    conv_sa[p] = sa.v
                    tok_silu = sa.v

                    def s4(e, en, p=p, mt=mt, tok_silu=tok_silu):
                        w(e, en, sa, tok_silu)
                        i(e.dma_start(out=dr_xbc.ap()[mt * 128:(mt + 1) * 128, :],
                                      in_=xstg[p][:, :]), y_xbc[p], 16)
                    S(s4)
                    xbc_stage[p] = y_xbc[p].v
                else:
                    mt = f0 // 128

                    def s5(e, en, p=p, mt=mt, tok_evac=tok_evac):
                        w(e, en, sv, tok_evac)
                        i(e.dma_start(out=dr_z.ap()[mt * 128:(mt + 1) * 128, :],
                                      in_=zstg[p][:, :]), y_z[p], 16)
                    S(s5)
                    z_stage[p] = y_z[p].v

            def v11(e, en, gate=dt_evac_tok):
                w(e, en, sv, gate)
                i(e.tensor_scalar(dt_t[:, :], dt_t[:, :], dtb_c[:, 0:1], None,
                                  ALU.add), sv)
            V(v11)
            tok_abs = sv.v

            def a5(e, en, tok_abs=tok_abs):
                w(e, en, sv, tok_abs)
                i(e.activation(dA[:, :], dt_t[:, :], AF.Abs), sa)
                w(e, en, sa, sa.v)
                i(e.activation(dA_cum[:, :], dA[:, :], AF.Exp, scale=-1.0), sa)
                w(e, en, sa, sa.v)
                i(e.activation(dA_cum[:, :], dA_cum[:, :], AF.Ln, bias=one_c[0:NH, 0:1]), sa)
            A(a5)
            tok_ln1p = sa.v

            def v12(e, en, tok_ln1p=tok_ln1p):
                w(e, en, sa, tok_ln1p)
                i(e.scalar_tensor_tensor(dt_t[:, :], dt_t[:, :], 0.0,
                                         dA_cum[:, :], ALU.max, ALU.add), sv)
                w(e, en, sv, sv.v)
                i(e.tensor_scalar(dA[:, :], dt_t[:, :], nA_col[:, 0:1], None,
                                  ALU.mult), sv)
                w(e, en, sv, sv.v)
                for z in range(NCH):
                    i(e.tensor_tensor_scan(
                        dA_cum[:, z * 128:(z + 1) * 128],
                        dA[:, z * 128:(z + 1) * 128],
                        APx(zcol, 0, [[1, NH], [0, 128]]),
                        0.0, ALU.add, ALU.add), sv)
                w(e, en, sv, sv.v)
                i(e.tensor_copy(dA_last[:, :],
                                APx(dA_cum, 127, [[TOK, NH], [128, NCH]])), sv)
            V(v12)
            tok_dal = sv.v

            def a6(e, en, tok_dal=tok_dal):
                w(e, en, sv, tok_dal)
                i(e.activation(cd_t[:, :], dA_last[:, :], AF.Exp), sa)
            A(a6)
            tok_cd = sa.v

            def v13(e, en, tok_cd=tok_cd):
                w(e, en, sa, tok_cd)
                i(e.tensor_tensor_scan(G_incl[:, :], cd_t[:, :],
                                       APx(zcol, 0, [[1, NH], [0, NCH]]),
                                       1.0, ALU.mult, ALU.add), sv)
                i(e.memset(G_pre[:, 0:1], 1.0), sv)
                if NCH > 1:
                    w(e, en, sv, sv.v)
                    i(e.tensor_copy(G_pre[:, 1:NCH], G_incl[:, 0:NCH - 1]), sv)
            V(v13)
            kb.barrier()

        with ExitStack() as pctx:
            pe = pctx.enter_context
            x_tok = [pe(nc.sbuf_tensor(f"x_tok{p}", [128, DI], F16)) for p in range(2)]
            Bfm = [pe(nc.sbuf_tensor(f"Bfm{p}", [128, 128], F16)) for p in range(2)]
            Cfm = [pe(nc.sbuf_tensor(f"Cfm{p}", [128, 128], F16)) for p in range(2)]
            Btok = [pe(nc.sbuf_tensor(f"Btok{p}", [128, 128], F16)) for p in range(2)]
            lhsE = [pe(nc.sbuf_tensor(f"lhsE{p}", [128, 16 * 128], F16)) for p in range(2)]
            rhsE = [pe(nc.sbuf_tensor(f"rhsE{p}", [128, 16 * 128], F16)) for p in range(2)]
            rhs2 = pe(nc.sbuf_tensor("rhs2", [2, NH * 128], F16))
            Hp16 = [pe(nc.sbuf_tensor(f"Hp16{p}", [NH, 128], F16)) for p in range(2)]
            Lp16 = [pe(nc.sbuf_tensor(f"Lp16{p}", [NH, 128], F16)) for p in range(2)]
            nH16 = [pe(nc.sbuf_tensor(f"nH16{p}", [NH, 128], F16)) for p in range(2)]
            nL16 = [pe(nc.sbuf_tensor(f"nL16{p}", [NH, 128], F16)) for p in range(2)]
            Ltmp = pe(nc.sbuf_tensor("Ltmp", [NH, 128], F32))
            M_sb = pe(nc.sbuf_tensor("M_sb", [128, NH * 128], F16))
            Mp_sb = pe(nc.sbuf_tensor("Mp_sb", [128, NH * 128], F16))
            CBT16 = pe(nc.sbuf_tensor("CBT16", [128, 128], F16))
            dtT16 = pe(nc.sbuf_tensor("dtT16", [128, NH], F16))
            decT16 = pe(nc.sbuf_tensor("decT16", [128, NH], F16))
            dec_f = pe(nc.sbuf_tensor("dec_f", [NH, 128], F32))
            cdrow = pe(nc.sbuf_tensor("cdrow", [1, NH], F32))
            cdnb = pe(nc.sbuf_tensor("cdnb", [128, NH], F32))
            Xdt = pe(nc.sbuf_tensor("Xdt", [128, DI], F16))
            Xdtd = pe(nc.sbuf_tensor("Xdtd", [128, DI], F16))
            ct1 = pe(nc.sbuf_tensor("ct1", [128, 512], F16))
            Yst = [pe(nc.sbuf_tensor(f"Yst{p}", [128, DI], F16)) for p in range(2)]
            pE = pe(nc.psum_tensor("pE", [128, 2048], F32))
            pEp = pe(nc.psum_tensor("pEp", [128, 1024], F32))
            pY = pe(nc.psum_tensor("pY", [128, 1024], F32))

            y_xt = [kb.sem(f"y_xt{p}") for p in range(2)]
            y_bc = [kb.sem(f"y_bc{p}") for p in range(2)]
            y_be = [kb.sem(f"y_be{p}") for p in range(2)]
            y_yp = [kb.sem(f"y_yp{p}") for p in range(2)]
            y_mp = kb.sem("y_mp")

            def v14(e, en):
                i(e.memset(carry16[:, :], 0.0), sv)
                for p in range(2):
                    i(e.memset(lhsE[p][:, :], 1.0), sv)
                    i(e.memset(rhsE[p][:, :], 1.0), sv)
            V(v14)
            kb.barrier()

            def hmap(rr):
                j, r = rr // 16, rr % 16
                return (r // 4) * 16 + j * 4 + (r % 4)

            hl_tok = [0, 0]
            be_done = [0, 0]
            mp_tok = [0] * 16
            m_tok = [0] * 16
            mp_done_v = [0, 0]
            e_evac = [0, 0, 0, 0]
            ysv = [0, 0]
            st_carry = [0] * 8
            y_stage_tok = [0, 0]
            rhs2_used = [0]
            xtok_used = [0, 0]
            cfm_used = [0, 0]
            btok_used = [0, 0]
            bfm_used = [0, 0]
            xdt_used_st = 0
            m_used_st = 0

            for z in range(NCH):
                p = z % 2

                def s6(e, en, z=z, p=p, g1_=xtok_used[p], g2_=cfm_used[p],
                       g3_=btok_used[p], g4_=bfm_used[p],
                       gb0=be_done[0], gb1=be_done[1]):
                    w(e, en, sv, g1_)
                    w(e, en, sv, g2_)
                    w(e, en, st_, g3_)
                    w(e, en, st_, g4_)
                    w(e, en, y_be[0], gb0)
                    w(e, en, y_be[1], gb1)
                    i(e.dma_start_transpose(
                        out=x_tok[p][:, :],
                        in_=dr_xbc.ap()[0:DI, z * 128:(z + 1) * 128]), y_xt[p], 16)
                    i(e.dma_start(out=Bfm[p][:, :],
                                  in_=dr_xbc.ap()[DI:DI + 128, z * 128:(z + 1) * 128]), y_bc[p], 16)
                    i(e.dma_start(out=Cfm[p][:, :],
                                  in_=dr_xbc.ap()[DI + 128:DI + 256, z * 128:(z + 1) * 128]), y_bc[p], 16)
                    i(e.dma_start_transpose(
                        out=Btok[p][:, :],
                        in_=dr_xbc.ap()[DI:DI + 128, z * 128:(z + 1) * 128]), y_bc[p], 16)
                S(s6)
                tok_xt = y_xt[p].v
                tok_bc = y_bc[p].v

                def v15(e, en, z=z, p=p, gate=be_done[p]):
                    w(e, en, y_be[p], gate)
                    i(e.tensor_copy(Hp16[p][:, :], dA_cum[:, z * 128:(z + 1) * 128]), sv)
                    w(e, en, sv, sv.v)
                    i(e.tensor_tensor(Ltmp[:, :], dA_cum[:, z * 128:(z + 1) * 128],
                                      Hp16[p][:, :], ALU.subtract), sv)
                    w(e, en, sv, sv.v)
                    i(e.tensor_copy(Lp16[p][:, :], Ltmp[:, :]), sv)
                    i(e.tensor_scalar_mul(nH16[p][:, :], Hp16[p][:, :], -1.0), sv)
                    w(e, en, sv, sv.v)
                    i(e.tensor_scalar_mul(nL16[p][:, :], Lp16[p][:, :], -1.0), sv)
                V(v15)
                hl_tok[p] = sv.v
                tok_hl = sv.v

                def s7(e, en, z=z, p=p, tok_hl=tok_hl, tok_xt=tok_xt,
                       tok_bc=tok_bc):
                    w(e, en, sv, tok_hl)
                    w(e, en, y_xt[p], tok_xt)
                    w(e, en, y_bc[p], tok_bc)
                    for j in range(4):
                        for (tile, row, src) in [(lhsE[p], 32 * j + 2, nH16[p]),
                                                 (lhsE[p], 32 * j + 3, nL16[p]),
                                                 (rhsE[p], 32 * j + 0, Hp16[p]),
                                                 (rhsE[p], 32 * j + 1, Lp16[p])]:
                            i(e.dma_start(out=tile[row:row + 1, 0:2048],
                                          in_=APx(src, 16 * j * 128, [[128, 16], [1, 128]])),
                              y_be[p], 16)
                    w(e, en, st_, rhs2_used[0])
                    i(e.dma_start(out=rhs2[0:1, :],
                                  in_=APx(Hp16[p], 0, [[128, NH], [1, 128]])), y_be[p], 16)
                    i(e.dma_start(out=rhs2[1:2, :],
                                  in_=APx(Lp16[p], 0, [[128, NH], [1, 128]])), y_be[p], 16)
                S(s7)
                be_done[p] = y_be[p].v
                tok_be = y_be[p].v

                def t5(e, en, p=p, tok_bc=tok_bc, g1_=ysv[0]):
                    w(e, en, y_bc[p], tok_bc)
                    w(e, en, sv, g1_)
                    i(e.matmul(pY[:, 0:128], Bfm[p][:, :], Cfm[p][:, :],
                               start=True, stop=True), st_)
                T(t5)
                bfm_used[p] = st_.v
                tok_cbt_mm = st_.v

                def v16(e, en, tok=tok_cbt_mm):
                    w(e, en, st_, tok)
                    i(e.tensor_copy(CBT16[:, :], pY[:, 0:128]), sv)
                V(v16)
                tok_cbt_cp = sv.v

                def g2(e, en, tok=tok_cbt_cp):
                    w(e, en, sv, tok)
                    i(e.affine_select(CBT16[:, :], CBT16[:, :], [[1, 128]],
                                      ALU.is_ge, 0.0, base=0,
                                      channel_multiplier=-1), sg)
                G(g2)
                tok_cbt = sg.v

                def v17(e, en, z=z):
                    i(e.tensor_scalar(dec_f[:, :], dA_cum[:, z * 128:(z + 1) * 128],
                                      dA_last[:, z:z + 1], None, ALU.subtract), sv)
                V(v17)
                tok_decf = sv.v

                def a7(e, en, tok=tok_decf):
                    w(e, en, sv, tok)
                    i(e.activation(dec_f[:, :], dec_f[:, :], AF.Exp, scale=-1.0), sa)
                A(a7)
                tok_dece = sa.v

                def t6(e, en, z=z, tok_dece=tok_dece, tok_cp=tok_cbt_cp):
                    w(e, en, sa, tok_dece)
                    w(e, en, sv, tok_cp)
                    i(e.transpose(pY[0:128, 128:192], dt_t[:, z * 128:(z + 1) * 128],
                                  idn32[0:64, 0:64]), st_)
                    i(e.transpose(pY[0:128, 192:256], dec_f[:, :], idn32[0:64, 0:64]), st_)
                    i(e.transpose(pY[0:1, 256:320], cd_t[:, z:z + 1], idn32[0:64, 0:64]), st_)
                T(t6)
                tok_tr = st_.v

                def v18(e, en, tok=tok_tr):
                    w(e, en, st_, tok)
                    i(e.tensor_copy(dtT16[:, :], pY[:, 128:192]), sv)
                    i(e.tensor_copy(decT16[:, :], pY[:, 192:256]), sv)
                    i(e.tensor_copy(cdrow[0:1, :], pY[0:1, 256:320]), sv)
                V(v18)
                tok_trc = sv.v

                def t7(e, en, tok=tok_trc):
                    w(e, en, sv, tok)
                    i(e.matmul(pY[:, 320:384], ones32[0:1, :], cdrow[0:1, :],
                               start=True, stop=True), st_)
                T(t7)
                tok_cdmm = st_.v

                def v19(e, en, tok=tok_cdmm):
                    w(e, en, st_, tok)
                    i(e.tensor_copy(cdnb[:, :], pY[:, 320:384]), sv)
                V(v19)
                tok_cdnb = sv.v

                def v20(e, en, p=p, tok_xt=tok_xt, tok_trc=tok_trc,
                        gate=xdt_used_st):
                    w(e, en, y_xt[p], tok_xt)
                    w(e, en, sv, tok_trc)
                    w(e, en, st_, gate)
                    i(e.tensor_tensor(Xdt[:, :], x_tok[p][:, :],
                                      APx(dtT16, 0, [[NH, 128], [1, NH], [0, HD]]),
                                      ALU.mult), sv)
                    w(e, en, sv, sv.v)
                    i(e.tensor_tensor(Xdtd[:, :], Xdt[:, :],
                                      APx(decT16, 0, [[NH, 128], [1, NH], [0, HD]]),
                                      ALU.mult), sv)
                V(v20)
                tok_xdt = sv.v
                xtok_used[p] = tok_xdt

                for q16 in range(16):
                    half = q16 % 2

                    def t8(e, en, p=p, q16=q16, half=half, tok_be=tok_be,
                           gate=mp_done_v[half]):
                        w(e, en, y_be[p], tok_be)
                        w(e, en, sv, gate)
                        i(e.matmul(pEp[:, half * 512:half * 512 + 512],
                                   ones2[0:2, :],
                                   rhs2[0:2, q16 * 512:(q16 + 1) * 512],
                                   start=True, stop=True), st_)
                    T(t8)
                    tok_ep = st_.v

                    def v21a(e, en, q16=q16, half=half, tok_ep=tok_ep,
                             gate=m_used_st, gate_mp=y_mp.v):
                        w(e, en, st_, tok_ep)
                        w(e, en, st_, gate)
                        w(e, en, y_mp, gate_mp)
                        i(e.tensor_scalar_min(Mp_sb[:, q16 * 512:(q16 + 1) * 512],
                                              pEp[:, half * 512:half * 512 + 512],
                                              0.0), sv)
                    V(v21a)
                    mp_done_v[half] = sv.v
                rhs2_used[0] = st_.v
                tok_mpraw = sv.v

                def a8(e, en, tok=tok_mpraw):
                    w(e, en, sv, tok)
                    i(e.activation(Mp_sb[:, 0:4096], Mp_sb[:, 0:4096], AF.Exp), sa)
                    i(e.activation(Mp_sb[:, 4096:8192], Mp_sb[:, 4096:8192],
                                   AF.Exp), sa)
                A(a8)
                tok_mpe = sa.v

                def v21(e, en, p=p, tok=tok_mpe):
                    w(e, en, sa, tok)
                    i(e.tensor_tensor(
                        Mp_sb[:, :],
                        Mp_sb[:, :],
                        APx(Cfm[p], 0, [[128, 128], [0, NH], [1, 128]]),
                        ALU.mult), sv)
                V(v21)
                for q16 in range(16):
                    mp_tok[q16] = sv.v
                cfm_used[p] = sv.v
                tok_mp_done = sv.v

                def s_mp(e, en, z=z, tok=tok_mp_done):
                    w(e, en, sv, tok)
                    i(e.dma_start(out=dr_mp.ap()[z, :, :], in_=Mp_sb[:, :]),
                      y_mp, 16)
                S(s_mp)

                for cyc in range(4):
                    def t9(e, en, p=p, cyc=cyc, tok_be=tok_be,
                           gates=tuple(e_evac)):
                        w(e, en, y_be[p], tok_be)
                        for b in range(4):
                            w(e, en, sv, gates[b])
                        for rr in range(4):
                            r = cyc * 4 + rr
                            for j in range(4):
                                i(e.matmul(
                                    pE[:, j * 512 + rr * 128:j * 512 + rr * 128 + 128],
                                    lhsE[p][32 * j:32 * j + 4, r * 128:(r + 1) * 128],
                                    rhsE[p][32 * j:32 * j + 4, r * 128:(r + 1) * 128],
                                    start=True, stop=True,
                                    tile_position=(32 * j, 0)), st_)
                    T(t9)
                    tok_ecyc = st_.v

                    for j in range(4):
                        def v9b(e, en, j=j, cyc=cyc, tok_ecyc=tok_ecyc,
                                gate=m_used_st):
                            w(e, en, st_, tok_ecyc)
                            w(e, en, st_, gate)
                            i(e.tensor_scalar_min(
                                M_sb[:, cyc * 2048 + j * 512:cyc * 2048 + (j + 1) * 512],
                                pE[:, j * 512:(j + 1) * 512], 0.0), sv)
                        V(v9b)
                        e_evac[j] = sv.v

                tok_mraw = sv.v

                def a9(e, en, tok=tok_mraw):
                    w(e, en, sv, tok)
                    i(e.activation(M_sb[:, 0:4096], M_sb[:, 0:4096], AF.Exp), sa)
                    i(e.activation(M_sb[:, 4096:8192], M_sb[:, 4096:8192],
                                   AF.Exp), sa)
                A(a9)
                tok_me = sa.v

                def v22(e, en, tok=tok_me, tok_cbt=tok_cbt):
                    w(e, en, sa, tok)
                    w(e, en, sg, tok_cbt)
                    i(e.tensor_tensor(
                        M_sb[:, :], M_sb[:, :],
                        APx(CBT16, 0, [[128, 128], [0, NH], [1, 128]]),
                        ALU.mult), sv)
                V(v22)
                for idx16 in range(16):
                    m_tok[idx16] = sv.v

                for q in range(8):
                    hp = q % 2

                    def t10(e, en, q=q, hp=hp, tok_xdt=tok_xdt,
                            tok_cdnb=tok_cdnb, gate_c=st_carry[q],
                            gate_y=ysv[hp]):
                        for h in range(q * 8, q * 8 + 8):
                            w(e, en, sv, m_tok[hmap(h) // 4])
                            w(e, en, sv, mp_tok[h // 4])
                        w(e, en, sv, tok_xdt)
                        w(e, en, sv, tok_cdnb)
                        w(e, en, sv, gate_c)
                        w(e, en, sv, gate_y)
                        for h in range(q * 8, q * 8 + 8):
                            c0 = hp * 512 + (h % 8) * 64
                            i(e.matmul(pY[:, c0:c0 + 64],
                                       M_sb[:, hmap(h) * 128:hmap(h) * 128 + 128],
                                       Xdt[:, h * 64:h * 64 + 64],
                                       start=True, stop=False), st_)
                            i(e.matmul(pY[:, c0:c0 + 64],
                                       Mp_sb[:, h * 128:h * 128 + 128],
                                       carry16[:, h * 64:h * 64 + 64],
                                       start=False, stop=True), st_)
                    T(t10)
                    tok_y = st_.v

                    def t11(e, en, p=p, q=q, hp=hp, tok_bc=tok_bc,
                            gate=mp_done_v[hp]):
                        w(e, en, y_bc[p], tok_bc)
                        w(e, en, sv, gate)
                        for h in range(q * 8, q * 8 + 8):
                            i(e.matmul(pEp[:, hp * 512 + (h % 8) * 64:
                                           hp * 512 + (h % 8) * 64 + 64],
                                       Btok[p][:, :], Xdtd[:, h * 64:h * 64 + 64],
                                       start=True, stop=True), st_)
                    T(t11)
                    btok_used[p] = st_.v
                    tok_stm = st_.v

                    def v23(e, en, q=q, hp=hp, tok_stm=tok_stm, tok_cdnb=tok_cdnb,
                            selfgate=sv.v):
                        w(e, en, st_, tok_stm)
                        w(e, en, sv, max(tok_cdnb, selfgate))
                        i(e.tensor_tensor(ct1[:, :], carry16[:, q * 512:(q + 1) * 512],
                                          APx(cdnb, q * 8, [[NH, 128], [1, 8], [0, HD]]),
                                          ALU.mult), sv)
                        w(e, en, sv, sv.v)
                        i(e.tensor_add(carry16[:, q * 512:(q + 1) * 512], ct1[:, :],
                                       pEp[:, hp * 512:hp * 512 + 512]), sv)
                    V(v23)
                    st_carry[q] = sv.v
                    mp_done_v[hp] = sv.v

                    def v24(e, en, z=z, q=q, hp=hp, tok_y=tok_y,
                            gate=y_stage_tok[z % 2]):
                        w(e, en, st_, tok_y)
                        w(e, en, y_yp[z % 2], gate)
                        i(e.tensor_copy(Yst[z % 2][:, q * 512:(q + 1) * 512],
                                        pY[:, hp * 512:hp * 512 + 512]), sv)
                    V(v24)
                    ysv[hp] = sv.v
                tok_yall = sv.v
                xdt_used_st = st_.v
                m_used_st = st_.v

                def s8(e, en, z=z, tok_yall=tok_yall):
                    w(e, en, sv, tok_yall)
                    i(e.dma_start(out=dr_ypre.ap()[z * 128:(z + 1) * 128, :],
                                  in_=Yst[z % 2][:, :]), y_yp[z % 2], 16)
                S(s8)
                y_stage_tok[z % 2] = y_yp[z % 2].v

            y_cc1 = kb.sem("y_cc1")
            y_cc2 = kb.sem("y_cc2")

            def g3a(e, en, gate=tuple(st_carry)):
                for q in range(8):
                    w(e, en, sv, gate[q])
                i(e.dma_start(out=dr_cci.ap(), in_=carry16[:, :]), y_cc1, 16)
            G(g3a)

            def g3(e, en):
                w(e, en, y_cc1, y_cc1.v)
                groups = [[c, c + 1] for c in range(0, ncores, 2)]
                i(e.collective_compute("AllReduce", ALU.add,
                                       replica_groups=groups,
                                       ins=[dr_cci.ap()], outs=[dr_cco.ap()]),
                  g_cc, 1)
            G(g3)

            def s10(e, en):
                w(e, en, g_cc, g_cc.v)
                i(e.dma_start(out=sum32[:, :], in_=dr_cco.ap()), y_cc2, 16)
            S(s10)

            def v26(e, en):
                w(e, en, y_cc2, y_cc2.v)
                i(e.tensor_tensor(sum32[:, :], sum32[:, :], carry16[:, :],
                                  ALU.subtract), sv)
                w(e, en, sv, sv.v)
                i(e.tensor_scalar(patch16[:, :], sum32[:, :], flag_c[:, 0:1],
                                  None, ALU.mult), sv)
            V(v26)
            kb.barrier()

        with ExitStack() as pctx:
            pe = pctx.enter_context
            Ypt = [pe(nc.sbuf_tensor(f"Ypt{p}", [128, DI], F16)) for p in range(2)]
            z_tk = [pe(nc.sbuf_tensor(f"z_tk{p}", [128, DI], F16)) for p in range(2)]
            x_tk = [pe(nc.sbuf_tensor(f"x_tk{p}", [128, DI], F16)) for p in range(2)]
            Mp2t = [pe(nc.sbuf_tensor(f"Mp2t{p}", [128, NH * 128], F16)) for p in range(2)]
            stz = pe(nc.sbuf_tensor("stz", [128, DI], F16))
            siluz = pe(nc.sbuf_tensor("siluz", [128, DI], F16))
            ygA = pe(nc.sbuf_tensor("ygA", [128, DI], F16))
            ygB = pe(nc.sbuf_tensor("ygB", [128, DI], F16))
            ssum = pe(nc.sbuf_tensor("ssum", [128, 1], F32))
            rstd2 = pe(nc.sbuf_tensor("rstd2", [128, 1], F32))
            ynst = [pe(nc.sbuf_tensor(f"ynst{p}", [128, DI], F16)) for p in range(2)]
            pYb = pe(nc.psum_tensor("pYb", [128, 1024], F32))
            pM2 = pe(nc.psum_tensor("pM2", [128, 512], F32))

            y_p4 = [kb.sem(f"y_p4_{p}") for p in range(2)]
            y_be2 = [kb.sem(f"y_be2_{p}") for p in range(2)]
            y_yn = [kb.sem(f"y_yn{p}") for p in range(2)]

            growall = pe(nc.sbuf_tensor("growall", [NCH, NH], F32))
            growflat = pe(nc.sbuf_tensor("growflat", [1, NCH * NH], F32))
            gnb_all = pe(nc.sbuf_tensor("gnb_all", [128, NCH * NH], F32))
            y_gf = kb.sem("y_gf")

            def tg0(e, en):
                i(e.transpose(pM2[0:NCH, 0:NH], G_pre[:, :], idn32[0:NH, 0:NH]), st_)
            T(tg0)
            tok_tg0 = st_.v

            def vg0(e, en, tok=tok_tg0):
                w(e, en, st_, tok)
                i(e.tensor_copy(growall[:, :], pM2[0:NCH, 0:NH]), sv)
            V(vg0)
            tok_vg0 = sv.v

            def sg0b(e, en, tok=tok_vg0):
                w(e, en, sv, tok)
                i(e.dma_start(out=growflat[0:1, :],
                              in_=APx(growall, 0, [[NH, NCH], [1, NH]])), y_gf, 16)
            S(sg0b)
            tok_gf = y_gf.v

            for qq in range((NCH * NH + 511) // 512):
                c0 = qq * 512
                cn = min(512, NCH * NH - c0)

                def tg1(e, en, c0=c0, cn=cn, tok=tok_gf, gate=sv.v):
                    w(e, en, y_gf, tok)
                    w(e, en, sv, gate)
                    i(e.matmul(pM2[:, 0:cn], ones32[0:1, :],
                               growflat[0:1, c0:c0 + cn],
                               start=True, stop=True), st_)
                T(tg1)
                tok_tg1 = st_.v

                def vg1(e, en, c0=c0, cn=cn, tok=tok_tg1):
                    w(e, en, st_, tok)
                    i(e.tensor_copy(gnb_all[:, c0:c0 + cn], pM2[:, 0:cn]), sv)
                V(vg1)
            tok_gnball = sv.v

            mp2_rd_gate = [0, 0]
            stz_used_st = 0
            ygq_tok = [0, 0]
            yn_stage_tok = [0, 0]
            p4_v_reads = [0, 0]
            p4_a_reads = [0, 0]
            prev_ss = 0
            prev_yg = 0
            prev_vv = 0
            prev_yn = 0

            for z in range(NCH):
                p = z % 2

                def s11(e, en, z=z, p=p, g1_=p4_v_reads[p], g2_=p4_a_reads[p],
                        gfl=y_gf.v):
                    w(e, en, sv, g1_)
                    w(e, en, sa, g2_)
                    w(e, en, y_gf, gfl)
                    i(e.dma_start(out=Ypt[p][:, :],
                                  in_=dr_ypre.ap()[z * 128:(z + 1) * 128, :]), y_p4[p], 16)
                    i(e.dma_start_transpose(
                        out=z_tk[p][:, :],
                        in_=dr_z.ap()[0:DI, z * 128:(z + 1) * 128]), y_p4[p], 16)
                    i(e.dma_start_transpose(
                        out=x_tk[p][:, :],
                        in_=dr_xbc.ap()[0:DI, z * 128:(z + 1) * 128]), y_p4[p], 16)
                S(s11)
                tok_rd = y_p4[p].v

                def s12(e, en, z=z, p=p, gate=mp2_rd_gate[p]):
                    w(e, en, st_, gate)
                    i(e.dma_start(out=Mp2t[p][:, :], in_=dr_mp.ap()[z, :, :]),
                      y_be2[p], 16)
                S(s12)
                tok_r2b = y_be2[p].v

                def a11(e, en, p=p, tok_rd=tok_rd, gate=prev_yg):
                    w(e, en, y_p4[p], tok_rd)
                    w(e, en, sv, gate)
                    i(e.activation(siluz[:, :], z_tk[p][:, :], AF.Silu), sa)
                A(a11)
                p4_a_reads[p] = sa.v
                tok_slz = sa.v

                def v29(e, en, z=z, gate=stz_used_st, tok=tok_gnball):
                    w(e, en, sv, tok)
                    w(e, en, st_, gate)
                    i(e.tensor_tensor(stz[:, :], patch16[:, :],
                                      APx(gnb_all, z * NH, [[NCH * NH, 128], [1, NH], [0, HD]]),
                                      ALU.mult), sv)
                V(v29)
                tok_stz = sv.v
                for q in range(8):
                    hp = q % 2

                    def t15(e, en, p=p, q=q, hp=hp, tok_stz=tok_stz,
                            gate=ygq_tok[hp], tok_r2b=tok_r2b):
                        w(e, en, y_be2[p], tok_r2b)
                        w(e, en, sv, tok_stz)
                        w(e, en, sv, gate)
                        for h in range(q * 8, q * 8 + 8):
                            c0 = hp * 512 + (h % 8) * 64
                            i(e.matmul(pYb[:, c0:c0 + 64],
                                       Mp2t[p][:, h * 128:h * 128 + 128],
                                       stz[:, h * 64:h * 64 + 64],
                                       start=True, stop=True), st_)
                    T(t15)
                    tok_pt = st_.v
                    mp2_rd_gate[p] = st_.v

                    def v31(e, en, p=p, q=q, hp=hp, tok_pt=tok_pt, tok_rd=tok_rd,
                            gate=prev_ss):
                        w(e, en, st_, tok_pt)
                        w(e, en, y_p4[p], tok_rd)
                        w(e, en, sa, gate)
                        i(e.tensor_add(ygA[:, q * 512:(q + 1) * 512],
                                       Ypt[p][:, q * 512:(q + 1) * 512],
                                       pYb[:, hp * 512:hp * 512 + 512]), sv)
                    V(v31)
                    ygq_tok[hp] = sv.v
                tok_yga = sv.v
                stz_used_st = st_.v

                def v32(e, en, p=p, tok_rd=tok_rd, tok_slz=tok_slz,
                        tok_yga=tok_yga, gate=prev_ss):
                    w(e, en, y_p4[p], tok_rd)
                    w(e, en, sa, gate)
                    i(e.tensor_tensor(ygB[:, :], x_tk[p][:, :],
                                      APx(D_nb, 0, [[NH, 128], [1, NH], [0, HD]]),
                                      ALU.mult), sv)
                    w(e, en, sv, sv.v)
                    w(e, en, sv, tok_yga)
                    i(e.tensor_add(ygB[:, :], ygA[:, :], ygB[:, :]), sv)
                    w(e, en, sv, sv.v)
                    w(e, en, sa, tok_slz)
                    i(e.tensor_tensor(ygA[:, :], ygB[:, :], siluz[:, :],
                                      ALU.mult), sv)
                V(v32)
                tok_yg = sv.v
                prev_yg = tok_yg
                p4_v_reads[p] = tok_yg

                def a12(e, en, tok=tok_yg, gate=prev_vv):
                    w(e, en, sv, tok)
                    w(e, en, sv, gate)
                    i(e.activation(ygB[:, :], ygA[:, :], AF.Square,
                                   accum_out=ssum[:, :]), sa)
                A(a12)
                tok_ss = sa.v
                prev_ss = tok_ss

                def v33(e, en, tok=tok_ss, gate=prev_yn):
                    w(e, en, sa, tok)
                    w(e, en, sv, gate)
                    i(e.tensor_scalar_mul(rstd2[:, :], ssum[:, :], -1.0 / DI), sv)
                V(v33)
                tok_vv = sv.v
                prev_vv = tok_vv

                def a13(e, en, tok=tok_vv):
                    w(e, en, sv, tok)
                    i(e.activation(rstd2[:, :], rstd2[:, :], AF.Ln, bias=eps_c[:, 0:1],
                                   scale=-1.0), sa)
                    w(e, en, sa, sa.v)
                    i(e.activation(rstd2[:, :], rstd2[:, :], AF.Exp, scale=-0.5), sa)
                A(a13)
                tok_rstd = sa.v

                def v34(e, en, z=z, tok=tok_rstd, gate=yn_stage_tok[z % 2]):
                    w(e, en, sa, tok)
                    w(e, en, y_yn[z % 2], gate)
                    i(e.scalar_tensor_tensor(ynst[z % 2][:, :], ygA[:, :],
                                             rstd2[:, 0:1], nw_bc[:, :],
                                             ALU.mult, ALU.mult), sv)
                V(v34)
                tok_yn = sv.v
                prev_yn = tok_yn

                def s13(e, en, z=z, tok=tok_yn):
                    w(e, en, sv, tok)
                    i(e.dma_start(out=dr_yn.ap()[z * 128:(z + 1) * 128, :],
                                  in_=ynst[z % 2][:, :]), y_yn[z % 2], 16)
                S(s13)
                yn_stage_tok[z % 2] = y_yn[z % 2].v
            kb.barrier()

        scan_ctx.close()

        with ExitStack() as pctx:
            pe = pctx.enter_context
            ynfm = pe(nc.sbuf_tensor("ynfm", [128, 32 * TOK], F16))
            wop = [pe(nc.sbuf_tensor(f"wop{p}", [128, 32 * 128], F16)) for p in range(2)]
            atst = [pe(nc.sbuf_tensor(f"atst{p}", [128, TOK], F16)) for p in range(2)]
            pC = [pe(nc.psum_tensor(f"pC{p}", [128, 4 * 512], F32)) for p in range(2)]
            y_ynfk = [kb.sem(f"y_ynfk{g}") for g in range(8)]
            g_w2 = [kb.sem(f"g_w2_{p}") for p in range(2)]
            y_at = [kb.sem(f"y_at{p}") for p in range(2)]

            def s14(e, en):
                for kc in range(32):
                    i(e.dma_start_transpose(
                        out=ynfm[:, kc * TOK:(kc + 1) * TOK],
                        in_=dr_yn.ap()[:, kc * 128:(kc + 1) * 128]), y_ynfk[kc // 4], 16)
            S(s14)

            wt2_last = [0, 0]
            evC = [[0] * 4, [0] * 4]
            at_stage = [0, 0]
            for m in range(16):
                p = m % 2

                def g4(e, en, m=m, p=p, gate=wt2_last[p]):
                    w(e, en, st_, gate)
                    src = APx(opw, m * 4096, [[16 * 32 * 128, 128], [1, 4096]])
                    dst = APx(wop[p], 0, [[32 * 128, 128], [1, 4096]])
                    i(e.dma_start(out=dst, in_=src), g_w2[p], 16)
                G(g4)
                tok_w2 = g_w2[p].v

                for g in range(NG):
                    def t16(e, en, m=m, p=p, g=g, tok_w2=tok_w2, gate=evC[p][g]):
                        w(e, en, g_w2[p], tok_w2)
                        w(e, en, sv, gate)
                        for kc in range(32):
                            w(e, en, y_ynfk[kc // 4], 64)
                            i(e.matmul(pC[p][:, g * 512:g * 512 + 512],
                                       wop[p][:, kc * 128:(kc + 1) * 128],
                                       ynfm[:, kc * TOK + g * 512:kc * TOK + g * 512 + 512],
                                       start=(kc == 0), stop=(kc == 31)), st_)
                    T(t16)
                    wt2_last[p] = st_.v
                    tok_c = st_.v

                    def v35(e, en, p=p, g=g, tok_c=tok_c, gate=at_stage[p]):
                        w(e, en, st_, tok_c)
                        w(e, en, y_at[p], gate)
                        i(e.tensor_copy(atst[p][:, g * 512:g * 512 + 512],
                                        pC[p][:, g * 512:g * 512 + 512]), sv)
                    V(v35)
                    evC[p][g] = sv.v
                tok_at = sv.v

                def s15(e, en, m=m, p=p, tok_at=tok_at):
                    w(e, en, sv, tok_at)
                    i(e.dma_start(out=dr_attn.ap()[m * 128:(m + 1) * 128, :],
                                  in_=atst[p][:, :]), y_at[p], 16)
                S(s15)
                at_stage[p] = y_at[p].v
            kb.barrier()

        with ExitStack() as pctx:
            pe = pctx.enter_context
            u2 = [pe(nc.sbuf_tensor(f"u2_{p}", [128, DM], F32)) for p in range(2)]
            att = [pe(nc.sbuf_tensor(f"att{p}", [128, DM], F16)) for p in range(2)]
            x1t = [pe(nc.sbuf_tensor(f"x1t{p}", [128, DM], F32)) for p in range(2)]
            xsq = [pe(nc.sbuf_tensor(f"xsq{p}", [128, DM], F32)) for p in range(2)]
            h2t = [pe(nc.sbuf_tensor(f"h2t{p}", [128, DM], F16)) for p in range(2)]
            ms2 = [pe(nc.sbuf_tensor(f"ms2_{p}", [128, 1], F32)) for p in range(2)]
            mn2 = [pe(nc.sbuf_tensor(f"mn2_{p}", [128, 1], F32)) for p in range(2)]
            qs2 = [pe(nc.sbuf_tensor(f"qs2_{p}", [128, 1], F32)) for p in range(2)]
            qq2 = [pe(nc.sbuf_tensor(f"qq2_{p}", [128, 1], F32)) for p in range(2)]
            nv2 = [pe(nc.sbuf_tensor(f"nv2_{p}", [128, 1], F32)) for p in range(2)]
            rs2 = [pe(nc.sbuf_tensor(f"rs2_{p}", [128, 1], F32)) for p in range(2)]
            mr2 = [pe(nc.sbuf_tensor(f"mr2_{p}", [128, 1], F32)) for p in range(2)]
            y_u2 = [kb.sem(f"y_u2_{p}") for p in range(2)]
            y_x1 = [kb.sem(f"y_x1_{p}") for p in range(2)]
            y_h2 = [kb.sem(f"y_h2_{p}") for p in range(2)]

            last_h2 = [0, 0]
            last_sq2 = [0, 0]
            for tt in range(TOK // 128):
                p = tt % 2

                def s16(e, en, tt=tt, p=p, gate=last_h2[p]):
                    w(e, en, sg, gate)
                    i(e.dma_start(out=u2[p][:, :],
                                  in_=u_in.ap()[HALO + tt * 128:HALO + (tt + 1) * 128, :]), y_u2[p], 16)
                    i(e.dma_start_transpose(
                        out=att[p][:, :],
                        in_=dr_attn.ap()[:, tt * 128:(tt + 1) * 128]), y_u2[p], 16)
                S(s16)
                tok_r = y_u2[p].v

                def v36(e, en, p=p, tok_r=tok_r, gate=y_x1[p].v):
                    w(e, en, y_u2[p], tok_r)
                    w(e, en, y_x1[p], gate)
                    i(e.tensor_add(x1t[p][:, :], u2[p][:, :], att[p][:, :]), sv)
                    w(e, en, sv, sv.v)
                    i(e.reduce_sum(ms2[p][:, :], x1t[p][:, :], AX.X), sv)
                    w(e, en, sv, sv.v)
                    i(e.tensor_scalar_mul(mn2[p][:, :], ms2[p][:, :], -1.0 / DM), sv)
                V(v36)
                tok_x1 = sv.v

                def s17(e, en, tt=tt, p=p, tok=tok_x1):
                    w(e, en, sv, tok)
                    i(e.dma_start(out=dr_x1.ap()[tt * 128:(tt + 1) * 128, :],
                                  in_=x1t[p][:, :]), y_x1[p], 16)
                S(s17)

                def g14(e, en, p=p, tok=tok_x1, gate=last_sq2[p]):
                    w(e, en, sv, tok)
                    w(e, en, sv, gate)
                    i(e.tensor_tensor(xsq[p][:, :], x1t[p][:, :],
                                      x1t[p][:, :], ALU.mult), sg)
                G(g14)
                tok_q2a = sg.v

                def v37a(e, en, p=p, tok=tok_q2a):
                    w(e, en, sg, tok)
                    i(e.reduce_sum(qs2[p][:, :], xsq[p][:, :], AX.X), sv)
                V(v37a)
                tok_q2 = sv.v

                def v37(e, en, p=p, tok=tok_q2):
                    w(e, en, sv, tok)
                    i(e.tensor_scalar_mul(qq2[p][:, :], qs2[p][:, :], 1.0 / DM), sv)
                    w(e, en, sv, sv.v)
                    i(e.scalar_tensor_tensor(nv2[p][:, :], mn2[p][:, :],
                                             mn2[p][:, 0:1], qq2[p][:, :],
                                             ALU.mult, ALU.subtract), sv)
                V(v37)
                tok_nv2 = sv.v
                last_sq2[p] = tok_nv2

                def a15(e, en, p=p, tok=tok_nv2):
                    w(e, en, sv, tok)
                    i(e.activation(rs2[p][:, :], nv2[p][:, :], AF.Ln, bias=eps_c[:, 0:1],
                                   scale=-1.0), sa)
                    w(e, en, sa, sa.v)
                    i(e.activation(rs2[p][:, :], rs2[p][:, :], AF.Exp, scale=-0.5), sa)
                A(a15)
                tok_rs2 = sa.v

                def v38(e, en, p=p, tok=tok_rs2):
                    w(e, en, sa, tok)
                    i(e.tensor_tensor(mr2[p][:, :], mn2[p][:, :], rs2[p][:, :],
                                      ALU.mult), sv)
                V(v38)
                tok_mr2 = sv.v

                def g16(e, en, p=p, tok=tok_mr2, gate=y_h2[p].v):
                    w(e, en, sv, tok)
                    w(e, en, y_h2[p], gate)
                    i(e.tensor_scalar(h2t[p][:, :], x1t[p][:, :],
                                      mn2[p][:, 0:1], rs2[p][:, 0:1],
                                      ALU.add, ALU.mult), sg)
                G(g16)
                last_h2[p] = sg.v
                tok_h2 = sg.v

                def s18(e, en, tt=tt, p=p, tok=tok_h2):
                    w(e, en, sg, tok)
                    i(e.dma_start(out=dr_h2.ap()[tt * 128:(tt + 1) * 128, :],
                                  in_=h2t[p][:, :]), y_h2[p], 16)
                S(s18)
            kb.barrier()

        with ExitStack() as pctx:
            pe = pctx.enter_context
            h2fm = pe(nc.sbuf_tensor("h2fm", [128, KC * TOK], F16))
            w0t = [pe(nc.sbuf_tensor(f"w0t{p}", [128, KC * 128], F16)) for p in range(2)]
            gst = [pe(nc.sbuf_tensor(f"gst{p}", [128, TOK], F16)) for p in range(2)]
            pD = [pe(nc.psum_tensor(f"pD{p}", [128, 4 * 512], F32)) for p in range(2)]
            y_h2fk = [kb.sem(f"y_h2fk{g}") for g in range(KC // 4)]
            g_w3 = [kb.sem(f"g_w3_{p}") for p in range(2)]
            y_g = [kb.sem(f"y_g{p}") for p in range(2)]

            def s19(e, en):
                for kc in range(KC):
                    i(e.dma_start_transpose(
                        out=h2fm[:, kc * TOK:(kc + 1) * TOK],
                        in_=dr_h2.ap()[:, kc * 128:(kc + 1) * 128]), y_h2fk[kc // 4], 16)
            S(s19)

            w3_last = [0, 0]
            evD = [[0] * 4, [0] * 4]
            g_stage = [0, 0]
            for m in range(HID // 128):
                p = m % 2

                def g5(e, en, m=m, p=p, gate=w3_last[p]):
                    w(e, en, st_, gate)
                    src = APx(w0, m * 2048, [[64 * 16 * 128, 128], [1, 2048]])
                    dst = APx(w0t[p], 0, [[KC * 128, 128], [1, 2048]])
                    i(e.dma_start(out=dst, in_=src), g_w3[p], 16)
                G(g5)
                tok_w3 = g_w3[p].v

                for g in range(NG):
                    def t17(e, en, m=m, p=p, g=g, tok_w3=tok_w3, gate=evD[p][g]):
                        w(e, en, g_w3[p], tok_w3)
                        w(e, en, sa, gate)
                        for kc in range(KC):
                            w(e, en, y_h2fk[kc // 4], 64)
                            i(e.matmul(pD[p][:, g * 512:g * 512 + 512],
                                       w0t[p][:, kc * 128:(kc + 1) * 128],
                                       h2fm[:, kc * TOK + g * 512:kc * TOK + g * 512 + 512],
                                       start=(kc == 0), stop=(kc == KC - 1)), st_)
                    T(t17)
                    w3_last[p] = st_.v
                    tok_d = st_.v

                    def a17(e, en, m=m, p=p, g=g, tok_d=tok_d, gate=g_stage[p]):
                        w(e, en, st_, tok_d)
                        w(e, en, y_g[p], gate)
                        i(e.activation(gst[p][:, g * 512:g * 512 + 512],
                                       pD[p][:, g * 512:g * 512 + 512],
                                       AF.Gelu, bias=b0_col[:, m:m + 1]), sa)
                    A(a17)
                    evD[p][g] = sa.v
                tok_g = sa.v

                def s20(e, en, m=m, p=p, tok_g=tok_g):
                    w(e, en, sa, tok_g)
                    i(e.dma_start(out=dr_g.ap()[m * 128:(m + 1) * 128, :],
                                  in_=gst[p][:, :]), y_g[p], 16)
                S(s20)
                g_stage[p] = y_g[p].v
            kb.barrier()

        with ExitStack() as pctx:
            pe = pctx.enter_context
            TH = TOK // 2
            NKH = HID // 128
            ghalf = pe(nc.sbuf_tensor("ghalf", [128, NKH * TH], F16))
            w1t = [pe(nc.sbuf_tensor(f"w1t{p}", [128, NKH * 128], F16)) for p in range(2)]
            est = [pe(nc.sbuf_tensor(f"est{p}", [128, TH], F16)) for p in range(2)]
            pE7 = [pe(nc.psum_tensor(f"pE7_{p}", [128, 1024], F32)) for p in range(2)]
            y_gh = kb.sem("y_gh")
            g_w4 = [kb.sem(f"g_w4_{p}") for p in range(2)]
            y_ef = [kb.sem(f"y_ef{p}") for p in range(2)]

            e2groups = [(s, min(512, TH - s)) for s in range(0, TH, 512)]
            w4_last = [0, 0]
            evE = [[0] * len(e2groups), [0] * len(e2groups)]
            e_stage = [0, 0]
            for gh in range(2):
                def s21(e, en, gh=gh, gate=max(w4_last)):
                    if gh > 0:
                        w(e, en, st_, gate)
                    for kc in range(NKH):
                        i(e.dma_start(
                            out=ghalf[:, kc * TH:(kc + 1) * TH],
                            in_=dr_g.ap()[kc * 128:(kc + 1) * 128,
                                          gh * TH:(gh + 1) * TH]), y_gh, 16)
                S(s21)
                y_gh_tok = y_gh.v

                for m in range(16):
                    p = m % 2

                    def g6(e, en, m=m, p=p, gate=w4_last[p]):
                        w(e, en, st_, gate)
                        src = APx(w1, m * 8192, [[16 * 64 * 128, 128], [1, 8192]])
                        dst = APx(w1t[p], 0, [[NKH * 128, 128], [1, 8192]])
                        i(e.dma_start(out=dst, in_=src), g_w4[p], 16)
                    G(g6)
                    tok_w4 = g_w4[p].v

                    for k2, (s0_, cn) in enumerate(e2groups):
                        def t18(e, en, m=m, p=p, k2=k2, s0_=s0_, cn=cn,
                                tok_w4=tok_w4, y_gh_tok=y_gh_tok, gate=evE[p][k2]):
                            w(e, en, g_w4[p], tok_w4)
                            w(e, en, y_gh, y_gh_tok)
                            w(e, en, sv, gate)
                            for kc in range(NKH):
                                i(e.matmul(pE7[p][:, k2 * 512:k2 * 512 + cn],
                                           w1t[p][:, kc * 128:(kc + 1) * 128],
                                           ghalf[:, kc * TH + s0_:kc * TH + s0_ + cn],
                                           start=(kc == 0), stop=(kc == NKH - 1)), st_)
                        T(t18)
                        w4_last[p] = st_.v
                        tok_e7 = st_.v

                        def v39(e, en, m=m, p=p, k2=k2, s0_=s0_, cn=cn,
                                tok_e7=tok_e7, gate=e_stage[p]):
                            w(e, en, st_, tok_e7)
                            w(e, en, y_ef[p], gate)
                            i(e.tensor_scalar(est[p][:, s0_:s0_ + cn],
                                              pE7[p][:, k2 * 512:k2 * 512 + cn],
                                              b1_col[:, m:m + 1], None, ALU.add), sv)
                        V(v39)
                        evE[p][k2] = sv.v
                    tok_es = sv.v

                    def s22(e, en, m=m, p=p, gh=gh, tok_es=tok_es):
                        w(e, en, sv, tok_es)
                        i(e.dma_start(out=dr_efm.ap()[m * 128:(m + 1) * 128,
                                                      gh * TH:(gh + 1) * TH],
                                      in_=est[p][:, :]), y_ef[p], 16)
                    S(s22)
                    e_stage[p] = y_ef[p].v
            kb.barrier()

        with ExitStack() as pctx:
            pe = pctx.enter_context
            ett = [pe(nc.sbuf_tensor(f"ett{p}", [128, DM], F16)) for p in range(2)]
            x1b = [pe(nc.sbuf_tensor(f"x1b{p}", [128, DM], F32)) for p in range(2)]
            outt = [pe(nc.sbuf_tensor(f"outt{p}", [128, DM], F32)) for p in range(2)]
            y_e2 = [kb.sem(f"y_e2_{p}") for p in range(2)]
            y_out = [kb.sem(f"y_out{p}") for p in range(2)]

            out_stage = [0, 0]
            for tt in range(TOK // 128):
                p = tt % 2

                def s23(e, en, tt=tt, p=p, gate=out_stage[p]):
                    w(e, en, y_out[p], gate)
                    i(e.dma_start_transpose(
                        out=ett[p][:, :],
                        in_=dr_efm.ap()[:, tt * 128:(tt + 1) * 128]), y_e2[p], 16)
                    i(e.dma_start(out=x1b[p][:, :],
                                  in_=dr_x1.ap()[tt * 128:(tt + 1) * 128, :]), y_e2[p], 16)
                S(s23)
                tok_e2r = y_e2[p].v

                def v40(e, en, p=p, tok=tok_e2r, gate=y_out[p].v):
                    w(e, en, y_e2[p], tok)
                    w(e, en, y_out[p], gate)
                    i(e.tensor_add(outt[p][:, :], x1b[p][:, :], ett[p][:, :]), sv)
                V(v40)
                tok_o = sv.v

                def s24(e, en, tt=tt, p=p, tok_o=tok_o):
                    w(e, en, sv, tok_o)
                    i(e.dma_start(out=out.ap()[tt * 128:(tt + 1) * 128, :],
                                  in_=outt[p][:, :]), y_out[p], 16)
                S(s24)
                out_stage[p] = y_out[p].v

            def s25(e, en):
                w(e, en, y_out[0], y_out[0].v)
                w(e, en, y_out[1], y_out[1].v)
            S(s25)

    return nc


_NC_CACHE = {}


def _get_nc(tok, ncores):
    key = (tok, ncores)
    if key not in _NC_CACHE:
        _NC_CACHE[key] = build_nc(tok, ncores)
    return _NC_CACHE[key]


HP = np.array([(rr % 16) * 4 + rr // 16 for rr in range(NH)])
FP = (HP[:, None] * HD + np.arange(HD)[None, :]).reshape(-1)


def _tile_w(wT, mw_last=None):
    K_, M_ = wT.shape
    nkc = K_ // 128
    tmp = wT.reshape(nkc, 128, M_)
    blocks = []
    t0 = 0
    while t0 < M_:
        mw = min(128, M_ - t0)
        blk = tmp[:, :, t0:t0 + mw].transpose(1, 0, 2).reshape(128, nkc * mw)
        blocks.append(blk)
        t0 += mw
    return np.ascontiguousarray(np.concatenate(blocks, axis=1).astype(np.float32))


def make_in_maps(u, weights, tok, ncores):
    ipw0 = weights["in_proj_w"].T.astype(np.float32)
    colperm = np.concatenate([FP, DI + FP,
                              np.arange(2 * DI, 2 * DI + 2 * DS),
                              2 * DI + 2 * DS + HP])
    ipw = _tile_w(np.ascontiguousarray(ipw0[:, colperm]))
    cpm = np.concatenate([FP, np.arange(DI, CD)])
    cwT = np.ascontiguousarray(weights["conv_w"].T.astype(np.float32)[:, cpm])
    cb = np.ascontiguousarray(weights["conv_b"].astype(np.float32)[cpm].reshape(34, 128))
    dtb = np.ascontiguousarray(weights["dt_bias"].astype(np.float32)[HP].reshape(NH, 1))
    alog = np.ascontiguousarray(weights["A_log"].astype(np.float32)[HP].reshape(NH, 1))
    dvec = np.ascontiguousarray(weights["D"].astype(np.float32)[HP].reshape(1, NH))
    nw = np.ascontiguousarray(weights["norm_weight"].astype(np.float32)[FP].reshape(1, DI))
    opw = _tile_w(np.ascontiguousarray(weights["out_proj_w"].T.astype(np.float32)[FP, :]))
    w0 = _tile_w(np.ascontiguousarray(weights["W0"].T.astype(np.float32)))
    b0 = np.ascontiguousarray(weights["bias0"].astype(np.float32).reshape(HID // 128, 128))
    w1 = _tile_w(np.ascontiguousarray(weights["W1"].T.astype(np.float32)))
    b1 = np.ascontiguousarray(weights["bias1"].astype(np.float32).reshape(DM // 128, 128))
    in_maps = []
    for c in range(ncores):
        seq, half = c // 2, c % 2
        if half == 0:
            u_core = np.concatenate(
                [np.zeros((HALO, DM), np.float32), u[seq, 0:tok]], axis=0)
        else:
            u_core = u[seq, tok - HALO:2 * tok]
        flag = np.full((128, 1), float(half), np.float32)
        in_maps.append(dict(u=np.ascontiguousarray(u_core.astype(np.float32)),
                            ipw=ipw, cwT=cwT, cb=cb, dtb=dtb, alog=alog,
                            dvec=dvec, nw=nw, opw=opw, w0=w0, b0=b0, w1=w1,
                            b1=b1, flag=flag))
    return in_maps


def kernel(L0_attn_u, L0_attn_in_proj_w, L0_attn_conv_w, L0_attn_conv_b,
           L0_attn_dt_bias, L0_attn_A_log, L0_attn_D, L0_attn_norm_weight,
           L0_attn_out_proj_w, L0_ffn_W0, L0_ffn_bias0, L0_ffn_W1,
           L0_ffn_bias1):
    u = np.ascontiguousarray(np.asarray(L0_attn_u, dtype=np.float32))
    B, L, _ = u.shape
    TOK = L // 2
    ncores = 8
    weights = dict(in_proj_w=np.asarray(L0_attn_in_proj_w),
                   conv_w=np.asarray(L0_attn_conv_w),
                   conv_b=np.asarray(L0_attn_conv_b),
                   dt_bias=np.asarray(L0_attn_dt_bias),
                   A_log=np.asarray(L0_attn_A_log),
                   D=np.asarray(L0_attn_D),
                   norm_weight=np.asarray(L0_attn_norm_weight),
                   out_proj_w=np.asarray(L0_attn_out_proj_w),
                   W0=np.asarray(L0_ffn_W0), bias0=np.asarray(L0_ffn_bias0),
                   W1=np.asarray(L0_ffn_W1), bias1=np.asarray(L0_ffn_bias1))
    in_maps = make_in_maps(u, weights, TOK, ncores)
    nc = _get_nc(TOK, ncores)
    res = run_bass_kernel_spmd(nc, in_maps, core_ids=list(range(ncores)))
    outp = np.empty((B, L, DM), np.float32)
    for c in range(ncores):
        seq, half = c // 2, c % 2
        outp[seq, half * TOK:(half + 1) * TOK] = res.results[c]["out"]
    return outp
